# version 08
# baseline (speedup 1.0000x reference)
import math
from contextlib import ExitStack

import numpy as np
import concourse.bass as bass
import concourse.mybir as mybir
from concourse.bass_utils import run_bass_kernel_spmd

F32 = mybir.dt.float32
BF16 = mybir.dt.bfloat16
I32 = mybir.dt.int32
AF = mybir.ActivationFunctionType
ALU = mybir.AluOpType

D = 2048
NH = 16
HD = 128
NCORES = 8
LN_EPS = 1e-5
ATT_SCALE = 1.0 / math.sqrt(HD)



class Prog:
    ENGS = ("t", "v", "s", "g", "y")
    _ATTR = {"t": "tensor", "v": "vector", "s": "scalar", "g": "gpsimd",
             "y": "sync"}

    def __init__(self, nc):
        self.nc = nc
        self.steps = {e: [] for e in self.ENGS}
        self.semh = {}
        self.cnt = {}
        self.waited = {e: {} for e in self.ENGS}
        self.base = []

    def sem(self, name):
        if name not in self.semh:
            self.semh[name] = self.nc.alloc_semaphore(name)
            self.cnt[name] = 0
        return name

    def tok(self, name):
        self.sem(name)
        return (name, self.cnt[name])

    def wait(self, e, token):
        if token is None:
            return
        name, val = token
        if val <= 0:
            return
        if self.waited[e].get(name, 0) >= val:
            return
        self.waited[e][name] = val
        h = self.semh[name]
        self.steps[e].append(lambda eng, h=h, v=val: eng.wait_ge(h, v))

    def op(self, e, meth, *args, waits=(), inc=None, dn=1, **kw):
        for tk in self.base:
            self.wait(e, tk)
        for tk in waits:
            self.wait(e, tk)
        if inc is None:
            inc = "_" + e
        inc = self.sem(inc)
        self.cnt[inc] += dn
        h = self.semh[inc]

        def go(eng, meth=meth, args=args, kw=kw, h=h, dn=dn):
            getattr(eng, meth)(*args, **kw).then_inc(h, dn)

        self.steps[e].append(go)
        return (inc, self.cnt[inc])

    def dma(self, e, out, in_, stream, waits=(), **kw):
        return self.op(e, "dma_start", waits=waits, inc=stream, dn=16,
                       out=out, in_=in_, **kw)

    def finalize(self):
        with self.nc.Block() as block:
            for e in self.ENGS:
                if not self.steps[e]:
                    continue
                steps = self.steps[e]

                def body(eng, steps=steps):
                    for f in steps:
                        f(eng)

                getattr(block, self._ATTR[e])(body)


class Consts:
    def __init__(self, p, fvals=()):
        nc = p.nc
        self.iota = nc.alloc_sbuf_tensor("c_iota", [128, 128], I32)
        self.ident = nc.alloc_sbuf_tensor("c_ident", [128, 128], BF16)
        self.ones_r = nc.alloc_sbuf_tensor("c_ones_r", [1, 512], BF16)
        self.ones_c = nc.alloc_sbuf_tensor("c_ones_c", [128, 1], BF16)
        self.ones_rf = nc.alloc_sbuf_tensor("c_ones_rf", [1, 512], F32)
        self.ones_cf = nc.alloc_sbuf_tensor("c_ones_cf", [128, 1], F32)
        t0 = p.op("g", "iota", self.iota[:, :], [[1, 128]], base=0,
                  channel_multiplier=-1)
        self.tok = [
            p.op("v", "tensor_scalar", self.ident[:, :], self.iota[:, :], 0,
                 None, ALU.is_equal, waits=[t0]),
            p.op("v", "memset", self.ones_r[:, :], 1.0),
            p.op("v", "memset", self.ones_c[:, :], 1.0),
            p.op("v", "memset", self.ones_rf[:, :], 1.0),
            p.op("v", "memset", self.ones_cf[:, :], 1.0),
        ]
        for i, val in enumerate(fvals):
            cst = nc.alloc_sbuf_tensor(f"c_f{i}", [128, 1], F32)
            self.tok.append(p.op("v", "memset", cst[:, :], float(val)))
            nc.const_aps.register(F32, float(val), cst[:, :])
        p.base = list(self.tok)


def emit_ln(p, x_ap, out_ap, junk_ap, st, waits, eps=LN_EPS, n=D, sc=0,
            junk2_ap=None, sum_on_act=False):
    j2 = junk2_ap if junk2_ap is not None else junk_ap
    t_sq = p.op("s", "activation", junk_ap, x_ap, AF.Square,
                accum_out=st[:, sc + 0:sc + 1], waits=waits)
    if sum_on_act:
        t_s1 = p.op("s", "activation", j2, x_ap, AF.Copy,
                    accum_out=st[:, sc + 1:sc + 2], waits=list(waits))
    else:
        t_s1 = p.op("v", "tensor_scalar", j2, x_ap, 1.0, None, ALU.mult,
                    ALU.add, accum_out=st[:, sc + 1:sc + 2],
                    waits=list(waits) +
                    ([] if junk2_ap is not None else [t_sq]))
    t_m = p.op("v", "tensor_scalar", st[:, sc + 2:sc + 3],
               st[:, sc + 1:sc + 2], 1.0 / n, None, ALU.mult, waits=[t_s1])
    t_mm = p.op("v", "tensor_tensor", st[:, sc + 3:sc + 4],
                st[:, sc + 2:sc + 3], st[:, sc + 2:sc + 3], ALU.mult,
                waits=[t_m])
    t_v = p.op("v", "scalar_tensor_tensor", st[:, sc + 4:sc + 5],
               st[:, sc + 0:sc + 1], 1.0 / n, st[:, sc + 3:sc + 4],
               ALU.mult, ALU.subtract, waits=[t_mm, t_sq])
    t_sd = p.op("s", "activation", st[:, sc + 5:sc + 6],
                st[:, sc + 4:sc + 5], AF.Sqrt, bias=float(eps), waits=[t_v])
    t_r = p.op("v", "reciprocal", st[:, sc + 6:sc + 7],
               st[:, sc + 5:sc + 6], waits=[t_sd])
    t_n = p.op("v", "tensor_scalar", out_ap, x_ap, st[:, sc + 2:sc + 3],
               st[:, sc + 6:sc + 7], ALU.subtract, ALU.mult, waits=[t_r])
    return t_n


def wcol_ap(w_dram, o0, width):
    return w_dram.ap()[:, o0:o0 + width].rearrange("(c p) o -> p c o", p=128)


def chunk3(sb_ap, width, nch=16):
    return sb_ap.rearrange("p (c o) -> p c o", c=nch)


def fm_ap(fm_dram, t0, width):
    return fm_dram.ap()[:, t0:t0 + width].rearrange("(c p) t -> p c t",
                                                    p=128)




def build_l0_attn(S=4096, SQ=1024):
    nT = S // 128
    nTB = S // 512
    nQ = SQ // 128
    TIDX = min(max(1, int(S * 0.2)), S - 1)
    assert S % 512 == 0 and SQ % 512 == 0 and nT <= 128

    nc = bass.Bass("TRN2", target_bir_lowering=False)
    p = Prog(nc)

    x_d = nc.dram_tensor("x_full", [S, D], F32, kind="ExternalInput")
    wq_d = nc.dram_tensor("Wq", [D, D], F32, kind="ExternalInput")
    wk_d = nc.dram_tensor("Wk", [D, D], F32, kind="ExternalInput")
    wv_d = nc.dram_tensor("Wv", [D, D], F32, kind="ExternalInput")
    wo_d = nc.dram_tensor("Wo", [D, D], F32, kind="ExternalInput")
    bq_d = nc.dram_tensor("bq", [1, D], F32, kind="ExternalInput")
    bk_d = nc.dram_tensor("bk", [1, D], F32, kind="ExternalInput")
    bv_d = nc.dram_tensor("bv", [1, D], F32, kind="ExternalInput")
    bo_d = nc.dram_tensor("bo", [1, D], F32, kind="ExternalInput")
    wi_d = nc.dram_tensor("Wi", [1, D], F32, kind="ExternalInput")
    bi_d = nc.dram_tensor("bi", [1, 1], F32, kind="ExternalInput")
    x1_d = nc.dram_tensor("x1", [SQ, D], F32, kind="ExternalOutput")

    xn_d = nc.dram_tensor("xn_scr", [D, S], BF16)
    k_d = nc.dram_tensor("k_scr", [D, S], BF16)
    v_d = nc.dram_tensor("v_scr", [S, D], BF16)
    mr_d = nc.dram_tensor("mr_scr", [nT, 128], BF16)

    q_fm = nc.alloc_sbuf_tensor("q_fm", [128, 16 * SQ], BF16)
    at_fm = nc.alloc_sbuf_tensor("at_fm", [128, 16 * SQ], BF16)
    ev = nc.alloc_sbuf_tensor("ev", [128, D], F32)
    ln_o = nc.alloc_sbuf_tensor("ln_o", [128, D], F32)
    x_t = [nc.alloc_sbuf_tensor(f"x_t{i}", [128, D], F32) for i in range(2)]
    bjunk = nc.alloc_sbuf_tensor("bjunk", [128, D], BF16)
    st = nc.alloc_sbuf_tensor("st", [128, 24], F32)
    imp_sb = nc.alloc_sbuf_tensor("imp_sb", [128, nT], F32)
    bis = nc.alloc_sbuf_tensor("bis", [128, 8], F32)
    bmask = nc.alloc_sbuf_tensor("bmask", [128, 2], I32)
    cnt_c = nc.alloc_sbuf_tensor("cnt_c", [128, 1], F32)
    c11 = nc.alloc_sbuf_tensor("c11", [1, 1], F32)
    mrow = nc.alloc_sbuf_tensor("mrow", [1, S], BF16)
    mbc = nc.alloc_sbuf_tensor("mbc", [128, S], BF16)
    mcol = nc.alloc_sbuf_tensor("mcol", [128, nT], F32)
    mcolb = nc.alloc_sbuf_tensor("mcolb", [128, nT], BF16)
    mjunk = nc.alloc_sbuf_tensor("mjunk", [128, nT], BF16)
    mstage = nc.alloc_sbuf_tensor("mstage", [128, 128], BF16)
    probs = [nc.alloc_sbuf_tensor(f"probs{i}", [128, 512], BF16)
             for i in range(2)]
    kv_out = [nc.alloc_sbuf_tensor(f"kv_out{i}", [128, 512], BF16)
              for i in range(2)]
    dnr = nc.alloc_sbuf_tensor("dnr", [1, 512], F32)
    dnb = nc.alloc_sbuf_tensor("dnb", [128, 512], F32)
    bias_sb = nc.alloc_sbuf_tensor("bias_sb", [1, D], BF16)
    bif = nc.alloc_sbuf_tensor("bif", [1, 1], F32)
    wirow = nc.alloc_sbuf_tensor("wirow", [1, D], F32)
    swi_c = nc.alloc_sbuf_tensor("swi_c", [128, 1], F32)
    bi_c = nc.alloc_sbuf_tensor("bi_c", [128, 1], F32)

    tp_ps = [nc.alloc_psum_tensor(f"tp{i}", [128, 128], BF16) for i in range(2)]
    mm_ps = [nc.alloc_psum_tensor(f"mm{i}", [128, 512], F32) for i in range(4)]
    dn_ps = nc.alloc_psum_tensor("dn_ps", [1, 512], F32)
    db_ps = nc.alloc_psum_tensor("db_ps", [128, 512], F32)

    C = Consts(p, fvals=(LN_EPS, ATT_SCALE))
    mm_free = {}

    t_wi = p.dma("y", wirow[0:1, :], wi_d[0:1, :], "ldwi")
    t_bi = p.dma("y", bif[0:1, :], bi_d[0:1, :], "ldbi")

    with nc.sbuf_tensor("xf_t0", [128, 16 * 128], BF16) as xf_t0, \
            nc.sbuf_tensor("xf_t1", [128, 16 * 128], BF16) as xf_t1, \
            nc.sbuf_tensor("bj0", [128, D], BF16) as bj0, \
            nc.sbuf_tensor("bj1", [128, D], BF16) as bj1, \
            nc.sbuf_tensor("jk0", [128, D], BF16) as jk0, \
            nc.sbuf_tensor("jk1", [128, D], BF16) as jk1, \
            nc.sbuf_tensor("jk2", [128, D], BF16) as jk2, \
            nc.sbuf_tensor("jk3", [128, D], BF16) as jk3, \
            nc.sbuf_tensor("ij0", [128, D], F32) as ij0, \
            nc.sbuf_tensor("ij1", [128, D], F32) as ij1, \
            nc.sbuf_tensor("wi_bc", [128, D], F32) as wi_bc:
        xf = [xf_t0, xf_t1]
        bj = [bj0, bj1]
        jka = [jk0, jk1]
        jkb = [jk2, jk3]
        ij = [ij0, ij1]
        wb_toks = []
        for ob in range(4):
            t_wb = p.op("t", "matmul", db_ps[:, :], C.ones_rf[0:1, 0:128],
                        wirow[0:1, ob * 512:(ob + 1) * 512], start=True,
                        stop=True, waits=[t_wi] + (wb_toks[-1:] or []))
            wb_toks.append(p.op("v", "tensor_copy",
                                wi_bc[:, ob * 512:(ob + 1) * 512],
                                db_ps[:, :], waits=[t_wb]))
        t_bb = p.op("t", "matmul", db_ps[:, 0:1], C.ones_rf[0:1, 0:128],
                    bif[0:1, 0:1], start=True, stop=True,
                    waits=[t_bi] + wb_toks[-1:])
        t_bic = p.op("v", "tensor_copy", bi_c[:, :], db_ps[:, 0:1],
                     waits=[t_bb])
        t_swi = p.op("v", "tensor_reduce", swi_c[:, :], wi_bc[:, :],
                     mybir.AxisListType.X, ALU.add, waits=wb_toks)
        imp_setup = [t_bic, t_swi]

        x_use = [None, None]
        bj_use = [None, None]
        xf_use = [None, None]
        ij_use = [None, None]
        ln_tok = [None, None]
        st_xn = [None, None]
        imp_last = None
        for ti in range(nT):
            q = ti % 2
            t_ld = p.dma("y", x_t[q][:, :], x_d[ti * 128:(ti + 1) * 128, :],
                         f"ldx{q}", waits=list(x_use[q] or ()))
            t_n = emit_ln(p, x_t[q][:, :], bj[q][:, :], jka[q][:, :], st,
                          waits=[t_ld] + ([bj_use[q]] if bj_use[q] else []) +
                          ([ln_tok[q]] if ln_tok[q] else []),
                          sc=8 * q, junk2_ap=jkb[q][:, :])
            ln_tok[q] = t_n
            t_mu = p.op("g", "tensor_tensor", ij[q][:, :], x_t[q][:, :],
                        wi_bc[:, :], ALU.mult,
                        waits=[t_ld] + imp_setup +
                        ([ij_use[q]] if ij_use[q] else []))
            t_us = p.op("v", "tensor_reduce", st[:, 16 + 4 * q:17 + 4 * q],
                        ij[q][:, :], mybir.AxisListType.X, ALU.add,
                        waits=[t_mu])
            ij_use[q] = t_us
            x_use[q] = (t_n, t_mu, p.tok("_s"))
            t_i1 = p.op("v", "tensor_tensor", st[:, 17 + 4 * q:18 + 4 * q],
                        st[:, 8 * q + 2:8 * q + 3], swi_c[:, :], ALU.mult,
                        waits=[t_us, t_n])
            t_i2 = p.op("v", "scalar_tensor_tensor",
                        st[:, 18 + 4 * q:19 + 4 * q],
                        st[:, 17 + 4 * q:18 + 4 * q], -1.0,
                        st[:, 16 + 4 * q:17 + 4 * q], ALU.mult, ALU.add,
                        waits=[t_i1])
            t_i3 = p.op("v", "tensor_tensor", st[:, 19 + 4 * q:20 + 4 * q],
                        st[:, 18 + 4 * q:19 + 4 * q],
                        st[:, 8 * q + 6:8 * q + 7], ALU.mult, waits=[t_i2])
            imp_last = p.op("v", "tensor_tensor", imp_sb[:, ti:ti + 1],
                            st[:, 19 + 4 * q:20 + 4 * q], bi_c[:, :],
                            ALU.add, waits=[t_i3])
            cp_toks = []
            for c in range(16):
                bk = tp_ps[c % 2]
                w = [t_n]
                if len(cp_toks) >= 2:
                    w.append(cp_toks[-2])
                if c == 0 and st_xn[q]:
                    w.append(st_xn[q])
                t_tr = p.op("t", "transpose", bk[:, :],
                            bj[q][:, c * 128:(c + 1) * 128], C.ident[:, :],
                            waits=w)
                if c % 2 == 0:
                    cp_toks.append(p.op("v", "tensor_copy",
                                        xf[q][:, c * 128:(c + 1) * 128],
                                        bk[:, :], waits=[t_tr]))
                else:
                    cp_toks.append(p.op("s", "activation",
                                        xf[q][:, c * 128:(c + 1) * 128],
                                        bk[:, :], AF.Copy, waits=[t_tr]))
            bj_use[q] = p.tok("_t")
            st_xn[q] = p.dma("s", fm_ap(xn_d, ti * 128, 128),
                             chunk3(xf[q].ap(), 128), f"stxn{q}",
                             waits=cp_toks)
    ln_done = [p.tok("_v"), p.tok("_t"), p.tok("_s"), p.tok("_g"),
               p.tok("stxn0"), p.tok("stxn1")]


    t0 = p.op("v", "memset", bis[:, 0:1], -1000.0, waits=[imp_last])
    t1 = p.op("v", "memset", bis[:, 1:2], 1000.0)
    prev = [t0, t1]
    for _ in range(50):
        t_a = p.op("v", "tensor_scalar", bis[:, 3:4], bis[:, 1:2], 0.5, None,
                   ALU.mult, waits=prev)
        t_mid = p.op("v", "scalar_tensor_tensor", bis[:, 2:3], bis[:, 0:1],
                     0.5, bis[:, 3:4], ALU.mult, ALU.add, waits=[t_a])
        t_c = p.op("v", "tensor_scalar", mjunk[:, 0:nT], imp_sb[:, 0:nT],
                   bis[:, 2:3], None, ALU.is_gt, ALU.add,
                   accum_out=cnt_c[:, 0:1], waits=[t_mid])
        t_s = p.op("t", "matmul", dn_ps[0:1, 0:1], C.ones_cf[:, :],
                   cnt_c[:, 0:1], start=True, stop=True,
                   waits=[t_c] + C.tok + ([mm_free["dn"]]
                                          if "dn" in mm_free else []))
        t_cp = p.op("v", "tensor_copy", c11[0:1, 0:1], dn_ps[0:1, 0:1],
                    waits=[t_s])
        mm_free["dn"] = t_cp
        t_bc = p.op("t", "matmul", db_ps[:, 0:1], C.ones_rf[0:1, 0:128],
                    c11[0:1, 0:1], start=True, stop=True,
                    waits=[t_cp] + ([mm_free["db"]] if "db" in mm_free else []))
        t_cd = p.op("v", "tensor_scalar", bmask[:, 0:1], db_ps[:, 0:1],
                    TIDX + 0.5, None, ALU.is_lt, waits=[t_bc])
        t_ic = p.op("v", "tensor_scalar", bmask[:, 1:2], db_ps[:, 0:1],
                    TIDX + 0.5, None, ALU.is_ge, waits=[t_cd])
        mm_free["db"] = t_ic
        t_h = p.op("v", "copy_predicated", bis[:, 1:2], bmask[:, 0:1],
                   bis[:, 2:3], waits=[t_ic])
        t_l = p.op("v", "copy_predicated", bis[:, 0:1], bmask[:, 1:2],
                   bis[:, 2:3], waits=[t_h])
        prev = [t_l]
    t_mc = p.op("v", "tensor_scalar", mcol[:, 0:nT], imp_sb[:, 0:nT],
                bis[:, 1:2], None, ALU.is_gt, waits=prev)
    t_mcb = p.op("v", "tensor_copy", mcolb[:, 0:nT], mcol[:, 0:nT],
                 waits=[t_mc])
    t_tr = p.op("t", "transpose", tp_ps[0][0:nT, :], mcolb[:, 0:nT],
                C.ident[:, :], waits=[t_mcb] + C.tok)
    t_ms = p.op("v", "tensor_copy", mstage[0:nT, :], tp_ps[0][0:nT, :],
                waits=[t_tr])
    t_st = p.dma("s", mr_d[:, :], mstage[0:nT, :], "mrb", waits=[t_ms])
    t_lr = p.dma("s", mrow[0:1, 0:S],
                 mr_d.ap().rearrange("(o a) b -> o (a b)", o=1), "mrb2",
                 waits=[t_st])
    for tb in range(nTB):
        t_b = p.op("t", "matmul", db_ps[:, :], C.ones_r[0:1, 0:128],
                   mrow[0:1, tb * 512:(tb + 1) * 512], start=True, stop=True,
                   waits=[t_lr, mm_free["db"]])
        mm_free["db"] = p.op("v", "tensor_copy",
                             mbc[:, tb * 512:(tb + 1) * 512], db_ps[:, :],
                             waits=[t_b])
    mask_done = [p.tok("_v")]

    kv_tok = [None, None]
    xuse = {}

    def proj_pass(kind, ph, xn_blk, wbuf, ldw_tok, bias_tok, xs):
        nblk = (SQ if kind == "Q" else S) // 512
        for tb in range(nblk):
            xb = tb % 2
            t_x = p.dma("s", chunk3(xn_blk[xb].ap(), 512),
                        fm_ap(xn_d, tb * 512, 512), f"{xs}{xb}",
                        waits=ln_done + ([xuse[xb]] if xb in xuse else []))
            if kind in ("K", "Q"):
                for o8 in range(8):
                    o = ph * 8 + o8
                    bidx = o8 % 2
                    bank = mm_ps[bidx]
                    w0 = [ldw_tok, bias_tok, t_x] + \
                        ([mm_free.get(bidx)] if mm_free.get(bidx) else [])
                    for c in range(16):
                        p.op("t", "matmul", bank[:, :],
                             wbuf[:, c * 1024 + o8 * 128:
                                  c * 1024 + (o8 + 1) * 128],
                             xn_blk[xb][:, c * 512:(c + 1) * 512],
                             start=(c == 0), stop=False,
                             waits=w0 if c == 0 else ())
                    t_mm = p.op("t", "matmul", bank[:, :],
                                bias_sb[0:1, o * 128:(o + 1) * 128],
                                C.ones_r[0:1, 0:512], start=False, stop=True)
                    if kind == "K":
                        ob = (tb * 8 + o8) % 2
                        t_e = p.op("v", "tensor_tensor", kv_out[ob][:, :],
                                   bank[:, :],
                                   mbc[:, tb * 512:(tb + 1) * 512], ALU.mult,
                                   waits=[t_mm] + mask_done +
                                   ([kv_tok[ob]] if kv_tok[ob] else []))
                        kv_tok[ob] = p.dma(
                            "y", k_d[o * 128:(o + 1) * 128,
                                     tb * 512:(tb + 1) * 512],
                            kv_out[ob][:, :], f"stkv{ob}", waits=[t_e])
                    else:
                        t_e = p.op("v", "tensor_copy",
                                   q_fm[:, o * SQ + tb * 512:
                                        o * SQ + (tb + 1) * 512],
                                   bank[:, :], waits=[t_mm])
                    mm_free[bidx] = t_e
                xuse[xb] = p.tok("_t")
            else:
                for tc4 in range(4):
                    for obl in range(2):
                        og = ph * 2 + obl
                        bidx = 2 + (tc4 * 2 + obl) % 2
                        bank = mm_ps[bidx]
                        w0 = [ldw_tok, bias_tok, t_x] + \
                            ([mm_free.get(bidx)] if mm_free.get(bidx) else [])
                        for c in range(16):
                            p.op("t", "matmul", bank[:, :],
                                 xn_blk[xb][:, c * 512 + tc4 * 128:
                                            c * 512 + (tc4 + 1) * 128],
                                 wbuf[:, c * 1024 + obl * 512:
                                      c * 1024 + (obl + 1) * 512],
                                 start=(c == 0), stop=False,
                                 waits=w0 if c == 0 else ())
                        t_mm = p.op("t", "matmul", bank[:, :],
                                    C.ones_r[0:1, 0:128],
                                    bias_sb[0:1, og * 512:(og + 1) * 512],
                                    start=False, stop=True)
                        tglob = tb * 4 + tc4
                        ob = (tglob * 2 + obl) % 2
                        t_e = p.op("v", "tensor_scalar", kv_out[ob][:, :],
                                   bank[:, :], mcol[:, tglob:tglob + 1],
                                   None, ALU.mult,
                                   waits=[t_mm] + mask_done +
                                   ([kv_tok[ob]] if kv_tok[ob] else []))
                        kv_tok[ob] = p.dma(
                            "y", v_d[tglob * 128:(tglob + 1) * 128,
                                     og * 512:(og + 1) * 512],
                            kv_out[ob][:, :], f"stkv{ob}", waits=[t_e])
                        mm_free[bidx] = t_e
            xuse[xb] = p.tok("_t")

    with nc.sbuf_tensor("wbuf", [128, 16 * 1024], BF16) as wbuf, \
            nc.sbuf_tensor("xnb0", [128, 16 * 512], BF16) as xnb0, \
            nc.sbuf_tensor("xnb1", [128, 16 * 512], BF16) as xnb1:
        xn_blk = [xnb0, xnb1]
        for kind, w_dram, b_dram in (("K", wk_d, bk_d), ("V", wv_d, bv_d),
                                     ("Q", wq_d, bq_d)):
            t_b = p.dma("g", bias_sb[0:1, :], b_dram[0:1, :], "ldbias",
                        waits=[p.tok("_t")])
            for ph in range(2):
                t_w = p.dma("g", chunk3(wbuf.ap(), 1024),
                            wcol_ap(w_dram, ph * 1024, 1024),
                            "ldwbuf", waits=[p.tok("_t")] + ln_done)
                proj_pass(kind, ph, xn_blk, wbuf, t_w, t_b, f"lx{kind}{ph}")

    proj_done = [p.tok("_t"), p.tok("_v"), p.tok("stkv0"), p.tok("stkv1")]

    nqh = max(1, SQ // 512)
    QW = min(512, SQ)
    with nc.sbuf_tensor("kh0", [128, S], BF16) as kh0, \
            nc.sbuf_tensor("kh1", [128, S], BF16) as kh1, \
            nc.sbuf_tensor("vh0", [128, S], BF16) as vh0, \
            nc.sbuf_tensor("vh1", [128, S], BF16) as vh1:
        kh = [kh0, kh1]
        vh = [vh0, vh1]
        kh_use = [None, None]
        kh_tok = [None, None]
        vh_tok = [None, None]
        for h in range(NH):
            b = h % 2
            kh_tok[b] = p.dma("y", kh[b][:, :], k_d[h * 128:(h + 1) * 128, :],
                              f"lkh{b}", waits=proj_done +
                              ([kh_use[b]] if kh_use[b] else []))
            vh_tok[b] = p.dma("g", chunk3(vh[b].ap(), 128, nch=nT),
                              v_d.ap()[:, h * 128:(h + 1) * 128]
                              .rearrange("(kc p) o -> p kc o", p=128),
                              f"lvh{b}", waits=proj_done +
                              ([kh_use[b]] if kh_use[b] else []))
            for qh in range(nqh):
                oidx = 2 + qh % 2
                o_bank = mm_ps[oidx]
                pr_use = [None, None]
                ex_tok = [None, None]
                t_dn = None
                t_av = None

                def emit_sc(kc):
                    pb = kc % 2
                    sc_bank = mm_ps[pb]
                    w_sc = [kh_tok[b], vh_tok[b]]
                    if ex_tok[pb]:
                        w_sc.append(ex_tok[pb])
                    elif mm_free.get(pb):
                        w_sc.append(mm_free.get(pb))
                    return p.op("t", "matmul", sc_bank[:, 0:QW],
                                kh[b][:, kc * 128:(kc + 1) * 128],
                                q_fm[:, h * SQ + qh * 512:
                                     h * SQ + qh * 512 + QW],
                                start=True, stop=True, skip_group_check=True,
                                waits=w_sc)

                sc_tok = {0: emit_sc(0)}
                for kc in range(nT):
                    pb = kc % 2
                    sc_bank = mm_ps[pb]
                    ex_tok[pb] = p.op("s", "activation", probs[pb][:, 0:QW],
                                      sc_bank[:, 0:QW], AF.Exp,
                                      scale=float(ATT_SCALE),
                                      waits=[sc_tok.pop(kc)] +
                                      ([pr_use[pb]] if pr_use[pb] else []))
                    if kc + 1 < nT:
                        sc_tok[kc + 1] = emit_sc(kc + 1)
                    w_dn = [ex_tok[pb]]
                    if kc == 0:
                        if mm_free.get("dn"):
                            w_dn.append(mm_free["dn"])
                        if mm_free.get(oidx):
                            w_dn.append(mm_free[oidx])
                    t_dn = p.op("t", "matmul", dn_ps[0:1, 0:QW],
                                C.ones_c[:, :], probs[pb][:, 0:QW],
                                start=(kc == 0), stop=(kc == nT - 1),
                                skip_group_check=True, waits=w_dn)
                    t_av = p.op("t", "matmul", o_bank[:, 0:QW],
                                vh[b][:, kc * 128:(kc + 1) * 128],
                                probs[pb][:, 0:QW],
                                start=(kc == 0), stop=(kc == nT - 1),
                                skip_group_check=True)
                    pr_use[pb] = t_av
                mm_free[0] = ex_tok[0]
                mm_free[1] = ex_tok[1]
                t_dr = p.op("v", "reciprocal", dnr[0:1, 0:QW],
                            dn_ps[0:1, 0:QW], waits=[t_dn, t_av])
                mm_free["dn"] = t_dr
                t_db = p.op("t", "matmul", db_ps[:, 0:QW],
                            C.ones_rf[0:1, 0:128], dnr[0:1, 0:QW],
                            start=True, stop=True,
                            waits=[t_dr] + ([mm_free["db"]]
                                            if "db" in mm_free else []))
                t_e1 = p.op("s", "activation", ev[:, 0:QW], o_bank[:, 0:QW],
                            AF.Copy,
                            waits=[t_av] + ([mm_free["ev"]]
                                            if "ev" in mm_free else []))
                mm_free[oidx] = t_e1
                t_e2 = p.op("v", "tensor_copy", dnb[:, 0:QW], db_ps[:, 0:QW],
                            waits=[t_db])
                mm_free["db"] = t_e2
                mm_free["ev"] = p.op(
                    "v", "tensor_tensor",
                    at_fm[:, h * SQ + qh * 512:h * SQ + qh * 512 + QW],
                    ev[:, 0:QW], dnb[:, 0:QW], ALU.mult,
                    waits=[t_e1, t_e2])
            kh_use[b] = p.tok("_t")
    attn_done = [p.tok("_v"), p.tok("_s"), p.tok("_t")]

    with nc.sbuf_tensor("wo_sb", [128, 16 * D], BF16) as wo_sb:
        t_wo = p.dma("g", chunk3(wo_sb.ap(), D), wcol_ap(wo_d, 0, D), "ldwo",
                     waits=attn_done)
        t_bo = p.dma("g", bias_sb[0:1, :], bo_d[0:1, :], "ldbias",
                     waits=attn_done)
        x_use = [None, None]
        st_tok = [None, None]
        prev_ln = None
        for tc in range(nQ):
            xb = tc % 2
            ev_toks = []
            for o4 in range(4):
                bidx = o4 % 2
                bank = mm_ps[bidx]
                w0 = [t_wo, t_bo] + attn_done + \
                    ([mm_free.get(bidx)] if mm_free.get(bidx) else [])
                for c in range(16):
                    p.op("t", "matmul", bank[:, :],
                         at_fm[:, c * SQ + tc * 128:c * SQ + (tc + 1) * 128],
                         wo_sb[:, c * D + o4 * 512:c * D + (o4 + 1) * 512],
                         start=(c == 0), stop=False,
                         waits=w0 if c == 0 else ())
                t_mm = p.op("t", "matmul", bank[:, :], C.ones_r[0:1, 0:128],
                            bias_sb[0:1, o4 * 512:(o4 + 1) * 512],
                            start=False, stop=True)
                w_e = [t_mm]
                if o4 == 0 and prev_ln:
                    w_e.append(prev_ln)
                if o4 == 0 and "ev" in mm_free:
                    w_e.append(mm_free.pop("ev"))
                t_e = p.op("s", "activation", ev[:, o4 * 512:(o4 + 1) * 512],
                           bank[:, :], AF.Copy, waits=w_e)
                mm_free[bidx] = t_e
                ev_toks.append(t_e)
            t_ld = p.dma("y", x_t[xb][:, :], x_d[tc * 128:(tc + 1) * 128, :],
                         f"lre{xb}",
                         waits=([x_use[xb]] if x_use[xb] else []) +
                         ([st_tok[xb]] if st_tok[xb] else []) +
                         ([p.tok("_v")] if tc < 2 else []))
            t_n = emit_ln(p, ev[:, :], ln_o[:, :], bjunk[:, :], st,
                          waits=ev_toks + ([prev_ln] if prev_ln else []))
            prev_ln = t_n
            t_add = p.op("v", "tensor_tensor", x_t[xb][:, :], x_t[xb][:, :],
                         ln_o[:, :], ALU.add,
                         waits=[t_n, t_ld] +
                         ([st_tok[xb]] if st_tok[xb] else []))
            x_use[xb] = t_add
            st_tok[xb] = p.dma("y", x1_d[tc * 128:(tc + 1) * 128, :],
                               x_t[xb][:, :], f"stx{xb}", waits=[t_add])
    p.op("y", "nop", waits=[p.tok("stx0"), p.tok("stx1")])

    p.finalize()
    return nc






def build_ffn0_parts():
    R = 1024
    nc = bass.Bass("TRN2", target_bir_lowering=False)
    p = Prog(nc)
    ins = [nc.dram_tensor(n, [R, D], F32, kind="ExternalInput")
           for n in ("up_sl", "gate_sl", "down_sl")]
    out_d = nc.dram_tensor("parts", [1, 4], F32, kind="ExternalOutput")

    w_t = [nc.alloc_sbuf_tensor(f"w_t{i}", [128, D], F32) for i in range(2)]
    junk = nc.alloc_sbuf_tensor("junk", [128, D], BF16)
    acc = nc.alloc_sbuf_tensor("acc", [128, 26], F32)
    red = nc.alloc_sbuf_tensor("red", [128, 4], F32)
    o_sb = nc.alloc_sbuf_tensor("o_sb", [1, 4], F32)
    ps = nc.alloc_psum_tensor("ps", [1, 4], F32)
    C = Consts(p)

    use_tok = [None, None]
    col = 0
    toks = []
    for mi, w_d in enumerate(ins):
        for ti in range(R // 128):
            b = ti % 2
            t_ld = p.dma("y", w_t[b][:, :], w_d[ti * 128:(ti + 1) * 128, :],
                         f"ld{b}", waits=[use_tok[b]] if use_tok[b] else [])
            t_a = p.op("v", "tensor_reduce", acc[:, col:col + 1],
                       w_t[b][:, :], mybir.AxisListType.X, ALU.add,
                       apply_absolute_value=True,
                       waits=[t_ld] + (toks[-1:] or []))
            use_tok[b] = t_a
            toks.append(t_a)
            col += 1
    rt = []
    for m in range(3):
        rt.append(p.op("v", "tensor_reduce", red[:, m:m + 1],
                       acc[:, m * 8:(m + 1) * 8], mybir.AxisListType.X,
                       ALU.add, waits=toks))
    t_z = p.op("v", "memset", red[:, 3:4], 0.0)
    t_s = p.op("t", "matmul", ps[0:1, :], C.ones_cf[:, :], red[:, :],
               start=True, stop=True, waits=rt + [t_z])
    t_c = p.op("v", "tensor_copy", o_sb[0:1, :], ps[0:1, :], waits=[t_s])
    p.dma("y", out_d[0:1, :], o_sb[0:1, :], "sto", waits=[t_c])
    p.op("y", "nop", waits=[p.tok("sto")])
    p.finalize()
    return nc




def build_ffn0(SQ=1024):
    DE = 8192
    nQ = SQ // 128
    nTB = SQ // 512
    nEC = DE // 128
    nc = bass.Bass("TRN2", target_bir_lowering=False)
    p = Prog(nc)

    x1_d = nc.dram_tensor("x1", [SQ, D], F32, kind="ExternalInput")
    upT_d = nc.dram_tensor("upT", [D, DE], F32, kind="ExternalInput")
    gateT_d = nc.dram_tensor("gateT", [D, DE], F32, kind="ExternalInput")
    downT_d = nc.dram_tensor("downT", [DE, D], F32, kind="ExternalInput")
    parts_d = nc.dram_tensor("parts", [8, 4], F32, kind="ExternalInput")
    x2_d = nc.dram_tensor("x2", [SQ, D], F32, kind="ExternalOutput")
    hid_d = nc.dram_tensor("hid_scr", [DE, SQ], BF16)
    wdq_d = nc.dram_tensor("wdq_scr", [DE, D], BF16)
    y_d = nc.dram_tensor("y_scr", [SQ, D], F32)

    ev = nc.alloc_sbuf_tensor("ev", [128, D], F32)
    ln_o = nc.alloc_sbuf_tensor("ln_o", [128, D], F32)
    x_t = [nc.alloc_sbuf_tensor(f"x_t{i}", [128, D], F32) for i in range(2)]
    bjunk = nc.alloc_sbuf_tensor("bjunk", [128, D], BF16)
    st = nc.alloc_sbuf_tensor("st", [128, 16], F32)
    xn_fm = nc.alloc_sbuf_tensor("xn_fm", [128, 16 * SQ], BF16)
    parts_sb = nc.alloc_sbuf_tensor("parts_sb", [8, 4], F32)
    srow = nc.alloc_sbuf_tensor("srow", [1, 10], F32)
    scl = nc.alloc_sbuf_tensor("scl", [128, 10], F32)
    hb = [nc.alloc_sbuf_tensor(f"hb{i}", [128, 512], BF16) for i in range(2)]
    hout = [nc.alloc_sbuf_tensor(f"hout{i}", [128, 512], BF16)
            for i in range(2)]
    nscl = nc.alloc_sbuf_tensor("nscl", [128, 3], F32)

    tp_ps = [nc.alloc_psum_tensor(f"tp{i}", [128, 128], BF16) for i in range(2)]
    mm_ps = [nc.alloc_psum_tensor(f"mm{i}", [128, 512], F32) for i in range(4)]
    db_ps = nc.alloc_psum_tensor("db_ps", [128, 512], F32)

    C = Consts(p, fvals=(LN_EPS,))
    mm_free = {}

    t_pl = p.dma("y", parts_sb[:, :], parts_d[:, :], "ldp")
    t_tot = p.op("t", "matmul", db_ps[0:1, 0:4], C.ones_cf[0:8, :],
                 parts_sb[0:8, :], start=True, stop=True, waits=[t_pl])
    t_sc = p.op("v", "tensor_scalar", srow[0:1, 0:3], db_ps[0:1, 0:3],
                1.0 / (DE * D), 1e-8, ALU.mult, ALU.add, waits=[t_tot])
    t_h2 = p.op("v", "tensor_scalar", srow[0:1, 3:6], srow[0:1, 0:3], 0.5,
                None, ALU.mult, waits=[t_sc])
    t_gd = p.op("v", "tensor_tensor", srow[0:1, 6:7], srow[0:1, 4:5],
                srow[0:1, 5:6], ALU.mult, waits=[t_h2])
    t_gd2 = p.op("v", "tensor_tensor", srow[0:1, 7:8], srow[0:1, 1:2],
                 srow[0:1, 5:6], ALU.mult, waits=[t_gd])
    t_gd3 = p.op("v", "tensor_tensor", srow[0:1, 8:9], srow[0:1, 1:2],
                 srow[0:1, 2:3], ALU.mult, waits=[t_gd2])
    t_bb = p.op("t", "matmul", db_ps[:, 0:9], C.ones_rf[0:1, 0:128],
                srow[0:1, 0:9], start=True, stop=True, waits=[t_gd3])
    t_sb = p.op("v", "tensor_copy", scl[:, 0:9], db_ps[:, 0:9], waits=[t_bb])
    scl_tok = [t_sb]

    x_use = [None, None]
    prev_ln = None
    cp_all = []
    with nc.sbuf_tensor("wq_f", [128, D], F32) as wq_f, \
            nc.sbuf_tensor("wq_f2", [128, D], F32) as wq_f2, \
            nc.sbuf_tensor("sa", [128, D], BF16) as sa, \
            nc.sbuf_tensor("q2a", [128, D], BF16) as q2a, \
            nc.sbuf_tensor("q2b", [128, D], BF16) as q2b:
        for ti in range(nQ):
            b = ti % 2
            t_ld = p.dma("y", x_t[b][:, :], x1_d[ti * 128:(ti + 1) * 128, :],
                         f"ldx{b}", waits=[x_use[b]] if x_use[b] else [])
            t_n = emit_ln(p, x_t[b][:, :], bjunk[:, :], ev[:, :], st,
                          waits=[t_ld] + ([prev_ln] if prev_ln else []))
            prev_ln = t_n
            x_use[b] = t_n
            cp_toks = []
            for c in range(16):
                bk = tp_ps[c % 2]
                w = [t_n]
                if len(cp_toks) >= 2:
                    w.append(cp_toks[-2])
                t_tr = p.op("t", "transpose", bk[:, :],
                            bjunk[:, c * 128:(c + 1) * 128], C.ident[:, :],
                            waits=w)
                cp_toks.append(p.op("v", "tensor_copy",
                                    xn_fm[:, c * SQ + ti * 128:
                                          c * SQ + (ti + 1) * 128],
                                    bk[:, :], waits=[t_tr]))
            cp_all.append(cp_toks[-1])
        ln_done = [p.tok("_v"), p.tok("_t")]

    ln_done = ln_done
    with nc.sbuf_tensor("wfu0", [128, D], F32) as wfu0, \
            nc.sbuf_tensor("wfu1", [128, D], F32) as wfu1, \
            nc.sbuf_tensor("wfg0", [128, D], F32) as wfg0, \
            nc.sbuf_tensor("wfg1", [128, D], F32) as wfg1, \
            nc.sbuf_tensor("sau", [128, D], BF16) as sau, \
            nc.sbuf_tensor("sag", [128, D], BF16) as sag, \
            nc.sbuf_tensor("q2u0", [128, D], BF16) as q2u0, \
            nc.sbuf_tensor("q2u1", [128, D], BF16) as q2u1, \
            nc.sbuf_tensor("q2g0", [128, D], BF16) as q2g0, \
            nc.sbuf_tensor("q2g1", [128, D], BF16) as q2g1, \
            nc.sbuf_tensor("wfd0", [128, D], F32) as wfd0, \
            nc.sbuf_tensor("wfd1", [128, D], F32) as wfd1, \
            nc.sbuf_tensor("sad", [128, D], BF16) as sad, \
            nc.sbuf_tensor("q2d0", [128, D], BF16) as q2d0, \
            nc.sbuf_tensor("q2d1", [128, D], BF16) as q2d1:
        wfd = [wfd0, wfd1]
        q2d = [q2d0, q2d1]
        wfu = [wfu0, wfu1]
        wfg = [wfg0, wfg1]
        q2u = [q2u0, q2u1]
        q2g = [q2g0, q2g1]
        t_ng = p.op("v", "tensor_scalar", nscl[:, 2:3], scl[:, 3:4], -1.0,
                    None, ALU.mult, waits=scl_tok)
        t_ng2 = p.op("v", "tensor_scalar", nscl[:, 0:1], scl[:, 4:5], -1.0,
                     None, ALU.mult, waits=scl_tok)
        t_ng3 = p.op("v", "tensor_scalar", nscl[:, 1:2], scl[:, 5:6], -1.0,
                     None, ALU.mult, waits=scl_tok)
        scl_tok = scl_tok + [t_ng, t_ng2, t_ng3]

        q_use = {}
        f_use = {"sau": None}
        ld_tok = {}
        hid_st = [None, None]
        dq_st = {}

        def issue_loads(ec):
            b = ec % 2
            ld_tok[("u", ec)] = p.dma(
                "s", chunk3(wfu[b].ap(), 128),
                upT_d.ap()[:, ec * 128:(ec + 1) * 128]
                .rearrange("(c p) o -> p c o", p=128), f"lwu{b}",
                waits=[f_use.get(("u", b))] if f_use.get(("u", b)) else [])
            ld_tok[("g", ec)] = p.dma(
                "g", chunk3(wfg[b].ap(), 128),
                gateT_d.ap()[:, ec * 128:(ec + 1) * 128]
                .rearrange("(c p) o -> p c o", p=128), f"lwg{b}",
                waits=[f_use.get(("g", b))] if f_use.get(("g", b)) else [])

        def quantize(ec):
            b = ec % 2
            w1 = [ld_tok[("u", ec)]] + scl_tok + \
                ([q_use[("u", b)]] if ("u", b) in q_use else [])
            if f_use.get("sau"):
                w1.append(f_use["sau"])
            tu1 = p.op("s", "activation", sau[:, :], wfu[b][:, :], AF.Sign,
                       bias=nscl[:, 2:3], waits=w1)
            tu2 = p.op("s", "activation", q2u[b][:, :], wfu[b][:, :],
                       AF.Sign, bias=scl[:, 3:4])
            f_use[("u", b)] = tu2
            tqu = p.op("v", "tensor_tensor", q2u[b][:, :], q2u[b][:, :],
                       sau[:, :], ALU.add, waits=[tu1, tu2])
            f_use["sau"] = tqu
            w2 = [ld_tok[("g", ec)]] + scl_tok + \
                ([q_use[("g", b)]] if ("g", b) in q_use else []) + \
                ([f_use.get("sag")] if f_use.get("sag") else [])
            tg1 = p.op("v", "tensor_scalar", q2g[b][:, :], wfg[b][:, :],
                       scl[:, 4:5], None, ALU.is_ge, waits=w2)
            tg2 = p.op("v", "tensor_scalar", sag[:, :], wfg[b][:, :],
                       nscl[:, 0:1], None, ALU.is_le)
            f_use[("g", b)] = tg2
            tqg = p.op("v", "tensor_tensor", q2g[b][:, :], q2g[b][:, :],
                       sag[:, :], ALU.subtract, waits=[tg1, tg2])
            f_use["sag"] = tqg
            return tqu, tqg

        issue_loads(0)
        issue_loads(1)
        qtoks = {0: quantize(0)}
        for ec in range(nEC):
            b = ec % 2
            if ec + 1 < nEC:
                if ec + 1 > 1:
                    issue_loads(ec + 1)
                qtoks[ec + 1] = quantize(ec + 1) if ec + 1 < nEC else None
            tqu, tqg = qtoks.pop(ec)
            for tb in range(nTB):
                bu = mm_ps[tb % 2]
                bg = mm_ps[2 + tb % 2]
                w0 = [tqu] + ln_done + \
                    ([mm_free.get(tb % 2)] if mm_free.get(tb % 2) else [])
                w1 = [tqg] + \
                    ([mm_free.get(2 + tb % 2)] if mm_free.get(2 + tb % 2)
                     else [])
                for c in range(16):
                    p.op("t", "matmul", bu[:, :],
                         q2u[b][:, c * 128:(c + 1) * 128],
                         xn_fm[:, c * SQ + tb * 512:c * SQ + (tb + 1) * 512],
                         start=(c == 0), stop=(c == 15),
                         waits=w0 if c == 0 else ())
                for c in range(16):
                    p.op("t", "matmul", bg[:, :],
                         q2g[b][:, c * 128:(c + 1) * 128],
                         xn_fm[:, c * SQ + tb * 512:c * SQ + (tb + 1) * 512],
                         start=(c == 0), stop=(c == 15),
                         waits=w1 if c == 0 else ())
                t_pe = p.tok("_t")
                t_si = p.op("s", "activation", hb[tb % 2][:, :], bu[:, :],
                            AF.Silu, scale=scl[:, 3:4],
                            waits=[t_pe] + ([mm_free.get(("hb", tb % 2))]
                                            if mm_free.get(("hb", tb % 2))
                                            else []))
                mm_free[tb % 2] = t_si
                ob = (ec * nTB + tb) % 2
                t_hm = p.op("v", "tensor_tensor", hout[ob][:, :],
                            hb[tb % 2][:, :], bg[:, :], ALU.mult,
                            waits=[t_si] +
                            ([hid_st[ob]] if hid_st[ob] else []))
                mm_free[("hb", tb % 2)] = t_hm
                mm_free[2 + tb % 2] = t_hm
                hid_st[ob] = p.dma("y",
                                   hid_d[ec * 128:(ec + 1) * 128,
                                         tb * 512:(tb + 1) * 512],
                                   hout[ob][:, :], f"sth{ob}", waits=[t_hm])
            q_use[("u", b)] = p.tok("_t")
            q_use[("g", b)] = p.tok("_t")
            t_wdl = p.dma("s", chunk3(wfd[b].ap(), 512, nch=4),
                          downT_d[ec * 128:(ec + 1) * 128, :]
                          .rearrange("(c p) o -> p c o", p=128),
                          f"lwd{b}",
                          waits=[f_use.get(("d", b))] if f_use.get(("d", b))
                          else [])
            td1 = p.op("g", "tensor_scalar", q2d[b][:, :], wfd[b][:, :],
                       scl[:, 5:6], None, ALU.is_ge,
                       waits=[t_wdl] + scl_tok +
                       ([f_use.get("sad")] if f_use.get("sad") else []) +
                       ([dq_st.get(b)] if dq_st.get(b) else []))
            td2 = p.op("g", "tensor_scalar", sad[:, :], wfd[b][:, :],
                       nscl[:, 1:2], None, ALU.is_le)
            f_use[("d", b)] = td2
            td3 = p.op("g", "tensor_tensor", q2d[b][:, :], q2d[b][:, :],
                       sad[:, :], ALU.subtract, waits=[td1, td2])
            f_use["sad"] = td3
            dq_st[b] = p.dma("y", wdq_d[ec * 128:(ec + 1) * 128, :],
                             q2d[b][:, :], f"stwd{b}", waits=[td3])
    up_done = [p.tok("_t"), p.tok("_v"), p.tok("_s"), p.tok("_g"),
               p.tok("sth0"), p.tok("sth1"), p.tok("stwd0"), p.tok("stwd1")]

    with nc.sbuf_tensor("wdb0", [128, nEC * 256], BF16) as wdb0, \
            nc.sbuf_tensor("wdb1", [128, nEC * 256], BF16) as wdb1, \
            nc.sbuf_tensor("hblk0", [128, nEC * 128], BF16) as hblk0, \
            nc.sbuf_tensor("hblk1", [128, nEC * 128], BF16) as hblk1:
        wdb = [wdb0, wdb1]
        hblk = [hblk0, hblk1]
        wd_tok = {}
        wd_use = {}
        yst = [None, None, None, None]
        huse = {}

        def load_wd(hd):
            b = hd % 2
            wd_tok[hd] = p.dma(
                "s", chunk3(wdb[b].ap(), 256, nch=nEC),
                wdq_d.ap()[:, hd * 256:(hd + 1) * 256]
                .rearrange("(c p) o -> p c o", p=128), f"lwq{b}",
                waits=up_done + ([wd_use.get(b)] if wd_use.get(b) else []))

        load_wd(0)
        load_wd(1)
        for hp in range(4):
            hd0, hd1 = 2 * hp, 2 * hp + 1
            for tc in range(nQ):
                hbuf = tc % 2
                t_h = p.dma("g", chunk3(hblk[hbuf].ap(), 128, nch=nEC),
                            hid_d.ap()[:, tc * 128:(tc + 1) * 128]
                            .rearrange("(c p) t -> p c t", p=128),
                            f"lh{hbuf}", waits=up_done +
                            ([huse.get(hbuf)] if huse.get(hbuf) else []))
                for hj, hd in ((0, hd0), (1, hd1)):
                    bk = mm_ps[2 * hj + tc % 2]
                    w0 = [wd_tok[hd], t_h] + \
                        ([mm_free.get(2 * hj + tc % 2)]
                         if mm_free.get(2 * hj + tc % 2) else [])
                    for c in range(nEC):
                        p.op("t", "matmul", bk[:, 0:256],
                             hblk[hbuf][:, c * 128:(c + 1) * 128],
                             wdb[hd % 2][:, c * 256:(c + 1) * 256],
                             start=(c == 0), stop=(c == nEC - 1),
                             waits=w0 if c == 0 else ())
                    t_pe = p.tok("_t")
                    ob = 2 * hj + tc % 2
                    t_y = p.op("s", "activation",
                               ev[:, ob * 256:ob * 256 + 256],
                               bk[:, 0:256], AF.Copy, scale=scl[:, 8:9],
                               waits=[t_pe] +
                               ([yst[ob]] if yst[ob] else []))
                    mm_free[2 * hj + tc % 2] = t_y
                    yst[ob] = p.dma("y", y_d[tc * 128:(tc + 1) * 128,
                                             hd * 256:(hd + 1) * 256],
                                    ev[:, ob * 256:ob * 256 + 256],
                                    f"sty{ob}", waits=[t_y])
                huse[hbuf] = p.tok("_t")
            wd_use[0] = p.tok("_t")
            wd_use[1] = p.tok("_t")
            if 2 * hp + 2 < 8:
                load_wd(2 * hp + 2)
                load_wd(2 * hp + 3)
    down_done = [p.tok("_t"), p.tok("_v"), p.tok("_s"), p.tok("sty0"),
                 p.tok("sty1"), p.tok("sty2"), p.tok("sty3")]

    x_use = [None, None]
    st_tok = [None, None]
    prev_ln = None
    y_t = [nc.alloc_sbuf_tensor(f"y_t{i}", [128, D], F32) for i in range(2)]
    for tc in range(nQ):
        xb = tc % 2
        t_ly = p.dma("s", y_t[xb][:, :], y_d[tc * 128:(tc + 1) * 128, :],
                     f"ly{xb}", waits=down_done +
                     ([x_use[xb]] if x_use[xb] else []))
        t_lx = p.dma("y", x_t[xb][:, :], x1_d[tc * 128:(tc + 1) * 128, :],
                     f"lre{xb}", waits=down_done +
                     ([x_use[xb]] if x_use[xb] else []) +
                     ([st_tok[xb]] if st_tok[xb] else []))
        t_n = emit_ln(p, y_t[xb][:, :], ln_o[:, :], bjunk[:, :], st,
                      waits=[t_ly] + ([prev_ln] if prev_ln else []))
        prev_ln = t_n
        t_add = p.op("v", "tensor_tensor", x_t[xb][:, :], x_t[xb][:, :],
                     ln_o[:, :], ALU.add, waits=[t_n, t_lx])
        x_use[xb] = t_add
        st_tok[xb] = p.dma("y", x2_d[tc * 128:(tc + 1) * 128, :],
                           x_t[xb][:, :], f"stx{xb}", waits=[t_add])
    p.op("y", "nop", waits=[p.tok("stx0"), p.tok("stx1")])
    p.finalize()
    return nc





BLOCK = 256


def build_l1a_local(SQ=1024):
    nQ = SQ // 128
    nTB = SQ // 512
    nB = SQ // BLOCK
    nc = bass.Bass("TRN2", target_bir_lowering=False)
    p = Prog(nc)

    x2_d = nc.dram_tensor("x2", [SQ, D], F32, kind="ExternalInput")
    wt_d = nc.dram_tensor("WqkvuT", [D, 4 * D], F32, kind="ExternalInput")
    sl_d = nc.dram_tensor("slope", [1, NH], F32, kind="ExternalInput")
    sc_d = nc.dram_tensor("sc", [NH * 128, 128], F32, kind="ExternalOutput")
    o_d = nc.dram_tensor("o_part", [SQ, D], F32, kind="ExternalOutput")
    qd_d = nc.dram_tensor("qd", [D, SQ], BF16, kind="ExternalOutput")
    u_d = nc.dram_tensor("u_o", [SQ, D], BF16, kind="ExternalOutput")

    ev = nc.alloc_sbuf_tensor("ev", [128, D], F32)
    x_t = [nc.alloc_sbuf_tensor(f"x_t{i}", [128, D], F32) for i in range(2)]
    bjunk = nc.alloc_sbuf_tensor("bjunk", [128, D], BF16)
    st = nc.alloc_sbuf_tensor("st", [128, 16], F32)
    xn_fm = nc.alloc_sbuf_tensor("xn_fm", [128, 16 * SQ], BF16)
    q_fm = nc.alloc_sbuf_tensor("q_fm", [128, 16 * SQ], BF16)
    k_fm = nc.alloc_sbuf_tensor("k_fm", [128, 16 * SQ], BF16)
    v_tm = nc.alloc_sbuf_tensor("v_tm", [128, nQ * D], BF16)
    uout = [nc.alloc_sbuf_tensor(f"uout{i}", [128, 512], BF16)
            for i in range(2)]
    arr_i = nc.alloc_sbuf_tensor("arr_i", [1, BLOCK], I32)
    arr_f = nc.alloc_sbuf_tensor("arr_f", [1, BLOCK], F32)
    iof = nc.alloc_sbuf_tensor("iof", [128, BLOCK], F32)
    iof2 = nc.alloc_sbuf_tensor("iof2", [128, 2 * BLOCK], F32)
    m01 = nc.alloc_sbuf_tensor("m01", [128, 2 * BLOCK], BF16)
    ddx = nc.alloc_sbuf_tensor("ddx", [128, 2 * BLOCK], F32)
    dd_t = nc.alloc_sbuf_tensor("dd_t", [128, 2 * BLOCK], BF16)
    bdr = nc.alloc_sbuf_tensor("bdr", [1, 4 * NH], F32)
    nsl_bc = nc.alloc_sbuf_tensor("nsl_bc", [128, NH], F32)
    s128_bc = nc.alloc_sbuf_tensor("s128_bc", [128, NH], F32)
    bd_bc = nc.alloc_sbuf_tensor("bd_bc", [128, NH], F32)
    bdp_bc = nc.alloc_sbuf_tensor("bdp_bc", [128, 4 * NH], F32)
    bdp_row = nc.alloc_sbuf_tensor("bdp_row", [1, 4 * NH], F32)
    qrow = nc.alloc_sbuf_tensor("qrow", [1, BLOCK], F32)
    krow = nc.alloc_sbuf_tensor("krow", [1, BLOCK], F32)
    qdec_bc = nc.alloc_sbuf_tensor("qdec_bc", [128, BLOCK], F32)
    kdec_bc = nc.alloc_sbuf_tensor("kdec_bc", [128, BLOCK], F32)
    qdq = nc.alloc_sbuf_tensor("qdq", [128, BLOCK], BF16)
    qds = nc.alloc_sbuf_tensor("qds", [128, BLOCK], BF16)
    kkd = nc.alloc_sbuf_tensor("kkd", [128, BLOCK], BF16)
    kkd_tm = nc.alloc_sbuf_tensor("kkd_tm", [128, BLOCK], BF16)
    qkd = nc.alloc_sbuf_tensor("qkd", [128, 2 * BLOCK], BF16)
    kv32 = nc.alloc_sbuf_tensor("kv32", [128, 128], F32)
    kv_bf = nc.alloc_sbuf_tensor("kv_bf", [128, 128], BF16)
    ob_sb = nc.alloc_sbuf_tensor("ob_sb", [128, 256], F32)

    tp_ps = [nc.alloc_psum_tensor(f"tp{i}", [128, 128], BF16) for i in range(2)]
    mm_ps = [nc.alloc_psum_tensor(f"mm{i}", [128, 512], F32) for i in range(4)]
    db_ps = nc.alloc_psum_tensor("db_ps", [128, 512], F32)
    c_ps = nc.alloc_psum_tensor("c_ps", [128, 128], F32)

    C = Consts(p, fvals=(LN_EPS, -256.0))
    mm_free = {}

    slr = nc.alloc_sbuf_tensor("slr", [1, NH], F32)
    nslr = nc.alloc_sbuf_tensor("nslr", [1, NH], F32)
    bneg = nc.alloc_sbuf_tensor("bneg", [1, NH], F32)
    t_sl = p.dma("y", slr[0:1, :], sl_d[0:1, :], "ldsl")
    t_ns = p.op("v", "tensor_scalar", nslr[0:1, :], slr[0:1, :], -1.0, None,
                ALU.mult, waits=[t_sl])
    t_bn = p.op("v", "tensor_scalar", bneg[0:1, :], slr[0:1, :], -256.0,
                None, ALU.mult, waits=[t_sl])
    t_bd = p.op("s", "activation", bdr[0:1, 0:NH], slr[0:1, :], AF.Exp,
                scale=-256.0, waits=[t_sl])
    t_p0 = p.op("v", "memset", bdp_row[0:1, 0:NH], 1.0)
    t_p1 = p.op("v", "tensor_copy", bdp_row[0:1, NH:2 * NH], bdr[0:1, 0:NH],
                waits=[t_bd])
    t_p2 = p.op("v", "tensor_tensor", bdp_row[0:1, 2 * NH:3 * NH],
                bdp_row[0:1, NH:2 * NH], bdr[0:1, 0:NH], ALU.mult,
                waits=[t_p1])
    t_p3 = p.op("v", "tensor_tensor", bdp_row[0:1, 3 * NH:4 * NH],
                bdp_row[0:1, 2 * NH:3 * NH], bdr[0:1, 0:NH], ALU.mult,
                waits=[t_p2])
    s128r = nc.alloc_sbuf_tensor("s128r", [1, NH], F32)
    t_s1r = p.op("v", "tensor_scalar", s128r[0:1, :], slr[0:1, :], 128.0,
                 None, ALU.mult, waits=[t_sl])
    t_i1 = p.op("g", "iota", arr_i[0:1, :], [[1, BLOCK]], base=1,
                channel_multiplier=0)
    t_i2 = p.op("v", "tensor_copy", arr_f[0:1, :], arr_i[0:1, :],
                waits=[t_i1])
    t_i4 = p.op("g", "iota", iof[:, :], [[1, BLOCK]], base=0,
                channel_multiplier=-1, allow_small_or_imprecise_dtypes=True)
    t_m0 = p.op("v", "tensor_scalar", m01[:, 0:BLOCK], iof[:, :], -0.5,
                None, ALU.is_gt, waits=[t_i4])
    t_m1 = p.op("v", "tensor_scalar", m01[:, BLOCK:], iof[:, :], 127.5,
                None, ALU.is_gt, waits=[t_i4])
    t_v0 = p.op("v", "tensor_scalar", iof2[:, 0:BLOCK], iof[:, :], -0.5,
                None, ALU.is_lt, waits=[t_i4])
    t_v1 = p.op("v", "tensor_scalar", iof2[:, BLOCK:], iof[:, :], 127.5,
                None, ALU.is_lt, waits=[t_i4])
    t_c0 = p.op("v", "scalar_tensor_tensor", iof2[:, 0:BLOCK],
                iof2[:, 0:BLOCK], 1e9, iof[:, :], ALU.mult, ALU.add,
                waits=[t_v0])
    t_c1 = p.op("v", "scalar_tensor_tensor", iof2[:, BLOCK:],
                iof2[:, BLOCK:], 1e9, iof[:, :], ALU.mult, ALU.add,
                waits=[t_v1])

    def bcast_row(dst, src_row, n, w):
        t_b = p.op("t", "matmul", db_ps[:, 0:n], C.ones_rf[0:1, 0:128],
                   src_row, start=True, stop=True,
                   waits=w + ([mm_free.get("db")] if mm_free.get("db")
                              else []))
        t_c = p.op("v", "tensor_copy", dst, db_ps[:, 0:n], waits=[t_b])
        mm_free["db"] = t_c
        return t_c

    t_b1 = bcast_row(nsl_bc[:, :], nslr[0:1, 0:NH], NH, [t_ns])
    t_b2 = bcast_row(s128_bc[:, :], s128r[0:1, 0:NH], NH, [t_s1r])
    t_b3 = bcast_row(bd_bc[:, :], bdr[0:1, 0:NH], NH, [t_bd])
    t_b4 = bcast_row(bdp_bc[:, :], bdp_row[0:1, :], 4 * NH, [t_p0, t_p3])
    dec_done = [t_m0, t_m1, t_b1, t_b2, t_b3, t_b4, t_i2, t_bn, t_c0, t_c1]

    x_use = [None, None]
    prev_ln = None
    for ti in range(nQ):
        b = ti % 2
        t_ld = p.dma("y", x_t[b][:, :], x2_d[ti * 128:(ti + 1) * 128, :],
                     f"ldx{b}", waits=[x_use[b]] if x_use[b] else [])
        t_n = emit_ln(p, x_t[b][:, :], bjunk[:, :], ev[:, :], st,
                      waits=[t_ld] + ([prev_ln] if prev_ln else []))
        prev_ln = t_n
        x_use[b] = t_n
        cp_toks = []
        for c in range(16):
            bk = tp_ps[c % 2]
            w = [t_n]
            if len(cp_toks) >= 2:
                w.append(cp_toks[-2])
            t_tr = p.op("t", "transpose", bk[:, :],
                        bjunk[:, c * 128:(c + 1) * 128], C.ident[:, :],
                        waits=w)
            cp_toks.append(p.op("v", "tensor_copy",
                                xn_fm[:, c * SQ + ti * 128:
                                      c * SQ + (ti + 1) * 128],
                                bk[:, :], waits=[t_tr]))
    ln_done = [p.tok("_v"), p.tok("_t")]

    wuse = [None, None]
    ust = [None, None]
    with nc.sbuf_tensor("wc0", [128, 16 * 128], BF16) as wc0, \
            nc.sbuf_tensor("wc1", [128, 16 * 128], BF16) as wc1, \
            nc.sbuf_tensor("wb0", [128, 16 * 512], BF16) as wb0:
        wc = [wc0, wc1]
        wb = [wb0, wb0]
        for kind in ("q", "k"):
            base = 0 if kind == "q" else D
            dst = q_fm if kind == "q" else k_fm
            for o in range(16):
                b = o % 2
                t_w = p.dma("g", chunk3(wc[b].ap(), 128),
                            wt_d.ap()[:, base + o * 128:base + (o + 1) * 128]
                            .rearrange("(c p) o -> p c o", p=128),
                            f"lw{b}", waits=[wuse[b]] if wuse[b] else [])
                for tb in range(nTB):
                    bk = mm_ps[tb % 2]
                    w0 = [t_w] + ln_done + \
                        ([mm_free.get(tb % 2)] if mm_free.get(tb % 2)
                         else [])
                    for c in range(16):
                        p.op("t", "matmul", bk[:, :],
                             wc[b][:, c * 128:(c + 1) * 128],
                             xn_fm[:, c * SQ + tb * 512:
                                   c * SQ + (tb + 1) * 512],
                             start=(c == 0), stop=(c == 15),
                             waits=w0 if c == 0 else ())
                    t_e = p.op("s", "activation",
                               dst[:, o * SQ + tb * 512:
                                   o * SQ + (tb + 1) * 512],
                               bk[:, :], AF.Silu, waits=[p.tok("_t")])
                    mm_free[tb % 2] = t_e
                wuse[b] = p.tok("_t")
        for kind in ("v", "u"):
            base = 2 * D if kind == "v" else 3 * D
            for ob in range(4):
                b = 0
                t_w = p.dma("g", chunk3(wb[b].ap(), 512),
                            wt_d.ap()[:, base + ob * 512:base + (ob + 1) * 512]
                            .rearrange("(c p) o -> p c o", p=128),
                            "lwb", waits=[p.tok("_t")])
                for tc in range(nQ):
                    bk = mm_ps[2 + tc % 2]
                    w0 = [t_w] + ln_done + \
                        ([mm_free.get(2 + tc % 2)]
                         if mm_free.get(2 + tc % 2) else [])
                    for c in range(16):
                        p.op("t", "matmul", bk[:, :],
                             xn_fm[:, c * SQ + tc * 128:
                                   c * SQ + (tc + 1) * 128],
                             wb[b][:, c * 512:(c + 1) * 512],
                             start=(c == 0), stop=(c == 15),
                             waits=w0 if c == 0 else ())
                    if kind == "v":
                        t_e = p.op("v", "tensor_copy",
                                   v_tm[:, tc * D + ob * 512:
                                        tc * D + (ob + 1) * 512],
                                   bk[:, :], waits=[p.tok("_t")])
                    else:
                        ou = tc % 2
                        t_e = p.op("v", "tensor_copy", uout[ou][:, :],
                                   bk[:, :],
                                   waits=[p.tok("_t")] +
                                   ([ust[ou]] if ust[ou] else []))
                        ust[ou] = p.dma("y",
                                        u_d[tc * 128:(tc + 1) * 128,
                                            ob * 512:(ob + 1) * 512],
                                        uout[ou][:, :], f"stu{ou}",
                                        waits=[t_e])
                    mm_free[2 + tc % 2] = t_e
                wuse[b] = p.tok("_t")
    proj_done = [p.tok("_t"), p.tok("_v"), p.tok("_s")]

    st_sc = None
    st_qd = None
    st_ob = None
    for h in range(NH):
        t_q = p.op("s", "activation", qrow[0:1, :], arr_f[0:1, :], AF.Exp,
                   scale=nslr[0:1, h:h + 1],
                   waits=dec_done + proj_done +
                   ([mm_free.get("qrow")] if mm_free.get("qrow") else []))
        t_k = p.op("s", "activation", krow[0:1, :], arr_f[0:1, :], AF.Exp,
                   scale=slr[0:1, h:h + 1], bias=bneg[0:1, h:h + 1],
                   waits=[t_q] + ([mm_free.get("krow")]
                                  if mm_free.get("krow") else []))
        t_qb = bcast_row(qdec_bc[:, :], qrow[0:1, :], BLOCK, [t_q])
        mm_free["qrow"] = t_qb
        t_kb = bcast_row(kdec_bc[:, :], krow[0:1, :], BLOCK, [t_k])
        mm_free["krow"] = t_kb
        dd_toks = []
        for kt in range(2):
            t_x = p.op("s", "activation", ddx[:, kt * BLOCK:(kt + 1) * BLOCK],
                       iof2[:, kt * BLOCK:(kt + 1) * BLOCK], AF.Exp,
                       scale=nsl_bc[:, h:h + 1],
                       bias=(s128_bc[:, h:h + 1] if kt == 1 else 0.0),
                       waits=dec_done + [t_k] +
                       ([mm_free.get(("dd", kt))]
                        if mm_free.get(("dd", kt)) else []))
            t_d = p.op("v", "tensor_tensor",
                       dd_t[:, kt * BLOCK:(kt + 1) * BLOCK],
                       ddx[:, kt * BLOCK:(kt + 1) * BLOCK],
                       m01[:, kt * BLOCK:(kt + 1) * BLOCK], ALU.mult,
                       waits=[t_x])
            mm_free[("dd", kt)] = t_d
            dd_toks.append(t_d)
        t_z = p.op("v", "memset", kv32[:, :], 0.0, waits=[st_sc] if st_sc
                   else [])
        t_zb = p.op("v", "memset", kv_bf[:, :], 0.0, waits=[t_z])
        kv_tok = t_zb
        for i in range(nB):
            tbase = i * BLOCK
            t_1 = p.op("v", "tensor_tensor", qdq[:, :],
                       q_fm[:, h * SQ + tbase:h * SQ + tbase + BLOCK],
                       qdec_bc[:, :], ALU.mult,
                       waits=[t_qb, kv_tok] + proj_done)
            t_2 = p.op("v", "tensor_scalar", qds[:, :], qdq[:, :],
                       bdp_bc[:, (i * NH + h):(i * NH + h) + 1], None,
                       ALU.mult, waits=[t_1] + ([st_qd] if st_qd else []))
            st_qd = p.dma("y", qd_d[h * 128:(h + 1) * 128,
                                    tbase:tbase + BLOCK],
                          qds[:, :], "stqd", waits=[t_2])
            for kt in range(2):
                bk = mm_ps[kt]
                t_s1 = p.op("t", "matmul", bk[:, 0:BLOCK],
                            k_fm[:, h * SQ + tbase + kt * 128:
                                 h * SQ + tbase + (kt + 1) * 128],
                            q_fm[:, h * SQ + tbase:h * SQ + tbase + BLOCK],
                            start=True, stop=True,
                            waits=[kv_tok] +
                            ([mm_free.get(("sc", kt))]
                             if mm_free.get(("sc", kt)) else []))
                t_s2 = p.op("v", "tensor_tensor",
                            qkd[:, kt * BLOCK:(kt + 1) * BLOCK],
                            bk[:, 0:BLOCK],
                            dd_t[:, kt * BLOCK:(kt + 1) * BLOCK], ALU.mult,
                            waits=[t_s1] + dd_toks)
                mm_free[("sc", kt)] = t_s2
            for qs in range(2):
                bk = mm_ps[2 + qs]
                w0 = [mm_free[("sc", 0)], mm_free[("sc", 1)]] + \
                    ([mm_free.get(("o", qs))] if mm_free.get(("o", qs))
                     else [])
                for kt in range(2):
                    vsl = v_tm[:, (2 * i + kt) * D + h * 128:
                               (2 * i + kt) * D + (h + 1) * 128]
                    p.op("t", "matmul", bk[:, 0:128],
                         qkd[:, kt * BLOCK + qs * 128:
                             kt * BLOCK + qs * 128 + 128],
                         vsl, start=(kt == 0), stop=False,
                         waits=w0 if kt == 0 else ())
                p.op("t", "matmul", bk[:, 0:128],
                     qdq[:, qs * 128:qs * 128 + 128], kv_bf[:, :],
                     start=False, stop=True, waits=[kv_tok])
                t_o = p.op("s", "activation", ob_sb[:, qs * 128:
                                                    (qs + 1) * 128],
                           bk[:, 0:128], AF.Copy,
                           waits=[p.tok("_t")] +
                           ([st_ob] if st_ob else []))
                mm_free[("o", qs)] = t_o
            p.dma("y", o_d[tbase:tbase + 128, h * 128:(h + 1) * 128],
                  ob_sb[:, 0:128], "stob", waits=[mm_free[("o", 0)]])
            st_ob = p.dma("y", o_d[tbase + 128:tbase + 256,
                                   h * 128:(h + 1) * 128],
                          ob_sb[:, 128:256], "stob",
                          waits=[mm_free[("o", 1)]])
            t_3 = p.op("v", "tensor_tensor", kkd[:, :],
                       k_fm[:, h * SQ + tbase:h * SQ + tbase + BLOCK],
                       kdec_bc[:, :], ALU.mult,
                       waits=[t_kb] + ([mm_free.get("kkd")]
                                       if mm_free.get("kkd") else []))
            tr_toks = []
            for kt in range(2):
                bk2 = tp_ps[kt]
                t_t = p.op("t", "transpose", bk2[:, :],
                           kkd[:, kt * 128:(kt + 1) * 128], C.ident[:, :],
                           waits=[t_3] + ([mm_free.get(("tp", kt))]
                                          if mm_free.get(("tp", kt))
                                          else []))
                t_c = p.op("v", "tensor_copy",
                           kkd_tm[:, kt * 128:(kt + 1) * 128], bk2[:, :],
                           waits=[t_t])
                mm_free[("tp", kt)] = t_c
                tr_toks.append(t_c)
            mm_free["kkd"] = tr_toks[-1]
            for kt in range(2):
                vsl = v_tm[:, (2 * i + kt) * D + h * 128:
                           (2 * i + kt) * D + (h + 1) * 128]
                p.op("t", "matmul", c_ps[:, :],
                     kkd_tm[:, kt * 128:(kt + 1) * 128], vsl,
                     start=(kt == 0), stop=(kt == 1),
                     waits=(tr_toks + [kv_tok]) if kt == 0 else ())
            t_u = p.op("v", "scalar_tensor_tensor", kv32[:, :], kv32[:, :],
                       bd_bc[:, h:h + 1], c_ps[:, :], ALU.mult, ALU.add,
                       waits=[p.tok("_t"), t_zb])
            kv_tok = p.op("v", "tensor_copy", kv_bf[:, :], kv32[:, :],
                          waits=[t_u])
        st_sc = p.dma("y", sc_d[h * 128:(h + 1) * 128, :], kv32[:, :],
                      "stsc", waits=[kv_tok])
    p.op("y", "nop", waits=[p.tok("stsc"), p.tok("stob"), p.tok("stqd"),
                            p.tok("stu0"), p.tok("stu1")])
    p.finalize()
    return nc






def build_l1a_fix(SQ=1024):
    nQ = SQ // 128
    nc = bass.Bass("TRN2", target_bir_lowering=False)
    p = Prog(nc)

    x2_d = nc.dram_tensor("x2", [SQ, D], F32, kind="ExternalInput")
    o_d = nc.dram_tensor("o_part", [SQ, D], F32, kind="ExternalInput")
    qd_d = nc.dram_tensor("qd", [D, SQ], BF16, kind="ExternalInput")
    u_d = nc.dram_tensor("u_o", [SQ, D], BF16, kind="ExternalInput")
    sp_d = [nc.dram_tensor(f"scp{j}", [NH * 128, 128], F32,
                           kind="ExternalInput") for j in (1, 2, 3)]
    sl_d = nc.dram_tensor("slope", [1, NH], F32, kind="ExternalInput")
    wo_d = nc.dram_tensor("WoutT", [D, D], F32, kind="ExternalInput")
    x3_d = nc.dram_tensor("x3", [SQ, D], F32, kind="ExternalOutput")

    ev = nc.alloc_sbuf_tensor("ev", [128, D], F32)
    ln_o = nc.alloc_sbuf_tensor("ln_o", [128, D], F32)
    x_t = [nc.alloc_sbuf_tensor(f"x_t{i}", [128, D], F32) for i in range(2)]
    bjunk = nc.alloc_sbuf_tensor("bjunk", [128, D], BF16)
    st = nc.alloc_sbuf_tensor("st", [128, 16], F32)
    qd_fm = nc.alloc_sbuf_tensor("qd_fm", [128, 16 * SQ], BF16)
    on_fm = nc.alloc_sbuf_tensor("on_fm", [128, 16 * SQ], BF16)
    o_sb = nc.alloc_sbuf_tensor("o_sb", [128, D], F32)
    u_sb = nc.alloc_sbuf_tensor("u_sb", [128, D], BF16)
    slr = nc.alloc_sbuf_tensor("slr", [1, NH], F32)
    bdr = nc.alloc_sbuf_tensor("bdr", [1, 3 * NH], F32)
    bd48 = nc.alloc_sbuf_tensor("bd48", [128, 2 * NH], F32)
    kvin = nc.alloc_sbuf_tensor("kvin", [128, 128 * 3], F32)
    kvin_b = nc.alloc_sbuf_tensor("kvin_b", [128, 16 * 128], BF16)

    tp_ps = [nc.alloc_psum_tensor(f"tp{i}", [128, 128], BF16) for i in range(2)]
    mm_ps = [nc.alloc_psum_tensor(f"mm{i}", [128, 512], F32) for i in range(4)]
    db_ps = nc.alloc_psum_tensor("db_ps", [128, 512], F32)

    C = Consts(p, fvals=(LN_EPS, 1e-6, -256.0))
    mm_free = {}

    t_sl = p.dma("y", slr[0:1, :], sl_d[0:1, :], "ldsl")
    t_bd = p.op("s", "activation", bdr[0:1, 0:NH], slr[0:1, :], AF.Exp,
                scale=-256.0, waits=[t_sl])
    t_b2 = p.op("v", "tensor_tensor", bdr[0:1, NH:2 * NH], bdr[0:1, 0:NH],
                bdr[0:1, 0:NH], ALU.mult, waits=[t_bd])
    t_b4 = p.op("v", "tensor_tensor", bdr[0:1, NH:2 * NH],
                bdr[0:1, NH:2 * NH], bdr[0:1, NH:2 * NH], ALU.mult,
                waits=[t_b2])
    t_b8 = p.op("v", "tensor_tensor", bdr[0:1, 2 * NH:3 * NH],
                bdr[0:1, NH:2 * NH], bdr[0:1, NH:2 * NH], ALU.mult,
                waits=[t_b4])
    t_bb = p.op("t", "matmul", db_ps[:, 0:2 * NH], C.ones_rf[0:1, 0:128],
                bdr[0:1, NH:3 * NH], start=True, stop=True, waits=[t_b8])
    t_bc = p.op("v", "tensor_copy", bd48[:, :], db_ps[:, 0:2 * NH],
                waits=[t_bb])

    with nc.sbuf_tensor("sp_t", [128, 3 * 128], F32) as sp_t:
        kv_toks = []
        for h in range(NH):
            t_l = [p.dma("y", sp_t[:, j * 128:(j + 1) * 128],
                         sp_d[j][h * 128:(h + 1) * 128, :], f"lsp{j}",
                         waits=kv_toks[-1:] if kv_toks else [])
                   for j in range(3)]
            t_1 = p.op("v", "scalar_tensor_tensor", kvin[:, 0:128],
                       sp_t[:, 128:256], bd48[:, h:h + 1],
                       sp_t[:, 0:128], ALU.mult, ALU.add,
                       waits=t_l + [t_bc])
            t_2 = p.op("v", "scalar_tensor_tensor", kvin[:, 128:256],
                       sp_t[:, 256:384], bd48[:, NH + h:NH + h + 1],
                       kvin[:, 0:128], ALU.mult, ALU.add, waits=[t_1])
            kv_toks.append(p.op("v", "tensor_copy",
                                kvin_b[:, h * 128:(h + 1) * 128],
                                kvin[:, 128:256], waits=[t_2]))
    kv_done = [p.tok("_v")]

    t_qd = p.dma("g", chunk3(qd_fm.ap(), SQ),
                 qd_d.ap().rearrange("(c p) t -> p c t", p=128), "ldqd")

    o_use = None
    u_use = None
    prev_ln = None
    for tc in range(nQ):
        t_lo = p.dma("y", o_sb[:, :], o_d[tc * 128:(tc + 1) * 128, :],
                     "ldo", waits=[o_use] if o_use else [])
        t_lu = p.dma("s", u_sb[:, :], u_d[tc * 128:(tc + 1) * 128, :],
                     "ldu", waits=[u_use] if u_use else [])
        add_toks = []
        for h in range(NH):
            bk = mm_ps[h % 2]
            t_m = p.op("t", "matmul", bk[:, 0:128],
                       qd_fm[:, h * SQ + tc * 128:h * SQ + (tc + 1) * 128],
                       kvin_b[:, h * 128:(h + 1) * 128],
                       start=True, stop=True,
                       waits=[t_qd] + kv_done +
                       ([mm_free.get(h % 2)] if mm_free.get(h % 2) else []))
            t_a = p.op("v", "tensor_tensor",
                       o_sb[:, h * 128:(h + 1) * 128],
                       o_sb[:, h * 128:(h + 1) * 128], bk[:, 0:128],
                       ALU.add, waits=[t_m, t_lo] + add_toks[-1:])
            mm_free[h % 2] = t_a
            add_toks.append(t_a)
        t_sq = p.op("s", "activation", bjunk[:, :], o_sb[:, :], AF.Square,
                    accum_out=st[:, 0:1],
                    waits=add_toks + ([prev_ln] if prev_ln else []))
        t_ms = p.op("v", "tensor_scalar", st[:, 1:2], st[:, 0:1], 1.0 / D,
                    None, ALU.mult, waits=[t_sq])
        t_sd = p.op("s", "activation", st[:, 2:3], st[:, 1:2], AF.Sqrt,
                    bias=1e-6, waits=[t_ms])
        t_r = p.op("v", "reciprocal", st[:, 3:4], st[:, 2:3], waits=[t_sd])
        t_n = p.op("v", "tensor_scalar", ln_o[:, :], o_sb[:, :],
                   st[:, 3:4], None, ALU.mult, waits=[t_r])
        t_g = p.op("v", "tensor_tensor", bjunk[:, :], ln_o[:, :],
                   u_sb[:, :], ALU.mult, waits=[t_n, t_lu, t_sq])
        prev_ln = t_g
        u_use = t_g
        o_use = t_g
        cp_toks = []
        for c in range(16):
            bk2 = tp_ps[c % 2]
            w = [t_g]
            if len(cp_toks) >= 2:
                w.append(cp_toks[-2])
            t_tr = p.op("t", "transpose", bk2[:, :],
                        bjunk[:, c * 128:(c + 1) * 128], C.ident[:, :],
                        waits=w)
            cp_toks.append(p.op("v", "tensor_copy",
                                on_fm[:, c * SQ + tc * 128:
                                      c * SQ + (tc + 1) * 128],
                                bk2[:, :], waits=[t_tr]))
        prev_ln = cp_toks[-1]
    gate_done = [p.tok("_v"), p.tok("_t"), p.tok("_s")]

    with nc.sbuf_tensor("wo_sb", [128, 16 * D], BF16) as wo_sb:
        t_wo = p.dma("g", chunk3(wo_sb.ap(), D), wcol_ap(wo_d, 0, D),
                     "ldwo", waits=gate_done)
        x_use = [None, None]
        st_tok = [None, None]
        prev_ln = None
        for tc in range(nQ):
            xb = tc % 2
            ev_toks = []
            for o4 in range(4):
                bidx = o4 % 2
                bank = mm_ps[bidx]
                w0 = [t_wo] + gate_done + \
                    ([mm_free.get(("c", bidx))] if mm_free.get(("c", bidx))
                     else [])
                for c in range(16):
                    p.op("t", "matmul", bank[:, :],
                         on_fm[:, c * SQ + tc * 128:c * SQ + (tc + 1) * 128],
                         wo_sb[:, c * D + o4 * 512:c * D + (o4 + 1) * 512],
                         start=(c == 0), stop=(c == 15),
                         waits=w0 if c == 0 else ())
                w_e = [p.tok("_t")]
                if o4 == 0 and prev_ln:
                    w_e.append(prev_ln)
                t_e = p.op("s", "activation", ev[:, o4 * 512:(o4 + 1) * 512],
                           bank[:, :], AF.Copy, waits=w_e)
                mm_free[("c", bidx)] = t_e
                ev_toks.append(t_e)
            t_ld = p.dma("y", x_t[xb][:, :], x2_d[tc * 128:(tc + 1) * 128, :],
                         f"lre{xb}",
                         waits=([x_use[xb]] if x_use[xb] else []) +
                         ([st_tok[xb]] if st_tok[xb] else []) +
                         ([p.tok("_v")] if tc < 2 else []))
            t_n = emit_ln(p, ev[:, :], ln_o[:, :], bjunk[:, :], st,
                          waits=ev_toks + ([prev_ln] if prev_ln else []))
            prev_ln = t_n
            t_add = p.op("v", "tensor_tensor", x_t[xb][:, :], x_t[xb][:, :],
                         ln_o[:, :], ALU.add, waits=[t_n, t_ld])
            x_use[xb] = t_add
            st_tok[xb] = p.dma("y", x3_d[tc * 128:(tc + 1) * 128, :],
                               x_t[xb][:, :], f"stx{xb}", waits=[t_add])
    p.op("y", "nop", waits=[p.tok("stx0"), p.tok("stx1")])
    p.finalize()
    return nc






def build_ffn1_dequant():
    W1, W2 = 1408, 256
    nc = bass.Bass("TRN2", target_bir_lowering=False)
    p = Prog(nc)
    gup_d = nc.dram_tensor("gup_T", [256, W1], I32, kind="ExternalInput")
    gus_d = nc.dram_tensor("gus", [256, W1], mybir.dt.float16,
                           kind="ExternalInput")
    dwn_d = nc.dram_tensor("dwn_T", [704, W2], I32, kind="ExternalInput")
    dws_d = nc.dram_tensor("dws", [704, W2], mybir.dt.float16,
                           kind="ExternalInput")
    wgu_d = nc.dram_tensor("wgu_sl", [2048, W1], BF16, kind="ExternalOutput")
    wd_d = nc.dram_tensor("wd_sl", [5632, W2], BF16, kind="ExternalOutput")

    W = max(W1, W2)
    pk = nc.alloc_sbuf_tensor("pk", [128, W], I32)
    scf = nc.alloc_sbuf_tensor("scf", [128, W], F32)
    sch = nc.alloc_sbuf_tensor("sch", [128, W], mybir.dt.float16)
    nib = nc.alloc_sbuf_tensor("nib", [128, W], I32)
    ti_ = nc.alloc_sbuf_tensor("ti_", [128, W], I32)
    si_ = nc.alloc_sbuf_tensor("si_", [128, W], I32)
    tf = nc.alloc_sbuf_tensor("tf", [128, W], F32)
    sf = nc.alloc_sbuf_tensor("sf", [128, W], F32)
    g5 = nc.alloc_sbuf_tensor("g5", [128, W], F32)
    g7 = nc.alloc_sbuf_tensor("g7", [128, W], F32)
    acc = nc.alloc_sbuf_tensor("acc", [128, W], F32)
    pl = [nc.alloc_sbuf_tensor(f"pl{i}", [128, W], BF16) for i in range(2)]
    p_ = p

    def dequant(src_d, ssrc_d, dst_d, rows, width, init_waits=()):
        nt = (rows + 127) // 128
        st_tok = [None, None]
        prev = list(init_waits)
        for t in range(nt):
            r0 = t * 128
            pr = min(128, rows - r0)
            t_l = p_.dma("y", pk[0:pr, 0:width], src_d[r0:r0 + pr, :],
                         "ldp", waits=prev)
            t_s = p_.dma("y", sch[0:pr, 0:width], ssrc_d[r0:r0 + pr, :],
                         "lds", waits=prev)
            t_sc = p_.op("v", "tensor_copy", scf[0:pr, 0:width],
                         sch[0:pr, 0:width], waits=[t_s])
            prev = []
            for k in range(8):
                t1 = p_.op("v", "tensor_scalar", nib[0:pr, 0:width],
                           pk[0:pr, 0:width], 4 * k, None,
                           ALU.logical_shift_right, waits=[t_l, t_sc])
                t2 = p_.op("v", "tensor_scalar", ti_[0:pr, 0:width],
                           nib[0:pr, 0:width], 7, None, ALU.bitwise_and,
                           waits=[t1])
                t3 = p_.op("v", "tensor_scalar", si_[0:pr, 0:width],
                           nib[0:pr, 0:width], 3, None,
                           ALU.logical_shift_right, waits=[t2])
                t3b = p_.op("v", "tensor_scalar", si_[0:pr, 0:width],
                            si_[0:pr, 0:width], 1, None, ALU.bitwise_and,
                            waits=[t3])
                t4 = p_.op("v", "tensor_copy", tf[0:pr, 0:width],
                           ti_[0:pr, 0:width], waits=[t2])
                t5 = p_.op("v", "tensor_copy", sf[0:pr, 0:width],
                           si_[0:pr, 0:width], waits=[t3b])
                t6 = p_.op("v", "tensor_scalar", g5[0:pr, 0:width],
                           tf[0:pr, 0:width], 4.5, None, ALU.is_gt,
                           waits=[t4])
                t7 = p_.op("v", "tensor_tensor", acc[0:pr, 0:width],
                           tf[0:pr, 0:width], g5[0:pr, 0:width], ALU.add,
                           waits=[t6])
                t8 = p_.op("v", "tensor_scalar", g5[0:pr, 0:width],
                           tf[0:pr, 0:width], 5.5, None, ALU.is_gt,
                           waits=[t7])
                t9 = p_.op("v", "tensor_tensor", acc[0:pr, 0:width],
                           acc[0:pr, 0:width], g5[0:pr, 0:width], ALU.add,
                           waits=[t8])
                t10 = p_.op("v", "tensor_scalar", g7[0:pr, 0:width],
                            tf[0:pr, 0:width], 6.5, None, ALU.is_gt,
                            waits=[t9])
                t11 = p_.op("v", "scalar_tensor_tensor", acc[0:pr, 0:width],
                            g7[0:pr, 0:width], 3.0, acc[0:pr, 0:width],
                            ALU.mult, ALU.add, waits=[t10])
                t12 = p_.op("v", "tensor_scalar", sf[0:pr, 0:width],
                            sf[0:pr, 0:width], -2.0, 1.0, ALU.mult,
                            ALU.add, waits=[t11, t5])
                t13 = p_.op("v", "tensor_tensor", acc[0:pr, 0:width],
                            acc[0:pr, 0:width], sf[0:pr, 0:width],
                            ALU.mult, waits=[t12])
                t14 = p_.op("v", "tensor_scalar", acc[0:pr, 0:width],
                            acc[0:pr, 0:width], 0.5, None, ALU.mult,
                            waits=[t13])
                ob = k % 2
                t15 = p_.op("v", "tensor_tensor", pl[ob][0:pr, 0:width],
                            acc[0:pr, 0:width], scf[0:pr, 0:width],
                            ALU.mult,
                            waits=[t14] + ([st_tok[ob]] if st_tok[ob]
                                           else []))
                dst = dst_d.ap().rearrange("(ip k) o -> k ip o", k=8)
                st_tok[ob] = p_.dma("s", dst[k, r0:r0 + pr, :],
                                    pl[ob][0:pr, 0:width], f"stw{ob}",
                                    waits=[t15])
                prev = [t15]
        return [p_.tok("stw0"), p_.tok("stw1")]

    dequant(gup_d, gus_d, wgu_d, 256, W1)
    toks = dequant(dwn_d, dws_d, wd_d, 704, W2,
                   init_waits=[p.tok("_v"), p.tok("stw0"), p.tok("stw1")])
    p.op("y", "nop", waits=toks)
    p.finalize()
    return nc




def build_ffn1(SQ=1024):
    I = 5632
    nQ = SQ // 128
    nTB = SQ // 512
    nEC = I // 128
    nc = bass.Bass("TRN2", target_bir_lowering=False)
    p = Prog(nc)

    x3_d = nc.dram_tensor("x3", [SQ, D], F32, kind="ExternalInput")
    wgu_d = nc.dram_tensor("wgu", [D, 2 * I], BF16, kind="ExternalInput")
    wd_d = nc.dram_tensor("wd", [I, D], BF16, kind="ExternalInput")
    x4_d = nc.dram_tensor("x4", [SQ, D], F32, kind="ExternalOutput")
    hid_d = nc.dram_tensor("hid_scr", [I, SQ], BF16)
    y_d = nc.dram_tensor("y_scr", [SQ, D], F32)

    ev = nc.alloc_sbuf_tensor("ev", [128, D], F32)
    ln_o = nc.alloc_sbuf_tensor("ln_o", [128, D], F32)
    x_t = [nc.alloc_sbuf_tensor(f"x_t{i}", [128, D], F32) for i in range(2)]
    bjunk = nc.alloc_sbuf_tensor("bjunk", [128, D], BF16)
    st = nc.alloc_sbuf_tensor("st", [128, 16], F32)
    xn_fm = nc.alloc_sbuf_tensor("xn_fm", [128, 16 * SQ], BF16)
    hb = [nc.alloc_sbuf_tensor(f"hb{i}", [128, 512], BF16) for i in range(2)]
    hout = [nc.alloc_sbuf_tensor(f"hout{i}", [128, 512], BF16)
            for i in range(2)]
    y_t = [nc.alloc_sbuf_tensor(f"y_t{i}", [128, D], F32) for i in range(2)]

    tp_ps = [nc.alloc_psum_tensor(f"tp{i}", [128, 128], BF16) for i in range(2)]
    mm_ps = [nc.alloc_psum_tensor(f"mm{i}", [128, 512], F32) for i in range(4)]

    C = Consts(p, fvals=(LN_EPS, 1e-6))
    mm_free = {}

    x_use = [None, None]
    prev_ln = None
    for ti in range(nQ):
        b = ti % 2
        t_ld = p.dma("y", x_t[b][:, :], x3_d[ti * 128:(ti + 1) * 128, :],
                     f"ldx{b}", waits=[x_use[b]] if x_use[b] else [])
        t_n = emit_ln(p, x_t[b][:, :], bjunk[:, :], ev[:, :], st,
                      waits=[t_ld] + ([prev_ln] if prev_ln else []))
        prev_ln = t_n
        x_use[b] = t_n
        cp_toks = []
        for c in range(16):
            bk = tp_ps[c % 2]
            w = [t_n]
            if len(cp_toks) >= 2:
                w.append(cp_toks[-2])
            t_tr = p.op("t", "transpose", bk[:, :],
                        bjunk[:, c * 128:(c + 1) * 128], C.ident[:, :],
                        waits=w)
            cp_toks.append(p.op("v", "tensor_copy",
                                xn_fm[:, c * SQ + ti * 128:
                                      c * SQ + (ti + 1) * 128],
                                bk[:, :], waits=[t_tr]))
    ln_done = [p.tok("_v"), p.tok("_t")]

    with nc.sbuf_tensor("wg0", [128, 16 * 128], BF16) as wg0, \
            nc.sbuf_tensor("wg1", [128, 16 * 128], BF16) as wg1, \
            nc.sbuf_tensor("wu0", [128, 16 * 128], BF16) as wu0, \
            nc.sbuf_tensor("wu1", [128, 16 * 128], BF16) as wu1:
        wg = [wg0, wg1]
        wu = [wu0, wu1]
        wguse = [None, None]
        hid_st = [None, None]
        for ec in range(nEC):
            b = ec % 2
            t_wg = p.dma("y", chunk3(wg[b].ap(), 128),
                         wgu_d.ap()[:, ec * 128:(ec + 1) * 128]
                         .rearrange("(c p) o -> p c o", p=128),
                         f"lwg{b}", waits=[wguse[b]] if wguse[b] else [])
            t_wu = p.dma("s", chunk3(wu[b].ap(), 128),
                         wgu_d.ap()[:, I + ec * 128:I + (ec + 1) * 128]
                         .rearrange("(c p) o -> p c o", p=128),
                         f"lwu{b}", waits=[wguse[b]] if wguse[b] else [])
            for tb in range(nTB):
                bg = mm_ps[tb % 2]
                bu = mm_ps[2 + tb % 2]
                w0 = [t_wg] + ln_done + \
                    ([mm_free.get(tb % 2)] if mm_free.get(tb % 2) else [])
                w1 = [t_wu] + \
                    ([mm_free.get(2 + tb % 2)] if mm_free.get(2 + tb % 2)
                     else [])
                for c in range(16):
                    p.op("t", "matmul", bg[:, :],
                         wg[b][:, c * 128:(c + 1) * 128],
                         xn_fm[:, c * SQ + tb * 512:c * SQ + (tb + 1) * 512],
                         start=(c == 0), stop=(c == 15),
                         waits=w0 if c == 0 else ())
                for c in range(16):
                    p.op("t", "matmul", bu[:, :],
                         wu[b][:, c * 128:(c + 1) * 128],
                         xn_fm[:, c * SQ + tb * 512:c * SQ + (tb + 1) * 512],
                         start=(c == 0), stop=(c == 15),
                         waits=w1 if c == 0 else ())
                t_pe = p.tok("_t")
                t_si = p.op("s", "activation", hb[tb % 2][:, :], bg[:, :],
                            AF.Silu,
                            waits=[t_pe] + ([mm_free.get(("hb", tb % 2))]
                                            if mm_free.get(("hb", tb % 2))
                                            else []))
                mm_free[tb % 2] = t_si
                ob = (ec * nTB + tb) % 2
                t_hm = p.op("v", "tensor_tensor", hout[ob][:, :],
                            hb[tb % 2][:, :], bu[:, :], ALU.mult,
                            waits=[t_si] + ([hid_st[ob]] if hid_st[ob]
                                            else []))
                mm_free[("hb", tb % 2)] = t_hm
                mm_free[2 + tb % 2] = t_hm
                hid_st[ob] = p.dma("y",
                                   hid_d[ec * 128:(ec + 1) * 128,
                                         tb * 512:(tb + 1) * 512],
                                   hout[ob][:, :], f"sth{ob}", waits=[t_hm])
            wguse[b] = p.tok("_t")
    up_done = [p.tok("_t"), p.tok("_v"), p.tok("_s"), p.tok("_g"),
               p.tok("sth0"), p.tok("sth1"), p.tok("stwd0"), p.tok("stwd1")]

    with nc.sbuf_tensor("wdb0", [128, nEC * 256], BF16) as wdb0, \
            nc.sbuf_tensor("wdb1", [128, nEC * 256], BF16) as wdb1, \
            nc.sbuf_tensor("hblk0", [128, nEC * 128], BF16) as hblk0, \
            nc.sbuf_tensor("hblk1", [128, nEC * 128], BF16) as hblk1:
        wdb = [wdb0, wdb1]
        hblk = [hblk0, hblk1]
        wd_tok = {}
        wd_use = {}
        yst = [None, None, None, None]
        huse = {}

        def load_wd(hd):
            b = hd % 2
            wd_tok[hd] = p.dma(
                "s", chunk3(wdb[b].ap(), 256, nch=nEC),
                wd_d.ap()[:, hd * 256:(hd + 1) * 256]
                .rearrange("(c p) o -> p c o", p=128), f"lwq{b}",
                waits=up_done + ([wd_use.get(b)] if wd_use.get(b) else []))

        load_wd(0)
        load_wd(1)
        for hp in range(4):
            hd0, hd1 = 2 * hp, 2 * hp + 1
            for tc in range(nQ):
                hbuf = tc % 2
                t_h = p.dma("g", chunk3(hblk[hbuf].ap(), 128, nch=nEC),
                            hid_d.ap()[:, tc * 128:(tc + 1) * 128]
                            .rearrange("(c p) t -> p c t", p=128),
                            f"lh{hbuf}", waits=up_done +
                            ([huse.get(hbuf)] if huse.get(hbuf) else []))
                for hj, hd in ((0, hd0), (1, hd1)):
                    bk = mm_ps[2 * hj + tc % 2]
                    w0 = [wd_tok[hd], t_h] + \
                        ([mm_free.get(2 * hj + tc % 2)]
                         if mm_free.get(2 * hj + tc % 2) else [])
                    for c in range(nEC):
                        p.op("t", "matmul", bk[:, 0:256],
                             hblk[hbuf][:, c * 128:(c + 1) * 128],
                             wdb[hd % 2][:, c * 256:(c + 1) * 256],
                             start=(c == 0), stop=(c == nEC - 1),
                             waits=w0 if c == 0 else ())
                    t_pe = p.tok("_t")
                    ob = 2 * hj + tc % 2
                    t_y = p.op("s", "activation",
                               ev[:, ob * 256:ob * 256 + 256],
                               bk[:, 0:256], AF.Copy,
                               waits=[t_pe] + ([yst[ob]] if yst[ob]
                                               else []))
                    mm_free[2 * hj + tc % 2] = t_y
                    yst[ob] = p.dma("y", y_d[tc * 128:(tc + 1) * 128,
                                             hd * 256:(hd + 1) * 256],
                                    ev[:, ob * 256:ob * 256 + 256],
                                    f"sty{ob}", waits=[t_y])
                huse[hbuf] = p.tok("_t")
            wd_use[0] = p.tok("_t")
            wd_use[1] = p.tok("_t")
            if 2 * hp + 2 < 8:
                load_wd(2 * hp + 2)
                load_wd(2 * hp + 3)
    down_done = [p.tok("_t"), p.tok("_v"), p.tok("_s"), p.tok("sty0"),
                 p.tok("sty1"), p.tok("sty2"), p.tok("sty3")]

    x_use = [None, None]
    st_tok = [None, None]
    prev_ln = None
    for tc in range(nQ):
        xb = tc % 2
        t_ly = p.dma("s", y_t[xb][:, :], y_d[tc * 128:(tc + 1) * 128, :],
                     f"ly{xb}", waits=down_done +
                     ([x_use[xb]] if x_use[xb] else []))
        t_lx = p.dma("y", x_t[xb][:, :], x3_d[tc * 128:(tc + 1) * 128, :],
                     f"lre{xb}", waits=down_done +
                     ([x_use[xb]] if x_use[xb] else []) +
                     ([st_tok[xb]] if st_tok[xb] else []))
        t_sq = p.op("s", "activation", bjunk[:, :], y_t[xb][:, :],
                    AF.Square, accum_out=st[:, 8:9],
                    waits=[t_ly] + ([prev_ln] if prev_ln else []))
        t_ms = p.op("v", "tensor_scalar", st[:, 9:10], st[:, 8:9], 1.0 / D,
                    None, ALU.mult, waits=[t_sq])
        t_sd = p.op("s", "activation", st[:, 10:11], st[:, 9:10], AF.Sqrt,
                    bias=1e-6, waits=[t_ms])
        t_r = p.op("v", "reciprocal", st[:, 11:12], st[:, 10:11],
                   waits=[t_sd])
        t_rm = p.op("v", "tensor_scalar", y_t[xb][:, :], y_t[xb][:, :],
                    st[:, 11:12], None, ALU.mult, waits=[t_r])
        t_n = emit_ln(p, y_t[xb][:, :], ln_o[:, :], bjunk[:, :], st,
                      waits=[t_rm])
        prev_ln = t_n
        t_add = p.op("v", "tensor_tensor", x_t[xb][:, :], x_t[xb][:, :],
                     ln_o[:, :], ALU.add, waits=[t_n, t_lx])
        x_use[xb] = t_add
        st_tok[xb] = p.dma("y", x4_d[tc * 128:(tc + 1) * 128, :],
                           x_t[xb][:, :], f"stx{xb}", waits=[t_add])
    p.op("y", "nop", waits=[p.tok("stx0"), p.tok("stx1")])
    p.finalize()
    return nc





S_FULL = 4096
SQ_OWN = 1024
CORES = list(range(NCORES))


def _run(nc, in_maps):
    return run_bass_kernel_spmd(nc, in_maps, core_ids=CORES)


def kernel(L0_attn_x, L0_attn_Wq, L0_attn_bq, L0_attn_Wk, L0_attn_bk,
           L0_attn_Wv, L0_attn_bv, L0_attn_Wo, L0_attn_bo, L0_attn_Wi,
           L0_attn_bi, L0_ffn_W_up, L0_ffn_W_gate, L0_ffn_W_down,
           L1_attn_slope_rate, L1_attn_W_qkvu, L1_attn_W_out,
           L1_ffn_gate_up_packed, L1_ffn_gate_up_scales,
           L1_ffn_down_packed, L1_ffn_down_scales):
    f32 = np.float32
    x = np.ascontiguousarray(np.asarray(L0_attn_x, dtype=f32))
    B = x.shape[0]
    row = lambda a: np.ascontiguousarray(np.asarray(a, f32).reshape(1, -1))

    W_up = np.asarray(L0_ffn_W_up, f32)
    W_gate = np.asarray(L0_ffn_W_gate, f32)
    W_down = np.asarray(L0_ffn_W_down, f32)
    gup = np.asarray(L1_ffn_gate_up_packed)
    gus = np.asarray(L1_ffn_gate_up_scales)
    dwn = np.asarray(L1_ffn_down_packed)
    dws = np.asarray(L1_ffn_down_scales)
    gupT = np.ascontiguousarray(gup.T)
    gusE = np.repeat(gus, 16, axis=0)
    dwnT = np.ascontiguousarray(dwn.T)
    dwsE = np.repeat(dws, 16, axis=0)

    nc = build_l0pre(SQ_OWN)
    common = {
        "Wq": np.asarray(L0_attn_Wq, f32), "Wk": np.asarray(L0_attn_Wk, f32),
        "Wv": np.asarray(L0_attn_Wv, f32),
        "bq": row(L0_attn_bq), "bk": row(L0_attn_bk), "bv": row(L0_attn_bv),
        "Wi": np.ascontiguousarray(np.asarray(L0_attn_Wi, f32).T),
        "bi": np.asarray(L0_attn_bi, f32).reshape(1, 1),
    }
    in_maps = []
    for c in CORES:
        b, qq = c // 4, c % 4
        in_maps.append(dict(
            common,
            x_own=np.ascontiguousarray(x[b, qq * SQ_OWN:(qq + 1) * SQ_OWN]),
            up_sl=np.ascontiguousarray(W_up[c * 1024:(c + 1) * 1024]),
            gate_sl=np.ascontiguousarray(W_gate[c * 1024:(c + 1) * 1024]),
            down_sl=np.ascontiguousarray(W_down[:, c * 1024:(c + 1) * 1024].T),
            gup_T=np.ascontiguousarray(gupT[:, c * 1408:(c + 1) * 1408]),
            gus=np.ascontiguousarray(gusE[:, c * 1408:(c + 1) * 1408]),
            dwn_T=np.ascontiguousarray(dwnT[:, c * 256:(c + 1) * 256]),
            dws=np.ascontiguousarray(dwsE[:, c * 256:(c + 1) * 256])))
    res = _run(nc, in_maps)
    q_o = [np.asarray(res.results[c]["q_o"]) for c in CORES]
    k_o = [np.asarray(res.results[c]["k_o"]) for c in CORES]
    v_o = [np.asarray(res.results[c]["v_o"]) for c in CORES]
    imp_o = [np.asarray(res.results[c]["imp_o"]) for c in CORES]
    parts = np.concatenate([np.asarray(res.results[c]["parts"])
                            for c in CORES], axis=0)
    wgu = np.concatenate([np.asarray(res.results[c]["wgu_sl"])
                          for c in CORES], axis=1)
    wd = np.concatenate([np.asarray(res.results[c]["wd_sl"])
                         for c in CORES], axis=1)

    nc = build_l0main(S_FULL, SQ_OWN)
    in_maps = []
    for c in CORES:
        b, qq = c // 4, c % 4
        grp = list(range(b * 4, b * 4 + 4))
        in_maps.append({
            "x_own": np.ascontiguousarray(
                x[b, qq * SQ_OWN:(qq + 1) * SQ_OWN]),
            "q": q_o[c],
            "kf": np.ascontiguousarray(
                np.concatenate([k_o[g] for g in grp], axis=1)),
            "vf": np.ascontiguousarray(
                np.concatenate([v_o[g] for g in grp], axis=0)),
            "impf": np.ascontiguousarray(
                np.concatenate([imp_o[g] for g in grp], axis=1)),
            "Wo": np.asarray(L0_attn_Wo, f32), "bo": row(L0_attn_bo)})
    res = _run(nc, in_maps)
    x1 = [np.asarray(res.results[c]["x1"]) for c in CORES]

    nc = build_ffn0(SQ_OWN)
    common = {
        "upT": np.ascontiguousarray(W_up.T),
        "gateT": np.ascontiguousarray(W_gate.T),
        "downT": np.ascontiguousarray(W_down.T),
        "parts": np.ascontiguousarray(parts),
    }
    res = _run(nc, [dict(common, x1=x1[c]) for c in CORES])
    x2 = [np.asarray(res.results[c]["x2"]) for c in CORES]

    slope = np.ascontiguousarray(
        np.asarray(L1_attn_slope_rate, f32).reshape(1, NH))
    nc = build_l1a_local(SQ_OWN)
    common = {
        "WqkvuT": np.ascontiguousarray(np.asarray(L1_attn_W_qkvu, f32).T),
        "slope": slope,
    }
    res = _run(nc, [dict(common, x2=x2[c]) for c in CORES])
    sc = [np.asarray(res.results[c]["sc"]) for c in CORES]
    o_part = [np.asarray(res.results[c]["o_part"]) for c in CORES]
    qd = [np.asarray(res.results[c]["qd"]) for c in CORES]
    u_o = [np.asarray(res.results[c]["u_o"]) for c in CORES]

    z = np.zeros((NH * 128, 128), f32)
    nc = build_l1a_fix(SQ_OWN)
    in_maps = []
    for c in CORES:
        b, qq = c // 4, c % 4
        pred = lambda j: sc[b * 4 + qq - j] if qq - j >= 0 else z
        in_maps.append({
            "x2": x2[c], "o_part": o_part[c], "qd": qd[c], "u_o": u_o[c],
            "scp1": pred(1), "scp2": pred(2), "scp3": pred(3),
            "slope": slope,
            "WoutT": np.ascontiguousarray(np.asarray(L1_attn_W_out, f32).T),
        })
    res = _run(nc, in_maps)
    x3 = [np.asarray(res.results[c]["x3"]) for c in CORES]

    nc = build_ffn1(SQ_OWN)
    common = {"wgu": np.ascontiguousarray(wgu),
              "wd": np.ascontiguousarray(wd)}
    res = _run(nc, [dict(common, x3=x3[c]) for c in CORES])
    x4 = [np.asarray(res.results[c]["x4"]) for c in CORES]

    out = np.empty_like(x)
    for c in CORES:
        b, qq = c // 4, c % 4
        out[b, qq * SQ_OWN:(qq + 1) * SQ_OWN] = x4[c]
    return out






def build_l0pre(SQ=1024):
    nQ = SQ // 128
    nTB = SQ // 512
    nc = bass.Bass("TRN2", target_bir_lowering=False)
    p = Prog(nc)

    x_d = nc.dram_tensor("x_own", [SQ, D], F32, kind="ExternalInput")
    wq_d = nc.dram_tensor("Wq", [D, D], F32, kind="ExternalInput")
    wk_d = nc.dram_tensor("Wk", [D, D], F32, kind="ExternalInput")
    wv_d = nc.dram_tensor("Wv", [D, D], F32, kind="ExternalInput")
    bq_d = nc.dram_tensor("bq", [1, D], F32, kind="ExternalInput")
    bk_d = nc.dram_tensor("bk", [1, D], F32, kind="ExternalInput")
    bv_d = nc.dram_tensor("bv", [1, D], F32, kind="ExternalInput")
    wi_d = nc.dram_tensor("Wi", [1, D], F32, kind="ExternalInput")
    bi_d = nc.dram_tensor("bi", [1, 1], F32, kind="ExternalInput")
    q_od = nc.dram_tensor("q_o", [D, SQ], BF16, kind="ExternalOutput")
    k_od = nc.dram_tensor("k_o", [D, SQ], BF16, kind="ExternalOutput")
    v_od = nc.dram_tensor("v_o", [SQ, D], BF16, kind="ExternalOutput")
    imp_od = nc.dram_tensor("imp_o", [128, nQ], F32, kind="ExternalOutput")
    psl = [nc.dram_tensor(n, [1024, D], F32, kind="ExternalInput")
           for n in ("up_sl", "gate_sl", "down_sl")]
    parts_od = nc.dram_tensor("parts", [1, 4], F32, kind="ExternalOutput")
    W1, W2 = 1408, 256
    gup_d = nc.dram_tensor("gup_T", [256, W1], I32, kind="ExternalInput")
    gus_d = nc.dram_tensor("gus", [256, W1], mybir.dt.float16,
                           kind="ExternalInput")
    dwn_d = nc.dram_tensor("dwn_T", [704, W2], I32, kind="ExternalInput")
    dws_d = nc.dram_tensor("dws", [704, W2], mybir.dt.float16,
                           kind="ExternalInput")
    wgu_od = nc.dram_tensor("wgu_sl", [2048, W1], BF16, kind="ExternalOutput")
    wd_od = nc.dram_tensor("wd_sl", [5632, W2], BF16, kind="ExternalOutput")

    x_t = [nc.alloc_sbuf_tensor(f"x_t{i}", [128, D], F32) for i in range(2)]
    st = nc.alloc_sbuf_tensor("st", [128, 24], F32)
    xn_fm = nc.alloc_sbuf_tensor("xn_fm", [128, 16 * SQ], BF16)
    imp_sb = nc.alloc_sbuf_tensor("imp_sb", [128, nQ], F32)
    bias_sb = nc.alloc_sbuf_tensor("bias_sb", [1, D], BF16)
    wirow = nc.alloc_sbuf_tensor("wirow", [1, D], F32)
    bif = nc.alloc_sbuf_tensor("bif", [1, 1], F32)
    swi_c = nc.alloc_sbuf_tensor("swi_c", [128, 1], F32)
    bi_c = nc.alloc_sbuf_tensor("bi_c", [128, 1], F32)
    kv_out = [nc.alloc_sbuf_tensor(f"kv_out{i}", [128, 512], BF16)
              for i in range(2)]
    o_sb = nc.alloc_sbuf_tensor("po_sb", [1, 4], F32)
    red = nc.alloc_sbuf_tensor("pred", [128, 4], F32)
    pw0 = nc.alloc_sbuf_tensor("pw0", [128, D], F32)
    pw1 = nc.alloc_sbuf_tensor("pw1", [128, D], F32)
    pacc = nc.alloc_sbuf_tensor("pacc", [128, 26], F32)
    W1d, W2d = 1408, 256
    W = max(W1d, W2d)
    pk = nc.alloc_sbuf_tensor("dq_pk", [128, W], I32)
    scf = nc.alloc_sbuf_tensor("dq_scf", [128, W], F32)
    sch = nc.alloc_sbuf_tensor("dq_sch", [128, W], mybir.dt.float16)
    nib = nc.alloc_sbuf_tensor("dq_nib", [128, W], I32)
    ti_ = nc.alloc_sbuf_tensor("dq_ti", [128, W], I32)
    si_ = nc.alloc_sbuf_tensor("dq_si", [128, W], I32)
    tf = nc.alloc_sbuf_tensor("dq_tf", [128, W], F32)
    sf = nc.alloc_sbuf_tensor("dq_sf", [128, W], F32)
    g5 = nc.alloc_sbuf_tensor("dq_g5", [128, W], F32)
    g7 = nc.alloc_sbuf_tensor("dq_g7", [128, W], F32)
    dacc = nc.alloc_sbuf_tensor("dq_acc", [128, W], F32)
    pl = [nc.alloc_sbuf_tensor(f"dq_pl{i}", [128, W], BF16) for i in range(2)]

    tp_ps = [nc.alloc_psum_tensor(f"tp{i}", [128, 128], BF16) for i in range(2)]
    mm_ps = [nc.alloc_psum_tensor(f"mm{i}", [128, 512], F32) for i in range(4)]
    db_ps = nc.alloc_psum_tensor("db_ps", [128, 512], F32)

    C = Consts(p, fvals=(LN_EPS,))
    mm_free = {}

    t_wi = p.dma("y", wirow[0:1, :], wi_d[0:1, :], "ldwi")
    t_bi = p.dma("y", bif[0:1, :], bi_d[0:1, :], "ldbi")

    with nc.sbuf_tensor("bj0", [128, D], BF16) as bj0, \
            nc.sbuf_tensor("bj1", [128, D], BF16) as bj1, \
            nc.sbuf_tensor("jk0", [128, D], BF16) as jk0, \
            nc.sbuf_tensor("jk1", [128, D], BF16) as jk1, \
            nc.sbuf_tensor("jk2", [128, D], BF16) as jk2, \
            nc.sbuf_tensor("jk3", [128, D], BF16) as jk3, \
            nc.sbuf_tensor("ij0", [128, D], F32) as ij0, \
            nc.sbuf_tensor("ij1", [128, D], F32) as ij1, \
            nc.sbuf_tensor("wi_bc", [128, D], F32) as wi_bc:
        bj = [bj0, bj1]
        jka = [jk0, jk1]
        jkb = [jk2, jk3]
        ij = [ij0, ij1]
        wb_toks = []
        for ob in range(4):
            t_wb = p.op("t", "matmul", db_ps[:, :], C.ones_rf[0:1, 0:128],
                        wirow[0:1, ob * 512:(ob + 1) * 512], start=True,
                        stop=True, waits=[t_wi] + (wb_toks[-1:] or []))
            wb_toks.append(p.op("v", "tensor_copy",
                                wi_bc[:, ob * 512:(ob + 1) * 512],
                                db_ps[:, :], waits=[t_wb]))
        t_bb = p.op("t", "matmul", db_ps[:, 0:1], C.ones_rf[0:1, 0:128],
                    bif[0:1, 0:1], start=True, stop=True,
                    waits=[t_bi] + wb_toks[-1:])
        t_bic = p.op("v", "tensor_copy", bi_c[:, :], db_ps[:, 0:1],
                     waits=[t_bb])
        t_swi = p.op("v", "tensor_reduce", swi_c[:, :], wi_bc[:, :],
                     mybir.AxisListType.X, ALU.add, waits=wb_toks)
        imp_setup = [t_bic, t_swi]

        x_use = [None, None]
        bj_use = [None, None]
        ij_use = [None, None]
        ln_tok = [None, None]
        for ti in range(nQ):
            q = ti % 2
            t_ld = p.dma("y", x_t[q][:, :], x_d[ti * 128:(ti + 1) * 128, :],
                         f"ldx{q}", waits=list(x_use[q] or ()))
            t_n = emit_ln(p, x_t[q][:, :], bj[q][:, :], jka[q][:, :], st,
                          waits=[t_ld] + ([bj_use[q]] if bj_use[q] else []) +
                          ([ln_tok[q]] if ln_tok[q] else []),
                          sc=8 * q, junk2_ap=jkb[q][:, :])
            ln_tok[q] = t_n
            t_mu = p.op("g", "tensor_tensor", ij[q][:, :], x_t[q][:, :],
                        wi_bc[:, :], ALU.mult,
                        waits=[t_ld] + imp_setup +
                        ([ij_use[q]] if ij_use[q] else []))
            t_us = p.op("v", "tensor_reduce", st[:, 16 + 4 * q:17 + 4 * q],
                        ij[q][:, :], mybir.AxisListType.X, ALU.add,
                        waits=[t_mu])
            ij_use[q] = t_us
            x_use[q] = (t_n, t_mu, p.tok("_s"))
            t_i1 = p.op("v", "tensor_tensor", st[:, 17 + 4 * q:18 + 4 * q],
                        st[:, 8 * q + 2:8 * q + 3], swi_c[:, :], ALU.mult,
                        waits=[t_us, t_n])
            t_i2 = p.op("v", "scalar_tensor_tensor",
                        st[:, 18 + 4 * q:19 + 4 * q],
                        st[:, 17 + 4 * q:18 + 4 * q], -1.0,
                        st[:, 16 + 4 * q:17 + 4 * q], ALU.mult, ALU.add,
                        waits=[t_i1])
            t_i3 = p.op("v", "tensor_tensor", st[:, 19 + 4 * q:20 + 4 * q],
                        st[:, 18 + 4 * q:19 + 4 * q],
                        st[:, 8 * q + 6:8 * q + 7], ALU.mult, waits=[t_i2])
            t_i4 = p.op("v", "tensor_tensor", imp_sb[:, ti:ti + 1],
                        st[:, 19 + 4 * q:20 + 4 * q], bi_c[:, :],
                        ALU.add, waits=[t_i3])
            cp_toks = []
            for c in range(16):
                bk = tp_ps[c % 2]
                w = [t_n]
                if len(cp_toks) >= 2:
                    w.append(cp_toks[-2])
                t_tr = p.op("t", "transpose", bk[:, :],
                            bj[q][:, c * 128:(c + 1) * 128], C.ident[:, :],
                            waits=w)
                cp_toks.append(p.op("v", "tensor_copy",
                                    xn_fm[:, c * SQ + ti * 128:
                                          c * SQ + (ti + 1) * 128],
                                    bk[:, :], waits=[t_tr]))
            bj_use[q] = p.tok("_t")
        t_imp_st = p.dma("y", imp_od[:, :], imp_sb[:, :], "stimp",
                         waits=[t_i4])
    ln_done = [p.tok("_v"), p.tok("_t")]

    kv_tok = [None, None]
    with nc.sbuf_tensor("wc0", [128, 16 * 128], BF16) as wc0, \
            nc.sbuf_tensor("wc1", [128, 16 * 128], BF16) as wc1, \
            nc.sbuf_tensor("wb0", [128, 16 * 512], BF16) as wb0:
        wc = [wc0, wc1]
        wuse = [None, None]
        for kind, w_dram, b_dram, dst in (("q", wq_d, bq_d, q_od),
                                          ("k", wk_d, bk_d, k_od)):
            t_b = p.dma("g", bias_sb[0:1, :], b_dram[0:1, :], "ldbias",
                        waits=[p.tok("_t")])
            for o in range(16):
                b = o % 2
                t_w = p.dma("g", chunk3(wc[b].ap(), 128),
                            wcol_ap(w_dram, o * 128, 128), f"lw{b}",
                            waits=[wuse[b]] if wuse[b] else [])
                for tb in range(nTB):
                    bk = mm_ps[tb % 2]
                    w0 = [t_w, t_b] + ln_done + \
                        ([mm_free.get(tb % 2)] if mm_free.get(tb % 2)
                         else [])
                    for c in range(16):
                        p.op("t", "matmul", bk[:, :],
                             wc[b][:, c * 128:(c + 1) * 128],
                             xn_fm[:, c * SQ + tb * 512:
                                   c * SQ + (tb + 1) * 512],
                             start=(c == 0), stop=False,
                             waits=w0 if c == 0 else ())
                    t_mm = p.op("t", "matmul", bk[:, :],
                                bias_sb[0:1, o * 128:(o + 1) * 128],
                                C.ones_r[0:1, 0:512], start=False,
                                stop=True)
                    ob = (o * nTB + tb) % 2
                    t_e = p.op("v", "tensor_copy", kv_out[ob][:, :],
                               bk[:, :],
                               waits=[t_mm] + ([kv_tok[ob]] if kv_tok[ob]
                                               else []))
                    mm_free[tb % 2] = t_e
                    kv_tok[ob] = p.dma("y",
                                       dst[o * 128:(o + 1) * 128,
                                           tb * 512:(tb + 1) * 512],
                                       kv_out[ob][:, :], f"stkv{ob}",
                                       waits=[t_e])
                wuse[b] = p.tok("_t")
        t_b = p.dma("g", bias_sb[0:1, :], bv_d[0:1, :], "ldbias",
                    waits=[p.tok("_t")])
        for ob4 in range(4):
            t_w = p.dma("g", chunk3(wb0.ap(), 512),
                        wcol_ap(wv_d, ob4 * 512, 512), "lwb",
                        waits=[p.tok("_t")])
            for tc in range(nQ):
                bk = mm_ps[2 + tc % 2]
                w0 = [t_w, t_b] + ln_done + \
                    ([mm_free.get(2 + tc % 2)] if mm_free.get(2 + tc % 2)
                     else [])
                for c in range(16):
                    p.op("t", "matmul", bk[:, :],
                         xn_fm[:, c * SQ + tc * 128:c * SQ + (tc + 1) * 128],
                         wb0[:, c * 512:(c + 1) * 512],
                         start=(c == 0), stop=False,
                         waits=w0 if c == 0 else ())
                t_mm = p.op("t", "matmul", bk[:, :], C.ones_r[0:1, 0:128],
                            bias_sb[0:1, ob4 * 512:(ob4 + 1) * 512],
                            start=False, stop=True)
                ob = (ob4 * nQ + tc) % 2
                t_e = p.op("v", "tensor_copy", kv_out[ob][:, :], bk[:, :],
                           waits=[t_mm] + ([kv_tok[ob]] if kv_tok[ob]
                                           else []))
                mm_free[2 + tc % 2] = t_e
                kv_tok[ob] = p.dma("y",
                                   v_od[tc * 128:(tc + 1) * 128,
                                        ob4 * 512:(ob4 + 1) * 512],
                                   kv_out[ob][:, :], f"stkv{ob}",
                                   waits=[t_e])

    if True:
        acc = pacc
        w_t = [pw0, pw1]
        use_tok = [None, None]
        col = 0
        toks = []
        for mi, w_d in enumerate(psl):
            for ti in range(8):
                b = ti % 2
                t_ld = p.dma("s", w_t[b][:, :],
                             w_d[ti * 128:(ti + 1) * 128, :], f"pld{b}",
                             waits=[use_tok[b]] if use_tok[b] else [])
                t_a = p.op("v", "tensor_reduce", acc[:, col:col + 1],
                           w_t[b][:, :], mybir.AxisListType.X, ALU.add,
                           apply_absolute_value=True,
                           waits=[t_ld] + (toks[-1:] or []))
                use_tok[b] = t_a
                toks.append(t_a)
                col += 1
        rt = []
        for m in range(3):
            rt.append(p.op("v", "tensor_reduce", red[:, m:m + 1],
                           acc[:, m * 8:(m + 1) * 8], mybir.AxisListType.X,
                           ALU.add, waits=toks))
        t_z = p.op("v", "memset", red[:, 3:4], 0.0)
        t_s = p.op("t", "matmul", db_ps[0:1, 0:4], C.ones_cf[:, :],
                   red[:, :], start=True, stop=True,
                   waits=rt + [t_z, p.tok("_v")])
        t_c = p.op("v", "tensor_copy", o_sb[0:1, :], db_ps[0:1, 0:4],
                   waits=[t_s])
        p.dma("y", parts_od[0:1, :], o_sb[0:1, :], "stpo", waits=[t_c])


    def dequant(src_d, ssrc_d, dst_d, rows, width, init_waits=()):
        nt = (rows + 127) // 128
        st_tok = [None, None]
        prev = list(init_waits)
        for t in range(nt):
            r0 = t * 128
            pr = min(128, rows - r0)
            t_l = p.dma("s", pk[0:pr, 0:width], src_d[r0:r0 + pr, :],
                        "dqldp", waits=prev)
            t_s = p.dma("s", sch[0:pr, 0:width], ssrc_d[r0:r0 + pr, :],
                        "dqlds", waits=prev)
            t_sc = p.op("v", "tensor_copy", scf[0:pr, 0:width],
                        sch[0:pr, 0:width], waits=[t_s])
            prev = []
            for k in range(8):
                t1 = p.op("v", "tensor_scalar", nib[0:pr, 0:width],
                          pk[0:pr, 0:width], 4 * k, None,
                          ALU.logical_shift_right, waits=[t_l, t_sc])
                t2 = p.op("v", "tensor_scalar", ti_[0:pr, 0:width],
                          nib[0:pr, 0:width], 7, None, ALU.bitwise_and,
                          waits=[t1])
                t3 = p.op("v", "tensor_scalar", si_[0:pr, 0:width],
                          nib[0:pr, 0:width], 3, None,
                          ALU.logical_shift_right, waits=[t2])
                t3b = p.op("v", "tensor_scalar", si_[0:pr, 0:width],
                           si_[0:pr, 0:width], 1, None, ALU.bitwise_and,
                           waits=[t3])
                t4 = p.op("v", "tensor_copy", tf[0:pr, 0:width],
                          ti_[0:pr, 0:width], waits=[t2])
                t5 = p.op("v", "tensor_copy", sf[0:pr, 0:width],
                          si_[0:pr, 0:width], waits=[t3b])
                t6 = p.op("v", "tensor_scalar", g5[0:pr, 0:width],
                          tf[0:pr, 0:width], 4.5, None, ALU.is_gt,
                          waits=[t4])
                t7 = p.op("v", "tensor_tensor", dacc[0:pr, 0:width],
                          tf[0:pr, 0:width], g5[0:pr, 0:width], ALU.add,
                          waits=[t6])
                t8 = p.op("v", "tensor_scalar", g5[0:pr, 0:width],
                          tf[0:pr, 0:width], 5.5, None, ALU.is_gt,
                          waits=[t7])
                t9 = p.op("v", "tensor_tensor", dacc[0:pr, 0:width],
                          dacc[0:pr, 0:width], g5[0:pr, 0:width], ALU.add,
                          waits=[t8])
                t10 = p.op("v", "tensor_scalar", g7[0:pr, 0:width],
                           tf[0:pr, 0:width], 6.5, None, ALU.is_gt,
                           waits=[t9])
                t11 = p.op("v", "scalar_tensor_tensor", dacc[0:pr, 0:width],
                           g7[0:pr, 0:width], 3.0, dacc[0:pr, 0:width],
                           ALU.mult, ALU.add, waits=[t10])
                t12 = p.op("v", "tensor_scalar", sf[0:pr, 0:width],
                           sf[0:pr, 0:width], -2.0, 1.0, ALU.mult,
                           ALU.add, waits=[t11, t5])
                t13 = p.op("v", "tensor_tensor", dacc[0:pr, 0:width],
                           dacc[0:pr, 0:width], sf[0:pr, 0:width],
                           ALU.mult, waits=[t12])
                t14 = p.op("v", "tensor_scalar", dacc[0:pr, 0:width],
                           dacc[0:pr, 0:width], 0.5, None, ALU.mult,
                           waits=[t13])
                ob = k % 2
                t15 = p.op("v", "tensor_tensor", pl[ob][0:pr, 0:width],
                           dacc[0:pr, 0:width], scf[0:pr, 0:width],
                           ALU.mult,
                           waits=[t14] + ([st_tok[ob]] if st_tok[ob]
                                          else []))
                dst = dst_d.ap().rearrange("(ip k) o -> k ip o", k=8)
                st_tok[ob] = p.dma("s", dst[k, r0:r0 + pr, :],
                                   pl[ob][0:pr, 0:width], f"dqst{ob}",
                                   waits=[t15])
                prev = [t15]
        return [p.tok("dqst0"), p.tok("dqst1")]

    dequant(gup_d, gus_d, wgu_od, 256, W1)
    dq_toks = dequant(dwn_d, dws_d, wd_od, 704, W2,
                      init_waits=[p.tok("_v"), p.tok("dqst0"),
                                  p.tok("dqst1")])
    p.op("y", "nop", waits=[p.tok("stkv0"), p.tok("stkv1"), p.tok("stimp"),
                            p.tok("stpo")] + dq_toks)
    p.finalize()
    return nc






def build_l0main(S=4096, SQ=1024):
    nT = S // 128
    nTB = S // 512
    nQ = SQ // 128
    TIDX = min(max(1, int(S * 0.2)), S - 1)
    nqh = max(1, SQ // 512)
    QW = min(512, SQ)

    nc = bass.Bass("TRN2", target_bir_lowering=False)
    p = Prog(nc)

    x_d = nc.dram_tensor("x_own", [SQ, D], F32, kind="ExternalInput")
    q_d = nc.dram_tensor("q", [D, SQ], BF16, kind="ExternalInput")
    kf_d = nc.dram_tensor("kf", [D, S], BF16, kind="ExternalInput")
    vf_d = nc.dram_tensor("vf", [S, D], BF16, kind="ExternalInput")
    impf_d = nc.dram_tensor("impf", [128, nT], F32, kind="ExternalInput")
    wo_d = nc.dram_tensor("Wo", [D, D], F32, kind="ExternalInput")
    bo_d = nc.dram_tensor("bo", [1, D], F32, kind="ExternalInput")
    x1_d = nc.dram_tensor("x1", [SQ, D], F32, kind="ExternalOutput")
    mr_d = nc.dram_tensor("mr_scr", [nT, 128], BF16)

    q_fm = nc.alloc_sbuf_tensor("q_fm", [128, 16 * SQ], BF16)
    at_fm = nc.alloc_sbuf_tensor("at_fm", [128, 16 * SQ], BF16)
    ev = nc.alloc_sbuf_tensor("ev", [128, D], F32)
    ln_o = nc.alloc_sbuf_tensor("ln_o", [128, D], F32)
    x_t = [nc.alloc_sbuf_tensor(f"x_t{i}", [128, D], F32) for i in range(2)]
    bjunk = nc.alloc_sbuf_tensor("bjunk", [128, D], BF16)
    st = nc.alloc_sbuf_tensor("st", [128, 24], F32)
    imp_sb = nc.alloc_sbuf_tensor("imp_sb", [128, nT], F32)
    bis = nc.alloc_sbuf_tensor("bis", [128, 8], F32)
    bmask = nc.alloc_sbuf_tensor("bmask", [128, 2], I32)
    cnt_c = nc.alloc_sbuf_tensor("cnt_c", [128, 1], F32)
    c11 = nc.alloc_sbuf_tensor("c11", [1, 1], F32)
    mrow = nc.alloc_sbuf_tensor("mrow", [1, S], BF16)
    mbc = nc.alloc_sbuf_tensor("mbc", [128, S], BF16)
    mcol = nc.alloc_sbuf_tensor("mcol", [128, nT], F32)
    mcolb = nc.alloc_sbuf_tensor("mcolb", [128, nT], BF16)
    mstage = nc.alloc_sbuf_tensor("mstage", [128, 128], BF16)
    probs = [nc.alloc_sbuf_tensor(f"probs{i}", [128, 512], BF16)
             for i in range(2)]
    vmask = nc.alloc_sbuf_tensor("vmask", [128, S], BF16)
    onesb = nc.alloc_sbuf_tensor("onesb", [128, 128], BF16)
    dnr = nc.alloc_sbuf_tensor("dnr", [1, 512], F32)
    dnb = nc.alloc_sbuf_tensor("dnb", [128, 512], F32)
    bias_sb = nc.alloc_sbuf_tensor("bias_sb", [1, D], BF16)

    tp_ps = [nc.alloc_psum_tensor(f"tp{i}", [128, 128], BF16) for i in range(2)]
    mm_ps = [nc.alloc_psum_tensor(f"mm{i}", [128, 512], F32) for i in range(4)]
    dn_ps = nc.alloc_psum_tensor("dn_ps", [1, 512], F32)
    db_ps = nc.alloc_psum_tensor("db_ps", [128, 512], F32)

    C = Consts(p, fvals=(LN_EPS, ATT_SCALE))
    mm_free = {}

    t_q = p.dma("y", chunk3(q_fm.ap(), SQ),
                q_d.ap().rearrange("(c p) t -> p c t", p=128), "ldq")
    t_if = p.dma("y", imp_sb[:, :], impf_d[:, :], "ldimp")

    t0 = p.op("v", "memset", bis[:, 0:1], -1000.0, waits=[t_if])
    t1 = p.op("v", "memset", bis[:, 1:2], 1000.0)
    prev = [t0, t1]
    for _ in range(50):
        t_a = p.op("v", "tensor_scalar", bis[:, 3:4], bis[:, 1:2], 0.5, None,
                   ALU.mult, waits=prev)
        t_mid = p.op("v", "scalar_tensor_tensor", bis[:, 2:3], bis[:, 0:1],
                     0.5, bis[:, 3:4], ALU.mult, ALU.add, waits=[t_a])
        t_c = p.op("v", "tensor_scalar", mcol[:, 0:nT], imp_sb[:, 0:nT],
                   bis[:, 2:3], None, ALU.is_gt, ALU.add,
                   accum_out=cnt_c[:, 0:1], waits=[t_mid])
        t_s = p.op("t", "matmul", dn_ps[0:1, 0:1], C.ones_cf[:, :],
                   cnt_c[:, 0:1], start=True, stop=True,
                   waits=[t_c] + ([mm_free["dn"]] if "dn" in mm_free
                                  else []))
        t_cp = p.op("v", "tensor_copy", c11[0:1, 0:1], dn_ps[0:1, 0:1],
                    waits=[t_s])
        mm_free["dn"] = t_cp
        t_bc = p.op("t", "matmul", db_ps[:, 0:1], C.ones_rf[0:1, 0:128],
                    c11[0:1, 0:1], start=True, stop=True,
                    waits=[t_cp] + ([mm_free["db"]] if "db" in mm_free
                                    else []))
        t_cd = p.op("v", "tensor_scalar", bmask[:, 0:1], db_ps[:, 0:1],
                    TIDX + 0.5, None, ALU.is_lt, waits=[t_bc])
        t_ic = p.op("v", "tensor_scalar", bmask[:, 1:2], db_ps[:, 0:1],
                    TIDX + 0.5, None, ALU.is_ge, waits=[t_cd])
        mm_free["db"] = t_ic
        t_h = p.op("v", "copy_predicated", bis[:, 1:2], bmask[:, 0:1],
                   bis[:, 2:3], waits=[t_ic])
        t_l = p.op("v", "copy_predicated", bis[:, 0:1], bmask[:, 1:2],
                   bis[:, 2:3], waits=[t_h])
        prev = [t_l]
    t_mc = p.op("v", "tensor_scalar", mcol[:, 0:nT], imp_sb[:, 0:nT],
                bis[:, 1:2], None, ALU.is_gt, waits=prev)
    t_mcb = p.op("v", "tensor_copy", mcolb[:, 0:nT], mcol[:, 0:nT],
                 waits=[t_mc])
    t_tr = p.op("t", "transpose", tp_ps[0][0:nT, :], mcolb[:, 0:nT],
                C.ident[:, :], waits=[t_mcb])
    t_ms = p.op("v", "tensor_copy", mstage[0:nT, :], tp_ps[0][0:nT, :],
                waits=[t_tr])
    t_st = p.dma("s", mr_d[:, :], mstage[0:nT, :], "mrb", waits=[t_ms])
    t_lr = p.dma("s", mrow[0:1, 0:S],
                 mr_d.ap().rearrange("(o a) b -> o (a b)", o=1), "mrb2",
                 waits=[t_st])
    for tb in range(nTB):
        t_b = p.op("t", "matmul", db_ps[:, :], C.ones_r[0:1, 0:128],
                   mrow[0:1, tb * 512:(tb + 1) * 512], start=True, stop=True,
                   waits=[t_lr, mm_free["db"]])
        mm_free["db"] = p.op("v", "tensor_copy",
                             mbc[:, tb * 512:(tb + 1) * 512], db_ps[:, :],
                             waits=[t_b])
    t_ob = p.op("v", "memset", onesb[:, :], 1.0)
    vm_toks = []
    for kc in range(nT):
        vm_toks.append(p.op("v", "tensor_scalar",
                            vmask[:, kc * 128:(kc + 1) * 128], onesb[:, :],
                            mcol[:, kc:kc + 1], None, ALU.mult,
                            waits=[t_ob, t_mc]))
    mask_done = [p.tok("_v")]

    with nc.sbuf_tensor("kh0", [128, S], BF16) as kh0, \
            nc.sbuf_tensor("kh1", [128, S], BF16) as kh1, \
            nc.sbuf_tensor("vh0", [128, S], BF16) as vh0, \
            nc.sbuf_tensor("vh1", [128, S], BF16) as vh1:
        kh = [kh0, kh1]
        vh = [vh0, vh1]
        kh_use = [None, None]
        kh_tok = [None, None]
        vh_tok = [None, None]
        for h in range(NH):
            b = h % 2
            t_kl = p.dma("y", kh[b][:, :], kf_d[h * 128:(h + 1) * 128, :],
                         f"lkh{b}", waits=([kh_use[b]] if kh_use[b] else []))
            t_vl = p.dma("g", chunk3(vh[b].ap(), 128, nch=nT),
                         vf_d.ap()[:, h * 128:(h + 1) * 128]
                         .rearrange("(kc p) o -> p kc o", p=128),
                         f"lvh{b}", waits=([kh_use[b]] if kh_use[b] else []))
            kh_tok[b] = p.op("v", "tensor_tensor", kh[b][:, :], kh[b][:, :],
                             mbc[:, :], ALU.mult,
                             waits=[t_kl] + mask_done)
            vh_tok[b] = p.op("v", "tensor_tensor", vh[b][:, :],
                             vh[b][:, :], vmask[:, :], ALU.mult,
                             waits=[t_vl] + mask_done)
            for qh in range(nqh):
                oidx = 2 + qh % 2
                o_bank = mm_ps[oidx]
                pr_use = [None, None]
                ex_tok = [None, None]
                t_dn = None
                t_av = None

                def emit_sc(kc):
                    pb = kc % 2
                    sc_bank = mm_ps[pb]
                    w_sc = [kh_tok[b], vh_tok[b], t_q]
                    if ex_tok[pb]:
                        w_sc.append(ex_tok[pb])
                    elif mm_free.get(pb):
                        w_sc.append(mm_free.get(pb))
                    return p.op("t", "matmul", sc_bank[:, 0:QW],
                                kh[b][:, kc * 128:(kc + 1) * 128],
                                q_fm[:, h * SQ + qh * 512:
                                     h * SQ + qh * 512 + QW],
                                start=True, stop=True, skip_group_check=True,
                                waits=w_sc)

                sc_tok = {0: emit_sc(0)}
                for kc in range(nT):
                    pb = kc % 2
                    sc_bank = mm_ps[pb]
                    ex_tok[pb] = p.op("s", "activation", probs[pb][:, 0:QW],
                                      sc_bank[:, 0:QW], AF.Exp,
                                      scale=float(ATT_SCALE),
                                      waits=[sc_tok.pop(kc)] +
                                      ([pr_use[pb]] if pr_use[pb] else []))
                    if kc + 1 < nT:
                        sc_tok[kc + 1] = emit_sc(kc + 1)
                    w_dn = [ex_tok[pb]]
                    if kc == 0:
                        if mm_free.get("dn"):
                            w_dn.append(mm_free["dn"])
                        if mm_free.get(oidx):
                            w_dn.append(mm_free[oidx])
                    t_dn = p.op("t", "matmul", dn_ps[0:1, 0:QW],
                                C.ones_c[:, :], probs[pb][:, 0:QW],
                                start=(kc == 0), stop=(kc == nT - 1),
                                skip_group_check=True, waits=w_dn)
                    t_av = p.op("t", "matmul", o_bank[:, 0:QW],
                                vh[b][:, kc * 128:(kc + 1) * 128],
                                probs[pb][:, 0:QW],
                                start=(kc == 0), stop=(kc == nT - 1),
                                skip_group_check=True)
                    pr_use[pb] = t_av
                mm_free[0] = ex_tok[0]
                mm_free[1] = ex_tok[1]
                t_dr = p.op("v", "reciprocal", dnr[0:1, 0:QW],
                            dn_ps[0:1, 0:QW], waits=[t_dn, t_av])
                mm_free["dn"] = t_dr
                t_db = p.op("t", "matmul", db_ps[:, 0:QW],
                            C.ones_rf[0:1, 0:128], dnr[0:1, 0:QW],
                            start=True, stop=True,
                            waits=[t_dr] + ([mm_free["db"]]
                                            if "db" in mm_free else []))
                t_e1 = p.op("s", "activation", ev[:, 0:QW], o_bank[:, 0:QW],
                            AF.Copy,
                            waits=[t_av] + ([mm_free["ev"]]
                                            if "ev" in mm_free else []))
                mm_free[oidx] = t_e1
                t_e2 = p.op("v", "tensor_copy", dnb[:, 0:QW], db_ps[:, 0:QW],
                            waits=[t_db])
                mm_free["db"] = t_e2
                mm_free["ev"] = p.op(
                    "v", "tensor_tensor",
                    at_fm[:, h * SQ + qh * 512:h * SQ + qh * 512 + QW],
                    ev[:, 0:QW], dnb[:, 0:QW], ALU.mult,
                    waits=[t_e1, t_e2])
            kh_use[b] = p.tok("_t")
    attn_done = [p.tok("_v"), p.tok("_s"), p.tok("_t")]

    with nc.sbuf_tensor("wo_sb", [128, 16 * D], BF16) as wo_sb:
        t_wo = p.dma("g", chunk3(wo_sb.ap(), D), wcol_ap(wo_d, 0, D), "ldwo",
                     waits=attn_done)
        t_bo = p.dma("g", bias_sb[0:1, :], bo_d[0:1, :], "ldbias")
        x_use = [None, None]
        st_tok = [None, None]
        prev_ln = None
        for tc in range(nQ):
            xb = tc % 2
            ev_toks = []
            for o4 in range(4):
                bidx = o4 % 2
                bank = mm_ps[bidx]
                w0 = [t_wo, t_bo] + attn_done + \
                    ([mm_free.get(bidx)] if mm_free.get(bidx) else [])
                for c in range(16):
                    p.op("t", "matmul", bank[:, :],
                         at_fm[:, c * SQ + tc * 128:c * SQ + (tc + 1) * 128],
                         wo_sb[:, c * D + o4 * 512:c * D + (o4 + 1) * 512],
                         start=(c == 0), stop=False,
                         waits=w0 if c == 0 else ())
                t_mm = p.op("t", "matmul", bank[:, :], C.ones_r[0:1, 0:128],
                            bias_sb[0:1, o4 * 512:(o4 + 1) * 512],
                            start=False, stop=True)
                w_e = [t_mm]
                if o4 == 0 and prev_ln:
                    w_e.append(prev_ln)
                if o4 == 0 and "ev" in mm_free:
                    w_e.append(mm_free.pop("ev"))
                t_e = p.op("s", "activation", ev[:, o4 * 512:(o4 + 1) * 512],
                           bank[:, :], AF.Copy, waits=w_e)
                mm_free[bidx] = t_e
                ev_toks.append(t_e)
            t_ld = p.dma("y", x_t[xb][:, :], x_d[tc * 128:(tc + 1) * 128, :],
                         f"lre{xb}",
                         waits=([x_use[xb]] if x_use[xb] else []) +
                         ([st_tok[xb]] if st_tok[xb] else []) +
                         ([p.tok("_v")] if tc < 2 else []))
            t_n = emit_ln(p, ev[:, :], ln_o[:, :], bjunk[:, :], st,
                          waits=ev_toks + ([prev_ln] if prev_ln else []))
            prev_ln = t_n
            t_add = p.op("v", "tensor_tensor", x_t[xb][:, :], x_t[xb][:, :],
                         ln_o[:, :], ALU.add, waits=[t_n, t_ld])
            x_use[xb] = t_add
            st_tok[xb] = p.dma("y", x1_d[tc * 128:(tc + 1) * 128, :],
                               x_t[xb][:, :], f"stx{xb}", waits=[t_add])
    p.op("y", "nop", waits=[p.tok("stx0"), p.tok("stx1")])
    p.finalize()
    return nc


if __name__ == "__main__":
    nc = build_l0_attn(2048, 512)
    print("build ok, instructions:", nc.n_instructions())


# version 09
# speedup vs baseline: 1.3305x; 1.3305x over previous
import math
from contextlib import ExitStack

import contextlib
import numpy as np
import concourse.bass as bass
import concourse.mybir as mybir
from concourse.bass_utils import run_bass_kernel_spmd

F32 = mybir.dt.float32
BF16 = mybir.dt.bfloat16
I32 = mybir.dt.int32
AF = mybir.ActivationFunctionType
ALU = mybir.AluOpType

D = 2048
NH = 16
HD = 128
NCORES = 8
LN_EPS = 1e-5
ATT_SCALE = 1.0 / math.sqrt(HD)



class Prog:
    ENGS = ("t", "v", "s", "g", "y")
    _ATTR = {"t": "tensor", "v": "vector", "s": "scalar", "g": "gpsimd",
             "y": "sync"}

    def __init__(self, nc):
        self.nc = nc
        self.steps = {e: [] for e in self.ENGS}
        self.semh = {}
        self.cnt = {}
        self.waited = {e: {} for e in self.ENGS}
        self.base = []

    def sem(self, name):
        if name not in self.semh:
            self.semh[name] = self.nc.alloc_semaphore(name)
            self.cnt[name] = 0
        return name

    def tok(self, name):
        self.sem(name)
        return (name, self.cnt[name])

    def wait(self, e, token):
        if token is None:
            return
        name, val = token
        if val <= 0:
            return
        if self.waited[e].get(name, 0) >= val:
            return
        self.waited[e][name] = val
        h = self.semh[name]
        self.steps[e].append(lambda eng, h=h, v=val: eng.wait_ge(h, v))

    def op(self, e, meth, *args, waits=(), inc=None, dn=1, **kw):
        for tk in self.base:
            self.wait(e, tk)
        for tk in waits:
            self.wait(e, tk)
        if inc is None:
            inc = "_" + e
        inc = self.sem(inc)
        self.cnt[inc] += dn
        h = self.semh[inc]

        def go(eng, meth=meth, args=args, kw=kw, h=h, dn=dn):
            getattr(eng, meth)(*args, **kw).then_inc(h, dn)

        self.steps[e].append(go)
        return (inc, self.cnt[inc])

    def dma(self, e, out, in_, stream, waits=(), **kw):
        return self.op(e, "dma_start", waits=waits, inc=stream, dn=16,
                       out=out, in_=in_, **kw)

    def finalize(self):
        with self.nc.Block() as block:
            for e in self.ENGS:
                if not self.steps[e]:
                    continue
                steps = self.steps[e]

                def body(eng, steps=steps):
                    for f in steps:
                        f(eng)

                getattr(block, self._ATTR[e])(body)


class Consts:
    def __init__(self, p, fvals=()):
        nc = p.nc
        self.iota = nc.alloc_sbuf_tensor("c_iota", [128, 128], I32)
        self.ident = nc.alloc_sbuf_tensor("c_ident", [128, 128], BF16)
        self.ones_r = nc.alloc_sbuf_tensor("c_ones_r", [1, 512], BF16)
        self.ones_c = nc.alloc_sbuf_tensor("c_ones_c", [128, 1], BF16)
        self.ones_rf = nc.alloc_sbuf_tensor("c_ones_rf", [1, 512], F32)
        self.ones_cf = nc.alloc_sbuf_tensor("c_ones_cf", [128, 1], F32)
        t0 = p.op("g", "iota", self.iota[:, :], [[1, 128]], base=0,
                  channel_multiplier=-1)
        self.tok = [
            p.op("v", "tensor_scalar", self.ident[:, :], self.iota[:, :], 0,
                 None, ALU.is_equal, waits=[t0]),
            p.op("v", "memset", self.ones_r[:, :], 1.0),
            p.op("v", "memset", self.ones_c[:, :], 1.0),
            p.op("v", "memset", self.ones_rf[:, :], 1.0),
            p.op("v", "memset", self.ones_cf[:, :], 1.0),
        ]
        for i, val in enumerate(fvals):
            cst = nc.alloc_sbuf_tensor(f"c_f{i}", [128, 1], F32)
            self.tok.append(p.op("v", "memset", cst[:, :], float(val)))
            nc.const_aps.register(F32, float(val), cst[:, :])
        p.base = list(self.tok)


def emit_ln(p, x_ap, out_ap, junk_ap, st, waits, eps=LN_EPS, n=D, sc=0,
            junk2_ap=None, sum_on_act=False):
    j2 = junk2_ap if junk2_ap is not None else junk_ap
    t_sq = p.op("s", "activation", junk_ap, x_ap, AF.Square,
                accum_out=st[:, sc + 0:sc + 1], waits=waits)
    if sum_on_act:
        t_s1 = p.op("s", "activation", j2, x_ap, AF.Copy,
                    accum_out=st[:, sc + 1:sc + 2], waits=list(waits))
    else:
        t_s1 = p.op("v", "tensor_scalar", j2, x_ap, 1.0, None, ALU.mult,
                    ALU.add, accum_out=st[:, sc + 1:sc + 2],
                    waits=list(waits) +
                    ([] if junk2_ap is not None else [t_sq]))
    t_m = p.op("v", "tensor_scalar", st[:, sc + 2:sc + 3],
               st[:, sc + 1:sc + 2], 1.0 / n, None, ALU.mult, waits=[t_s1])
    t_mm = p.op("v", "tensor_tensor", st[:, sc + 3:sc + 4],
                st[:, sc + 2:sc + 3], st[:, sc + 2:sc + 3], ALU.mult,
                waits=[t_m])
    t_v = p.op("v", "scalar_tensor_tensor", st[:, sc + 4:sc + 5],
               st[:, sc + 0:sc + 1], 1.0 / n, st[:, sc + 3:sc + 4],
               ALU.mult, ALU.subtract, waits=[t_mm, t_sq])
    t_sd = p.op("s", "activation", st[:, sc + 5:sc + 6],
                st[:, sc + 4:sc + 5], AF.Sqrt, bias=float(eps), waits=[t_v])
    t_r = p.op("v", "reciprocal", st[:, sc + 6:sc + 7],
               st[:, sc + 5:sc + 6], waits=[t_sd])
    t_n = p.op("v", "tensor_scalar", out_ap, x_ap, st[:, sc + 2:sc + 3],
               st[:, sc + 6:sc + 7], ALU.subtract, ALU.mult, waits=[t_r])
    return t_n


def wcol_ap(w_dram, o0, width):
    return w_dram.ap()[:, o0:o0 + width].rearrange("(c p) o -> p c o", p=128)


def chunk3(sb_ap, width, nch=16):
    return sb_ap.rearrange("p (c o) -> p c o", c=nch)


def fm_ap(fm_dram, t0, width):
    return fm_dram.ap()[:, t0:t0 + width].rearrange("(c p) t -> p c t",
                                                    p=128)




def build_l0_attn(S=4096, SQ=1024):
    nT = S // 128
    nTB = S // 512
    nQ = SQ // 128
    TIDX = min(max(1, int(S * 0.2)), S - 1)
    assert S % 512 == 0 and SQ % 512 == 0 and nT <= 128

    nc = bass.Bass("TRN2", target_bir_lowering=False)
    p = Prog(nc)

    x_d = nc.dram_tensor("x_full", [S, D], F32, kind="ExternalInput")
    wq_d = nc.dram_tensor("Wq", [D, D], F32, kind="ExternalInput")
    wk_d = nc.dram_tensor("Wk", [D, D], F32, kind="ExternalInput")
    wv_d = nc.dram_tensor("Wv", [D, D], F32, kind="ExternalInput")
    wo_d = nc.dram_tensor("Wo", [D, D], F32, kind="ExternalInput")
    bq_d = nc.dram_tensor("bq", [1, D], F32, kind="ExternalInput")
    bk_d = nc.dram_tensor("bk", [1, D], F32, kind="ExternalInput")
    bv_d = nc.dram_tensor("bv", [1, D], F32, kind="ExternalInput")
    bo_d = nc.dram_tensor("bo", [1, D], F32, kind="ExternalInput")
    wi_d = nc.dram_tensor("Wi", [1, D], F32, kind="ExternalInput")
    bi_d = nc.dram_tensor("bi", [1, 1], F32, kind="ExternalInput")
    x1_d = nc.dram_tensor("x1", [SQ, D], F32, kind="ExternalOutput")

    xn_d = nc.dram_tensor("xn_scr", [D, S], BF16)
    k_d = nc.dram_tensor("k_scr", [D, S], BF16)
    v_d = nc.dram_tensor("v_scr", [S, D], BF16)
    mr_d = nc.dram_tensor("mr_scr", [nT, 128], BF16)

    at_fm = nc.alloc_sbuf_tensor("at_fm", [128, 16 * SQ], BF16)
    ev = nc.alloc_sbuf_tensor("ev", [128, D], F32)
    ln_o = nc.alloc_sbuf_tensor("ln_o", [128, D], F32)
    x_t = [nc.alloc_sbuf_tensor(f"x_t{i}", [128, D], F32) for i in range(2)]
    bjunk = nc.alloc_sbuf_tensor("bjunk", [128, D], BF16)
    st = nc.alloc_sbuf_tensor("st", [128, 24], F32)
    imp_sb = nc.alloc_sbuf_tensor("imp_sb", [128, nT], F32)
    bis = nc.alloc_sbuf_tensor("bis", [128, 8], F32)
    bmask = nc.alloc_sbuf_tensor("bmask", [128, 2], I32)
    cnt_c = nc.alloc_sbuf_tensor("cnt_c", [128, 1], F32)
    c11 = nc.alloc_sbuf_tensor("c11", [1, 1], F32)
    mrow = nc.alloc_sbuf_tensor("mrow", [1, S], BF16)
    mbc = nc.alloc_sbuf_tensor("mbc", [128, S], BF16)
    mcol = nc.alloc_sbuf_tensor("mcol", [128, nT], F32)
    mcolb = nc.alloc_sbuf_tensor("mcolb", [128, nT], BF16)
    mjunk = nc.alloc_sbuf_tensor("mjunk", [128, nT], BF16)
    mstage = nc.alloc_sbuf_tensor("mstage", [128, 128], BF16)
    probs = [nc.alloc_sbuf_tensor(f"probs{i}", [128, 512], BF16)
             for i in range(2)]
    kv_out = [nc.alloc_sbuf_tensor(f"kv_out{i}", [128, 512], BF16)
              for i in range(2)]
    dnr = nc.alloc_sbuf_tensor("dnr", [1, 512], F32)
    dnb = nc.alloc_sbuf_tensor("dnb", [128, 512], F32)
    bias_sb = nc.alloc_sbuf_tensor("bias_sb", [1, D], BF16)
    bif = nc.alloc_sbuf_tensor("bif", [1, 1], F32)
    wirow = nc.alloc_sbuf_tensor("wirow", [1, D], F32)
    swi_c = nc.alloc_sbuf_tensor("swi_c", [128, 1], F32)
    bi_c = nc.alloc_sbuf_tensor("bi_c", [128, 1], F32)

    tp_ps = [nc.alloc_psum_tensor(f"tp{i}", [128, 128], BF16) for i in range(2)]
    mm_ps = [nc.alloc_psum_tensor(f"mm{i}", [128, 512], F32) for i in range(4)]
    dn_ps = nc.alloc_psum_tensor("dn_ps", [1, 512], F32)
    db_ps = nc.alloc_psum_tensor("db_ps", [128, 512], F32)

    C = Consts(p, fvals=(LN_EPS, ATT_SCALE))
    mm_free = {}

    t_wi = p.dma("y", wirow[0:1, :], wi_d[0:1, :], "ldwi")
    t_bi = p.dma("y", bif[0:1, :], bi_d[0:1, :], "ldbi")

    with nc.sbuf_tensor("xf_t0", [128, 16 * 128], BF16) as xf_t0, \
            nc.sbuf_tensor("xf_t1", [128, 16 * 128], BF16) as xf_t1, \
            nc.sbuf_tensor("bj0", [128, D], BF16) as bj0, \
            nc.sbuf_tensor("bj1", [128, D], BF16) as bj1, \
            nc.sbuf_tensor("jk0", [128, D], BF16) as jk0, \
            nc.sbuf_tensor("jk1", [128, D], BF16) as jk1, \
            nc.sbuf_tensor("jk2", [128, D], BF16) as jk2, \
            nc.sbuf_tensor("jk3", [128, D], BF16) as jk3, \
            nc.sbuf_tensor("ij0", [128, D], F32) as ij0, \
            nc.sbuf_tensor("ij1", [128, D], F32) as ij1, \
            nc.sbuf_tensor("wi_bc", [128, D], F32) as wi_bc:
        xf = [xf_t0, xf_t1]
        bj = [bj0, bj1]
        jka = [jk0, jk1]
        jkb = [jk2, jk3]
        ij = [ij0, ij1]
        wb_toks = []
        for ob in range(4):
            t_wb = p.op("t", "matmul", db_ps[:, :], C.ones_rf[0:1, 0:128],
                        wirow[0:1, ob * 512:(ob + 1) * 512], start=True,
                        stop=True, waits=[t_wi] + (wb_toks[-1:] or []))
            wb_toks.append(p.op("v", "tensor_copy",
                                wi_bc[:, ob * 512:(ob + 1) * 512],
                                db_ps[:, :], waits=[t_wb]))
        t_bb = p.op("t", "matmul", db_ps[:, 0:1], C.ones_rf[0:1, 0:128],
                    bif[0:1, 0:1], start=True, stop=True,
                    waits=[t_bi] + wb_toks[-1:])
        t_bic = p.op("v", "tensor_copy", bi_c[:, :], db_ps[:, 0:1],
                     waits=[t_bb])
        t_swi = p.op("v", "tensor_reduce", swi_c[:, :], wi_bc[:, :],
                     mybir.AxisListType.X, ALU.add, waits=wb_toks)
        imp_setup = [t_bic, t_swi]

        x_use = [None, None]
        bj_use = [None, None]
        xf_use = [None, None]
        ij_use = [None, None]
        ln_tok = [None, None]
        st_xn = [None, None]
        imp_last = None
        for ti in range(nT):
            q = ti % 2
            t_ld = p.dma("y", x_t[q][:, :], x_d[ti * 128:(ti + 1) * 128, :],
                         f"ldx{q}", waits=list(x_use[q] or ()))
            t_n = emit_ln(p, x_t[q][:, :], bj[q][:, :], jka[q][:, :], st,
                          waits=[t_ld] + ([bj_use[q]] if bj_use[q] else []) +
                          ([ln_tok[q]] if ln_tok[q] else []),
                          sc=8 * q, junk2_ap=jkb[q][:, :])
            ln_tok[q] = t_n
            t_mu = p.op("g", "tensor_tensor", ij[q][:, :], x_t[q][:, :],
                        wi_bc[:, :], ALU.mult,
                        waits=[t_ld] + imp_setup +
                        ([ij_use[q]] if ij_use[q] else []))
            t_us = p.op("v", "tensor_reduce", st[:, 16 + 4 * q:17 + 4 * q],
                        ij[q][:, :], mybir.AxisListType.X, ALU.add,
                        waits=[t_mu])
            ij_use[q] = t_us
            x_use[q] = (t_n, t_mu, p.tok("_s"))
            t_i1 = p.op("v", "tensor_tensor", st[:, 17 + 4 * q:18 + 4 * q],
                        st[:, 8 * q + 2:8 * q + 3], swi_c[:, :], ALU.mult,
                        waits=[t_us, t_n])
            t_i2 = p.op("v", "scalar_tensor_tensor",
                        st[:, 18 + 4 * q:19 + 4 * q],
                        st[:, 17 + 4 * q:18 + 4 * q], -1.0,
                        st[:, 16 + 4 * q:17 + 4 * q], ALU.mult, ALU.add,
                        waits=[t_i1])
            t_i3 = p.op("v", "tensor_tensor", st[:, 19 + 4 * q:20 + 4 * q],
                        st[:, 18 + 4 * q:19 + 4 * q],
                        st[:, 8 * q + 6:8 * q + 7], ALU.mult, waits=[t_i2])
            imp_last = p.op("v", "tensor_tensor", imp_sb[:, ti:ti + 1],
                            st[:, 19 + 4 * q:20 + 4 * q], bi_c[:, :],
                            ALU.add, waits=[t_i3])
            cp_toks = []
            for c in range(16):
                bk = tp_ps[c % 2]
                w = [t_n]
                if len(cp_toks) >= 2:
                    w.append(cp_toks[-2])
                if c == 0 and st_xn[q]:
                    w.append(st_xn[q])
                t_tr = p.op("t", "transpose", bk[:, :],
                            bj[q][:, c * 128:(c + 1) * 128], C.ident[:, :],
                            waits=w)
                if c % 2 == 0:
                    cp_toks.append(p.op("v", "tensor_copy",
                                        xf[q][:, c * 128:(c + 1) * 128],
                                        bk[:, :], waits=[t_tr]))
                else:
                    cp_toks.append(p.op("s", "activation",
                                        xf[q][:, c * 128:(c + 1) * 128],
                                        bk[:, :], AF.Copy, waits=[t_tr]))
            bj_use[q] = p.tok("_t")
            st_xn[q] = p.dma("s", fm_ap(xn_d, ti * 128, 128),
                             chunk3(xf[q].ap(), 128), f"stxn{q}",
                             waits=cp_toks)
    ln_done = [p.tok("_v"), p.tok("_t"), p.tok("_s"), p.tok("_g"),
               p.tok("stxn0"), p.tok("stxn1")]


    t0 = p.op("v", "memset", bis[:, 0:1], -1000.0, waits=[imp_last])
    t1 = p.op("v", "memset", bis[:, 1:2], 1000.0)
    prev = [t0, t1]
    for _ in range(50):
        t_a = p.op("v", "tensor_scalar", bis[:, 3:4], bis[:, 1:2], 0.5, None,
                   ALU.mult, waits=prev)
        t_mid = p.op("v", "scalar_tensor_tensor", bis[:, 2:3], bis[:, 0:1],
                     0.5, bis[:, 3:4], ALU.mult, ALU.add, waits=[t_a])
        t_c = p.op("v", "tensor_scalar", mjunk[:, 0:nT], imp_sb[:, 0:nT],
                   bis[:, 2:3], None, ALU.is_gt, ALU.add,
                   accum_out=cnt_c[:, 0:1], waits=[t_mid])
        t_s = p.op("t", "matmul", dn_ps[0:1, 0:1], C.ones_cf[:, :],
                   cnt_c[:, 0:1], start=True, stop=True,
                   waits=[t_c] + C.tok + ([mm_free["dn"]]
                                          if "dn" in mm_free else []))
        t_cp = p.op("v", "tensor_copy", c11[0:1, 0:1], dn_ps[0:1, 0:1],
                    waits=[t_s])
        mm_free["dn"] = t_cp
        t_bc = p.op("t", "matmul", db_ps[:, 0:1], C.ones_rf[0:1, 0:128],
                    c11[0:1, 0:1], start=True, stop=True,
                    waits=[t_cp] + ([mm_free["db"]] if "db" in mm_free else []))
        t_cd = p.op("v", "tensor_scalar", bmask[:, 0:1], db_ps[:, 0:1],
                    TIDX + 0.5, None, ALU.is_lt, waits=[t_bc])
        t_ic = p.op("v", "tensor_scalar", bmask[:, 1:2], db_ps[:, 0:1],
                    TIDX + 0.5, None, ALU.is_ge, waits=[t_cd])
        mm_free["db"] = t_ic
        t_h = p.op("v", "copy_predicated", bis[:, 1:2], bmask[:, 0:1],
                   bis[:, 2:3], waits=[t_ic])
        t_l = p.op("v", "copy_predicated", bis[:, 0:1], bmask[:, 1:2],
                   bis[:, 2:3], waits=[t_h])
        prev = [t_l]
    t_mc = p.op("v", "tensor_scalar", mcol[:, 0:nT], imp_sb[:, 0:nT],
                bis[:, 1:2], None, ALU.is_gt, waits=prev)
    t_mcb = p.op("v", "tensor_copy", mcolb[:, 0:nT], mcol[:, 0:nT],
                 waits=[t_mc])
    t_tr = p.op("t", "transpose", tp_ps[0][0:nT, :], mcolb[:, 0:nT],
                C.ident[:, :], waits=[t_mcb] + C.tok)
    t_ms = p.op("v", "tensor_copy", mstage[0:nT, :], tp_ps[0][0:nT, :],
                waits=[t_tr])
    t_st = p.dma("s", mr_d[:, :], mstage[0:nT, :], "mrb", waits=[t_ms])
    t_lr = p.dma("s", mrow[0:1, 0:S],
                 mr_d.ap().rearrange("(o a) b -> o (a b)", o=1), "mrb2",
                 waits=[t_st])
    for tb in range(nTB):
        t_b = p.op("t", "matmul", db_ps[:, :], C.ones_r[0:1, 0:128],
                   mrow[0:1, tb * 512:(tb + 1) * 512], start=True, stop=True,
                   waits=[t_lr, mm_free["db"]])
        mm_free["db"] = p.op("v", "tensor_copy",
                             mbc[:, tb * 512:(tb + 1) * 512], db_ps[:, :],
                             waits=[t_b])
    mask_done = [p.tok("_v")]

    kv_tok = [None, None]
    xuse = {}

    def proj_pass(kind, ph, xn_blk, wbuf, ldw_tok, bias_tok, xs):
        nblk = (SQ if kind == "Q" else S) // 512
        for tb in range(nblk):
            xb = tb % 2
            t_x = p.dma("s", chunk3(xn_blk[xb].ap(), 512),
                        fm_ap(xn_d, tb * 512, 512), f"{xs}{xb}",
                        waits=ln_done + ([xuse[xb]] if xb in xuse else []))
            if kind in ("K", "Q"):
                for o8 in range(8):
                    o = ph * 8 + o8
                    bidx = o8 % 2
                    bank = mm_ps[bidx]
                    w0 = [ldw_tok, bias_tok, t_x] + \
                        ([mm_free.get(bidx)] if mm_free.get(bidx) else [])
                    for c in range(16):
                        p.op("t", "matmul", bank[:, :],
                             wbuf[:, c * 1024 + o8 * 128:
                                  c * 1024 + (o8 + 1) * 128],
                             xn_blk[xb][:, c * 512:(c + 1) * 512],
                             start=(c == 0), stop=False,
                             waits=w0 if c == 0 else ())
                    t_mm = p.op("t", "matmul", bank[:, :],
                                bias_sb[0:1, o * 128:(o + 1) * 128],
                                C.ones_r[0:1, 0:512], start=False, stop=True)
                    if kind == "K":
                        ob = (tb * 8 + o8) % 2
                        t_e = p.op("v", "tensor_tensor", kv_out[ob][:, :],
                                   bank[:, :],
                                   mbc[:, tb * 512:(tb + 1) * 512], ALU.mult,
                                   waits=[t_mm] + mask_done +
                                   ([kv_tok[ob]] if kv_tok[ob] else []))
                        kv_tok[ob] = p.dma(
                            "y", k_d[o * 128:(o + 1) * 128,
                                     tb * 512:(tb + 1) * 512],
                            kv_out[ob][:, :], f"stkv{ob}", waits=[t_e])
                    else:
                        t_e = p.op("v", "tensor_copy",
                                   q_fm[:, o * SQ + tb * 512:
                                        o * SQ + (tb + 1) * 512],
                                   bank[:, :], waits=[t_mm])
                    mm_free[bidx] = t_e
                xuse[xb] = p.tok("_t")
            else:
                for tc4 in range(4):
                    for obl in range(2):
                        og = ph * 2 + obl
                        bidx = 2 + (tc4 * 2 + obl) % 2
                        bank = mm_ps[bidx]
                        w0 = [ldw_tok, bias_tok, t_x] + \
                            ([mm_free.get(bidx)] if mm_free.get(bidx) else [])
                        for c in range(16):
                            p.op("t", "matmul", bank[:, :],
                                 xn_blk[xb][:, c * 512 + tc4 * 128:
                                            c * 512 + (tc4 + 1) * 128],
                                 wbuf[:, c * 1024 + obl * 512:
                                      c * 1024 + (obl + 1) * 512],
                                 start=(c == 0), stop=False,
                                 waits=w0 if c == 0 else ())
                        t_mm = p.op("t", "matmul", bank[:, :],
                                    C.ones_r[0:1, 0:128],
                                    bias_sb[0:1, og * 512:(og + 1) * 512],
                                    start=False, stop=True)
                        tglob = tb * 4 + tc4
                        ob = (tglob * 2 + obl) % 2
                        t_e = p.op("v", "tensor_scalar", kv_out[ob][:, :],
                                   bank[:, :], mcol[:, tglob:tglob + 1],
                                   None, ALU.mult,
                                   waits=[t_mm] + mask_done +
                                   ([kv_tok[ob]] if kv_tok[ob] else []))
                        kv_tok[ob] = p.dma(
                            "y", v_d[tglob * 128:(tglob + 1) * 128,
                                     og * 512:(og + 1) * 512],
                            kv_out[ob][:, :], f"stkv{ob}", waits=[t_e])
                        mm_free[bidx] = t_e
            xuse[xb] = p.tok("_t")

    with nc.sbuf_tensor("wbuf", [128, 16 * 1024], BF16) as wbuf, \
            nc.sbuf_tensor("xnb0", [128, 16 * 512], BF16) as xnb0, \
            nc.sbuf_tensor("xnb1", [128, 16 * 512], BF16) as xnb1:
        xn_blk = [xnb0, xnb1]
        for kind, w_dram, b_dram in (("K", wk_d, bk_d), ("V", wv_d, bv_d),
                                     ("Q", wq_d, bq_d)):
            t_b = p.dma("g", bias_sb[0:1, :], b_dram[0:1, :], "ldbias",
                        waits=[p.tok("_t")])
            for ph in range(2):
                t_w = p.dma("g", chunk3(wbuf.ap(), 1024),
                            wcol_ap(w_dram, ph * 1024, 1024),
                            "ldwbuf", waits=[p.tok("_t")] + ln_done)
                proj_pass(kind, ph, xn_blk, wbuf, t_w, t_b, f"lx{kind}{ph}")

    proj_done = [p.tok("_t"), p.tok("_v"), p.tok("stkv0"), p.tok("stkv1")]

    nqh = max(1, SQ // 512)
    QW = min(512, SQ)
    with nc.sbuf_tensor("kh0", [128, S], BF16) as kh0, \
            nc.sbuf_tensor("kh1", [128, S], BF16) as kh1, \
            nc.sbuf_tensor("vh0", [128, S], BF16) as vh0, \
            nc.sbuf_tensor("vh1", [128, S], BF16) as vh1:
        kh = [kh0, kh1]
        vh = [vh0, vh1]
        kh_use = [None, None]
        kh_tok = [None, None]
        vh_tok = [None, None]
        for h in range(NH):
            b = h % 2
            kh_tok[b] = p.dma("y", kh[b][:, :], k_d[h * 128:(h + 1) * 128, :],
                              f"lkh{b}", waits=proj_done +
                              ([kh_use[b]] if kh_use[b] else []))
            vh_tok[b] = p.dma("g", chunk3(vh[b].ap(), 128, nch=nT),
                              v_d.ap()[:, h * 128:(h + 1) * 128]
                              .rearrange("(kc p) o -> p kc o", p=128),
                              f"lvh{b}", waits=proj_done +
                              ([kh_use[b]] if kh_use[b] else []))
            for qh in range(nqh):
                oidx = 2 + qh % 2
                o_bank = mm_ps[oidx]
                pr_use = [None, None]
                ex_tok = [None, None]
                t_dn = None
                t_av = None

                def emit_sc(kc):
                    pb = kc % 2
                    sc_bank = mm_ps[pb]
                    w_sc = [kh_tok[b], vh_tok[b]]
                    if ex_tok[pb]:
                        w_sc.append(ex_tok[pb])
                    elif mm_free.get(pb):
                        w_sc.append(mm_free.get(pb))
                    return p.op("t", "matmul", sc_bank[:, 0:QW],
                                kh[b][:, kc * 128:(kc + 1) * 128],
                                q_fm[:, h * SQ + qh * 512:
                                     h * SQ + qh * 512 + QW],
                                start=True, stop=True, skip_group_check=True,
                                waits=w_sc)

                sc_tok = {0: emit_sc(0)}
                for kc in range(nT):
                    pb = kc % 2
                    sc_bank = mm_ps[pb]
                    ex_tok[pb] = p.op("s", "activation", probs[pb][:, 0:QW],
                                      sc_bank[:, 0:QW], AF.Exp,
                                      scale=float(ATT_SCALE),
                                      waits=[sc_tok.pop(kc)] +
                                      ([pr_use[pb]] if pr_use[pb] else []))
                    if kc + 1 < nT:
                        sc_tok[kc + 1] = emit_sc(kc + 1)
                    w_dn = [ex_tok[pb]]
                    if kc == 0:
                        if mm_free.get("dn"):
                            w_dn.append(mm_free["dn"])
                        if mm_free.get(oidx):
                            w_dn.append(mm_free[oidx])
                    t_dn = p.op("t", "matmul", dn_ps[0:1, 0:QW],
                                C.ones_c[:, :], probs[pb][:, 0:QW],
                                start=(kc == 0), stop=(kc == nT - 1),
                                skip_group_check=True, waits=w_dn)
                    t_av = p.op("t", "matmul", o_bank[:, 0:QW],
                                vh[b][:, kc * 128:(kc + 1) * 128],
                                probs[pb][:, 0:QW],
                                start=(kc == 0), stop=(kc == nT - 1),
                                skip_group_check=True)
                    pr_use[pb] = t_av
                mm_free[0] = ex_tok[0]
                mm_free[1] = ex_tok[1]
                t_dr = p.op("v", "reciprocal", dnr[0:1, 0:QW],
                            dn_ps[0:1, 0:QW], waits=[t_dn, t_av])
                mm_free["dn"] = t_dr
                t_db = p.op("t", "matmul", db_ps[:, 0:QW],
                            C.ones_rf[0:1, 0:128], dnr[0:1, 0:QW],
                            start=True, stop=True,
                            waits=[t_dr] + ([mm_free["db"]]
                                            if "db" in mm_free else []))
                t_e1 = p.op("s", "activation", ev[:, 0:QW], o_bank[:, 0:QW],
                            AF.Copy,
                            waits=[t_av] + ([mm_free["ev"]]
                                            if "ev" in mm_free else []))
                mm_free[oidx] = t_e1
                t_e2 = p.op("v", "tensor_copy", dnb[:, 0:QW], db_ps[:, 0:QW],
                            waits=[t_db])
                mm_free["db"] = t_e2
                mm_free["ev"] = p.op(
                    "v", "tensor_tensor",
                    at_fm[:, h * SQ + qh * 512:h * SQ + qh * 512 + QW],
                    ev[:, 0:QW], dnb[:, 0:QW], ALU.mult,
                    waits=[t_e1, t_e2])
            kh_use[b] = p.tok("_t")
    attn_done = [p.tok("_v"), p.tok("_s"), p.tok("_t")]

    with nc.sbuf_tensor("wo_sb", [128, 16 * D], BF16) as wo_sb:
        t_wo = p.dma("g", chunk3(wo_sb.ap(), D), wcol_ap(wo_d, 0, D), "ldwo",
                     waits=attn_done)
        t_bo = p.dma("g", bias_sb[0:1, :], bo_d[0:1, :], "ldbias",
                     waits=attn_done)
        x_use = [None, None]
        st_tok = [None, None]
        prev_ln = None
        for tc in range(nQ):
            xb = tc % 2
            ev_toks = []
            for o4 in range(4):
                bidx = o4 % 2
                bank = mm_ps[bidx]
                w0 = [t_wo, t_bo] + attn_done + \
                    ([mm_free.get(bidx)] if mm_free.get(bidx) else [])
                for c in range(16):
                    p.op("t", "matmul", bank[:, :],
                         at_fm[:, c * SQ + tc * 128:c * SQ + (tc + 1) * 128],
                         wo_sb[:, c * D + o4 * 512:c * D + (o4 + 1) * 512],
                         start=(c == 0), stop=False,
                         waits=w0 if c == 0 else ())
                t_mm = p.op("t", "matmul", bank[:, :], C.ones_r[0:1, 0:128],
                            bias_sb[0:1, o4 * 512:(o4 + 1) * 512],
                            start=False, stop=True)
                w_e = [t_mm]
                if o4 == 0 and prev_ln:
                    w_e.append(prev_ln)
                if o4 == 0 and "ev" in mm_free:
                    w_e.append(mm_free.pop("ev"))
                t_e = p.op("s", "activation", ev[:, o4 * 512:(o4 + 1) * 512],
                           bank[:, :], AF.Copy, waits=w_e)
                mm_free[bidx] = t_e
                ev_toks.append(t_e)
            t_ld = p.dma("y", x_t[xb][:, :], x_d[tc * 128:(tc + 1) * 128, :],
                         f"lre{xb}",
                         waits=([x_use[xb]] if x_use[xb] else []) +
                         ([st_tok[xb]] if st_tok[xb] else []) +
                         ([p.tok("_v")] if tc < 2 else []))
            t_n = emit_ln(p, ev[:, :], ln_o[:, :], bjunk[:, :], st,
                          waits=ev_toks + ([prev_ln] if prev_ln else []))
            prev_ln = t_n
            t_add = p.op("v", "tensor_tensor", x_t[xb][:, :], x_t[xb][:, :],
                         ln_o[:, :], ALU.add,
                         waits=[t_n, t_ld] +
                         ([st_tok[xb]] if st_tok[xb] else []))
            x_use[xb] = t_add
            st_tok[xb] = p.dma("y", x1_d[tc * 128:(tc + 1) * 128, :],
                               x_t[xb][:, :], f"stx{xb}", waits=[t_add])
    p.op("y", "nop", waits=[p.tok("stx0"), p.tok("stx1")])

    p.finalize()
    return nc






def build_ffn0_parts():
    R = 1024
    nc = bass.Bass("TRN2", target_bir_lowering=False)
    p = Prog(nc)
    ins = [nc.dram_tensor(n, [R, D], F32, kind="ExternalInput")
           for n in ("up_sl", "gate_sl", "down_sl")]
    out_d = nc.dram_tensor("parts", [1, 4], F32, kind="ExternalOutput")

    w_t = [nc.alloc_sbuf_tensor(f"w_t{i}", [128, D], F32) for i in range(2)]
    junk = nc.alloc_sbuf_tensor("junk", [128, D], BF16)
    acc = nc.alloc_sbuf_tensor("acc", [128, 26], F32)
    red = nc.alloc_sbuf_tensor("red", [128, 4], F32)
    o_sb = nc.alloc_sbuf_tensor("o_sb", [1, 4], F32)
    ps = nc.alloc_psum_tensor("ps", [1, 4], F32)
    C = Consts(p)

    use_tok = [None, None]
    col = 0
    toks = []
    for mi, w_d in enumerate(ins):
        for ti in range(R // 128):
            b = ti % 2
            t_ld = p.dma("y", w_t[b][:, :], w_d[ti * 128:(ti + 1) * 128, :],
                         f"ld{b}", waits=[use_tok[b]] if use_tok[b] else [])
            t_a = p.op("v", "tensor_reduce", acc[:, col:col + 1],
                       w_t[b][:, :], mybir.AxisListType.X, ALU.add,
                       apply_absolute_value=True,
                       waits=[t_ld] + (toks[-1:] or []))
            use_tok[b] = t_a
            toks.append(t_a)
            col += 1
    rt = []
    for m in range(3):
        rt.append(p.op("v", "tensor_reduce", red[:, m:m + 1],
                       acc[:, m * 8:(m + 1) * 8], mybir.AxisListType.X,
                       ALU.add, waits=toks))
    t_z = p.op("v", "memset", red[:, 3:4], 0.0)
    t_s = p.op("t", "matmul", ps[0:1, :], C.ones_cf[:, :], red[:, :],
               start=True, stop=True, waits=rt + [t_z])
    t_c = p.op("v", "tensor_copy", o_sb[0:1, :], ps[0:1, :], waits=[t_s])
    p.dma("y", out_d[0:1, :], o_sb[0:1, :], "sto", waits=[t_c])
    p.op("y", "nop", waits=[p.tok("sto")])
    p.finalize()
    return nc




def build_ffn0(SQ=1024):
    DE = 8192
    nQ = SQ // 128
    nTB = SQ // 512
    nEC = DE // 128
    nc = bass.Bass("TRN2", target_bir_lowering=False)
    p = Prog(nc)

    x1_d = nc.dram_tensor("x1", [SQ, D], F32, kind="ExternalInput")
    upT_d = nc.dram_tensor("upT", [D, DE], F32, kind="ExternalInput")
    gateT_d = nc.dram_tensor("gateT", [D, DE], F32, kind="ExternalInput")
    parts_d = nc.dram_tensor("parts", [8, 4], F32, kind="ExternalInput")
    wdq_d = nc.dram_tensor("wdq", [DE, D], BF16, kind="ExternalInput")
    x2_d = nc.dram_tensor("x2", [SQ, D], F32, kind="ExternalOutput")
    hid_d = nc.dram_tensor("hid_scr", [DE, SQ], BF16)
    y_d = nc.dram_tensor("y_scr", [SQ, D], F32)

    ev = nc.alloc_sbuf_tensor("ev", [128, D], F32)
    ln_o = nc.alloc_sbuf_tensor("ln_o", [128, D], F32)
    x_t = [nc.alloc_sbuf_tensor(f"x_t{i}", [128, D], F32) for i in range(2)]
    bjunk = nc.alloc_sbuf_tensor("bjunk", [128, D], BF16)
    st = nc.alloc_sbuf_tensor("st", [128, 16], F32)
    xn_fm = nc.alloc_sbuf_tensor("xn_fm", [128, 16 * SQ], BF16)
    parts_sb = nc.alloc_sbuf_tensor("parts_sb", [8, 4], F32)
    srow = nc.alloc_sbuf_tensor("srow", [1, 10], F32)
    scl = nc.alloc_sbuf_tensor("scl", [128, 10], F32)
    hb = [nc.alloc_sbuf_tensor(f"hb{i}", [128, 512], BF16) for i in range(2)]
    hout = [nc.alloc_sbuf_tensor(f"hout{i}", [128, 512], BF16)
            for i in range(2)]
    nscl = nc.alloc_sbuf_tensor("nscl", [128, 3], F32)
    thp = nc.alloc_sbuf_tensor("thp", [128, D], F32)
    thn = nc.alloc_sbuf_tensor("thn", [128, D], F32)

    tp_ps = [nc.alloc_psum_tensor(f"tp{i}", [128, 128], BF16) for i in range(2)]
    mm_ps = [nc.alloc_psum_tensor(f"mm{i}", [128, 512], F32) for i in range(4)]
    db_ps = nc.alloc_psum_tensor("db_ps", [128, 512], F32)

    C = Consts(p, fvals=(LN_EPS,))
    mm_free = {}

    t_pl = p.dma("y", parts_sb[:, :], parts_d[:, :], "ldp")
    t_tot = p.op("t", "matmul", db_ps[0:1, 0:4], C.ones_cf[0:8, :],
                 parts_sb[0:8, :], start=True, stop=True, waits=[t_pl])
    t_sc = p.op("v", "tensor_scalar", srow[0:1, 0:3], db_ps[0:1, 0:3],
                1.0 / (DE * D), 1e-8, ALU.mult, ALU.add, waits=[t_tot])
    t_h2 = p.op("v", "tensor_scalar", srow[0:1, 3:6], srow[0:1, 0:3], 0.5,
                None, ALU.mult, waits=[t_sc])
    t_gd = p.op("v", "tensor_tensor", srow[0:1, 6:7], srow[0:1, 4:5],
                srow[0:1, 5:6], ALU.mult, waits=[t_h2])
    t_gd2 = p.op("v", "tensor_tensor", srow[0:1, 7:8], srow[0:1, 1:2],
                 srow[0:1, 5:6], ALU.mult, waits=[t_gd])
    t_gd3 = p.op("v", "tensor_tensor", srow[0:1, 8:9], srow[0:1, 1:2],
                 srow[0:1, 2:3], ALU.mult, waits=[t_gd2])
    t_bb = p.op("t", "matmul", db_ps[:, 0:9], C.ones_rf[0:1, 0:128],
                srow[0:1, 0:9], start=True, stop=True, waits=[t_gd3])
    t_sb = p.op("v", "tensor_copy", scl[:, 0:9], db_ps[:, 0:9], waits=[t_bb])
    scl_tok = [t_sb]

    x_use = [None, None]
    prev_ln = None
    cp_all = []
    with nc.sbuf_tensor("wq_f", [128, D], F32) as wq_f, \
            nc.sbuf_tensor("wq_f2", [128, D], F32) as wq_f2, \
            nc.sbuf_tensor("sa", [128, D], BF16) as sa, \
            nc.sbuf_tensor("q2a", [128, D], BF16) as q2a, \
            nc.sbuf_tensor("q2b", [128, D], BF16) as q2b:
        for ti in range(nQ):
            b = ti % 2
            t_ld = p.dma("y", x_t[b][:, :], x1_d[ti * 128:(ti + 1) * 128, :],
                         f"ldx{b}", waits=[x_use[b]] if x_use[b] else [])
            t_n = emit_ln(p, x_t[b][:, :], bjunk[:, :], ev[:, :], st,
                          waits=[t_ld] + ([prev_ln] if prev_ln else []))
            prev_ln = t_n
            x_use[b] = t_n
            cp_toks = []
            for c in range(16):
                bk = tp_ps[c % 2]
                w = [t_n]
                if len(cp_toks) >= 2:
                    w.append(cp_toks[-2])
                t_tr = p.op("t", "transpose", bk[:, :],
                            bjunk[:, c * 128:(c + 1) * 128], C.ident[:, :],
                            waits=w)
                cp_toks.append(p.op("v", "tensor_copy",
                                    xn_fm[:, c * SQ + ti * 128:
                                          c * SQ + (ti + 1) * 128],
                                    bk[:, :], waits=[t_tr]))
            cp_all.append(cp_toks[-1])
        ln_done = [p.tok("_v"), p.tok("_t")]

    ln_done = ln_done
    with nc.sbuf_tensor("wfu0", [128, D], F32) as wfu0, \
            nc.sbuf_tensor("wfu1", [128, D], F32) as wfu1, \
            nc.sbuf_tensor("wfg0", [128, D], F32) as wfg0, \
            nc.sbuf_tensor("wfg1", [128, D], F32) as wfg1, \
            nc.sbuf_tensor("sau", [128, D], BF16) as sau, \
            nc.sbuf_tensor("sag", [128, D], BF16) as sag, \
            nc.sbuf_tensor("q2u0", [128, D], BF16) as q2u0, \
            nc.sbuf_tensor("q2u1", [128, D], BF16) as q2u1, \
            nc.sbuf_tensor("q2g0", [128, D], BF16) as q2g0, \
            nc.sbuf_tensor("q2g1", [128, D], BF16) as q2g1:
        wfu = [wfu0, wfu1]
        wfg = [wfg0, wfg1]
        q2u = [q2u0, q2u1]
        q2g = [q2g0, q2g1]
        t_ng = p.op("v", "tensor_scalar", nscl[:, 2:3], scl[:, 3:4], -1.0,
                    None, ALU.mult, waits=scl_tok)
        t_ng2 = p.op("v", "tensor_scalar", nscl[:, 0:1], scl[:, 4:5], -1.0,
                     None, ALU.mult, waits=scl_tok)
        t_ng3 = p.op("v", "tensor_scalar", nscl[:, 1:2], scl[:, 5:6], -1.0,
                     None, ALU.mult, waits=scl_tok)
        t_m1 = p.op("v", "memset", thp[:, :], 1.0)
        t_tp = p.op("s", "activation", thp[:, :], thp[:, :], AF.Copy,
                    scale=scl[:, 4:5], waits=[t_m1] + scl_tok)
        t_m2 = p.op("v", "memset", thn[:, :], 1.0)
        t_tn = p.op("s", "activation", thn[:, :], thn[:, :], AF.Copy,
                    scale=nscl[:, 0:1], waits=[t_m2, t_ng2])
        scl_tok = scl_tok + [t_ng, t_ng2, t_ng3, t_tp, t_tn]

        q_use = {}
        f_use = {"sau": None}
        ld_tok = {}
        hid_st = [None, None]
        dq_st = {}

        def issue_loads(ec):
            b = ec % 2
            ld_tok[("u", ec)] = p.dma(
                ("s", "y")[b], chunk3(wfu[b].ap(), 128),
                upT_d.ap()[:, ec * 128:(ec + 1) * 128]
                .rearrange("(c p) o -> p c o", p=128), f"lwu{b}",
                waits=[f_use.get(("u", b))] if f_use.get(("u", b)) else [])
            ld_tok[("g", ec)] = p.dma(
                ("g", "g")[b], chunk3(wfg[b].ap(), 128),
                gateT_d.ap()[:, ec * 128:(ec + 1) * 128]
                .rearrange("(c p) o -> p c o", p=128), f"lwg{b}",
                waits=[f_use.get(("g", b))] if f_use.get(("g", b)) else [])

        def quantize(ec):
            b = ec % 2
            w1 = [ld_tok[("u", ec)]] + scl_tok + \
                ([q_use[("u", b)]] if ("u", b) in q_use else [])
            if f_use.get("sau"):
                w1.append(f_use["sau"])
            tu1 = p.op("s", "activation", sau[:, :], wfu[b][:, :], AF.Sign,
                       bias=nscl[:, 2:3], waits=w1)
            tu2 = p.op("s", "activation", q2u[b][:, :], wfu[b][:, :],
                       AF.Sign, bias=scl[:, 3:4])
            f_use[("u", b)] = tu2
            tqu = p.op("v", "tensor_tensor", q2u[b][:, :], q2u[b][:, :],
                       sau[:, :], ALU.add, waits=[tu1, tu2])
            f_use["sau"] = tqu
            w2 = [ld_tok[("g", ec)]] + scl_tok + \
                ([q_use[("g", b)]] if ("g", b) in q_use else []) + \
                ([f_use.get("sag")] if f_use.get("sag") else [])
            tg1 = p.op("v", "tensor_tensor", q2g[b][:, :], wfg[b][:, :],
                       thp[:, :], ALU.is_ge, waits=w2)
            tg2 = p.op("v", "tensor_tensor", sag[:, :], wfg[b][:, :],
                       thn[:, :], ALU.is_le)
            f_use[("g", b)] = tg2
            tqg = p.op("v", "tensor_tensor", q2g[b][:, :], q2g[b][:, :],
                       sag[:, :], ALU.subtract, waits=[tg1, tg2])
            f_use["sag"] = tqg
            return tqu, tqg

        issue_loads(0)
        issue_loads(1)
        qtoks = {0: quantize(0)}
        for ec in range(nEC):
            b = ec % 2
            if ec + 1 < nEC:
                if ec + 1 > 1:
                    issue_loads(ec + 1)
                qtoks[ec + 1] = quantize(ec + 1) if ec + 1 < nEC else None
            tqu, tqg = qtoks.pop(ec)
            for tb in range(nTB):
                bu = mm_ps[tb % 2]
                bg = mm_ps[2 + tb % 2]
                w0 = [tqu] + ln_done + \
                    ([mm_free.get(tb % 2)] if mm_free.get(tb % 2) else [])
                w1 = [tqg] + \
                    ([mm_free.get(2 + tb % 2)] if mm_free.get(2 + tb % 2)
                     else [])
                for c in range(16):
                    p.op("t", "matmul", bu[:, :],
                         q2u[b][:, c * 128:(c + 1) * 128],
                         xn_fm[:, c * SQ + tb * 512:c * SQ + (tb + 1) * 512],
                         start=(c == 0), stop=(c == 15),
                         waits=w0 if c == 0 else ())
                for c in range(16):
                    p.op("t", "matmul", bg[:, :],
                         q2g[b][:, c * 128:(c + 1) * 128],
                         xn_fm[:, c * SQ + tb * 512:c * SQ + (tb + 1) * 512],
                         start=(c == 0), stop=(c == 15),
                         waits=w1 if c == 0 else ())
                t_pe = p.tok("_t")
                t_si = p.op("s", "activation", hb[tb % 2][:, :], bu[:, :],
                            AF.Silu, scale=scl[:, 3:4],
                            waits=[t_pe] + ([mm_free.get(("hb", tb % 2))]
                                            if mm_free.get(("hb", tb % 2))
                                            else []))
                mm_free[tb % 2] = t_si
                ob = (ec * nTB + tb) % 2
                t_hm = p.op("v", "tensor_tensor", hout[ob][:, :],
                            hb[tb % 2][:, :], bg[:, :], ALU.mult,
                            waits=[t_si] +
                            ([hid_st[ob]] if hid_st[ob] else []))
                mm_free[("hb", tb % 2)] = t_hm
                mm_free[2 + tb % 2] = t_hm
                hid_st[ob] = p.dma("y",
                                   hid_d[ec * 128:(ec + 1) * 128,
                                         tb * 512:(tb + 1) * 512],
                                   hout[ob][:, :], f"sth{ob}", waits=[t_hm])
            q_use[("u", b)] = p.tok("_t")
            q_use[("g", b)] = p.tok("_t")
    up_done = [p.tok("_t"), p.tok("_v"), p.tok("_s"), p.tok("_g"),
               p.tok("sth0"), p.tok("sth1")]

    with nc.sbuf_tensor("wdb0", [128, nEC * 256], BF16) as wdb0, \
            nc.sbuf_tensor("wdb1", [128, nEC * 256], BF16) as wdb1, \
            nc.sbuf_tensor("hblk0", [128, nEC * 128], BF16) as hblk0, \
            nc.sbuf_tensor("hblk1", [128, nEC * 128], BF16) as hblk1:
        wdb = [wdb0, wdb1]
        hblk = [hblk0, hblk1]
        wd_tok = {}
        wd_use = {}
        yst = [None, None, None, None]
        huse = {}

        def load_wd(hd):
            b = hd % 2
            wd_tok[hd] = p.dma(
                ("s", "s")[b], chunk3(wdb[b].ap(), 256, nch=nEC),
                wdq_d.ap()[:, hd * 256:(hd + 1) * 256]
                .rearrange("(c p) o -> p c o", p=128), f"lwq{b}",
                waits=up_done + ([wd_use.get(b)] if wd_use.get(b) else []))

        load_wd(0)
        load_wd(1)
        for hp in range(4):
            hd0, hd1 = 2 * hp, 2 * hp + 1
            for tc in range(nQ):
                hbuf = tc % 2
                hsrc = hid_d.ap()[:, tc * 128:(tc + 1) * 128] \
                    .rearrange("(c p) t -> p c t", p=128)
                hv = chunk3(hblk[hbuf].ap(), 128, nch=nEC)
                nh2 = nEC // 2
                wl = up_done + ([huse.get(hbuf)] if huse.get(hbuf) else [])
                t_ha = p.dma("y", hv[:, 0:nh2, :], hsrc[:, 0:nh2, :],
                             f"lha{hbuf}", waits=wl)
                t_hb = p.dma("g", hv[:, nh2:nEC, :], hsrc[:, nh2:nEC, :],
                             f"lhb{hbuf}", waits=wl)
                t_h = (t_ha, t_hb)
                for hj, hd in ((0, hd0), (1, hd1)):
                    bk = mm_ps[2 * hj + tc % 2]
                    w0 = [wd_tok[hd], t_ha, t_hb] + \
                        ([mm_free.get(2 * hj + tc % 2)]
                         if mm_free.get(2 * hj + tc % 2) else [])
                    for c in range(nEC):
                        p.op("t", "matmul", bk[:, 0:256],
                             hblk[hbuf][:, c * 128:(c + 1) * 128],
                             wdb[hd % 2][:, c * 256:(c + 1) * 256],
                             start=(c == 0), stop=(c == nEC - 1),
                             waits=w0 if c == 0 else ())
                    t_pe = p.tok("_t")
                    ob = 2 * hj + tc % 2
                    t_y = p.op("s", "activation",
                               ev[:, ob * 256:ob * 256 + 256],
                               bk[:, 0:256], AF.Copy, scale=scl[:, 7:8],
                               waits=[t_pe] +
                               ([yst[ob]] if yst[ob] else []))
                    mm_free[2 * hj + tc % 2] = t_y
                    yst[ob] = p.dma("y", y_d[tc * 128:(tc + 1) * 128,
                                             hd * 256:(hd + 1) * 256],
                                    ev[:, ob * 256:ob * 256 + 256],
                                    f"sty{ob}", waits=[t_y])
                huse[hbuf] = p.tok("_t")
            wd_use[0] = p.tok("_t")
            wd_use[1] = p.tok("_t")
            if 2 * hp + 2 < 8:
                load_wd(2 * hp + 2)
                load_wd(2 * hp + 3)
    down_done = [p.tok("_t"), p.tok("_v"), p.tok("_s"), p.tok("sty0"),
                 p.tok("sty1"), p.tok("sty2"), p.tok("sty3")]

    x_use = [None, None]
    st_tok = [None, None]
    prev_ln = None
    y_t = [nc.alloc_sbuf_tensor(f"y_t{i}", [128, D], F32) for i in range(2)]
    for tc in range(nQ):
        xb = tc % 2
        t_ly = p.dma("s", y_t[xb][:, :], y_d[tc * 128:(tc + 1) * 128, :],
                     f"ly{xb}", waits=down_done +
                     ([x_use[xb]] if x_use[xb] else []))
        t_lx = p.dma("y", x_t[xb][:, :], x1_d[tc * 128:(tc + 1) * 128, :],
                     f"lre{xb}", waits=down_done +
                     ([x_use[xb]] if x_use[xb] else []) +
                     ([st_tok[xb]] if st_tok[xb] else []))
        t_n = emit_ln(p, y_t[xb][:, :], ln_o[:, :], bjunk[:, :], st,
                      waits=[t_ly] + ([prev_ln] if prev_ln else []))
        prev_ln = t_n
        t_add = p.op("v", "tensor_tensor", x_t[xb][:, :], x_t[xb][:, :],
                     ln_o[:, :], ALU.add, waits=[t_n, t_lx])
        x_use[xb] = t_add
        st_tok[xb] = p.dma("y", x2_d[tc * 128:(tc + 1) * 128, :],
                           x_t[xb][:, :], f"stx{xb}", waits=[t_add])
    p.op("y", "nop", waits=[p.tok("stx0"), p.tok("stx1")])
    p.finalize()
    return nc





BLOCK = 256


def build_l1a_local(SQ=1024):
    nQ = SQ // 128
    nTB = SQ // 512
    nB = SQ // BLOCK
    nc = bass.Bass("TRN2", target_bir_lowering=False)
    p = Prog(nc)

    x2_d = nc.dram_tensor("x2", [SQ, D], F32, kind="ExternalInput")
    wt_d = nc.dram_tensor("WqkvuT", [D, 4 * D], F32, kind="ExternalInput")
    sl_d = nc.dram_tensor("slope", [1, NH], F32, kind="ExternalInput")
    sc_d = nc.dram_tensor("sc", [NH * 128, 128], F32, kind="ExternalOutput")
    o_d = nc.dram_tensor("o_part", [SQ, D], F32, kind="ExternalOutput")
    qd_d = nc.dram_tensor("qd", [D, SQ], BF16, kind="ExternalOutput")
    u_d = nc.dram_tensor("u_o", [SQ, D], BF16, kind="ExternalOutput")

    ev = nc.alloc_sbuf_tensor("ev", [128, D], F32)
    x_t = [nc.alloc_sbuf_tensor(f"x_t{i}", [128, D], F32) for i in range(2)]
    bjunk = nc.alloc_sbuf_tensor("bjunk", [128, D], BF16)
    st = nc.alloc_sbuf_tensor("st", [128, 16], F32)
    xn_fm = nc.alloc_sbuf_tensor("xn_fm", [128, 16 * SQ], BF16)
    q_fm = nc.alloc_sbuf_tensor("q_fm", [128, 16 * SQ], BF16)
    k_fm = nc.alloc_sbuf_tensor("k_fm", [128, 16 * SQ], BF16)
    v_tm = nc.alloc_sbuf_tensor("v_tm", [128, nQ * D], BF16)
    uout = [nc.alloc_sbuf_tensor(f"uout{i}", [128, 512], BF16)
            for i in range(2)]
    arr_i = nc.alloc_sbuf_tensor("arr_i", [1, BLOCK], I32)
    arr_f = nc.alloc_sbuf_tensor("arr_f", [1, BLOCK], F32)
    iof = nc.alloc_sbuf_tensor("iof", [128, BLOCK], F32)
    iof2 = nc.alloc_sbuf_tensor("iof2", [128, 2 * BLOCK], F32)
    m01 = nc.alloc_sbuf_tensor("m01", [128, 2 * BLOCK], BF16)
    ddx = nc.alloc_sbuf_tensor("ddx", [128, 2 * BLOCK], F32)
    dd_t = nc.alloc_sbuf_tensor("dd_t", [128, 2 * BLOCK], BF16)
    bdr = nc.alloc_sbuf_tensor("bdr", [1, 4 * NH], F32)
    nsl_bc = nc.alloc_sbuf_tensor("nsl_bc", [128, NH], F32)
    s128_bc = nc.alloc_sbuf_tensor("s128_bc", [128, NH], F32)
    bd_bc = nc.alloc_sbuf_tensor("bd_bc", [128, NH], F32)
    bdp_bc = nc.alloc_sbuf_tensor("bdp_bc", [128, 4 * NH], F32)
    bdp_row = nc.alloc_sbuf_tensor("bdp_row", [1, 4 * NH], F32)
    qrow = nc.alloc_sbuf_tensor("qrow", [1, BLOCK], F32)
    krow = nc.alloc_sbuf_tensor("krow", [1, BLOCK], F32)
    qdec_bc = nc.alloc_sbuf_tensor("qdec_bc", [128, BLOCK], F32)
    kdec_bc = nc.alloc_sbuf_tensor("kdec_bc", [128, BLOCK], F32)
    qdq = nc.alloc_sbuf_tensor("qdq", [128, BLOCK], BF16)
    qds = nc.alloc_sbuf_tensor("qds", [128, BLOCK], BF16)
    kkd = nc.alloc_sbuf_tensor("kkd", [128, BLOCK], BF16)
    kkd_tm = nc.alloc_sbuf_tensor("kkd_tm", [128, BLOCK], BF16)
    qkd = nc.alloc_sbuf_tensor("qkd", [128, 2 * BLOCK], BF16)
    kv32 = nc.alloc_sbuf_tensor("kv32", [128, 128], F32)
    kv_bf = nc.alloc_sbuf_tensor("kv_bf", [128, 128], BF16)
    ob_sb = nc.alloc_sbuf_tensor("ob_sb", [128, 256], F32)

    tp_ps = [nc.alloc_psum_tensor(f"tp{i}", [128, 128], BF16) for i in range(2)]
    mm_ps = [nc.alloc_psum_tensor(f"mm{i}", [128, 512], F32) for i in range(4)]
    db_ps = nc.alloc_psum_tensor("db_ps", [128, 512], F32)
    c_ps = nc.alloc_psum_tensor("c_ps", [128, 128], F32)

    C = Consts(p, fvals=(LN_EPS, -256.0))
    mm_free = {}

    slr = nc.alloc_sbuf_tensor("slr", [1, NH], F32)
    nslr = nc.alloc_sbuf_tensor("nslr", [1, NH], F32)
    bneg = nc.alloc_sbuf_tensor("bneg", [1, NH], F32)
    t_sl = p.dma("y", slr[0:1, :], sl_d[0:1, :], "ldsl")
    t_ns = p.op("v", "tensor_scalar", nslr[0:1, :], slr[0:1, :], -1.0, None,
                ALU.mult, waits=[t_sl])
    t_bn = p.op("v", "tensor_scalar", bneg[0:1, :], slr[0:1, :], -256.0,
                None, ALU.mult, waits=[t_sl])
    t_bd = p.op("s", "activation", bdr[0:1, 0:NH], slr[0:1, :], AF.Exp,
                scale=-256.0, waits=[t_sl])
    t_p0 = p.op("v", "memset", bdp_row[0:1, 0:NH], 1.0)
    t_p1 = p.op("v", "tensor_copy", bdp_row[0:1, NH:2 * NH], bdr[0:1, 0:NH],
                waits=[t_bd])
    t_p2 = p.op("v", "tensor_tensor", bdp_row[0:1, 2 * NH:3 * NH],
                bdp_row[0:1, NH:2 * NH], bdr[0:1, 0:NH], ALU.mult,
                waits=[t_p1])
    t_p3 = p.op("v", "tensor_tensor", bdp_row[0:1, 3 * NH:4 * NH],
                bdp_row[0:1, 2 * NH:3 * NH], bdr[0:1, 0:NH], ALU.mult,
                waits=[t_p2])
    s128r = nc.alloc_sbuf_tensor("s128r", [1, NH], F32)
    t_s1r = p.op("v", "tensor_scalar", s128r[0:1, :], slr[0:1, :], 128.0,
                 None, ALU.mult, waits=[t_sl])
    t_i1 = p.op("g", "iota", arr_i[0:1, :], [[1, BLOCK]], base=1,
                channel_multiplier=0)
    t_i2 = p.op("v", "tensor_copy", arr_f[0:1, :], arr_i[0:1, :],
                waits=[t_i1])
    t_i4 = p.op("g", "iota", iof[:, :], [[1, BLOCK]], base=0,
                channel_multiplier=-1, allow_small_or_imprecise_dtypes=True)
    t_m0 = p.op("v", "tensor_scalar", m01[:, 0:BLOCK], iof[:, :], -0.5,
                None, ALU.is_gt, waits=[t_i4])
    t_m1 = p.op("v", "tensor_scalar", m01[:, BLOCK:], iof[:, :], 127.5,
                None, ALU.is_gt, waits=[t_i4])
    t_v0 = p.op("v", "tensor_scalar", iof2[:, 0:BLOCK], iof[:, :], -0.5,
                None, ALU.is_lt, waits=[t_i4])
    t_v1 = p.op("v", "tensor_scalar", iof2[:, BLOCK:], iof[:, :], 127.5,
                None, ALU.is_lt, waits=[t_i4])
    t_c0 = p.op("v", "scalar_tensor_tensor", iof2[:, 0:BLOCK],
                iof2[:, 0:BLOCK], 1e9, iof[:, :], ALU.mult, ALU.add,
                waits=[t_v0])
    t_c1 = p.op("v", "scalar_tensor_tensor", iof2[:, BLOCK:],
                iof2[:, BLOCK:], 1e9, iof[:, :], ALU.mult, ALU.add,
                waits=[t_v1])

    def bcast_row(dst, src_row, n, w):
        t_b = p.op("t", "matmul", db_ps[:, 0:n], C.ones_rf[0:1, 0:128],
                   src_row, start=True, stop=True,
                   waits=w + ([mm_free.get("db")] if mm_free.get("db")
                              else []))
        t_c = p.op("v", "tensor_copy", dst, db_ps[:, 0:n], waits=[t_b])
        mm_free["db"] = t_c
        return t_c

    t_b1 = bcast_row(nsl_bc[:, :], nslr[0:1, 0:NH], NH, [t_ns])
    t_b2 = bcast_row(s128_bc[:, :], s128r[0:1, 0:NH], NH, [t_s1r])
    t_b3 = bcast_row(bd_bc[:, :], bdr[0:1, 0:NH], NH, [t_bd])
    t_b4 = bcast_row(bdp_bc[:, :], bdp_row[0:1, :], 4 * NH, [t_p0, t_p3])
    dec_done = [t_m0, t_m1, t_b1, t_b2, t_b3, t_b4, t_i2, t_bn, t_c0, t_c1]

    x_use = [None, None]
    prev_ln = None
    for ti in range(nQ):
        b = ti % 2
        t_ld = p.dma("y", x_t[b][:, :], x2_d[ti * 128:(ti + 1) * 128, :],
                     f"ldx{b}", waits=[x_use[b]] if x_use[b] else [])
        t_n = emit_ln(p, x_t[b][:, :], bjunk[:, :], ev[:, :], st,
                      waits=[t_ld] + ([prev_ln] if prev_ln else []))
        prev_ln = t_n
        x_use[b] = t_n
        cp_toks = []
        for c in range(16):
            bk = tp_ps[c % 2]
            w = [t_n]
            if len(cp_toks) >= 2:
                w.append(cp_toks[-2])
            t_tr = p.op("t", "transpose", bk[:, :],
                        bjunk[:, c * 128:(c + 1) * 128], C.ident[:, :],
                        waits=w)
            cp_toks.append(p.op("v", "tensor_copy",
                                xn_fm[:, c * SQ + ti * 128:
                                      c * SQ + (ti + 1) * 128],
                                bk[:, :], waits=[t_tr]))
    ln_done = [p.tok("_v"), p.tok("_t")]

    wuse = [None, None]
    ust = [None, None]
    with nc.sbuf_tensor("wc0", [128, 16 * 128], BF16) as wc0, \
            nc.sbuf_tensor("wc1", [128, 16 * 128], BF16) as wc1, \
            nc.sbuf_tensor("wb0", [128, 16 * 512], BF16) as wb0:
        wc = [wc0, wc1]
        wb = [wb0, wb0]
        for kind in ("q", "k"):
            base = 0 if kind == "q" else D
            dst = q_fm if kind == "q" else k_fm
            for o in range(16):
                b = o % 2
                t_w = p.dma("g", chunk3(wc[b].ap(), 128),
                            wt_d.ap()[:, base + o * 128:base + (o + 1) * 128]
                            .rearrange("(c p) o -> p c o", p=128),
                            f"lw{b}", waits=[wuse[b]] if wuse[b] else [])
                for tb in range(nTB):
                    bk = mm_ps[tb % 2]
                    w0 = [t_w] + ln_done + \
                        ([mm_free.get(tb % 2)] if mm_free.get(tb % 2)
                         else [])
                    for c in range(16):
                        p.op("t", "matmul", bk[:, :],
                             wc[b][:, c * 128:(c + 1) * 128],
                             xn_fm[:, c * SQ + tb * 512:
                                   c * SQ + (tb + 1) * 512],
                             start=(c == 0), stop=(c == 15),
                             waits=w0 if c == 0 else ())
                    t_e = p.op("s", "activation",
                               dst[:, o * SQ + tb * 512:
                                   o * SQ + (tb + 1) * 512],
                               bk[:, :], AF.Silu, waits=[p.tok("_t")])
                    mm_free[tb % 2] = t_e
                wuse[b] = p.tok("_t")
        for kind in ("v", "u"):
            base = 2 * D if kind == "v" else 3 * D
            for ob in range(4):
                b = 0
                t_w = p.dma("g", chunk3(wb[b].ap(), 512),
                            wt_d.ap()[:, base + ob * 512:base + (ob + 1) * 512]
                            .rearrange("(c p) o -> p c o", p=128),
                            "lwb", waits=[p.tok("_t")])
                for tc in range(nQ):
                    bk = mm_ps[2 + tc % 2]
                    w0 = [t_w] + ln_done + \
                        ([mm_free.get(2 + tc % 2)]
                         if mm_free.get(2 + tc % 2) else [])
                    for c in range(16):
                        p.op("t", "matmul", bk[:, :],
                             xn_fm[:, c * SQ + tc * 128:
                                   c * SQ + (tc + 1) * 128],
                             wb[b][:, c * 512:(c + 1) * 512],
                             start=(c == 0), stop=(c == 15),
                             waits=w0 if c == 0 else ())
                    if kind == "v":
                        t_e = p.op("v", "tensor_copy",
                                   v_tm[:, tc * D + ob * 512:
                                        tc * D + (ob + 1) * 512],
                                   bk[:, :], waits=[p.tok("_t")])
                    else:
                        ou = tc % 2
                        t_e = p.op("v", "tensor_copy", uout[ou][:, :],
                                   bk[:, :],
                                   waits=[p.tok("_t")] +
                                   ([ust[ou]] if ust[ou] else []))
                        ust[ou] = p.dma("y",
                                        u_d[tc * 128:(tc + 1) * 128,
                                            ob * 512:(ob + 1) * 512],
                                        uout[ou][:, :], f"stu{ou}",
                                        waits=[t_e])
                    mm_free[2 + tc % 2] = t_e
                wuse[b] = p.tok("_t")
    proj_done = [p.tok("_t"), p.tok("_v"), p.tok("_s")]

    st_sc = None
    st_qd = None
    st_ob = None
    for h in range(NH):
        t_q = p.op("s", "activation", qrow[0:1, :], arr_f[0:1, :], AF.Exp,
                   scale=nslr[0:1, h:h + 1],
                   waits=dec_done + proj_done +
                   ([mm_free.get("qrow")] if mm_free.get("qrow") else []))
        t_k = p.op("s", "activation", krow[0:1, :], arr_f[0:1, :], AF.Exp,
                   scale=slr[0:1, h:h + 1], bias=bneg[0:1, h:h + 1],
                   waits=[t_q] + ([mm_free.get("krow")]
                                  if mm_free.get("krow") else []))
        t_qb = bcast_row(qdec_bc[:, :], qrow[0:1, :], BLOCK, [t_q])
        mm_free["qrow"] = t_qb
        t_kb = bcast_row(kdec_bc[:, :], krow[0:1, :], BLOCK, [t_k])
        mm_free["krow"] = t_kb
        dd_toks = []
        for kt in range(2):
            t_x = p.op("s", "activation", ddx[:, kt * BLOCK:(kt + 1) * BLOCK],
                       iof2[:, kt * BLOCK:(kt + 1) * BLOCK], AF.Exp,
                       scale=nsl_bc[:, h:h + 1],
                       bias=(s128_bc[:, h:h + 1] if kt == 1 else 0.0),
                       waits=dec_done + [t_k] +
                       ([mm_free.get(("dd", kt))]
                        if mm_free.get(("dd", kt)) else []))
            t_d = p.op("v", "tensor_tensor",
                       dd_t[:, kt * BLOCK:(kt + 1) * BLOCK],
                       ddx[:, kt * BLOCK:(kt + 1) * BLOCK],
                       m01[:, kt * BLOCK:(kt + 1) * BLOCK], ALU.mult,
                       waits=[t_x])
            mm_free[("dd", kt)] = t_d
            dd_toks.append(t_d)
        t_z = p.op("v", "memset", kv32[:, :], 0.0, waits=[st_sc] if st_sc
                   else [])
        t_zb = p.op("v", "memset", kv_bf[:, :], 0.0, waits=[t_z])
        kv_tok = t_zb
        for i in range(nB):
            tbase = i * BLOCK
            t_1 = p.op("v", "tensor_tensor", qdq[:, :],
                       q_fm[:, h * SQ + tbase:h * SQ + tbase + BLOCK],
                       qdec_bc[:, :], ALU.mult,
                       waits=[t_qb, kv_tok] + proj_done)
            t_2 = p.op("v", "tensor_scalar", qds[:, :], qdq[:, :],
                       bdp_bc[:, (i * NH + h):(i * NH + h) + 1], None,
                       ALU.mult, waits=[t_1] + ([st_qd] if st_qd else []))
            st_qd = p.dma("y", qd_d[h * 128:(h + 1) * 128,
                                    tbase:tbase + BLOCK],
                          qds[:, :], "stqd", waits=[t_2])
            for kt in range(2):
                bk = mm_ps[kt]
                t_s1 = p.op("t", "matmul", bk[:, 0:BLOCK],
                            k_fm[:, h * SQ + tbase + kt * 128:
                                 h * SQ + tbase + (kt + 1) * 128],
                            q_fm[:, h * SQ + tbase:h * SQ + tbase + BLOCK],
                            start=True, stop=True,
                            waits=[kv_tok] +
                            ([mm_free.get(("sc", kt))]
                             if mm_free.get(("sc", kt)) else []))
                t_s2 = p.op("v", "tensor_tensor",
                            qkd[:, kt * BLOCK:(kt + 1) * BLOCK],
                            bk[:, 0:BLOCK],
                            dd_t[:, kt * BLOCK:(kt + 1) * BLOCK], ALU.mult,
                            waits=[t_s1] + dd_toks)
                mm_free[("sc", kt)] = t_s2
            for qs in range(2):
                bk = mm_ps[2 + qs]
                w0 = [mm_free[("sc", 0)], mm_free[("sc", 1)]] + \
                    ([mm_free.get(("o", qs))] if mm_free.get(("o", qs))
                     else [])
                for kt in range(2):
                    vsl = v_tm[:, (2 * i + kt) * D + h * 128:
                               (2 * i + kt) * D + (h + 1) * 128]
                    p.op("t", "matmul", bk[:, 0:128],
                         qkd[:, kt * BLOCK + qs * 128:
                             kt * BLOCK + qs * 128 + 128],
                         vsl, start=(kt == 0), stop=False,
                         waits=w0 if kt == 0 else ())
                p.op("t", "matmul", bk[:, 0:128],
                     qdq[:, qs * 128:qs * 128 + 128], kv_bf[:, :],
                     start=False, stop=True, waits=[kv_tok])
                t_o = p.op("s", "activation", ob_sb[:, qs * 128:
                                                    (qs + 1) * 128],
                           bk[:, 0:128], AF.Copy,
                           waits=[p.tok("_t")] +
                           ([st_ob] if st_ob else []))
                mm_free[("o", qs)] = t_o
            p.dma("y", o_d[tbase:tbase + 128, h * 128:(h + 1) * 128],
                  ob_sb[:, 0:128], "stob", waits=[mm_free[("o", 0)]])
            st_ob = p.dma("y", o_d[tbase + 128:tbase + 256,
                                   h * 128:(h + 1) * 128],
                          ob_sb[:, 128:256], "stob",
                          waits=[mm_free[("o", 1)]])
            t_3 = p.op("v", "tensor_tensor", kkd[:, :],
                       k_fm[:, h * SQ + tbase:h * SQ + tbase + BLOCK],
                       kdec_bc[:, :], ALU.mult,
                       waits=[t_kb] + ([mm_free.get("kkd")]
                                       if mm_free.get("kkd") else []))
            tr_toks = []
            for kt in range(2):
                bk2 = tp_ps[kt]
                t_t = p.op("t", "transpose", bk2[:, :],
                           kkd[:, kt * 128:(kt + 1) * 128], C.ident[:, :],
                           waits=[t_3] + ([mm_free.get(("tp", kt))]
                                          if mm_free.get(("tp", kt))
                                          else []))
                t_c = p.op("v", "tensor_copy",
                           kkd_tm[:, kt * 128:(kt + 1) * 128], bk2[:, :],
                           waits=[t_t])
                mm_free[("tp", kt)] = t_c
                tr_toks.append(t_c)
            mm_free["kkd"] = tr_toks[-1]
            for kt in range(2):
                vsl = v_tm[:, (2 * i + kt) * D + h * 128:
                           (2 * i + kt) * D + (h + 1) * 128]
                p.op("t", "matmul", c_ps[:, :],
                     kkd_tm[:, kt * 128:(kt + 1) * 128], vsl,
                     start=(kt == 0), stop=(kt == 1),
                     waits=(tr_toks + [kv_tok]) if kt == 0 else ())
            t_u = p.op("v", "scalar_tensor_tensor", kv32[:, :], kv32[:, :],
                       bd_bc[:, h:h + 1], c_ps[:, :], ALU.mult, ALU.add,
                       waits=[p.tok("_t"), t_zb])
            kv_tok = p.op("v", "tensor_copy", kv_bf[:, :], kv32[:, :],
                          waits=[t_u])
        st_sc = p.dma("y", sc_d[h * 128:(h + 1) * 128, :], kv32[:, :],
                      "stsc", waits=[kv_tok])
    p.op("y", "nop", waits=[p.tok("stsc"), p.tok("stob"), p.tok("stqd"),
                            p.tok("stu0"), p.tok("stu1")])
    p.finalize()
    return nc






def build_l1a_fix(SQ=1024):
    nQ = SQ // 128
    nc = bass.Bass("TRN2", target_bir_lowering=False)
    p = Prog(nc)

    x2_d = nc.dram_tensor("x2", [SQ, D], F32, kind="ExternalInput")
    o_d = nc.dram_tensor("o_part", [SQ, D], F32, kind="ExternalInput")
    qd_d = nc.dram_tensor("qd", [D, SQ], BF16, kind="ExternalInput")
    u_d = nc.dram_tensor("u_o", [SQ, D], BF16, kind="ExternalInput")
    sp_d = [nc.dram_tensor(f"scp{j}", [NH * 128, 128], F32,
                           kind="ExternalInput") for j in (1, 2, 3)]
    sl_d = nc.dram_tensor("slope", [1, NH], F32, kind="ExternalInput")
    wo_d = nc.dram_tensor("WoutT", [D, D], F32, kind="ExternalInput")
    x3_d = nc.dram_tensor("x3", [SQ, D], F32, kind="ExternalOutput")

    ev = nc.alloc_sbuf_tensor("ev", [128, D], F32)
    ln_o = nc.alloc_sbuf_tensor("ln_o", [128, D], F32)
    x_t = [nc.alloc_sbuf_tensor(f"x_t{i}", [128, D], F32) for i in range(2)]
    bjunk = nc.alloc_sbuf_tensor("bjunk", [128, D], BF16)
    st = nc.alloc_sbuf_tensor("st", [128, 16], F32)
    qd_fm = nc.alloc_sbuf_tensor("qd_fm", [128, 16 * SQ], BF16)
    on_fm = nc.alloc_sbuf_tensor("on_fm", [128, 16 * SQ], BF16)
    o_sb = nc.alloc_sbuf_tensor("o_sb", [128, D], F32)
    u_sb = nc.alloc_sbuf_tensor("u_sb", [128, D], BF16)
    slr = nc.alloc_sbuf_tensor("slr", [1, NH], F32)
    bdr = nc.alloc_sbuf_tensor("bdr", [1, 3 * NH], F32)
    bd48 = nc.alloc_sbuf_tensor("bd48", [128, 2 * NH], F32)
    kvin = nc.alloc_sbuf_tensor("kvin", [128, 128 * 3], F32)
    kvin_b = nc.alloc_sbuf_tensor("kvin_b", [128, 16 * 128], BF16)

    tp_ps = [nc.alloc_psum_tensor(f"tp{i}", [128, 128], BF16) for i in range(2)]
    mm_ps = [nc.alloc_psum_tensor(f"mm{i}", [128, 512], F32) for i in range(4)]
    db_ps = nc.alloc_psum_tensor("db_ps", [128, 512], F32)

    C = Consts(p, fvals=(LN_EPS, 1e-6, -256.0))
    mm_free = {}

    t_sl = p.dma("y", slr[0:1, :], sl_d[0:1, :], "ldsl")
    t_bd = p.op("s", "activation", bdr[0:1, 0:NH], slr[0:1, :], AF.Exp,
                scale=-256.0, waits=[t_sl])
    t_b2 = p.op("v", "tensor_tensor", bdr[0:1, NH:2 * NH], bdr[0:1, 0:NH],
                bdr[0:1, 0:NH], ALU.mult, waits=[t_bd])
    t_b4 = p.op("v", "tensor_tensor", bdr[0:1, NH:2 * NH],
                bdr[0:1, NH:2 * NH], bdr[0:1, NH:2 * NH], ALU.mult,
                waits=[t_b2])
    t_b8 = p.op("v", "tensor_tensor", bdr[0:1, 2 * NH:3 * NH],
                bdr[0:1, NH:2 * NH], bdr[0:1, NH:2 * NH], ALU.mult,
                waits=[t_b4])
    t_bb = p.op("t", "matmul", db_ps[:, 0:2 * NH], C.ones_rf[0:1, 0:128],
                bdr[0:1, NH:3 * NH], start=True, stop=True, waits=[t_b8])
    t_bc = p.op("v", "tensor_copy", bd48[:, :], db_ps[:, 0:2 * NH],
                waits=[t_bb])

    with nc.sbuf_tensor("sp_t", [128, 3 * 128], F32) as sp_t:
        kv_toks = []
        for h in range(NH):
            t_l = [p.dma("y", sp_t[:, j * 128:(j + 1) * 128],
                         sp_d[j][h * 128:(h + 1) * 128, :], f"lsp{j}",
                         waits=kv_toks[-1:] if kv_toks else [])
                   for j in range(3)]
            t_1 = p.op("v", "scalar_tensor_tensor", kvin[:, 0:128],
                       sp_t[:, 128:256], bd48[:, h:h + 1],
                       sp_t[:, 0:128], ALU.mult, ALU.add,
                       waits=t_l + [t_bc])
            t_2 = p.op("v", "scalar_tensor_tensor", kvin[:, 128:256],
                       sp_t[:, 256:384], bd48[:, NH + h:NH + h + 1],
                       kvin[:, 0:128], ALU.mult, ALU.add, waits=[t_1])
            kv_toks.append(p.op("v", "tensor_copy",
                                kvin_b[:, h * 128:(h + 1) * 128],
                                kvin[:, 128:256], waits=[t_2]))
    kv_done = [p.tok("_v")]

    t_qd = p.dma("g", chunk3(qd_fm.ap(), SQ),
                 qd_d.ap().rearrange("(c p) t -> p c t", p=128), "ldqd")

    o_use = None
    u_use = None
    prev_ln = None
    for tc in range(nQ):
        t_lo = p.dma("y", o_sb[:, :], o_d[tc * 128:(tc + 1) * 128, :],
                     "ldo", waits=[o_use] if o_use else [])
        t_lu = p.dma("s", u_sb[:, :], u_d[tc * 128:(tc + 1) * 128, :],
                     "ldu", waits=[u_use] if u_use else [])
        add_toks = []
        for h in range(NH):
            bk = mm_ps[h % 2]
            t_m = p.op("t", "matmul", bk[:, 0:128],
                       qd_fm[:, h * SQ + tc * 128:h * SQ + (tc + 1) * 128],
                       kvin_b[:, h * 128:(h + 1) * 128],
                       start=True, stop=True,
                       waits=[t_qd] + kv_done +
                       ([mm_free.get(h % 2)] if mm_free.get(h % 2) else []))
            t_a = p.op("v", "tensor_tensor",
                       o_sb[:, h * 128:(h + 1) * 128],
                       o_sb[:, h * 128:(h + 1) * 128], bk[:, 0:128],
                       ALU.add, waits=[t_m, t_lo] + add_toks[-1:])
            mm_free[h % 2] = t_a
            add_toks.append(t_a)
        t_sq = p.op("s", "activation", bjunk[:, :], o_sb[:, :], AF.Square,
                    accum_out=st[:, 0:1],
                    waits=add_toks + ([prev_ln] if prev_ln else []))
        t_ms = p.op("v", "tensor_scalar", st[:, 1:2], st[:, 0:1], 1.0 / D,
                    None, ALU.mult, waits=[t_sq])
        t_sd = p.op("s", "activation", st[:, 2:3], st[:, 1:2], AF.Sqrt,
                    bias=1e-6, waits=[t_ms])
        t_r = p.op("v", "reciprocal", st[:, 3:4], st[:, 2:3], waits=[t_sd])
        t_n = p.op("v", "tensor_scalar", ln_o[:, :], o_sb[:, :],
                   st[:, 3:4], None, ALU.mult, waits=[t_r])
        t_g = p.op("v", "tensor_tensor", bjunk[:, :], ln_o[:, :],
                   u_sb[:, :], ALU.mult, waits=[t_n, t_lu, t_sq])
        prev_ln = t_g
        u_use = t_g
        o_use = t_g
        cp_toks = []
        for c in range(16):
            bk2 = tp_ps[c % 2]
            w = [t_g]
            if len(cp_toks) >= 2:
                w.append(cp_toks[-2])
            t_tr = p.op("t", "transpose", bk2[:, :],
                        bjunk[:, c * 128:(c + 1) * 128], C.ident[:, :],
                        waits=w)
            cp_toks.append(p.op("v", "tensor_copy",
                                on_fm[:, c * SQ + tc * 128:
                                      c * SQ + (tc + 1) * 128],
                                bk2[:, :], waits=[t_tr]))
        prev_ln = cp_toks[-1]
    gate_done = [p.tok("_v"), p.tok("_t"), p.tok("_s")]

    with nc.sbuf_tensor("wo_sb", [128, 16 * D], BF16) as wo_sb:
        t_wo = p.dma("g", chunk3(wo_sb.ap(), D), wcol_ap(wo_d, 0, D),
                     "ldwo", waits=gate_done)
        x_use = [None, None]
        st_tok = [None, None]
        prev_ln = None
        for tc in range(nQ):
            xb = tc % 2
            ev_toks = []
            for o4 in range(4):
                bidx = o4 % 2
                bank = mm_ps[bidx]
                w0 = [t_wo] + gate_done + \
                    ([mm_free.get(("c", bidx))] if mm_free.get(("c", bidx))
                     else [])
                for c in range(16):
                    p.op("t", "matmul", bank[:, :],
                         on_fm[:, c * SQ + tc * 128:c * SQ + (tc + 1) * 128],
                         wo_sb[:, c * D + o4 * 512:c * D + (o4 + 1) * 512],
                         start=(c == 0), stop=(c == 15),
                         waits=w0 if c == 0 else ())
                w_e = [p.tok("_t")]
                if o4 == 0 and prev_ln:
                    w_e.append(prev_ln)
                t_e = p.op("s", "activation", ev[:, o4 * 512:(o4 + 1) * 512],
                           bank[:, :], AF.Copy, waits=w_e)
                mm_free[("c", bidx)] = t_e
                ev_toks.append(t_e)
            t_ld = p.dma("y", x_t[xb][:, :], x2_d[tc * 128:(tc + 1) * 128, :],
                         f"lre{xb}",
                         waits=([x_use[xb]] if x_use[xb] else []) +
                         ([st_tok[xb]] if st_tok[xb] else []) +
                         ([p.tok("_v")] if tc < 2 else []))
            t_n = emit_ln(p, ev[:, :], ln_o[:, :], bjunk[:, :], st,
                          waits=ev_toks + ([prev_ln] if prev_ln else []))
            prev_ln = t_n
            t_add = p.op("v", "tensor_tensor", x_t[xb][:, :], x_t[xb][:, :],
                         ln_o[:, :], ALU.add, waits=[t_n, t_ld])
            x_use[xb] = t_add
            st_tok[xb] = p.dma("y", x3_d[tc * 128:(tc + 1) * 128, :],
                               x_t[xb][:, :], f"stx{xb}", waits=[t_add])
    p.op("y", "nop", waits=[p.tok("stx0"), p.tok("stx1")])
    p.finalize()
    return nc






def build_ffn1_dequant():
    W1, W2 = 1408, 256
    nc = bass.Bass("TRN2", target_bir_lowering=False)
    p = Prog(nc)
    gup_d = nc.dram_tensor("gup_T", [256, W1], I32, kind="ExternalInput")
    gus_d = nc.dram_tensor("gus", [256, W1], mybir.dt.float16,
                           kind="ExternalInput")
    dwn_d = nc.dram_tensor("dwn_T", [704, W2], I32, kind="ExternalInput")
    dws_d = nc.dram_tensor("dws", [704, W2], mybir.dt.float16,
                           kind="ExternalInput")
    wgu_d = nc.dram_tensor("wgu_sl", [2048, W1], BF16, kind="ExternalOutput")
    wd_d = nc.dram_tensor("wd_sl", [5632, W2], BF16, kind="ExternalOutput")

    W = max(W1, W2)
    pk = nc.alloc_sbuf_tensor("pk", [128, W], I32)
    scf = nc.alloc_sbuf_tensor("scf", [128, W], F32)
    sch = nc.alloc_sbuf_tensor("sch", [128, W], mybir.dt.float16)
    nib = nc.alloc_sbuf_tensor("nib", [128, W], I32)
    ti_ = nc.alloc_sbuf_tensor("ti_", [128, W], I32)
    si_ = nc.alloc_sbuf_tensor("si_", [128, W], I32)
    tf = nc.alloc_sbuf_tensor("tf", [128, W], F32)
    sf = nc.alloc_sbuf_tensor("sf", [128, W], F32)
    g5 = nc.alloc_sbuf_tensor("g5", [128, W], F32)
    g7 = nc.alloc_sbuf_tensor("g7", [128, W], F32)
    acc = nc.alloc_sbuf_tensor("acc", [128, W], F32)
    pl = [nc.alloc_sbuf_tensor(f"pl{i}", [128, W], BF16) for i in range(2)]
    p_ = p

    def dequant(src_d, ssrc_d, dst_d, rows, width, init_waits=()):
        nt = (rows + 127) // 128
        st_tok = [None, None]
        prev = list(init_waits)
        for t in range(nt):
            r0 = t * 128
            pr = min(128, rows - r0)
            t_l = p_.dma("y", pk[0:pr, 0:width], src_d[r0:r0 + pr, :],
                         "ldp", waits=prev)
            t_s = p_.dma("y", sch[0:pr, 0:width], ssrc_d[r0:r0 + pr, :],
                         "lds", waits=prev)
            t_sc = p_.op("v", "tensor_copy", scf[0:pr, 0:width],
                         sch[0:pr, 0:width], waits=[t_s])
            prev = []
            for k in range(8):
                t1 = p_.op("v", "tensor_scalar", nib[0:pr, 0:width],
                           pk[0:pr, 0:width], 4 * k, None,
                           ALU.logical_shift_right, waits=[t_l, t_sc])
                t2 = p_.op("v", "tensor_scalar", ti_[0:pr, 0:width],
                           nib[0:pr, 0:width], 7, None, ALU.bitwise_and,
                           waits=[t1])
                t3 = p_.op("v", "tensor_scalar", si_[0:pr, 0:width],
                           nib[0:pr, 0:width], 3, None,
                           ALU.logical_shift_right, waits=[t2])
                t3b = p_.op("v", "tensor_scalar", si_[0:pr, 0:width],
                            si_[0:pr, 0:width], 1, None, ALU.bitwise_and,
                            waits=[t3])
                t4 = p_.op("v", "tensor_copy", tf[0:pr, 0:width],
                           ti_[0:pr, 0:width], waits=[t2])
                t5 = p_.op("v", "tensor_copy", sf[0:pr, 0:width],
                           si_[0:pr, 0:width], waits=[t3b])
                t6 = p_.op("v", "tensor_scalar", g5[0:pr, 0:width],
                           tf[0:pr, 0:width], 4.5, None, ALU.is_gt,
                           waits=[t4])
                t7 = p_.op("v", "tensor_tensor", acc[0:pr, 0:width],
                           tf[0:pr, 0:width], g5[0:pr, 0:width], ALU.add,
                           waits=[t6])
                t8 = p_.op("v", "tensor_scalar", g5[0:pr, 0:width],
                           tf[0:pr, 0:width], 5.5, None, ALU.is_gt,
                           waits=[t7])
                t9 = p_.op("v", "tensor_tensor", acc[0:pr, 0:width],
                           acc[0:pr, 0:width], g5[0:pr, 0:width], ALU.add,
                           waits=[t8])
                t10 = p_.op("v", "tensor_scalar", g7[0:pr, 0:width],
                            tf[0:pr, 0:width], 6.5, None, ALU.is_gt,
                            waits=[t9])
                t11 = p_.op("v", "scalar_tensor_tensor", acc[0:pr, 0:width],
                            g7[0:pr, 0:width], 3.0, acc[0:pr, 0:width],
                            ALU.mult, ALU.add, waits=[t10])
                t12 = p_.op("v", "tensor_scalar", sf[0:pr, 0:width],
                            sf[0:pr, 0:width], -2.0, 1.0, ALU.mult,
                            ALU.add, waits=[t11, t5])
                t13 = p_.op("v", "tensor_tensor", acc[0:pr, 0:width],
                            acc[0:pr, 0:width], sf[0:pr, 0:width],
                            ALU.mult, waits=[t12])
                t14 = p_.op("v", "tensor_scalar", acc[0:pr, 0:width],
                            acc[0:pr, 0:width], 0.5, None, ALU.mult,
                            waits=[t13])
                ob = k % 2
                t15 = p_.op("v", "tensor_tensor", pl[ob][0:pr, 0:width],
                            acc[0:pr, 0:width], scf[0:pr, 0:width],
                            ALU.mult,
                            waits=[t14] + ([st_tok[ob]] if st_tok[ob]
                                           else []))
                dst = dst_d.ap().rearrange("(ip k) o -> k ip o", k=8)
                st_tok[ob] = p_.dma("s", dst[k, r0:r0 + pr, :],
                                    pl[ob][0:pr, 0:width], f"stw{ob}",
                                    waits=[t15])
                prev = [t15]
        return [p_.tok("stw0"), p_.tok("stw1")]

    dequant(gup_d, gus_d, wgu_d, 256, W1)
    toks = dequant(dwn_d, dws_d, wd_d, 704, W2,
                   init_waits=[p.tok("_v"), p.tok("stw0"), p.tok("stw1")])
    p.op("y", "nop", waits=toks)
    p.finalize()
    return nc




def build_ffn1(SQ=1024):
    I = 5632
    nQ = SQ // 128
    nTB = SQ // 512
    nEC = I // 128
    nc = bass.Bass("TRN2", target_bir_lowering=False)
    p = Prog(nc)

    x3_d = nc.dram_tensor("x3", [SQ, D], F32, kind="ExternalInput")
    wgu_d = nc.dram_tensor("wgu", [D, 2 * I], BF16, kind="ExternalInput")
    wd_d = nc.dram_tensor("wd", [I, D], BF16, kind="ExternalInput")
    x4_d = nc.dram_tensor("x4", [SQ, D], F32, kind="ExternalOutput")
    hid_d = nc.dram_tensor("hid_scr", [I, SQ], BF16)
    y_d = nc.dram_tensor("y_scr", [SQ, D], F32)

    ev = nc.alloc_sbuf_tensor("ev", [128, D], F32)
    ln_o = nc.alloc_sbuf_tensor("ln_o", [128, D], F32)
    x_t = [nc.alloc_sbuf_tensor(f"x_t{i}", [128, D], F32) for i in range(2)]
    bjunk = nc.alloc_sbuf_tensor("bjunk", [128, D], BF16)
    st = nc.alloc_sbuf_tensor("st", [128, 16], F32)
    xn_fm = nc.alloc_sbuf_tensor("xn_fm", [128, 16 * SQ], BF16)
    hb = [nc.alloc_sbuf_tensor(f"hb{i}", [128, 512], BF16) for i in range(2)]
    hout = [nc.alloc_sbuf_tensor(f"hout{i}", [128, 512], BF16)
            for i in range(2)]
    y_t = [nc.alloc_sbuf_tensor(f"y_t{i}", [128, D], F32) for i in range(2)]

    tp_ps = [nc.alloc_psum_tensor(f"tp{i}", [128, 128], BF16) for i in range(2)]
    mm_ps = [nc.alloc_psum_tensor(f"mm{i}", [128, 512], F32) for i in range(4)]

    C = Consts(p, fvals=(LN_EPS, 1e-6))
    mm_free = {}

    x_use = [None, None]
    prev_ln = None
    for ti in range(nQ):
        b = ti % 2
        t_ld = p.dma("y", x_t[b][:, :], x3_d[ti * 128:(ti + 1) * 128, :],
                     f"ldx{b}", waits=[x_use[b]] if x_use[b] else [])
        t_n = emit_ln(p, x_t[b][:, :], bjunk[:, :], ev[:, :], st,
                      waits=[t_ld] + ([prev_ln] if prev_ln else []))
        prev_ln = t_n
        x_use[b] = t_n
        cp_toks = []
        for c in range(16):
            bk = tp_ps[c % 2]
            w = [t_n]
            if len(cp_toks) >= 2:
                w.append(cp_toks[-2])
            t_tr = p.op("t", "transpose", bk[:, :],
                        bjunk[:, c * 128:(c + 1) * 128], C.ident[:, :],
                        waits=w)
            cp_toks.append(p.op("v", "tensor_copy",
                                xn_fm[:, c * SQ + ti * 128:
                                      c * SQ + (ti + 1) * 128],
                                bk[:, :], waits=[t_tr]))
    ln_done = [p.tok("_v"), p.tok("_t")]

    with nc.sbuf_tensor("wg0", [128, 16 * 128], BF16) as wg0, \
            nc.sbuf_tensor("wg1", [128, 16 * 128], BF16) as wg1, \
            nc.sbuf_tensor("wu0", [128, 16 * 128], BF16) as wu0, \
            nc.sbuf_tensor("wu1", [128, 16 * 128], BF16) as wu1:
        wg = [wg0, wg1]
        wu = [wu0, wu1]
        wguse = [None, None]
        hid_st = [None, None]
        for ec in range(nEC):
            b = ec % 2
            t_wg = p.dma(("g", "y")[b], chunk3(wg[b].ap(), 128),
                         wgu_d.ap()[:, ec * 128:(ec + 1) * 128]
                         .rearrange("(c p) o -> p c o", p=128),
                         f"lwg{b}", waits=[wguse[b]] if wguse[b] else [])
            t_wu = p.dma(("s", "s")[b], chunk3(wu[b].ap(), 128),
                         wgu_d.ap()[:, I + ec * 128:I + (ec + 1) * 128]
                         .rearrange("(c p) o -> p c o", p=128),
                         f"lwu{b}", waits=[wguse[b]] if wguse[b] else [])
            for tb in range(nTB):
                bg = mm_ps[tb % 2]
                bu = mm_ps[2 + tb % 2]
                w0 = [t_wg] + ln_done + \
                    ([mm_free.get(tb % 2)] if mm_free.get(tb % 2) else [])
                w1 = [t_wu] + \
                    ([mm_free.get(2 + tb % 2)] if mm_free.get(2 + tb % 2)
                     else [])
                for c in range(16):
                    p.op("t", "matmul", bg[:, :],
                         wg[b][:, c * 128:(c + 1) * 128],
                         xn_fm[:, c * SQ + tb * 512:c * SQ + (tb + 1) * 512],
                         start=(c == 0), stop=(c == 15),
                         waits=w0 if c == 0 else ())
                for c in range(16):
                    p.op("t", "matmul", bu[:, :],
                         wu[b][:, c * 128:(c + 1) * 128],
                         xn_fm[:, c * SQ + tb * 512:c * SQ + (tb + 1) * 512],
                         start=(c == 0), stop=(c == 15),
                         waits=w1 if c == 0 else ())
                t_pe = p.tok("_t")
                t_si = p.op("s", "activation", hb[tb % 2][:, :], bg[:, :],
                            AF.Silu,
                            waits=[t_pe] + ([mm_free.get(("hb", tb % 2))]
                                            if mm_free.get(("hb", tb % 2))
                                            else []))
                mm_free[tb % 2] = t_si
                ob = (ec * nTB + tb) % 2
                t_hm = p.op("v", "tensor_tensor", hout[ob][:, :],
                            hb[tb % 2][:, :], bu[:, :], ALU.mult,
                            waits=[t_si] + ([hid_st[ob]] if hid_st[ob]
                                            else []))
                mm_free[("hb", tb % 2)] = t_hm
                mm_free[2 + tb % 2] = t_hm
                hid_st[ob] = p.dma("y",
                                   hid_d[ec * 128:(ec + 1) * 128,
                                         tb * 512:(tb + 1) * 512],
                                   hout[ob][:, :], f"sth{ob}", waits=[t_hm])
            wguse[b] = p.tok("_t")
    up_done = [p.tok("_t"), p.tok("_v"), p.tok("_s"), p.tok("_g"),
               p.tok("sth0"), p.tok("sth1"), p.tok("stwd0"), p.tok("stwd1")]

    with nc.sbuf_tensor("wdb0", [128, nEC * 256], BF16) as wdb0, \
            nc.sbuf_tensor("wdb1", [128, nEC * 256], BF16) as wdb1, \
            nc.sbuf_tensor("hblk0", [128, nEC * 128], BF16) as hblk0, \
            nc.sbuf_tensor("hblk1", [128, nEC * 128], BF16) as hblk1:
        wdb = [wdb0, wdb1]
        hblk = [hblk0, hblk1]
        wd_tok = {}
        wd_use = {}
        yst = [None, None, None, None]
        huse = {}

        def load_wd(hd):
            b = hd % 2
            wd_tok[hd] = p.dma(
                ("s", "g")[b], chunk3(wdb[b].ap(), 256, nch=nEC),
                wd_d.ap()[:, hd * 256:(hd + 1) * 256]
                .rearrange("(c p) o -> p c o", p=128), f"lwq{b}",
                waits=up_done + ([wd_use.get(b)] if wd_use.get(b) else []))

        load_wd(0)
        load_wd(1)
        for hp in range(4):
            hd0, hd1 = 2 * hp, 2 * hp + 1
            for tc in range(nQ):
                hbuf = tc % 2
                hsrc = hid_d.ap()[:, tc * 128:(tc + 1) * 128] \
                    .rearrange("(c p) t -> p c t", p=128)
                hv = chunk3(hblk[hbuf].ap(), 128, nch=nEC)
                nh2 = nEC // 2
                wl = up_done + ([huse.get(hbuf)] if huse.get(hbuf) else [])
                t_ha = p.dma("y", hv[:, 0:nh2, :], hsrc[:, 0:nh2, :],
                             f"lha{hbuf}", waits=wl)
                t_hb = p.dma("g", hv[:, nh2:nEC, :], hsrc[:, nh2:nEC, :],
                             f"lhb{hbuf}", waits=wl)
                t_h = (t_ha, t_hb)
                for hj, hd in ((0, hd0), (1, hd1)):
                    bk = mm_ps[2 * hj + tc % 2]
                    w0 = [wd_tok[hd], t_ha, t_hb] + \
                        ([mm_free.get(2 * hj + tc % 2)]
                         if mm_free.get(2 * hj + tc % 2) else [])
                    for c in range(nEC):
                        p.op("t", "matmul", bk[:, 0:256],
                             hblk[hbuf][:, c * 128:(c + 1) * 128],
                             wdb[hd % 2][:, c * 256:(c + 1) * 256],
                             start=(c == 0), stop=(c == nEC - 1),
                             waits=w0 if c == 0 else ())
                    t_pe = p.tok("_t")
                    ob = 2 * hj + tc % 2
                    t_y = p.op("s", "activation",
                               ev[:, ob * 256:ob * 256 + 256],
                               bk[:, 0:256], AF.Copy,
                               waits=[t_pe] + ([yst[ob]] if yst[ob]
                                               else []))
                    mm_free[2 * hj + tc % 2] = t_y
                    yst[ob] = p.dma("y", y_d[tc * 128:(tc + 1) * 128,
                                             hd * 256:(hd + 1) * 256],
                                    ev[:, ob * 256:ob * 256 + 256],
                                    f"sty{ob}", waits=[t_y])
                huse[hbuf] = p.tok("_t")
            wd_use[0] = p.tok("_t")
            wd_use[1] = p.tok("_t")
            if 2 * hp + 2 < 8:
                load_wd(2 * hp + 2)
                load_wd(2 * hp + 3)
    down_done = [p.tok("_t"), p.tok("_v"), p.tok("_s"), p.tok("sty0"),
                 p.tok("sty1"), p.tok("sty2"), p.tok("sty3")]

    x_use = [None, None]
    st_tok = [None, None]
    prev_ln = None
    for tc in range(nQ):
        xb = tc % 2
        t_ly = p.dma("s", y_t[xb][:, :], y_d[tc * 128:(tc + 1) * 128, :],
                     f"ly{xb}", waits=down_done +
                     ([x_use[xb]] if x_use[xb] else []))
        t_lx = p.dma("y", x_t[xb][:, :], x3_d[tc * 128:(tc + 1) * 128, :],
                     f"lre{xb}", waits=down_done +
                     ([x_use[xb]] if x_use[xb] else []) +
                     ([st_tok[xb]] if st_tok[xb] else []))
        t_sq = p.op("s", "activation", bjunk[:, :], y_t[xb][:, :],
                    AF.Square, accum_out=st[:, 8:9],
                    waits=[t_ly] + ([prev_ln] if prev_ln else []))
        t_ms = p.op("v", "tensor_scalar", st[:, 9:10], st[:, 8:9], 1.0 / D,
                    None, ALU.mult, waits=[t_sq])
        t_sd = p.op("s", "activation", st[:, 10:11], st[:, 9:10], AF.Sqrt,
                    bias=1e-6, waits=[t_ms])
        t_r = p.op("v", "reciprocal", st[:, 11:12], st[:, 10:11],
                   waits=[t_sd])
        t_rm = p.op("v", "tensor_scalar", y_t[xb][:, :], y_t[xb][:, :],
                    st[:, 11:12], None, ALU.mult, waits=[t_r])
        t_n = emit_ln(p, y_t[xb][:, :], ln_o[:, :], bjunk[:, :], st,
                      waits=[t_rm])
        prev_ln = t_n
        t_add = p.op("v", "tensor_tensor", x_t[xb][:, :], x_t[xb][:, :],
                     ln_o[:, :], ALU.add, waits=[t_n, t_lx])
        x_use[xb] = t_add
        st_tok[xb] = p.dma("y", x4_d[tc * 128:(tc + 1) * 128, :],
                           x_t[xb][:, :], f"stx{xb}", waits=[t_add])
    p.op("y", "nop", waits=[p.tok("stx0"), p.tok("stx1")])
    p.finalize()
    return nc





S_FULL = 4096
SQ_OWN = 1024
CORES = list(range(NCORES))


def _run(nc, in_maps):
    return run_bass_kernel_spmd(nc, in_maps, core_ids=CORES)


def kernel(L0_attn_x, L0_attn_Wq, L0_attn_bq, L0_attn_Wk, L0_attn_bk,
           L0_attn_Wv, L0_attn_bv, L0_attn_Wo, L0_attn_bo, L0_attn_Wi,
           L0_attn_bi, L0_ffn_W_up, L0_ffn_W_gate, L0_ffn_W_down,
           L1_attn_slope_rate, L1_attn_W_qkvu, L1_attn_W_out,
           L1_ffn_gate_up_packed, L1_ffn_gate_up_scales,
           L1_ffn_down_packed, L1_ffn_down_scales):
    f32 = np.float32
    x = np.ascontiguousarray(np.asarray(L0_attn_x, dtype=f32))
    B = x.shape[0]
    row = lambda a: np.ascontiguousarray(np.asarray(a, f32).reshape(1, -1))

    W_up = np.asarray(L0_ffn_W_up, f32)
    W_gate = np.asarray(L0_ffn_W_gate, f32)
    W_down = np.asarray(L0_ffn_W_down, f32)
    gup = np.asarray(L1_ffn_gate_up_packed)
    gus = np.asarray(L1_ffn_gate_up_scales)
    dwn = np.asarray(L1_ffn_down_packed)
    dws = np.asarray(L1_ffn_down_scales)
    gupT = np.ascontiguousarray(gup.T)
    gusE = np.repeat(gus, 16, axis=0)
    dwnT = np.ascontiguousarray(dwn.T)
    dwsE = np.repeat(dws, 16, axis=0)

    nc = build_l0pre(SQ_OWN)
    common = {
        "Wq": np.asarray(L0_attn_Wq, f32), "Wk": np.asarray(L0_attn_Wk, f32),
        "Wv": np.asarray(L0_attn_Wv, f32),
        "bq": row(L0_attn_bq), "bk": row(L0_attn_bk), "bv": row(L0_attn_bv),
        "Wi": np.ascontiguousarray(np.asarray(L0_attn_Wi, f32).T),
        "bi": np.asarray(L0_attn_bi, f32).reshape(1, 1),
    }
    in_maps = []
    for c in CORES:
        b, qq = c // 4, c % 4
        in_maps.append(dict(
            common,
            x_own=np.ascontiguousarray(x[b, qq * SQ_OWN:(qq + 1) * SQ_OWN]),
            up_sl=np.ascontiguousarray(W_up[c * 1024:(c + 1) * 1024]),
            gate_sl=np.ascontiguousarray(W_gate[c * 1024:(c + 1) * 1024]),
            down_sl=np.ascontiguousarray(W_down[:, c * 1024:(c + 1) * 1024].T),
            gup_T=np.ascontiguousarray(gupT[:, c * 1408:(c + 1) * 1408]),
            gus=np.ascontiguousarray(gusE[:, c * 1408:(c + 1) * 1408]),
            dwn_T=np.ascontiguousarray(dwnT[:, c * 256:(c + 1) * 256]),
            dws=np.ascontiguousarray(dwsE[:, c * 256:(c + 1) * 256])))
    res = _run(nc, in_maps)
    q_o = [np.asarray(res.results[c]["q_o"]) for c in CORES]
    k_o = [np.asarray(res.results[c]["k_o"]) for c in CORES]
    v_o = [np.asarray(res.results[c]["v_o"]) for c in CORES]
    imp_o = [np.asarray(res.results[c]["imp_o"]) for c in CORES]
    parts = np.concatenate([np.asarray(res.results[c]["parts"])
                            for c in CORES], axis=0)
    wgu = np.concatenate([np.asarray(res.results[c]["wgu_sl"])
                          for c in CORES], axis=1)
    wd = np.concatenate([np.asarray(res.results[c]["wd_sl"])
                         for c in CORES], axis=1)

    W_downT = np.ascontiguousarray(W_down.T)
    nc = build_l0main(S_FULL, SQ_OWN)
    in_maps = []
    for c in CORES:
        b, qq = c // 4, c % 4
        grp = list(range(b * 4, b * 4 + 4))
        in_maps.append({
            "x_own": np.ascontiguousarray(
                x[b, qq * SQ_OWN:(qq + 1) * SQ_OWN]),
            "q": q_o[c],
            "kf": np.ascontiguousarray(
                np.concatenate([k_o[g] for g in grp], axis=1)),
            "vf": np.ascontiguousarray(
                np.concatenate([v_o[g] for g in grp], axis=0)),
            "impf": np.ascontiguousarray(
                np.concatenate([imp_o[g] for g in grp], axis=1)),
            "down_slq": np.ascontiguousarray(
                W_downT[c * 1024:(c + 1) * 1024]),
            "parts": np.ascontiguousarray(parts),
            "Wo": np.asarray(L0_attn_Wo, f32), "bo": row(L0_attn_bo)})
    res = _run(nc, in_maps)
    x1 = [np.asarray(res.results[c]["x1"]) for c in CORES]
    wdq = np.concatenate([np.asarray(res.results[c]["wdq_sl"])
                          for c in CORES], axis=0)

    nc = build_ffn0(SQ_OWN)
    common = {
        "upT": np.ascontiguousarray(W_up.T),
        "gateT": np.ascontiguousarray(W_gate.T),
        "wdq": wdq,
        "parts": np.ascontiguousarray(parts),
    }
    res = _run(nc, [dict(common, x1=x1[c]) for c in CORES])
    x2 = [np.asarray(res.results[c]["x2"]) for c in CORES]

    slope = np.ascontiguousarray(
        np.asarray(L1_attn_slope_rate, f32).reshape(1, NH))
    nc = build_l1a_local(SQ_OWN)
    common = {
        "WqkvuT": np.ascontiguousarray(np.asarray(L1_attn_W_qkvu, f32).T),
        "slope": slope,
    }
    res = _run(nc, [dict(common, x2=x2[c]) for c in CORES])
    sc = [np.asarray(res.results[c]["sc"]) for c in CORES]
    o_part = [np.asarray(res.results[c]["o_part"]) for c in CORES]
    qd = [np.asarray(res.results[c]["qd"]) for c in CORES]
    u_o = [np.asarray(res.results[c]["u_o"]) for c in CORES]

    z = np.zeros((NH * 128, 128), f32)
    nc = build_l1a_fix(SQ_OWN)
    in_maps = []
    for c in CORES:
        b, qq = c // 4, c % 4
        pred = lambda j: sc[b * 4 + qq - j] if qq - j >= 0 else z
        in_maps.append({
            "x2": x2[c], "o_part": o_part[c], "qd": qd[c], "u_o": u_o[c],
            "scp1": pred(1), "scp2": pred(2), "scp3": pred(3),
            "slope": slope,
            "WoutT": np.ascontiguousarray(np.asarray(L1_attn_W_out, f32).T),
        })
    res = _run(nc, in_maps)
    x3 = [np.asarray(res.results[c]["x3"]) for c in CORES]

    nc = build_ffn1(SQ_OWN)
    common = {"wgu": np.ascontiguousarray(wgu),
              "wd": np.ascontiguousarray(wd)}
    res = _run(nc, [dict(common, x3=x3[c]) for c in CORES])
    x4 = [np.asarray(res.results[c]["x4"]) for c in CORES]

    out = np.empty_like(x)
    for c in CORES:
        b, qq = c // 4, c % 4
        out[b, qq * SQ_OWN:(qq + 1) * SQ_OWN] = x4[c]
    return out






def build_l0pre(SQ=1024):
    nQ = SQ // 128
    nTB = SQ // 512
    nc = bass.Bass("TRN2", target_bir_lowering=False)
    p = Prog(nc)

    x_d = nc.dram_tensor("x_own", [SQ, D], F32, kind="ExternalInput")
    wq_d = nc.dram_tensor("Wq", [D, D], F32, kind="ExternalInput")
    wk_d = nc.dram_tensor("Wk", [D, D], F32, kind="ExternalInput")
    wv_d = nc.dram_tensor("Wv", [D, D], F32, kind="ExternalInput")
    bq_d = nc.dram_tensor("bq", [1, D], F32, kind="ExternalInput")
    bk_d = nc.dram_tensor("bk", [1, D], F32, kind="ExternalInput")
    bv_d = nc.dram_tensor("bv", [1, D], F32, kind="ExternalInput")
    wi_d = nc.dram_tensor("Wi", [1, D], F32, kind="ExternalInput")
    bi_d = nc.dram_tensor("bi", [1, 1], F32, kind="ExternalInput")
    q_od = nc.dram_tensor("q_o", [D, SQ], BF16, kind="ExternalOutput")
    k_od = nc.dram_tensor("k_o", [D, SQ], BF16, kind="ExternalOutput")
    v_od = nc.dram_tensor("v_o", [SQ, D], BF16, kind="ExternalOutput")
    imp_od = nc.dram_tensor("imp_o", [128, nQ], F32, kind="ExternalOutput")
    psl = [nc.dram_tensor(n, [1024, D], F32, kind="ExternalInput")
           for n in ("up_sl", "gate_sl", "down_sl")]
    parts_od = nc.dram_tensor("parts", [1, 4], F32, kind="ExternalOutput")
    W1, W2 = 1408, 256
    gup_d = nc.dram_tensor("gup_T", [256, W1], I32, kind="ExternalInput")
    gus_d = nc.dram_tensor("gus", [256, W1], mybir.dt.float16,
                           kind="ExternalInput")
    dwn_d = nc.dram_tensor("dwn_T", [704, W2], I32, kind="ExternalInput")
    dws_d = nc.dram_tensor("dws", [704, W2], mybir.dt.float16,
                           kind="ExternalInput")
    wgu_od = nc.dram_tensor("wgu_sl", [2048, W1], BF16, kind="ExternalOutput")
    wd_od = nc.dram_tensor("wd_sl", [5632, W2], BF16, kind="ExternalOutput")

    x_t = [nc.alloc_sbuf_tensor(f"x_t{i}", [128, D], F32) for i in range(2)]
    st = nc.alloc_sbuf_tensor("st", [128, 24], F32)
    xn_fm = nc.alloc_sbuf_tensor("xn_fm", [128, 16 * SQ], BF16)
    imp_sb = nc.alloc_sbuf_tensor("imp_sb", [128, nQ], F32)
    bias_sb = nc.alloc_sbuf_tensor("bias_sb", [1, D], BF16)
    wirow = nc.alloc_sbuf_tensor("wirow", [1, D], F32)
    bif = nc.alloc_sbuf_tensor("bif", [1, 1], F32)
    swi_c = nc.alloc_sbuf_tensor("swi_c", [128, 1], F32)
    bi_c = nc.alloc_sbuf_tensor("bi_c", [128, 1], F32)
    kv_out = [nc.alloc_sbuf_tensor(f"kv_out{i}", [128, 512], BF16)
              for i in range(2)]
    o_sb = nc.alloc_sbuf_tensor("po_sb", [1, 4], F32)
    red = nc.alloc_sbuf_tensor("pred", [128, 4], F32)
    pw0 = nc.alloc_sbuf_tensor("pw0", [128, D], F32)
    pw1 = nc.alloc_sbuf_tensor("pw1", [128, D], F32)
    pacc = nc.alloc_sbuf_tensor("pacc", [128, 26], F32)
    W1d, W2d = 1408, 256
    W = max(W1d, W2d)
    pk = nc.alloc_sbuf_tensor("dq_pk", [128, W], I32)
    scf = nc.alloc_sbuf_tensor("dq_scf", [128, W], F32)
    sch = nc.alloc_sbuf_tensor("dq_sch", [128, W], mybir.dt.float16)
    nib = nc.alloc_sbuf_tensor("dq_nib", [128, W], I32)
    ti_ = nc.alloc_sbuf_tensor("dq_ti", [128, W], I32)
    si_ = nc.alloc_sbuf_tensor("dq_si", [128, W], I32)
    tf = nc.alloc_sbuf_tensor("dq_tf", [128, W], F32)
    sf = nc.alloc_sbuf_tensor("dq_sf", [128, W], F32)
    g5 = nc.alloc_sbuf_tensor("dq_g5", [128, W], F32)
    g7 = nc.alloc_sbuf_tensor("dq_g7", [128, W], F32)
    dacc = nc.alloc_sbuf_tensor("dq_acc", [128, W], F32)
    pl = [nc.alloc_sbuf_tensor(f"dq_pl{i}", [128, W], BF16) for i in range(2)]

    tp_ps = [nc.alloc_psum_tensor(f"tp{i}", [128, 128], BF16) for i in range(2)]
    mm_ps = [nc.alloc_psum_tensor(f"mm{i}", [128, 512], F32) for i in range(4)]
    db_ps = nc.alloc_psum_tensor("db_ps", [128, 512], F32)

    C = Consts(p, fvals=(LN_EPS,))
    mm_free = {}

    t_wi = p.dma("y", wirow[0:1, :], wi_d[0:1, :], "ldwi")
    t_bi = p.dma("y", bif[0:1, :], bi_d[0:1, :], "ldbi")

    with nc.sbuf_tensor("bj0", [128, D], BF16) as bj0, \
            nc.sbuf_tensor("bj1", [128, D], BF16) as bj1, \
            nc.sbuf_tensor("jk0", [128, D], BF16) as jk0, \
            nc.sbuf_tensor("jk1", [128, D], BF16) as jk1, \
            nc.sbuf_tensor("jk2", [128, D], BF16) as jk2, \
            nc.sbuf_tensor("jk3", [128, D], BF16) as jk3, \
            nc.sbuf_tensor("ij0", [128, D], F32) as ij0, \
            nc.sbuf_tensor("ij1", [128, D], F32) as ij1, \
            nc.sbuf_tensor("wi_bc", [128, D], F32) as wi_bc:
        bj = [bj0, bj1]
        jka = [jk0, jk1]
        jkb = [jk2, jk3]
        ij = [ij0, ij1]
        wb_toks = []
        for ob in range(4):
            t_wb = p.op("t", "matmul", db_ps[:, :], C.ones_rf[0:1, 0:128],
                        wirow[0:1, ob * 512:(ob + 1) * 512], start=True,
                        stop=True, waits=[t_wi] + (wb_toks[-1:] or []))
            wb_toks.append(p.op("v", "tensor_copy",
                                wi_bc[:, ob * 512:(ob + 1) * 512],
                                db_ps[:, :], waits=[t_wb]))
        t_bb = p.op("t", "matmul", db_ps[:, 0:1], C.ones_rf[0:1, 0:128],
                    bif[0:1, 0:1], start=True, stop=True,
                    waits=[t_bi] + wb_toks[-1:])
        t_bic = p.op("v", "tensor_copy", bi_c[:, :], db_ps[:, 0:1],
                     waits=[t_bb])
        t_swi = p.op("v", "tensor_reduce", swi_c[:, :], wi_bc[:, :],
                     mybir.AxisListType.X, ALU.add, waits=wb_toks)
        imp_setup = [t_bic, t_swi]

        x_use = [None, None]
        bj_use = [None, None]
        ij_use = [None, None]
        ln_tok = [None, None]
        for ti in range(nQ):
            q = ti % 2
            t_ld = p.dma("y", x_t[q][:, :], x_d[ti * 128:(ti + 1) * 128, :],
                         f"ldx{q}", waits=list(x_use[q] or ()))
            t_n = emit_ln(p, x_t[q][:, :], bj[q][:, :], jka[q][:, :], st,
                          waits=[t_ld] + ([bj_use[q]] if bj_use[q] else []) +
                          ([ln_tok[q]] if ln_tok[q] else []),
                          sc=8 * q, junk2_ap=jkb[q][:, :])
            ln_tok[q] = t_n
            t_mu = p.op("g", "tensor_tensor", ij[q][:, :], x_t[q][:, :],
                        wi_bc[:, :], ALU.mult,
                        waits=[t_ld] + imp_setup +
                        ([ij_use[q]] if ij_use[q] else []))
            t_us = p.op("v", "tensor_reduce", st[:, 16 + 4 * q:17 + 4 * q],
                        ij[q][:, :], mybir.AxisListType.X, ALU.add,
                        waits=[t_mu])
            ij_use[q] = t_us
            x_use[q] = (t_n, t_mu, p.tok("_s"))
            t_i1 = p.op("v", "tensor_tensor", st[:, 17 + 4 * q:18 + 4 * q],
                        st[:, 8 * q + 2:8 * q + 3], swi_c[:, :], ALU.mult,
                        waits=[t_us, t_n])
            t_i2 = p.op("v", "scalar_tensor_tensor",
                        st[:, 18 + 4 * q:19 + 4 * q],
                        st[:, 17 + 4 * q:18 + 4 * q], -1.0,
                        st[:, 16 + 4 * q:17 + 4 * q], ALU.mult, ALU.add,
                        waits=[t_i1])
            t_i3 = p.op("v", "tensor_tensor", st[:, 19 + 4 * q:20 + 4 * q],
                        st[:, 18 + 4 * q:19 + 4 * q],
                        st[:, 8 * q + 6:8 * q + 7], ALU.mult, waits=[t_i2])
            t_i4 = p.op("v", "tensor_tensor", imp_sb[:, ti:ti + 1],
                        st[:, 19 + 4 * q:20 + 4 * q], bi_c[:, :],
                        ALU.add, waits=[t_i3])
            cp_toks = []
            for c in range(16):
                bk = tp_ps[c % 2]
                w = [t_n]
                if len(cp_toks) >= 2:
                    w.append(cp_toks[-2])
                t_tr = p.op("t", "transpose", bk[:, :],
                            bj[q][:, c * 128:(c + 1) * 128], C.ident[:, :],
                            waits=w)
                cp_toks.append(p.op("v", "tensor_copy",
                                    xn_fm[:, c * SQ + ti * 128:
                                          c * SQ + (ti + 1) * 128],
                                    bk[:, :], waits=[t_tr]))
            bj_use[q] = p.tok("_t")
        t_imp_st = p.dma("y", imp_od[:, :], imp_sb[:, :], "stimp",
                         waits=[t_i4])
    ln_done = [p.tok("_v"), p.tok("_t")]

    kv_tok = [None, None]
    with nc.sbuf_tensor("wc0", [128, 16 * 128], BF16) as wc0, \
            nc.sbuf_tensor("wc1", [128, 16 * 128], BF16) as wc1, \
            nc.sbuf_tensor("wb0", [128, 16 * 512], BF16) as wb0:
        wc = [wc0, wc1]
        wuse = [None, None]
        for kind, w_dram, b_dram, dst in (("q", wq_d, bq_d, q_od),
                                          ("k", wk_d, bk_d, k_od)):
            t_b = p.dma("g", bias_sb[0:1, :], b_dram[0:1, :], "ldbias",
                        waits=[p.tok("_t")])
            for o in range(16):
                b = o % 2
                t_w = p.dma("g", chunk3(wc[b].ap(), 128),
                            wcol_ap(w_dram, o * 128, 128), f"lw{b}",
                            waits=[wuse[b]] if wuse[b] else [])
                for tb in range(nTB):
                    bk = mm_ps[tb % 2]
                    w0 = [t_w, t_b] + ln_done + \
                        ([mm_free.get(tb % 2)] if mm_free.get(tb % 2)
                         else [])
                    for c in range(16):
                        p.op("t", "matmul", bk[:, :],
                             wc[b][:, c * 128:(c + 1) * 128],
                             xn_fm[:, c * SQ + tb * 512:
                                   c * SQ + (tb + 1) * 512],
                             start=(c == 0), stop=False,
                             waits=w0 if c == 0 else ())
                    t_mm = p.op("t", "matmul", bk[:, :],
                                bias_sb[0:1, o * 128:(o + 1) * 128],
                                C.ones_r[0:1, 0:512], start=False,
                                stop=True)
                    ob = (o * nTB + tb) % 2
                    t_e = p.op("v", "tensor_copy", kv_out[ob][:, :],
                               bk[:, :],
                               waits=[t_mm] + ([kv_tok[ob]] if kv_tok[ob]
                                               else []))
                    mm_free[tb % 2] = t_e
                    kv_tok[ob] = p.dma("y",
                                       dst[o * 128:(o + 1) * 128,
                                           tb * 512:(tb + 1) * 512],
                                       kv_out[ob][:, :], f"stkv{ob}",
                                       waits=[t_e])
                wuse[b] = p.tok("_t")
        t_b = p.dma("g", bias_sb[0:1, :], bv_d[0:1, :], "ldbias",
                    waits=[p.tok("_t")])
        for ob4 in range(4):
            t_w = p.dma("g", chunk3(wb0.ap(), 512),
                        wcol_ap(wv_d, ob4 * 512, 512), "lwb",
                        waits=[p.tok("_t")])
            for tc in range(nQ):
                bk = mm_ps[2 + tc % 2]
                w0 = [t_w, t_b] + ln_done + \
                    ([mm_free.get(2 + tc % 2)] if mm_free.get(2 + tc % 2)
                     else [])
                for c in range(16):
                    p.op("t", "matmul", bk[:, :],
                         xn_fm[:, c * SQ + tc * 128:c * SQ + (tc + 1) * 128],
                         wb0[:, c * 512:(c + 1) * 512],
                         start=(c == 0), stop=False,
                         waits=w0 if c == 0 else ())
                t_mm = p.op("t", "matmul", bk[:, :], C.ones_r[0:1, 0:128],
                            bias_sb[0:1, ob4 * 512:(ob4 + 1) * 512],
                            start=False, stop=True)
                ob = (ob4 * nQ + tc) % 2
                t_e = p.op("v", "tensor_copy", kv_out[ob][:, :], bk[:, :],
                           waits=[t_mm] + ([kv_tok[ob]] if kv_tok[ob]
                                           else []))
                mm_free[2 + tc % 2] = t_e
                kv_tok[ob] = p.dma("y",
                                   v_od[tc * 128:(tc + 1) * 128,
                                        ob4 * 512:(ob4 + 1) * 512],
                                   kv_out[ob][:, :], f"stkv{ob}",
                                   waits=[t_e])

    if True:
        acc = pacc
        w_t = [pw0, pw1]
        use_tok = [None, None]
        col = 0
        toks = []
        for mi, w_d in enumerate(psl):
            for ti in range(8):
                b = ti % 2
                t_ld = p.dma("s", w_t[b][:, :],
                             w_d[ti * 128:(ti + 1) * 128, :], f"pld{b}",
                             waits=[use_tok[b]] if use_tok[b] else [])
                t_a = p.op("v", "tensor_reduce", acc[:, col:col + 1],
                           w_t[b][:, :], mybir.AxisListType.X, ALU.add,
                           apply_absolute_value=True,
                           waits=[t_ld] + (toks[-1:] or []))
                use_tok[b] = t_a
                toks.append(t_a)
                col += 1
        rt = []
        for m in range(3):
            rt.append(p.op("v", "tensor_reduce", red[:, m:m + 1],
                           acc[:, m * 8:(m + 1) * 8], mybir.AxisListType.X,
                           ALU.add, waits=toks))
        t_z = p.op("v", "memset", red[:, 3:4], 0.0)
        t_s = p.op("t", "matmul", db_ps[0:1, 0:4], C.ones_cf[:, :],
                   red[:, :], start=True, stop=True,
                   waits=rt + [t_z, p.tok("_v")])
        t_c = p.op("v", "tensor_copy", o_sb[0:1, :], db_ps[0:1, 0:4],
                   waits=[t_s])
        p.dma("y", parts_od[0:1, :], o_sb[0:1, :], "stpo", waits=[t_c])


    def dequant(src_d, ssrc_d, dst_d, rows, width, init_waits=()):
        nt = (rows + 127) // 128
        st_tok = [None, None]
        prev = list(init_waits)
        for t in range(nt):
            r0 = t * 128
            pr = min(128, rows - r0)
            t_l = p.dma("s", pk[0:pr, 0:width], src_d[r0:r0 + pr, :],
                        "dqldp", waits=prev)
            t_s = p.dma("s", sch[0:pr, 0:width], ssrc_d[r0:r0 + pr, :],
                        "dqlds", waits=prev)
            t_sc = p.op("v", "tensor_copy", scf[0:pr, 0:width],
                        sch[0:pr, 0:width], waits=[t_s])
            prev = []
            for k in range(8):
                t1 = p.op("v", "tensor_scalar", nib[0:pr, 0:width],
                          pk[0:pr, 0:width], 4 * k, None,
                          ALU.logical_shift_right, waits=[t_l, t_sc])
                t2 = p.op("v", "tensor_scalar", ti_[0:pr, 0:width],
                          nib[0:pr, 0:width], 7, None, ALU.bitwise_and,
                          waits=[t1])
                t3 = p.op("v", "tensor_scalar", si_[0:pr, 0:width],
                          nib[0:pr, 0:width], 3, None,
                          ALU.logical_shift_right, waits=[t2])
                t3b = p.op("v", "tensor_scalar", si_[0:pr, 0:width],
                           si_[0:pr, 0:width], 1, None, ALU.bitwise_and,
                           waits=[t3])
                t4 = p.op("v", "tensor_copy", tf[0:pr, 0:width],
                          ti_[0:pr, 0:width], waits=[t2])
                t5 = p.op("v", "tensor_copy", sf[0:pr, 0:width],
                          si_[0:pr, 0:width], waits=[t3b])
                t6 = p.op("v", "tensor_scalar", g5[0:pr, 0:width],
                          tf[0:pr, 0:width], 4.5, None, ALU.is_gt,
                          waits=[t4])
                t7 = p.op("v", "tensor_tensor", dacc[0:pr, 0:width],
                          tf[0:pr, 0:width], g5[0:pr, 0:width], ALU.add,
                          waits=[t6])
                t8 = p.op("v", "tensor_scalar", g5[0:pr, 0:width],
                          tf[0:pr, 0:width], 5.5, None, ALU.is_gt,
                          waits=[t7])
                t9 = p.op("v", "tensor_tensor", dacc[0:pr, 0:width],
                          dacc[0:pr, 0:width], g5[0:pr, 0:width], ALU.add,
                          waits=[t8])
                t10 = p.op("v", "tensor_scalar", g7[0:pr, 0:width],
                           tf[0:pr, 0:width], 6.5, None, ALU.is_gt,
                           waits=[t9])
                t11 = p.op("v", "scalar_tensor_tensor", dacc[0:pr, 0:width],
                           g7[0:pr, 0:width], 3.0, dacc[0:pr, 0:width],
                           ALU.mult, ALU.add, waits=[t10])
                t12 = p.op("v", "tensor_scalar", sf[0:pr, 0:width],
                           sf[0:pr, 0:width], -2.0, 1.0, ALU.mult,
                           ALU.add, waits=[t11, t5])
                t13 = p.op("v", "tensor_tensor", dacc[0:pr, 0:width],
                           dacc[0:pr, 0:width], sf[0:pr, 0:width],
                           ALU.mult, waits=[t12])
                t14 = p.op("v", "tensor_scalar", dacc[0:pr, 0:width],
                           dacc[0:pr, 0:width], 0.5, None, ALU.mult,
                           waits=[t13])
                ob = k % 2
                t15 = p.op("v", "tensor_tensor", pl[ob][0:pr, 0:width],
                           dacc[0:pr, 0:width], scf[0:pr, 0:width],
                           ALU.mult,
                           waits=[t14] + ([st_tok[ob]] if st_tok[ob]
                                          else []))
                dst = dst_d.ap().rearrange("(ip k) o -> k ip o", k=8)
                st_tok[ob] = p.dma("s", dst[k, r0:r0 + pr, :],
                                   pl[ob][0:pr, 0:width], f"dqst{ob}",
                                   waits=[t15])
                prev = [t15]
        return [p.tok("dqst0"), p.tok("dqst1")]

    dequant(gup_d, gus_d, wgu_od, 256, W1)
    dq_toks = dequant(dwn_d, dws_d, wd_od, 704, W2,
                      init_waits=[p.tok("_v"), p.tok("dqst0"),
                                  p.tok("dqst1")])
    p.op("y", "nop", waits=[p.tok("stkv0"), p.tok("stkv1"), p.tok("stimp"),
                            p.tok("stpo")] + dq_toks)
    p.finalize()
    return nc






def build_l0main(S=4096, SQ=1024):
    nT = S // 128
    nTB = S // 512
    nQ = SQ // 128
    TIDX = min(max(1, int(S * 0.2)), S - 1)
    nqh = max(1, SQ // 512)
    QW = min(512, SQ)

    nc = bass.Bass("TRN2", target_bir_lowering=False)
    p = Prog(nc)
    es = contextlib.ExitStack()

    x_d = nc.dram_tensor("x_own", [SQ, D], F32, kind="ExternalInput")
    q_d = nc.dram_tensor("q", [D, SQ], BF16, kind="ExternalInput")
    kf_d = nc.dram_tensor("kf", [D, S], BF16, kind="ExternalInput")
    vf_d = nc.dram_tensor("vf", [S, D], BF16, kind="ExternalInput")
    impf_d = nc.dram_tensor("impf", [128, nT], F32, kind="ExternalInput")
    wo_d = nc.dram_tensor("Wo", [D, D], F32, kind="ExternalInput")
    bo_d = nc.dram_tensor("bo", [1, D], F32, kind="ExternalInput")
    dsl_d = nc.dram_tensor("down_slq", [1024, D], F32, kind="ExternalInput")
    parts_d = nc.dram_tensor("parts", [8, 4], F32, kind="ExternalInput")
    x1_d = nc.dram_tensor("x1", [SQ, D], F32, kind="ExternalOutput")
    wdq_od = nc.dram_tensor("wdq_sl", [1024, D], BF16, kind="ExternalOutput")
    mr_d = nc.dram_tensor("mr_scr", [nT, 128], BF16)

    at_fm = nc.alloc_sbuf_tensor("at_fm", [128, 16 * SQ], BF16)
    ev = nc.alloc_sbuf_tensor("ev", [128, D], F32)
    ln_o = nc.alloc_sbuf_tensor("ln_o", [128, D], F32)
    x_t = [nc.alloc_sbuf_tensor(f"x_t{i}", [128, D], F32) for i in range(2)]
    bjunk = nc.alloc_sbuf_tensor("bjunk", [128, D], BF16)
    st = nc.alloc_sbuf_tensor("st", [128, 24], F32)
    imp_sb = nc.alloc_sbuf_tensor("imp_sb", [128, nT], F32)
    bis = nc.alloc_sbuf_tensor("bis", [128, 8], F32)
    bmask = nc.alloc_sbuf_tensor("bmask", [128, 2], I32)
    cnt_c = nc.alloc_sbuf_tensor("cnt_c", [128, 1], F32)
    c11 = nc.alloc_sbuf_tensor("c11", [1, 1], F32)
    mrow = nc.alloc_sbuf_tensor("mrow", [1, S], BF16)
    mbc = nc.alloc_sbuf_tensor("mbc", [128, S], BF16)
    mcol = nc.alloc_sbuf_tensor("mcol", [128, nT], F32)
    mcolb = nc.alloc_sbuf_tensor("mcolb", [128, nT], BF16)
    mstage = nc.alloc_sbuf_tensor("mstage", [128, 128], BF16)
    probs = [nc.alloc_sbuf_tensor(f"probs{i}", [128, 512], BF16)
             for i in range(2)]
    vmask = nc.alloc_sbuf_tensor("vmask", [128, S], BF16)
    onesb = nc.alloc_sbuf_tensor("onesb", [128, 128], BF16)
    dnr = nc.alloc_sbuf_tensor("dnr", [1, 512], F32)
    dnb = nc.alloc_sbuf_tensor("dnb", [128, 512], F32)
    bias_sb = nc.alloc_sbuf_tensor("bias_sb", [1, D], BF16)
    parts_sb = nc.alloc_sbuf_tensor("parts_sb", [8, 4], F32)
    srowq = nc.alloc_sbuf_tensor("srowq", [1, 3], F32)
    sdc = nc.alloc_sbuf_tensor("sdc", [128, 2], F32)
    wfd = [nc.alloc_sbuf_tensor(f"wfd{i}", [128, D], F32) for i in range(2)]
    q2d = [nc.alloc_sbuf_tensor(f"q2d{i}", [128, D], BF16) for i in range(2)]
    sad = nc.alloc_sbuf_tensor("sad", [128, D], BF16)

    tp_ps = [nc.alloc_psum_tensor(f"tp{i}", [128, 128], BF16) for i in range(2)]
    mm_ps = [nc.alloc_psum_tensor(f"mm{i}", [128, 512], F32) for i in range(4)]
    dn_ps = nc.alloc_psum_tensor("dn_ps", [1, 512], F32)
    db_ps = nc.alloc_psum_tensor("db_ps", [128, 512], F32)

    C = Consts(p, fvals=(LN_EPS, ATT_SCALE))
    mm_free = {}

    q_fm = es.enter_context(nc.sbuf_tensor("q_fm", [128, 16 * SQ], BF16))
    t_q = p.dma("y", chunk3(q_fm.ap(), SQ),
                q_d.ap().rearrange("(c p) t -> p c t", p=128), "ldq")
    t_if = p.dma("y", imp_sb[:, :], impf_d[:, :], "ldimp")

    DE = 8192
    t_pl = p.dma("s", parts_sb[:, :], parts_d[:, :], "ldp")
    t_tot = p.op("t", "matmul", dn_ps[0:1, 0:4], C.ones_cf[0:8, :],
                 parts_sb[0:8, :], start=True, stop=True, waits=[t_pl])
    t_sd = p.op("v", "tensor_scalar", srowq[0:1, 0:1], dn_ps[0:1, 2:3],
                1.0 / (DE * D), 1e-8, ALU.mult, ALU.add, waits=[t_tot])
    t_h = p.op("v", "tensor_scalar", srowq[0:1, 1:2], srowq[0:1, 0:1], 0.5,
               None, ALU.mult, waits=[t_sd])
    t_nh = p.op("v", "tensor_scalar", srowq[0:1, 2:3], srowq[0:1, 0:1],
                -0.5, None, ALU.mult, waits=[t_h])
    t_bb = p.op("t", "matmul", db_ps[:, 0:2], C.ones_rf[0:1, 0:128],
                srowq[0:1, 1:3], start=True, stop=True, waits=[t_nh])
    t_sdc = p.op("v", "tensor_copy", sdc[:, 0:2], db_ps[:, 0:2],
                 waits=[t_bb])
    mm_free["dn"] = t_sd
    mm_free["db"] = t_sdc

    t0 = p.op("v", "memset", bis[:, 0:1], -1000.0, waits=[t_if])
    t1 = p.op("v", "memset", bis[:, 1:2], 1000.0)
    prev = [t0, t1]
    for _ in range(50):
        t_a = p.op("v", "tensor_scalar", bis[:, 3:4], bis[:, 1:2], 0.5, None,
                   ALU.mult, waits=prev)
        t_mid = p.op("v", "scalar_tensor_tensor", bis[:, 2:3], bis[:, 0:1],
                     0.5, bis[:, 3:4], ALU.mult, ALU.add, waits=[t_a])
        t_c = p.op("v", "tensor_scalar", mcol[:, 0:nT], imp_sb[:, 0:nT],
                   bis[:, 2:3], None, ALU.is_gt, ALU.add,
                   accum_out=cnt_c[:, 0:1], waits=[t_mid])
        t_s = p.op("t", "matmul", dn_ps[0:1, 0:1], C.ones_cf[:, :],
                   cnt_c[:, 0:1], start=True, stop=True,
                   waits=[t_c] + ([mm_free["dn"]] if "dn" in mm_free
                                  else []))
        t_cp = p.op("v", "tensor_copy", c11[0:1, 0:1], dn_ps[0:1, 0:1],
                    waits=[t_s])
        mm_free["dn"] = t_cp
        t_bc = p.op("t", "matmul", db_ps[:, 0:1], C.ones_rf[0:1, 0:128],
                    c11[0:1, 0:1], start=True, stop=True,
                    waits=[t_cp] + ([mm_free["db"]] if "db" in mm_free
                                    else []))
        t_cd = p.op("v", "tensor_scalar", bmask[:, 0:1], db_ps[:, 0:1],
                    TIDX + 0.5, None, ALU.is_lt, waits=[t_bc])
        t_ic = p.op("v", "tensor_scalar", bmask[:, 1:2], db_ps[:, 0:1],
                    TIDX + 0.5, None, ALU.is_ge, waits=[t_cd])
        mm_free["db"] = t_ic
        t_h = p.op("v", "copy_predicated", bis[:, 1:2], bmask[:, 0:1],
                   bis[:, 2:3], waits=[t_ic])
        t_l = p.op("v", "copy_predicated", bis[:, 0:1], bmask[:, 1:2],
                   bis[:, 2:3], waits=[t_h])
        prev = [t_l]
    t_mc = p.op("v", "tensor_scalar", mcol[:, 0:nT], imp_sb[:, 0:nT],
                bis[:, 1:2], None, ALU.is_gt, waits=prev)
    t_mcb = p.op("v", "tensor_copy", mcolb[:, 0:nT], mcol[:, 0:nT],
                 waits=[t_mc])
    t_tr = p.op("t", "transpose", tp_ps[0][0:nT, :], mcolb[:, 0:nT],
                C.ident[:, :], waits=[t_mcb])
    t_ms = p.op("v", "tensor_copy", mstage[0:nT, :], tp_ps[0][0:nT, :],
                waits=[t_tr])
    t_st = p.dma("s", mr_d[:, :], mstage[0:nT, :], "mrb", waits=[t_ms])
    t_lr = p.dma("s", mrow[0:1, 0:S],
                 mr_d.ap().rearrange("(o a) b -> o (a b)", o=1), "mrb2",
                 waits=[t_st])
    for tb in range(nTB):
        t_b = p.op("t", "matmul", db_ps[:, :], C.ones_r[0:1, 0:128],
                   mrow[0:1, tb * 512:(tb + 1) * 512], start=True, stop=True,
                   waits=[t_lr, mm_free["db"]])
        mm_free["db"] = p.op("v", "tensor_copy",
                             mbc[:, tb * 512:(tb + 1) * 512], db_ps[:, :],
                             waits=[t_b])
    t_ob = p.op("v", "memset", onesb[:, :], 1.0)
    vm_toks = []
    for kc in range(nT):
        vm_toks.append(p.op("v", "tensor_scalar",
                            vmask[:, kc * 128:(kc + 1) * 128], onesb[:, :],
                            mcol[:, kc:kc + 1], None, ALU.mult,
                            waits=[t_ob, t_mc]))
    mask_done = [p.tok("_v")]

    f_use = {}
    dq_st = {}
    for ec in range(8):
        qb = ec % 2
        t_wdl = p.dma("s", chunk3(wfd[qb].ap(), 512, nch=4),
                      dsl_d[ec * 128:(ec + 1) * 128, :]
                      .rearrange("(c p) o -> p c o", p=128), f"lwd{qb}",
                      waits=[f_use.get(("d", qb))] if f_use.get(("d", qb))
                      else [t_pl])
        td1 = p.op("s", "activation", sad[:, :], wfd[qb][:, :], AF.Sign,
                   bias=sdc[:, 1:2],
                   waits=[t_wdl, t_sdc] +
                   ([f_use.get("sad")] if f_use.get("sad") else []))
        td2 = p.op("s", "activation", q2d[qb][:, :], wfd[qb][:, :],
                   AF.Sign, bias=sdc[:, 0:1],
                   waits=[dq_st.get(qb)] if dq_st.get(qb) else [])
        f_use[("d", qb)] = td2
        td3 = p.op("v", "tensor_tensor", q2d[qb][:, :], q2d[qb][:, :],
                   sad[:, :], ALU.add, waits=[td1, td2])
        f_use["sad"] = td3
        dq_st[qb] = p.dma("s", wdq_od[ec * 128:(ec + 1) * 128, :],
                          q2d[qb][:, :], f"stwd{qb}", waits=[td3])

    with nc.sbuf_tensor("kh0", [128, S], BF16) as kh0, \
            nc.sbuf_tensor("kh1", [128, S], BF16) as kh1, \
            nc.sbuf_tensor("vh0", [128, S], BF16) as vh0, \
            nc.sbuf_tensor("vh1", [128, S], BF16) as vh1:
        kh = [kh0, kh1]
        vh = [vh0, vh1]
        kh_use = [None, None]
        kh_tok = [None, None]
        vh_tok = [None, None]
        for h in range(NH):
            b = h % 2
            t_kl = p.dma("y", kh[b][:, :], kf_d[h * 128:(h + 1) * 128, :],
                         f"lkh{b}", waits=([kh_use[b]] if kh_use[b] else []))
            t_vl = p.dma("g", chunk3(vh[b].ap(), 128, nch=nT),
                         vf_d.ap()[:, h * 128:(h + 1) * 128]
                         .rearrange("(kc p) o -> p kc o", p=128),
                         f"lvh{b}", waits=([kh_use[b]] if kh_use[b] else []))
            kh_tok[b] = p.op("v", "tensor_tensor", kh[b][:, :], kh[b][:, :],
                             mbc[:, :], ALU.mult,
                             waits=[t_kl] + mask_done)
            vh_tok[b] = p.op("v", "tensor_tensor", vh[b][:, :],
                             vh[b][:, :], vmask[:, :], ALU.mult,
                             waits=[t_vl] + mask_done)
            for qh in range(nqh):
                oidx = 2 + qh % 2
                o_bank = mm_ps[oidx]
                pr_use = [None, None]
                ex_tok = [None, None]
                t_dn = None
                t_av = None

                def emit_sc(kc):
                    pb = kc % 2
                    sc_bank = mm_ps[pb]
                    w_sc = [kh_tok[b], vh_tok[b], t_q]
                    if ex_tok[pb]:
                        w_sc.append(ex_tok[pb])
                    elif mm_free.get(pb):
                        w_sc.append(mm_free.get(pb))
                    return p.op("t", "matmul", sc_bank[:, 0:QW],
                                kh[b][:, kc * 128:(kc + 1) * 128],
                                q_fm[:, h * SQ + qh * 512:
                                     h * SQ + qh * 512 + QW],
                                start=True, stop=True, skip_group_check=True,
                                waits=w_sc)

                sc_tok = {0: emit_sc(0)}
                for kc in range(nT):
                    pb = kc % 2
                    sc_bank = mm_ps[pb]
                    ex_tok[pb] = p.op("s", "activation", probs[pb][:, 0:QW],
                                      sc_bank[:, 0:QW], AF.Exp,
                                      scale=float(ATT_SCALE),
                                      waits=[sc_tok.pop(kc)] +
                                      ([pr_use[pb]] if pr_use[pb] else []))
                    if kc + 1 < nT:
                        sc_tok[kc + 1] = emit_sc(kc + 1)
                    w_dn = [ex_tok[pb]]
                    if kc == 0:
                        if mm_free.get("dn"):
                            w_dn.append(mm_free["dn"])
                        if mm_free.get(oidx):
                            w_dn.append(mm_free[oidx])
                    t_dn = p.op("t", "matmul", dn_ps[0:1, 0:QW],
                                C.ones_c[:, :], probs[pb][:, 0:QW],
                                start=(kc == 0), stop=(kc == nT - 1),
                                skip_group_check=True, waits=w_dn)
                    t_av = p.op("t", "matmul", o_bank[:, 0:QW],
                                vh[b][:, kc * 128:(kc + 1) * 128],
                                probs[pb][:, 0:QW],
                                start=(kc == 0), stop=(kc == nT - 1),
                                skip_group_check=True)
                    pr_use[pb] = t_av
                mm_free[0] = ex_tok[0]
                mm_free[1] = ex_tok[1]
                t_dr = p.op("v", "reciprocal", dnr[0:1, 0:QW],
                            dn_ps[0:1, 0:QW], waits=[t_dn, t_av])
                mm_free["dn"] = t_dr
                t_db = p.op("t", "matmul", db_ps[:, 0:QW],
                            C.ones_rf[0:1, 0:128], dnr[0:1, 0:QW],
                            start=True, stop=True,
                            waits=[t_dr] + ([mm_free["db"]]
                                            if "db" in mm_free else []))
                t_e1 = p.op("s", "activation", ev[:, 0:QW], o_bank[:, 0:QW],
                            AF.Copy,
                            waits=[t_av] + ([mm_free["ev"]]
                                            if "ev" in mm_free else []))
                mm_free[oidx] = t_e1
                t_e2 = p.op("v", "tensor_copy", dnb[:, 0:QW], db_ps[:, 0:QW],
                            waits=[t_db])
                mm_free["db"] = t_e2
                mm_free["ev"] = p.op(
                    "v", "tensor_tensor",
                    at_fm[:, h * SQ + qh * 512:h * SQ + qh * 512 + QW],
                    ev[:, 0:QW], dnb[:, 0:QW], ALU.mult,
                    waits=[t_e1, t_e2])
            kh_use[b] = p.tok("_t")
    attn_done = [p.tok("_v"), p.tok("_s"), p.tok("_t")]
    es.close()

    with nc.sbuf_tensor("wo_sb", [128, 16 * D], BF16) as wo_sb:
        t_wo = p.dma("g", chunk3(wo_sb.ap(), D), wcol_ap(wo_d, 0, D), "ldwo",
                     waits=attn_done)
        t_bo = p.dma("g", bias_sb[0:1, :], bo_d[0:1, :], "ldbias")
        x_use = [None, None]
        st_tok = [None, None]
        prev_ln = None
        for tc in range(nQ):
            xb = tc % 2
            ev_toks = []
            for o4 in range(4):
                bidx = o4 % 2
                bank = mm_ps[bidx]
                w0 = [t_wo, t_bo] + attn_done + \
                    ([mm_free.get(bidx)] if mm_free.get(bidx) else [])
                for c in range(16):
                    p.op("t", "matmul", bank[:, :],
                         at_fm[:, c * SQ + tc * 128:c * SQ + (tc + 1) * 128],
                         wo_sb[:, c * D + o4 * 512:c * D + (o4 + 1) * 512],
                         start=(c == 0), stop=False,
                         waits=w0 if c == 0 else ())
                t_mm = p.op("t", "matmul", bank[:, :], C.ones_r[0:1, 0:128],
                            bias_sb[0:1, o4 * 512:(o4 + 1) * 512],
                            start=False, stop=True)
                w_e = [t_mm]
                if o4 == 0 and prev_ln:
                    w_e.append(prev_ln)
                if o4 == 0 and "ev" in mm_free:
                    w_e.append(mm_free.pop("ev"))
                t_e = p.op("s", "activation", ev[:, o4 * 512:(o4 + 1) * 512],
                           bank[:, :], AF.Copy, waits=w_e)
                mm_free[bidx] = t_e
                ev_toks.append(t_e)
            t_ld = p.dma("y", x_t[xb][:, :], x_d[tc * 128:(tc + 1) * 128, :],
                         f"lre{xb}",
                         waits=([x_use[xb]] if x_use[xb] else []) +
                         ([st_tok[xb]] if st_tok[xb] else []) +
                         ([p.tok("_v")] if tc < 2 else []))
            t_n = emit_ln(p, ev[:, :], ln_o[:, :], bjunk[:, :], st,
                          waits=ev_toks + ([prev_ln] if prev_ln else []))
            prev_ln = t_n
            t_add = p.op("v", "tensor_tensor", x_t[xb][:, :], x_t[xb][:, :],
                         ln_o[:, :], ALU.add, waits=[t_n, t_ld])
            x_use[xb] = t_add
            st_tok[xb] = p.dma("y", x1_d[tc * 128:(tc + 1) * 128, :],
                               x_t[xb][:, :], f"stx{xb}", waits=[t_add])
    p.op("y", "nop", waits=[p.tok("stx0"), p.tok("stx1"), p.tok("stwd0"),
                            p.tok("stwd1")])
    p.finalize()
    return nc


if __name__ == "__main__":
    nc = build_l0_attn(2048, 512)
    print("build ok, instructions:", nc.n_instructions())


# version 10
# speedup vs baseline: 1.4479x; 1.0882x over previous
import math
from contextlib import ExitStack

import contextlib
import numpy as np
import concourse.bass as bass
import concourse.mybir as mybir
from concourse.bass_utils import run_bass_kernel_spmd

F32 = mybir.dt.float32
BF16 = mybir.dt.bfloat16
I32 = mybir.dt.int32
AF = mybir.ActivationFunctionType
ALU = mybir.AluOpType

D = 2048
NH = 16
HD = 128
NCORES = 8
LN_EPS = 1e-5
ATT_SCALE = 1.0 / math.sqrt(HD)



class Prog:
    ENGS = ("t", "v", "s", "g", "y")
    _ATTR = {"t": "tensor", "v": "vector", "s": "scalar", "g": "gpsimd",
             "y": "sync"}

    def __init__(self, nc):
        self.nc = nc
        self.steps = {e: [] for e in self.ENGS}
        self.semh = {}
        self.cnt = {}
        self.waited = {e: {} for e in self.ENGS}
        self.base = []

    def sem(self, name):
        if name not in self.semh:
            self.semh[name] = self.nc.alloc_semaphore(name)
            self.cnt[name] = 0
        return name

    def tok(self, name):
        self.sem(name)
        return (name, self.cnt[name])

    def wait(self, e, token):
        if token is None:
            return
        name, val = token
        if val <= 0:
            return
        if self.waited[e].get(name, 0) >= val:
            return
        self.waited[e][name] = val
        h = self.semh[name]
        self.steps[e].append(lambda eng, h=h, v=val: eng.wait_ge(h, v))

    def op(self, e, meth, *args, waits=(), inc=None, dn=1, **kw):
        for tk in self.base:
            self.wait(e, tk)
        for tk in waits:
            self.wait(e, tk)
        if inc is None:
            inc = "_" + e
        inc = self.sem(inc)
        self.cnt[inc] += dn
        h = self.semh[inc]

        def go(eng, meth=meth, args=args, kw=kw, h=h, dn=dn):
            getattr(eng, meth)(*args, **kw).then_inc(h, dn)

        self.steps[e].append(go)
        return (inc, self.cnt[inc])

    def dma(self, e, out, in_, stream, waits=(), **kw):
        return self.op(e, "dma_start", waits=waits, inc=stream, dn=16,
                       out=out, in_=in_, **kw)

    def finalize(self):
        with self.nc.Block() as block:
            for e in self.ENGS:
                if not self.steps[e]:
                    continue
                steps = self.steps[e]

                def body(eng, steps=steps):
                    for f in steps:
                        f(eng)

                getattr(block, self._ATTR[e])(body)


class Consts:
    def __init__(self, p, fvals=()):
        nc = p.nc
        self.iota = nc.alloc_sbuf_tensor("c_iota", [128, 128], I32)
        self.ident = nc.alloc_sbuf_tensor("c_ident", [128, 128], BF16)
        self.ones_r = nc.alloc_sbuf_tensor("c_ones_r", [1, 512], BF16)
        self.ones_c = nc.alloc_sbuf_tensor("c_ones_c", [128, 1], BF16)
        self.ones_rf = nc.alloc_sbuf_tensor("c_ones_rf", [1, 512], F32)
        self.ones_cf = nc.alloc_sbuf_tensor("c_ones_cf", [128, 1], F32)
        t0 = p.op("g", "iota", self.iota[:, :], [[1, 128]], base=0,
                  channel_multiplier=-1)
        self.tok = [
            p.op("v", "tensor_scalar", self.ident[:, :], self.iota[:, :], 0,
                 None, ALU.is_equal, waits=[t0]),
            p.op("v", "memset", self.ones_r[:, :], 1.0),
            p.op("v", "memset", self.ones_c[:, :], 1.0),
            p.op("v", "memset", self.ones_rf[:, :], 1.0),
            p.op("v", "memset", self.ones_cf[:, :], 1.0),
        ]
        for i, val in enumerate(fvals):
            cst = nc.alloc_sbuf_tensor(f"c_f{i}", [128, 1], F32)
            self.tok.append(p.op("v", "memset", cst[:, :], float(val)))
            nc.const_aps.register(F32, float(val), cst[:, :])
        p.base = list(self.tok)


def emit_ln(p, x_ap, out_ap, junk_ap, st, waits, eps=LN_EPS, n=D, sc=0,
            junk2_ap=None, sum_on_act=False):
    j2 = junk2_ap if junk2_ap is not None else junk_ap
    t_sq = p.op("s", "activation", junk_ap, x_ap, AF.Square,
                accum_out=st[:, sc + 0:sc + 1], waits=waits)
    if sum_on_act:
        t_s1 = p.op("s", "activation", j2, x_ap, AF.Copy,
                    accum_out=st[:, sc + 1:sc + 2], waits=list(waits))
    else:
        t_s1 = p.op("v", "tensor_scalar", j2, x_ap, 1.0, None, ALU.mult,
                    ALU.add, accum_out=st[:, sc + 1:sc + 2],
                    waits=list(waits) +
                    ([] if junk2_ap is not None else [t_sq]))
    t_m = p.op("v", "tensor_scalar", st[:, sc + 2:sc + 3],
               st[:, sc + 1:sc + 2], 1.0 / n, None, ALU.mult, waits=[t_s1])
    t_mm = p.op("v", "tensor_tensor", st[:, sc + 3:sc + 4],
                st[:, sc + 2:sc + 3], st[:, sc + 2:sc + 3], ALU.mult,
                waits=[t_m])
    t_v = p.op("v", "scalar_tensor_tensor", st[:, sc + 4:sc + 5],
               st[:, sc + 0:sc + 1], 1.0 / n, st[:, sc + 3:sc + 4],
               ALU.mult, ALU.subtract, waits=[t_mm, t_sq])
    t_sd = p.op("s", "activation", st[:, sc + 5:sc + 6],
                st[:, sc + 4:sc + 5], AF.Sqrt, bias=float(eps), waits=[t_v])
    t_r = p.op("v", "reciprocal", st[:, sc + 6:sc + 7],
               st[:, sc + 5:sc + 6], waits=[t_sd])
    t_n = p.op("v", "tensor_scalar", out_ap, x_ap, st[:, sc + 2:sc + 3],
               st[:, sc + 6:sc + 7], ALU.subtract, ALU.mult, waits=[t_r])
    return t_n


def wcol_ap(w_dram, o0, width):
    return w_dram.ap()[:, o0:o0 + width].rearrange("(c p) o -> p c o", p=128)


def chunk3(sb_ap, width, nch=16):
    return sb_ap.rearrange("p (c o) -> p c o", c=nch)


def fm_ap(fm_dram, t0, width):
    return fm_dram.ap()[:, t0:t0 + width].rearrange("(c p) t -> p c t",
                                                    p=128)




def build_l0_attn(S=4096, SQ=1024):
    nT = S // 128
    nTB = S // 512
    nQ = SQ // 128
    TIDX = min(max(1, int(S * 0.2)), S - 1)
    assert S % 512 == 0 and SQ % 512 == 0 and nT <= 128

    nc = bass.Bass("TRN2", target_bir_lowering=False)
    p = Prog(nc)

    x_d = nc.dram_tensor("x_full", [S, D], F32, kind="ExternalInput")
    wq_d = nc.dram_tensor("Wq", [D, D], F32, kind="ExternalInput")
    wk_d = nc.dram_tensor("Wk", [D, D], F32, kind="ExternalInput")
    wv_d = nc.dram_tensor("Wv", [D, D], F32, kind="ExternalInput")
    wo_d = nc.dram_tensor("Wo", [D, D], F32, kind="ExternalInput")
    bq_d = nc.dram_tensor("bq", [1, D], F32, kind="ExternalInput")
    bk_d = nc.dram_tensor("bk", [1, D], F32, kind="ExternalInput")
    bv_d = nc.dram_tensor("bv", [1, D], F32, kind="ExternalInput")
    bo_d = nc.dram_tensor("bo", [1, D], F32, kind="ExternalInput")
    wi_d = nc.dram_tensor("Wi", [1, D], F32, kind="ExternalInput")
    bi_d = nc.dram_tensor("bi", [1, 1], F32, kind="ExternalInput")
    x1_d = nc.dram_tensor("x1", [SQ, D], F32, kind="ExternalOutput")

    xn_d = nc.dram_tensor("xn_scr", [D, S], BF16)
    k_d = nc.dram_tensor("k_scr", [D, S], BF16)
    v_d = nc.dram_tensor("v_scr", [S, D], BF16)
    mr_d = nc.dram_tensor("mr_scr", [nT, 128], BF16)

    at_fm = nc.alloc_sbuf_tensor("at_fm", [128, 16 * SQ], BF16)
    ev = nc.alloc_sbuf_tensor("ev", [128, D], F32)
    ln_o = nc.alloc_sbuf_tensor("ln_o", [128, D], F32)
    x_t = [nc.alloc_sbuf_tensor(f"x_t{i}", [128, D], F32) for i in range(2)]
    bjunk = nc.alloc_sbuf_tensor("bjunk", [128, D], BF16)
    st = nc.alloc_sbuf_tensor("st", [128, 24], F32)
    imp_sb = nc.alloc_sbuf_tensor("imp_sb", [128, nT], F32)
    bis = nc.alloc_sbuf_tensor("bis", [128, 8], F32)
    bmask = nc.alloc_sbuf_tensor("bmask", [128, 2], I32)
    cnt_c = nc.alloc_sbuf_tensor("cnt_c", [128, 1], F32)
    c11 = nc.alloc_sbuf_tensor("c11", [1, 1], F32)
    mrow = nc.alloc_sbuf_tensor("mrow", [1, S], BF16)
    mbc = nc.alloc_sbuf_tensor("mbc", [128, S], BF16)
    mcol = nc.alloc_sbuf_tensor("mcol", [128, nT], F32)
    mcolb = nc.alloc_sbuf_tensor("mcolb", [128, nT], BF16)
    mjunk = nc.alloc_sbuf_tensor("mjunk", [128, nT], BF16)
    mstage = nc.alloc_sbuf_tensor("mstage", [128, 128], BF16)
    probs = [nc.alloc_sbuf_tensor(f"probs{i}", [128, 512], BF16)
             for i in range(2)]
    kv_out = [nc.alloc_sbuf_tensor(f"kv_out{i}", [128, 512], BF16)
              for i in range(2)]
    dnr = nc.alloc_sbuf_tensor("dnr", [1, 512], F32)
    dnb = nc.alloc_sbuf_tensor("dnb", [128, 512], F32)
    bias_sb = nc.alloc_sbuf_tensor("bias_sb", [1, D], BF16)
    bif = nc.alloc_sbuf_tensor("bif", [1, 1], F32)
    wirow = nc.alloc_sbuf_tensor("wirow", [1, D], F32)
    swi_c = nc.alloc_sbuf_tensor("swi_c", [128, 1], F32)
    bi_c = nc.alloc_sbuf_tensor("bi_c", [128, 1], F32)

    tp_ps = [nc.alloc_psum_tensor(f"tp{i}", [128, 128], BF16) for i in range(2)]
    mm_ps = [nc.alloc_psum_tensor(f"mm{i}", [128, 512], F32) for i in range(4)]
    dn_ps = nc.alloc_psum_tensor("dn_ps", [1, 512], F32)
    db_ps = nc.alloc_psum_tensor("db_ps", [128, 512], F32)

    C = Consts(p, fvals=(LN_EPS, ATT_SCALE))
    mm_free = {}

    t_wi = p.dma("y", wirow[0:1, :], wi_d[0:1, :], "ldwi")
    t_bi = p.dma("y", bif[0:1, :], bi_d[0:1, :], "ldbi")

    with nc.sbuf_tensor("xf_t0", [128, 16 * 128], BF16) as xf_t0, \
            nc.sbuf_tensor("xf_t1", [128, 16 * 128], BF16) as xf_t1, \
            nc.sbuf_tensor("bj0", [128, D], BF16) as bj0, \
            nc.sbuf_tensor("bj1", [128, D], BF16) as bj1, \
            nc.sbuf_tensor("jk0", [128, D], BF16) as jk0, \
            nc.sbuf_tensor("jk1", [128, D], BF16) as jk1, \
            nc.sbuf_tensor("jk2", [128, D], BF16) as jk2, \
            nc.sbuf_tensor("jk3", [128, D], BF16) as jk3, \
            nc.sbuf_tensor("ij0", [128, D], F32) as ij0, \
            nc.sbuf_tensor("ij1", [128, D], F32) as ij1, \
            nc.sbuf_tensor("wi_bc", [128, D], F32) as wi_bc:
        xf = [xf_t0, xf_t1]
        bj = [bj0, bj1]
        jka = [jk0, jk1]
        jkb = [jk2, jk3]
        ij = [ij0, ij1]
        wb_toks = []
        for ob in range(4):
            t_wb = p.op("t", "matmul", db_ps[:, :], C.ones_rf[0:1, 0:128],
                        wirow[0:1, ob * 512:(ob + 1) * 512], start=True,
                        stop=True, waits=[t_wi] + (wb_toks[-1:] or []))
            wb_toks.append(p.op("v", "tensor_copy",
                                wi_bc[:, ob * 512:(ob + 1) * 512],
                                db_ps[:, :], waits=[t_wb]))
        t_bb = p.op("t", "matmul", db_ps[:, 0:1], C.ones_rf[0:1, 0:128],
                    bif[0:1, 0:1], start=True, stop=True,
                    waits=[t_bi] + wb_toks[-1:])
        t_bic = p.op("v", "tensor_copy", bi_c[:, :], db_ps[:, 0:1],
                     waits=[t_bb])
        t_swi = p.op("v", "tensor_reduce", swi_c[:, :], wi_bc[:, :],
                     mybir.AxisListType.X, ALU.add, waits=wb_toks)
        imp_setup = [t_bic, t_swi]

        x_use = [None, None]
        bj_use = [None, None]
        xf_use = [None, None]
        ij_use = [None, None]
        ln_tok = [None, None]
        st_xn = [None, None]
        imp_last = None
        for ti in range(nT):
            q = ti % 2
            t_ld = p.dma("y", x_t[q][:, :], x_d[ti * 128:(ti + 1) * 128, :],
                         f"ldx{q}", waits=list(x_use[q] or ()))
            t_n = emit_ln(p, x_t[q][:, :], bj[q][:, :], jka[q][:, :], st,
                          waits=[t_ld] + ([bj_use[q]] if bj_use[q] else []) +
                          ([ln_tok[q]] if ln_tok[q] else []),
                          sc=8 * q, junk2_ap=jkb[q][:, :])
            ln_tok[q] = t_n
            t_mu = p.op("g", "tensor_tensor", ij[q][:, :], x_t[q][:, :],
                        wi_bc[:, :], ALU.mult,
                        waits=[t_ld] + imp_setup +
                        ([ij_use[q]] if ij_use[q] else []))
            t_us = p.op("v", "tensor_reduce", st[:, 16 + 4 * q:17 + 4 * q],
                        ij[q][:, :], mybir.AxisListType.X, ALU.add,
                        waits=[t_mu])
            ij_use[q] = t_us
            x_use[q] = (t_n, t_mu, p.tok("_s"))
            t_i1 = p.op("v", "tensor_tensor", st[:, 17 + 4 * q:18 + 4 * q],
                        st[:, 8 * q + 2:8 * q + 3], swi_c[:, :], ALU.mult,
                        waits=[t_us, t_n])
            t_i2 = p.op("v", "scalar_tensor_tensor",
                        st[:, 18 + 4 * q:19 + 4 * q],
                        st[:, 17 + 4 * q:18 + 4 * q], -1.0,
                        st[:, 16 + 4 * q:17 + 4 * q], ALU.mult, ALU.add,
                        waits=[t_i1])
            t_i3 = p.op("v", "tensor_tensor", st[:, 19 + 4 * q:20 + 4 * q],
                        st[:, 18 + 4 * q:19 + 4 * q],
                        st[:, 8 * q + 6:8 * q + 7], ALU.mult, waits=[t_i2])
            imp_last = p.op("v", "tensor_tensor", imp_sb[:, ti:ti + 1],
                            st[:, 19 + 4 * q:20 + 4 * q], bi_c[:, :],
                            ALU.add, waits=[t_i3])
            cp_toks = []
            for c in range(16):
                bk = tp_ps[c % 2]
                w = [t_n]
                if len(cp_toks) >= 2:
                    w.append(cp_toks[-2])
                if c == 0 and st_xn[q]:
                    w.append(st_xn[q])
                t_tr = p.op("t", "transpose", bk[:, :],
                            bj[q][:, c * 128:(c + 1) * 128], C.ident[:, :],
                            waits=w)
                if c % 2 == 0:
                    cp_toks.append(p.op("v", "tensor_copy",
                                        xf[q][:, c * 128:(c + 1) * 128],
                                        bk[:, :], waits=[t_tr]))
                else:
                    cp_toks.append(p.op("s", "activation",
                                        xf[q][:, c * 128:(c + 1) * 128],
                                        bk[:, :], AF.Copy, waits=[t_tr]))
            bj_use[q] = p.tok("_t")
            st_xn[q] = p.dma("s", fm_ap(xn_d, ti * 128, 128),
                             chunk3(xf[q].ap(), 128), f"stxn{q}",
                             waits=cp_toks)
    ln_done = [p.tok("_v"), p.tok("_t"), p.tok("_s"), p.tok("_g"),
               p.tok("stxn0"), p.tok("stxn1")]


    t0 = p.op("v", "memset", bis[:, 0:1], -1000.0, waits=[imp_last])
    t1 = p.op("v", "memset", bis[:, 1:2], 1000.0)
    prev = [t0, t1]
    for _ in range(50):
        t_a = p.op("v", "tensor_scalar", bis[:, 3:4], bis[:, 1:2], 0.5, None,
                   ALU.mult, waits=prev)
        t_mid = p.op("v", "scalar_tensor_tensor", bis[:, 2:3], bis[:, 0:1],
                     0.5, bis[:, 3:4], ALU.mult, ALU.add, waits=[t_a])
        t_c = p.op("v", "tensor_scalar", mjunk[:, 0:nT], imp_sb[:, 0:nT],
                   bis[:, 2:3], None, ALU.is_gt, ALU.add,
                   accum_out=cnt_c[:, 0:1], waits=[t_mid])
        t_s = p.op("t", "matmul", dn_ps[0:1, 0:1], C.ones_cf[:, :],
                   cnt_c[:, 0:1], start=True, stop=True,
                   waits=[t_c] + C.tok + ([mm_free["dn"]]
                                          if "dn" in mm_free else []))
        t_cp = p.op("v", "tensor_copy", c11[0:1, 0:1], dn_ps[0:1, 0:1],
                    waits=[t_s])
        mm_free["dn"] = t_cp
        t_bc = p.op("t", "matmul", db_ps[:, 0:1], C.ones_rf[0:1, 0:128],
                    c11[0:1, 0:1], start=True, stop=True,
                    waits=[t_cp] + ([mm_free["db"]] if "db" in mm_free else []))
        t_cd = p.op("v", "tensor_scalar", bmask[:, 0:1], db_ps[:, 0:1],
                    TIDX + 0.5, None, ALU.is_lt, waits=[t_bc])
        t_ic = p.op("v", "tensor_scalar", bmask[:, 1:2], db_ps[:, 0:1],
                    TIDX + 0.5, None, ALU.is_ge, waits=[t_cd])
        mm_free["db"] = t_ic
        t_h = p.op("v", "copy_predicated", bis[:, 1:2], bmask[:, 0:1],
                   bis[:, 2:3], waits=[t_ic])
        t_l = p.op("v", "copy_predicated", bis[:, 0:1], bmask[:, 1:2],
                   bis[:, 2:3], waits=[t_h])
        prev = [t_l]
    t_mc = p.op("v", "tensor_scalar", mcol[:, 0:nT], imp_sb[:, 0:nT],
                bis[:, 1:2], None, ALU.is_gt, waits=prev)
    t_mcb = p.op("v", "tensor_copy", mcolb[:, 0:nT], mcol[:, 0:nT],
                 waits=[t_mc])
    t_tr = p.op("t", "transpose", tp_ps[0][0:nT, :], mcolb[:, 0:nT],
                C.ident[:, :], waits=[t_mcb] + C.tok)
    t_ms = p.op("v", "tensor_copy", mstage[0:nT, :], tp_ps[0][0:nT, :],
                waits=[t_tr])
    t_st = p.dma("s", mr_d[:, :], mstage[0:nT, :], "mrb", waits=[t_ms])
    t_lr = p.dma("s", mrow[0:1, 0:S],
                 mr_d.ap().rearrange("(o a) b -> o (a b)", o=1), "mrb2",
                 waits=[t_st])
    for tb in range(nTB):
        t_b = p.op("t", "matmul", db_ps[:, :], C.ones_r[0:1, 0:128],
                   mrow[0:1, tb * 512:(tb + 1) * 512], start=True, stop=True,
                   waits=[t_lr, mm_free["db"]])
        mm_free["db"] = p.op("v", "tensor_copy",
                             mbc[:, tb * 512:(tb + 1) * 512], db_ps[:, :],
                             waits=[t_b])
    mask_done = [p.tok("_v")]

    kv_tok = [None, None]
    xuse = {}

    def proj_pass(kind, ph, xn_blk, wbuf, ldw_tok, bias_tok, xs):
        nblk = (SQ if kind == "Q" else S) // 512
        for tb in range(nblk):
            xb = tb % 2
            t_x = p.dma("s", chunk3(xn_blk[xb].ap(), 512),
                        fm_ap(xn_d, tb * 512, 512), f"{xs}{xb}",
                        waits=ln_done + ([xuse[xb]] if xb in xuse else []))
            if kind in ("K", "Q"):
                for o8 in range(8):
                    o = ph * 8 + o8
                    bidx = o8 % 2
                    bank = mm_ps[bidx]
                    w0 = [ldw_tok, bias_tok, t_x] + \
                        ([mm_free.get(bidx)] if mm_free.get(bidx) else [])
                    for c in range(16):
                        p.op("t", "matmul", bank[:, :],
                             wbuf[:, c * 1024 + o8 * 128:
                                  c * 1024 + (o8 + 1) * 128],
                             xn_blk[xb][:, c * 512:(c + 1) * 512],
                             start=(c == 0), stop=False,
                             waits=w0 if c == 0 else ())
                    t_mm = p.op("t", "matmul", bank[:, :],
                                bias_sb[0:1, o * 128:(o + 1) * 128],
                                C.ones_r[0:1, 0:512], start=False, stop=True)
                    if kind == "K":
                        ob = (tb * 8 + o8) % 2
                        t_e = p.op("v", "tensor_tensor", kv_out[ob][:, :],
                                   bank[:, :],
                                   mbc[:, tb * 512:(tb + 1) * 512], ALU.mult,
                                   waits=[t_mm] + mask_done +
                                   ([kv_tok[ob]] if kv_tok[ob] else []))
                        kv_tok[ob] = p.dma(
                            "y", k_d[o * 128:(o + 1) * 128,
                                     tb * 512:(tb + 1) * 512],
                            kv_out[ob][:, :], f"stkv{ob}", waits=[t_e])
                    else:
                        t_e = p.op("v", "tensor_copy",
                                   q_fm[:, o * SQ + tb * 512:
                                        o * SQ + (tb + 1) * 512],
                                   bank[:, :], waits=[t_mm])
                    mm_free[bidx] = t_e
                xuse[xb] = p.tok("_t")
            else:
                for tc4 in range(4):
                    for obl in range(2):
                        og = ph * 2 + obl
                        bidx = 2 + (tc4 * 2 + obl) % 2
                        bank = mm_ps[bidx]
                        w0 = [ldw_tok, bias_tok, t_x] + \
                            ([mm_free.get(bidx)] if mm_free.get(bidx) else [])
                        for c in range(16):
                            p.op("t", "matmul", bank[:, :],
                                 xn_blk[xb][:, c * 512 + tc4 * 128:
                                            c * 512 + (tc4 + 1) * 128],
                                 wbuf[:, c * 1024 + obl * 512:
                                      c * 1024 + (obl + 1) * 512],
                                 start=(c == 0), stop=False,
                                 waits=w0 if c == 0 else ())
                        t_mm = p.op("t", "matmul", bank[:, :],
                                    C.ones_r[0:1, 0:128],
                                    bias_sb[0:1, og * 512:(og + 1) * 512],
                                    start=False, stop=True)
                        tglob = tb * 4 + tc4
                        ob = (tglob * 2 + obl) % 2
                        t_e = p.op("v", "tensor_scalar", kv_out[ob][:, :],
                                   bank[:, :], mcol[:, tglob:tglob + 1],
                                   None, ALU.mult,
                                   waits=[t_mm] + mask_done +
                                   ([kv_tok[ob]] if kv_tok[ob] else []))
                        kv_tok[ob] = p.dma(
                            "y", v_d[tglob * 128:(tglob + 1) * 128,
                                     og * 512:(og + 1) * 512],
                            kv_out[ob][:, :], f"stkv{ob}", waits=[t_e])
                        mm_free[bidx] = t_e
            xuse[xb] = p.tok("_t")

    with nc.sbuf_tensor("wbuf", [128, 16 * 1024], BF16) as wbuf, \
            nc.sbuf_tensor("xnb0", [128, 16 * 512], BF16) as xnb0, \
            nc.sbuf_tensor("xnb1", [128, 16 * 512], BF16) as xnb1:
        xn_blk = [xnb0, xnb1]
        for kind, w_dram, b_dram in (("K", wk_d, bk_d), ("V", wv_d, bv_d),
                                     ("Q", wq_d, bq_d)):
            t_b = p.dma("g", bias_sb[0:1, :], b_dram[0:1, :], "ldbias",
                        waits=[p.tok("_t")])
            for ph in range(2):
                t_w = p.dma("g", chunk3(wbuf.ap(), 1024),
                            wcol_ap(w_dram, ph * 1024, 1024),
                            "ldwbuf", waits=[p.tok("_t")] + ln_done)
                proj_pass(kind, ph, xn_blk, wbuf, t_w, t_b, f"lx{kind}{ph}")

    proj_done = [p.tok("_t"), p.tok("_v"), p.tok("stkv0"), p.tok("stkv1")]

    nqh = max(1, SQ // 512)
    QW = min(512, SQ)
    with nc.sbuf_tensor("kh0", [128, S], BF16) as kh0, \
            nc.sbuf_tensor("kh1", [128, S], BF16) as kh1, \
            nc.sbuf_tensor("vh0", [128, S], BF16) as vh0, \
            nc.sbuf_tensor("vh1", [128, S], BF16) as vh1:
        kh = [kh0, kh1]
        vh = [vh0, vh1]
        kh_use = [None, None]
        kh_tok = [None, None]
        vh_tok = [None, None]
        for h in range(NH):
            b = h % 2
            kh_tok[b] = p.dma("y", kh[b][:, :], k_d[h * 128:(h + 1) * 128, :],
                              f"lkh{b}", waits=proj_done +
                              ([kh_use[b]] if kh_use[b] else []))
            vh_tok[b] = p.dma("g", chunk3(vh[b].ap(), 128, nch=nT),
                              v_d.ap()[:, h * 128:(h + 1) * 128]
                              .rearrange("(kc p) o -> p kc o", p=128),
                              f"lvh{b}", waits=proj_done +
                              ([kh_use[b]] if kh_use[b] else []))
            for qh in range(nqh):
                oidx = 2 + qh % 2
                o_bank = mm_ps[oidx]
                pr_use = [None, None]
                ex_tok = [None, None]
                t_dn = None
                t_av = None

                def emit_sc(kc):
                    pb = kc % 2
                    sc_bank = mm_ps[pb]
                    w_sc = [kh_tok[b], vh_tok[b]]
                    if ex_tok[pb]:
                        w_sc.append(ex_tok[pb])
                    elif mm_free.get(pb):
                        w_sc.append(mm_free.get(pb))
                    return p.op("t", "matmul", sc_bank[:, 0:QW],
                                kh[b][:, kc * 128:(kc + 1) * 128],
                                q_fm[:, h * SQ + qh * 512:
                                     h * SQ + qh * 512 + QW],
                                start=True, stop=True, skip_group_check=True,
                                waits=w_sc)

                sc_tok = {0: emit_sc(0)}
                for kc in range(nT):
                    pb = kc % 2
                    sc_bank = mm_ps[pb]
                    ex_tok[pb] = p.op("s", "activation", probs[pb][:, 0:QW],
                                      sc_bank[:, 0:QW], AF.Exp,
                                      scale=float(ATT_SCALE),
                                      waits=[sc_tok.pop(kc)] +
                                      ([pr_use[pb]] if pr_use[pb] else []))
                    if kc + 1 < nT:
                        sc_tok[kc + 1] = emit_sc(kc + 1)
                    w_dn = [ex_tok[pb]]
                    if kc == 0:
                        if mm_free.get("dn"):
                            w_dn.append(mm_free["dn"])
                        if mm_free.get(oidx):
                            w_dn.append(mm_free[oidx])
                    t_dn = p.op("t", "matmul", dn_ps[0:1, 0:QW],
                                C.ones_c[:, :], probs[pb][:, 0:QW],
                                start=(kc == 0), stop=(kc == nT - 1),
                                skip_group_check=True, waits=w_dn)
                    t_av = p.op("t", "matmul", o_bank[:, 0:QW],
                                vh[b][:, kc * 128:(kc + 1) * 128],
                                probs[pb][:, 0:QW],
                                start=(kc == 0), stop=(kc == nT - 1),
                                skip_group_check=True)
                    pr_use[pb] = t_av
                mm_free[0] = ex_tok[0]
                mm_free[1] = ex_tok[1]
                t_dr = p.op("v", "reciprocal", dnr[0:1, 0:QW],
                            dn_ps[0:1, 0:QW], waits=[t_dn, t_av])
                mm_free["dn"] = t_dr
                t_db = p.op("t", "matmul", db_ps[:, 0:QW],
                            C.ones_rf[0:1, 0:128], dnr[0:1, 0:QW],
                            start=True, stop=True,
                            waits=[t_dr] + ([mm_free["db"]]
                                            if "db" in mm_free else []))
                t_e1 = p.op("s", "activation", ev[:, 0:QW], o_bank[:, 0:QW],
                            AF.Copy,
                            waits=[t_av] + ([mm_free["ev"]]
                                            if "ev" in mm_free else []))
                mm_free[oidx] = t_e1
                t_e2 = p.op("v", "tensor_copy", dnb[:, 0:QW], db_ps[:, 0:QW],
                            waits=[t_db])
                mm_free["db"] = t_e2
                mm_free["ev"] = p.op(
                    "v", "tensor_tensor",
                    at_fm[:, h * SQ + qh * 512:h * SQ + qh * 512 + QW],
                    ev[:, 0:QW], dnb[:, 0:QW], ALU.mult,
                    waits=[t_e1, t_e2])
            kh_use[b] = p.tok("_t")
    attn_done = [p.tok("_v"), p.tok("_s"), p.tok("_t")]

    with nc.sbuf_tensor("wo_sb", [128, 16 * D], BF16) as wo_sb:
        t_wo = p.dma("g", chunk3(wo_sb.ap(), D), wcol_ap(wo_d, 0, D), "ldwo",
                     waits=attn_done)
        t_bo = p.dma("g", bias_sb[0:1, :], bo_d[0:1, :], "ldbias",
                     waits=attn_done)
        x_use = [None, None]
        st_tok = [None, None]
        prev_ln = None
        for tc in range(nQ):
            xb = tc % 2
            ev_toks = []
            for o4 in range(4):
                bidx = o4 % 2
                bank = mm_ps[bidx]
                w0 = [t_wo, t_bo] + attn_done + \
                    ([mm_free.get(bidx)] if mm_free.get(bidx) else [])
                for c in range(16):
                    p.op("t", "matmul", bank[:, :],
                         at_fm[:, c * SQ + tc * 128:c * SQ + (tc + 1) * 128],
                         wo_sb[:, c * D + o4 * 512:c * D + (o4 + 1) * 512],
                         start=(c == 0), stop=False,
                         waits=w0 if c == 0 else ())
                t_mm = p.op("t", "matmul", bank[:, :], C.ones_r[0:1, 0:128],
                            bias_sb[0:1, o4 * 512:(o4 + 1) * 512],
                            start=False, stop=True)
                w_e = [t_mm]
                if o4 == 0 and prev_ln:
                    w_e.append(prev_ln)
                if o4 == 0 and "ev" in mm_free:
                    w_e.append(mm_free.pop("ev"))
                t_e = p.op("s", "activation", ev[:, o4 * 512:(o4 + 1) * 512],
                           bank[:, :], AF.Copy, waits=w_e)
                mm_free[bidx] = t_e
                ev_toks.append(t_e)
            t_ld = p.dma("y", x_t[xb][:, :], x_d[tc * 128:(tc + 1) * 128, :],
                         f"lre{xb}",
                         waits=([x_use[xb]] if x_use[xb] else []) +
                         ([st_tok[xb]] if st_tok[xb] else []) +
                         ([p.tok("_v")] if tc < 2 else []))
            t_n = emit_ln(p, ev[:, :], ln_o[:, :], bjunk[:, :], st,
                          waits=ev_toks + ([prev_ln] if prev_ln else []))
            prev_ln = t_n
            t_add = p.op("v", "tensor_tensor", x_t[xb][:, :], x_t[xb][:, :],
                         ln_o[:, :], ALU.add,
                         waits=[t_n, t_ld] +
                         ([st_tok[xb]] if st_tok[xb] else []))
            x_use[xb] = t_add
            st_tok[xb] = p.dma("y", x1_d[tc * 128:(tc + 1) * 128, :],
                               x_t[xb][:, :], f"stx{xb}", waits=[t_add])
    p.op("y", "nop", waits=[p.tok("stx0"), p.tok("stx1")])

    p.finalize()
    return nc






def build_ffn0_parts():
    R = 1024
    nc = bass.Bass("TRN2", target_bir_lowering=False)
    p = Prog(nc)
    ins = [nc.dram_tensor(n, [R, D], F32, kind="ExternalInput")
           for n in ("up_sl", "gate_sl", "down_sl")]
    out_d = nc.dram_tensor("parts", [1, 4], F32, kind="ExternalOutput")

    w_t = [nc.alloc_sbuf_tensor(f"w_t{i}", [128, D], F32) for i in range(2)]
    junk = nc.alloc_sbuf_tensor("junk", [128, D], BF16)
    acc = nc.alloc_sbuf_tensor("acc", [128, 26], F32)
    red = nc.alloc_sbuf_tensor("red", [128, 4], F32)
    o_sb = nc.alloc_sbuf_tensor("o_sb", [1, 4], F32)
    ps = nc.alloc_psum_tensor("ps", [1, 4], F32)
    C = Consts(p)

    use_tok = [None, None]
    col = 0
    toks = []
    for mi, w_d in enumerate(ins):
        for ti in range(R // 128):
            b = ti % 2
            t_ld = p.dma("y", w_t[b][:, :], w_d[ti * 128:(ti + 1) * 128, :],
                         f"ld{b}", waits=[use_tok[b]] if use_tok[b] else [])
            t_a = p.op("v", "tensor_reduce", acc[:, col:col + 1],
                       w_t[b][:, :], mybir.AxisListType.X, ALU.add,
                       apply_absolute_value=True,
                       waits=[t_ld] + (toks[-1:] or []))
            use_tok[b] = t_a
            toks.append(t_a)
            col += 1
    rt = []
    for m in range(3):
        rt.append(p.op("v", "tensor_reduce", red[:, m:m + 1],
                       acc[:, m * 8:(m + 1) * 8], mybir.AxisListType.X,
                       ALU.add, waits=toks))
    t_z = p.op("v", "memset", red[:, 3:4], 0.0)
    t_s = p.op("t", "matmul", ps[0:1, :], C.ones_cf[:, :], red[:, :],
               start=True, stop=True, waits=rt + [t_z])
    t_c = p.op("v", "tensor_copy", o_sb[0:1, :], ps[0:1, :], waits=[t_s])
    p.dma("y", out_d[0:1, :], o_sb[0:1, :], "sto", waits=[t_c])
    p.op("y", "nop", waits=[p.tok("sto")])
    p.finalize()
    return nc




def build_ffn0(SQ=1024):
    DE = 8192
    nQ = SQ // 128
    nTB = SQ // 512
    nEC = DE // 128
    nc = bass.Bass("TRN2", target_bir_lowering=False)
    p = Prog(nc)

    x1_d = nc.dram_tensor("x1", [SQ, D], F32, kind="ExternalInput")
    upT_d = nc.dram_tensor("upT", [D, DE], F32, kind="ExternalInput")
    gateT_d = nc.dram_tensor("gateT", [D, DE], F32, kind="ExternalInput")
    parts_d = nc.dram_tensor("parts", [8, 4], F32, kind="ExternalInput")
    wdq_d = nc.dram_tensor("wdq", [DE, D], BF16, kind="ExternalInput")
    x2_d = nc.dram_tensor("x2", [SQ, D], F32, kind="ExternalOutput")
    hid_d = nc.dram_tensor("hid_scr", [DE, SQ], BF16)
    y_d = nc.dram_tensor("y_scr", [SQ, D], F32)

    ev = nc.alloc_sbuf_tensor("ev", [128, D], F32)
    ln_o = nc.alloc_sbuf_tensor("ln_o", [128, D], F32)
    x_t = [nc.alloc_sbuf_tensor(f"x_t{i}", [128, D], F32) for i in range(2)]
    bjunk = nc.alloc_sbuf_tensor("bjunk", [128, D], BF16)
    st = nc.alloc_sbuf_tensor("st", [128, 16], F32)
    xn_fm = nc.alloc_sbuf_tensor("xn_fm", [128, 16 * SQ], BF16)
    parts_sb = nc.alloc_sbuf_tensor("parts_sb", [8, 4], F32)
    srow = nc.alloc_sbuf_tensor("srow", [1, 10], F32)
    scl = nc.alloc_sbuf_tensor("scl", [128, 10], F32)
    hb = [nc.alloc_sbuf_tensor(f"hb{i}", [128, 512], BF16) for i in range(2)]
    hout = [nc.alloc_sbuf_tensor(f"hout{i}", [128, 512], BF16)
            for i in range(2)]
    nscl = nc.alloc_sbuf_tensor("nscl", [128, 3], F32)
    thp = nc.alloc_sbuf_tensor("thp", [128, D], F32)
    thn = nc.alloc_sbuf_tensor("thn", [128, D], F32)

    tp_ps = [nc.alloc_psum_tensor(f"tp{i}", [128, 128], BF16) for i in range(2)]
    mm_ps = [nc.alloc_psum_tensor(f"mm{i}", [128, 512], F32) for i in range(4)]
    db_ps = nc.alloc_psum_tensor("db_ps", [128, 512], F32)

    C = Consts(p, fvals=(LN_EPS,))
    mm_free = {}

    t_pl = p.dma("y", parts_sb[:, :], parts_d[:, :], "ldp")
    t_tot = p.op("t", "matmul", db_ps[0:1, 0:4], C.ones_cf[0:8, :],
                 parts_sb[0:8, :], start=True, stop=True, waits=[t_pl])
    t_sc = p.op("v", "tensor_scalar", srow[0:1, 0:3], db_ps[0:1, 0:3],
                1.0 / (DE * D), 1e-8, ALU.mult, ALU.add, waits=[t_tot])
    t_h2 = p.op("v", "tensor_scalar", srow[0:1, 3:6], srow[0:1, 0:3], 0.5,
                None, ALU.mult, waits=[t_sc])
    t_gd = p.op("v", "tensor_tensor", srow[0:1, 6:7], srow[0:1, 4:5],
                srow[0:1, 5:6], ALU.mult, waits=[t_h2])
    t_gd2 = p.op("v", "tensor_tensor", srow[0:1, 7:8], srow[0:1, 1:2],
                 srow[0:1, 5:6], ALU.mult, waits=[t_gd])
    t_gd3 = p.op("v", "tensor_tensor", srow[0:1, 8:9], srow[0:1, 1:2],
                 srow[0:1, 2:3], ALU.mult, waits=[t_gd2])
    t_bb = p.op("t", "matmul", db_ps[:, 0:9], C.ones_rf[0:1, 0:128],
                srow[0:1, 0:9], start=True, stop=True, waits=[t_gd3])
    t_sb = p.op("v", "tensor_copy", scl[:, 0:9], db_ps[:, 0:9], waits=[t_bb])
    scl_tok = [t_sb]

    x_use = [None, None]
    prev_ln = None
    cp_all = []
    with nc.sbuf_tensor("wq_f", [128, D], F32) as wq_f, \
            nc.sbuf_tensor("wq_f2", [128, D], F32) as wq_f2, \
            nc.sbuf_tensor("sa", [128, D], BF16) as sa, \
            nc.sbuf_tensor("q2a", [128, D], BF16) as q2a, \
            nc.sbuf_tensor("q2b", [128, D], BF16) as q2b:
        for ti in range(nQ):
            b = ti % 2
            t_ld = p.dma("y", x_t[b][:, :], x1_d[ti * 128:(ti + 1) * 128, :],
                         f"ldx{b}", waits=[x_use[b]] if x_use[b] else [])
            t_n = emit_ln(p, x_t[b][:, :], bjunk[:, :], ev[:, :], st,
                          waits=[t_ld] + ([prev_ln] if prev_ln else []))
            prev_ln = t_n
            x_use[b] = t_n
            cp_toks = []
            for c in range(16):
                bk = tp_ps[c % 2]
                w = [t_n]
                if len(cp_toks) >= 2:
                    w.append(cp_toks[-2])
                t_tr = p.op("t", "transpose", bk[:, :],
                            bjunk[:, c * 128:(c + 1) * 128], C.ident[:, :],
                            waits=w)
                cp_toks.append(p.op("v", "tensor_copy",
                                    xn_fm[:, c * SQ + ti * 128:
                                          c * SQ + (ti + 1) * 128],
                                    bk[:, :], waits=[t_tr]))
            cp_all.append(cp_toks[-1])
        ln_done = [p.tok("_v"), p.tok("_t")]

    ln_done = ln_done
    with nc.sbuf_tensor("wfu0", [128, D], F32) as wfu0, \
            nc.sbuf_tensor("wfu1", [128, D], F32) as wfu1, \
            nc.sbuf_tensor("wfg0", [128, D], F32) as wfg0, \
            nc.sbuf_tensor("wfg1", [128, D], F32) as wfg1, \
            nc.sbuf_tensor("sau", [128, D], BF16) as sau, \
            nc.sbuf_tensor("sag", [128, D], BF16) as sag, \
            nc.sbuf_tensor("q2u0", [128, D], BF16) as q2u0, \
            nc.sbuf_tensor("q2u1", [128, D], BF16) as q2u1, \
            nc.sbuf_tensor("q2g0", [128, D], BF16) as q2g0, \
            nc.sbuf_tensor("q2g1", [128, D], BF16) as q2g1:
        wfu = [wfu0, wfu1]
        wfg = [wfg0, wfg1]
        q2u = [q2u0, q2u1]
        q2g = [q2g0, q2g1]
        t_ng = p.op("v", "tensor_scalar", nscl[:, 2:3], scl[:, 3:4], -1.0,
                    None, ALU.mult, waits=scl_tok)
        t_ng2 = p.op("v", "tensor_scalar", nscl[:, 0:1], scl[:, 4:5], -1.0,
                     None, ALU.mult, waits=scl_tok)
        t_ng3 = p.op("v", "tensor_scalar", nscl[:, 1:2], scl[:, 5:6], -1.0,
                     None, ALU.mult, waits=scl_tok)
        t_m1 = p.op("v", "memset", thp[:, :], 1.0)
        t_tp = p.op("s", "activation", thp[:, :], thp[:, :], AF.Copy,
                    scale=scl[:, 4:5], waits=[t_m1] + scl_tok)
        t_m2 = p.op("v", "memset", thn[:, :], 1.0)
        t_tn = p.op("s", "activation", thn[:, :], thn[:, :], AF.Copy,
                    scale=nscl[:, 0:1], waits=[t_m2, t_ng2])
        scl_tok = scl_tok + [t_ng, t_ng2, t_ng3, t_tp, t_tn]

        q_use = {}
        f_use = {"sau": None}
        ld_tok = {}
        hid_st = [None, None]
        dq_st = {}

        def issue_loads(ec):
            b = ec % 2
            ld_tok[("u", ec)] = p.dma(
                "s", chunk3(wfu[b].ap(), 128),
                upT_d.ap()[:, ec * 128:(ec + 1) * 128]
                .rearrange("(c p) o -> p c o", p=128), f"lwu{b}",
                waits=[f_use.get(("u", b))] if f_use.get(("u", b)) else [])
            ld_tok[("g", ec)] = p.dma(
                "g", chunk3(wfg[b].ap(), 128),
                gateT_d.ap()[:, ec * 128:(ec + 1) * 128]
                .rearrange("(c p) o -> p c o", p=128), f"lwg{b}",
                waits=[f_use.get(("g", b))] if f_use.get(("g", b)) else [])

        def quantize(ec):
            b = ec % 2
            w1 = [ld_tok[("u", ec)]] + scl_tok + \
                ([q_use[("u", b)]] if ("u", b) in q_use else [])
            if f_use.get("sau"):
                w1.append(f_use["sau"])
            tu1 = p.op("s", "activation", sau[:, :], wfu[b][:, :], AF.Sign,
                       bias=nscl[:, 2:3], waits=w1)
            tu2 = p.op("s", "activation", q2u[b][:, :], wfu[b][:, :],
                       AF.Sign, bias=scl[:, 3:4])
            f_use[("u", b)] = tu2
            tqu = p.op("v", "tensor_tensor", q2u[b][:, :], q2u[b][:, :],
                       sau[:, :], ALU.add, waits=[tu1, tu2])
            f_use["sau"] = tqu
            w2 = [ld_tok[("g", ec)]] + scl_tok + \
                ([q_use[("g", b)]] if ("g", b) in q_use else []) + \
                ([f_use.get("sag")] if f_use.get("sag") else [])
            tg1 = p.op("v", "tensor_tensor", q2g[b][:, :], wfg[b][:, :],
                       thp[:, :], ALU.is_ge, waits=w2)
            tg2 = p.op("v", "tensor_tensor", sag[:, :], wfg[b][:, :],
                       thn[:, :], ALU.is_le)
            f_use[("g", b)] = tg2
            tqg = p.op("v", "tensor_tensor", q2g[b][:, :], q2g[b][:, :],
                       sag[:, :], ALU.subtract, waits=[tg1, tg2])
            f_use["sag"] = tqg
            return tqu, tqg

        issue_loads(0)
        issue_loads(1)
        qtoks = {0: quantize(0)}
        for ec in range(nEC):
            b = ec % 2
            if ec + 1 < nEC:
                if ec + 1 > 1:
                    issue_loads(ec + 1)
                qtoks[ec + 1] = quantize(ec + 1) if ec + 1 < nEC else None
            tqu, tqg = qtoks.pop(ec)
            for tb in range(nTB):
                bu = mm_ps[tb % 2]
                bg = mm_ps[2 + tb % 2]
                w0 = [tqu] + ln_done + \
                    ([mm_free.get(tb % 2)] if mm_free.get(tb % 2) else [])
                w1 = [tqg] + \
                    ([mm_free.get(2 + tb % 2)] if mm_free.get(2 + tb % 2)
                     else [])
                for c in range(16):
                    p.op("t", "matmul", bu[:, :],
                         q2u[b][:, c * 128:(c + 1) * 128],
                         xn_fm[:, c * SQ + tb * 512:c * SQ + (tb + 1) * 512],
                         start=(c == 0), stop=(c == 15),
                         waits=w0 if c == 0 else ())
                for c in range(16):
                    p.op("t", "matmul", bg[:, :],
                         q2g[b][:, c * 128:(c + 1) * 128],
                         xn_fm[:, c * SQ + tb * 512:c * SQ + (tb + 1) * 512],
                         start=(c == 0), stop=(c == 15),
                         waits=w1 if c == 0 else ())
                t_pe = p.tok("_t")
                t_si = p.op("s", "activation", hb[tb % 2][:, :], bu[:, :],
                            AF.Silu, scale=scl[:, 3:4],
                            waits=[t_pe] + ([mm_free.get(("hb", tb % 2))]
                                            if mm_free.get(("hb", tb % 2))
                                            else []))
                mm_free[tb % 2] = t_si
                ob = (ec * nTB + tb) % 2
                t_hm = p.op("v", "tensor_tensor", hout[ob][:, :],
                            hb[tb % 2][:, :], bg[:, :], ALU.mult,
                            waits=[t_si] +
                            ([hid_st[ob]] if hid_st[ob] else []))
                mm_free[("hb", tb % 2)] = t_hm
                mm_free[2 + tb % 2] = t_hm
                hid_st[ob] = p.dma("y",
                                   hid_d[ec * 128:(ec + 1) * 128,
                                         tb * 512:(tb + 1) * 512],
                                   hout[ob][:, :], f"sth{ob}", waits=[t_hm])
            q_use[("u", b)] = p.tok("_t")
            q_use[("g", b)] = p.tok("_t")
    up_done = [p.tok("_t"), p.tok("_v"), p.tok("_s"), p.tok("_g"),
               p.tok("sth0"), p.tok("sth1")]

    with nc.sbuf_tensor("wdb0", [128, nEC * 256], BF16) as wdb0, \
            nc.sbuf_tensor("wdb1", [128, nEC * 256], BF16) as wdb1, \
            nc.sbuf_tensor("hblk0", [128, nEC * 128], BF16) as hblk0, \
            nc.sbuf_tensor("hblk1", [128, nEC * 128], BF16) as hblk1:
        wdb = [wdb0, wdb1]
        hblk = [hblk0, hblk1]
        wd_tok = {}
        wd_use = {}
        yst = [None, None, None, None]
        huse = {}

        def load_wd(hd):
            b = hd % 2
            wd_tok[hd] = p.dma(
                "s", chunk3(wdb[b].ap(), 256, nch=nEC),
                wdq_d.ap()[:, hd * 256:(hd + 1) * 256]
                .rearrange("(c p) o -> p c o", p=128), f"lwq{b}",
                waits=up_done + ([wd_use.get(b)] if wd_use.get(b) else []))

        load_wd(0)
        load_wd(1)
        for hp in range(4):
            hd0, hd1 = 2 * hp, 2 * hp + 1
            for tc in range(nQ):
                hbuf = tc % 2
                t_h = p.dma("g", chunk3(hblk[hbuf].ap(), 128, nch=nEC),
                            hid_d.ap()[:, tc * 128:(tc + 1) * 128]
                            .rearrange("(c p) t -> p c t", p=128),
                            f"lh{hbuf}", waits=up_done +
                            ([huse.get(hbuf)] if huse.get(hbuf) else []))
                for hj, hd in ((0, hd0), (1, hd1)):
                    bk = mm_ps[2 * hj + tc % 2]
                    w0 = [wd_tok[hd], t_h] + \
                        ([mm_free.get(2 * hj + tc % 2)]
                         if mm_free.get(2 * hj + tc % 2) else [])
                    for c in range(nEC):
                        p.op("t", "matmul", bk[:, 0:256],
                             hblk[hbuf][:, c * 128:(c + 1) * 128],
                             wdb[hd % 2][:, c * 256:(c + 1) * 256],
                             start=(c == 0), stop=(c == nEC - 1),
                             waits=w0 if c == 0 else ())
                    t_pe = p.tok("_t")
                    ob = 2 * hj + tc % 2
                    t_y = p.op("s", "activation",
                               ev[:, ob * 256:ob * 256 + 256],
                               bk[:, 0:256], AF.Copy, scale=scl[:, 7:8],
                               waits=[t_pe] +
                               ([yst[ob]] if yst[ob] else []))
                    mm_free[2 * hj + tc % 2] = t_y
                    yst[ob] = p.dma("y", y_d[tc * 128:(tc + 1) * 128,
                                             hd * 256:(hd + 1) * 256],
                                    ev[:, ob * 256:ob * 256 + 256],
                                    f"sty{ob}", waits=[t_y])
                huse[hbuf] = p.tok("_t")
            wd_use[0] = p.tok("_t")
            wd_use[1] = p.tok("_t")
            if 2 * hp + 2 < 8:
                load_wd(2 * hp + 2)
                load_wd(2 * hp + 3)
    down_done = [p.tok("_t"), p.tok("_v"), p.tok("_s"), p.tok("sty0"),
                 p.tok("sty1"), p.tok("sty2"), p.tok("sty3")]

    x_use = [None, None]
    st_tok = [None, None]
    prev_ln = None
    y_t = [nc.alloc_sbuf_tensor(f"y_t{i}", [128, D], F32) for i in range(2)]
    for tc in range(nQ):
        xb = tc % 2
        t_ly = p.dma("s", y_t[xb][:, :], y_d[tc * 128:(tc + 1) * 128, :],
                     f"ly{xb}", waits=down_done +
                     ([x_use[xb]] if x_use[xb] else []))
        t_lx = p.dma("y", x_t[xb][:, :], x1_d[tc * 128:(tc + 1) * 128, :],
                     f"lre{xb}", waits=down_done +
                     ([x_use[xb]] if x_use[xb] else []) +
                     ([st_tok[xb]] if st_tok[xb] else []))
        t_n = emit_ln(p, y_t[xb][:, :], ln_o[:, :], bjunk[:, :], st,
                      waits=[t_ly] + ([prev_ln] if prev_ln else []))
        prev_ln = t_n
        t_add = p.op("v", "tensor_tensor", x_t[xb][:, :], x_t[xb][:, :],
                     ln_o[:, :], ALU.add, waits=[t_n, t_lx])
        x_use[xb] = t_add
        st_tok[xb] = p.dma("y", x2_d[tc * 128:(tc + 1) * 128, :],
                           x_t[xb][:, :], f"stx{xb}", waits=[t_add])
    p.op("y", "nop", waits=[p.tok("stx0"), p.tok("stx1")])
    p.finalize()
    return nc





BLOCK = 256


def build_l1a_local(SQ=1024):
    nQ = SQ // 128
    nTB = SQ // 512
    nB = SQ // BLOCK
    nc = bass.Bass("TRN2", target_bir_lowering=False)
    p = Prog(nc)

    x2_d = nc.dram_tensor("x2", [SQ, D], F32, kind="ExternalInput")
    wt_d = nc.dram_tensor("WqkvuT", [D, 4 * D], F32, kind="ExternalInput")
    sl_d = nc.dram_tensor("slope", [1, NH], F32, kind="ExternalInput")
    sc_d = nc.dram_tensor("sc", [NH * 128, 128], F32, kind="ExternalOutput")
    o_d = nc.dram_tensor("o_part", [SQ, D], F32, kind="ExternalOutput")
    qd_d = nc.dram_tensor("qd", [D, SQ], BF16, kind="ExternalOutput")
    u_d = nc.dram_tensor("u_o", [SQ, D], BF16, kind="ExternalOutput")

    ev = nc.alloc_sbuf_tensor("ev", [128, D], F32)
    x_t = [nc.alloc_sbuf_tensor(f"x_t{i}", [128, D], F32) for i in range(2)]
    bjunk = nc.alloc_sbuf_tensor("bjunk", [128, D], BF16)
    st = nc.alloc_sbuf_tensor("st", [128, 16], F32)
    xn_fm = nc.alloc_sbuf_tensor("xn_fm", [128, 16 * SQ], BF16)
    q_fm = nc.alloc_sbuf_tensor("q_fm", [128, 16 * SQ], BF16)
    k_fm = nc.alloc_sbuf_tensor("k_fm", [128, 16 * SQ], BF16)
    v_tm = nc.alloc_sbuf_tensor("v_tm", [128, nQ * D], BF16)
    uout = [nc.alloc_sbuf_tensor(f"uout{i}", [128, 512], BF16)
            for i in range(2)]
    arr_i = nc.alloc_sbuf_tensor("arr_i", [1, BLOCK], I32)
    arr_f = nc.alloc_sbuf_tensor("arr_f", [1, BLOCK], F32)
    iof = nc.alloc_sbuf_tensor("iof", [128, BLOCK], F32)
    iof2 = nc.alloc_sbuf_tensor("iof2", [128, 2 * BLOCK], F32)
    m01 = nc.alloc_sbuf_tensor("m01", [128, 2 * BLOCK], BF16)
    ddx = nc.alloc_sbuf_tensor("ddx", [128, 2 * BLOCK], F32)
    dd_t = nc.alloc_sbuf_tensor("dd_t", [128, 2 * BLOCK], BF16)
    bdr = nc.alloc_sbuf_tensor("bdr", [1, 4 * NH], F32)
    nsl_bc = nc.alloc_sbuf_tensor("nsl_bc", [128, NH], F32)
    s128_bc = nc.alloc_sbuf_tensor("s128_bc", [128, NH], F32)
    bd_bc = nc.alloc_sbuf_tensor("bd_bc", [128, NH], F32)
    bdp_bc = nc.alloc_sbuf_tensor("bdp_bc", [128, 4 * NH], F32)
    bdp_row = nc.alloc_sbuf_tensor("bdp_row", [1, 4 * NH], F32)
    qrow = nc.alloc_sbuf_tensor("qrow", [1, BLOCK], F32)
    krow = nc.alloc_sbuf_tensor("krow", [1, BLOCK], F32)
    qdec_bc = nc.alloc_sbuf_tensor("qdec_bc", [128, BLOCK], F32)
    kdec_bc = nc.alloc_sbuf_tensor("kdec_bc", [128, BLOCK], F32)
    qdq = nc.alloc_sbuf_tensor("qdq", [128, BLOCK], BF16)
    qds = nc.alloc_sbuf_tensor("qds", [128, BLOCK], BF16)
    kkd = nc.alloc_sbuf_tensor("kkd", [128, BLOCK], BF16)
    kkd_tm = nc.alloc_sbuf_tensor("kkd_tm", [128, BLOCK], BF16)
    qkd = nc.alloc_sbuf_tensor("qkd", [128, 2 * BLOCK], BF16)
    kv32 = nc.alloc_sbuf_tensor("kv32", [128, 128], F32)
    kv_bf = nc.alloc_sbuf_tensor("kv_bf", [128, 128], BF16)
    ob_sb = nc.alloc_sbuf_tensor("ob_sb", [128, 256], F32)

    tp_ps = [nc.alloc_psum_tensor(f"tp{i}", [128, 128], BF16) for i in range(2)]
    mm_ps = [nc.alloc_psum_tensor(f"mm{i}", [128, 512], F32) for i in range(4)]
    db_ps = nc.alloc_psum_tensor("db_ps", [128, 512], F32)
    c_ps = nc.alloc_psum_tensor("c_ps", [128, 128], F32)

    C = Consts(p, fvals=(LN_EPS, -256.0))
    mm_free = {}

    slr = nc.alloc_sbuf_tensor("slr", [1, NH], F32)
    nslr = nc.alloc_sbuf_tensor("nslr", [1, NH], F32)
    bneg = nc.alloc_sbuf_tensor("bneg", [1, NH], F32)
    t_sl = p.dma("y", slr[0:1, :], sl_d[0:1, :], "ldsl")
    t_ns = p.op("v", "tensor_scalar", nslr[0:1, :], slr[0:1, :], -1.0, None,
                ALU.mult, waits=[t_sl])
    t_bn = p.op("v", "tensor_scalar", bneg[0:1, :], slr[0:1, :], -256.0,
                None, ALU.mult, waits=[t_sl])
    t_bd = p.op("s", "activation", bdr[0:1, 0:NH], slr[0:1, :], AF.Exp,
                scale=-256.0, waits=[t_sl])
    t_p0 = p.op("v", "memset", bdp_row[0:1, 0:NH], 1.0)
    t_p1 = p.op("v", "tensor_copy", bdp_row[0:1, NH:2 * NH], bdr[0:1, 0:NH],
                waits=[t_bd])
    t_p2 = p.op("v", "tensor_tensor", bdp_row[0:1, 2 * NH:3 * NH],
                bdp_row[0:1, NH:2 * NH], bdr[0:1, 0:NH], ALU.mult,
                waits=[t_p1])
    t_p3 = p.op("v", "tensor_tensor", bdp_row[0:1, 3 * NH:4 * NH],
                bdp_row[0:1, 2 * NH:3 * NH], bdr[0:1, 0:NH], ALU.mult,
                waits=[t_p2])
    s128r = nc.alloc_sbuf_tensor("s128r", [1, NH], F32)
    t_s1r = p.op("v", "tensor_scalar", s128r[0:1, :], slr[0:1, :], 128.0,
                 None, ALU.mult, waits=[t_sl])
    t_i1 = p.op("g", "iota", arr_i[0:1, :], [[1, BLOCK]], base=1,
                channel_multiplier=0)
    t_i2 = p.op("v", "tensor_copy", arr_f[0:1, :], arr_i[0:1, :],
                waits=[t_i1])
    t_i4 = p.op("g", "iota", iof[:, :], [[1, BLOCK]], base=0,
                channel_multiplier=-1, allow_small_or_imprecise_dtypes=True)
    t_m0 = p.op("v", "tensor_scalar", m01[:, 0:BLOCK], iof[:, :], -0.5,
                None, ALU.is_gt, waits=[t_i4])
    t_m1 = p.op("v", "tensor_scalar", m01[:, BLOCK:], iof[:, :], 127.5,
                None, ALU.is_gt, waits=[t_i4])
    t_v0 = p.op("v", "tensor_scalar", iof2[:, 0:BLOCK], iof[:, :], -0.5,
                None, ALU.is_lt, waits=[t_i4])
    t_v1 = p.op("v", "tensor_scalar", iof2[:, BLOCK:], iof[:, :], 127.5,
                None, ALU.is_lt, waits=[t_i4])
    t_c0 = p.op("v", "scalar_tensor_tensor", iof2[:, 0:BLOCK],
                iof2[:, 0:BLOCK], 1e9, iof[:, :], ALU.mult, ALU.add,
                waits=[t_v0])
    t_c1 = p.op("v", "scalar_tensor_tensor", iof2[:, BLOCK:],
                iof2[:, BLOCK:], 1e9, iof[:, :], ALU.mult, ALU.add,
                waits=[t_v1])

    def bcast_row(dst, src_row, n, w):
        t_b = p.op("t", "matmul", db_ps[:, 0:n], C.ones_rf[0:1, 0:128],
                   src_row, start=True, stop=True,
                   waits=w + ([mm_free.get("db")] if mm_free.get("db")
                              else []))
        t_c = p.op("v", "tensor_copy", dst, db_ps[:, 0:n], waits=[t_b])
        mm_free["db"] = t_c
        return t_c

    t_b1 = bcast_row(nsl_bc[:, :], nslr[0:1, 0:NH], NH, [t_ns])
    t_b2 = bcast_row(s128_bc[:, :], s128r[0:1, 0:NH], NH, [t_s1r])
    t_b3 = bcast_row(bd_bc[:, :], bdr[0:1, 0:NH], NH, [t_bd])
    t_b4 = bcast_row(bdp_bc[:, :], bdp_row[0:1, :], 4 * NH, [t_p0, t_p3])
    dec_done = [t_m0, t_m1, t_b1, t_b2, t_b3, t_b4, t_i2, t_bn, t_c0, t_c1]

    x_use = [None, None]
    prev_ln = None
    for ti in range(nQ):
        b = ti % 2
        t_ld = p.dma("y", x_t[b][:, :], x2_d[ti * 128:(ti + 1) * 128, :],
                     f"ldx{b}", waits=[x_use[b]] if x_use[b] else [])
        t_n = emit_ln(p, x_t[b][:, :], bjunk[:, :], ev[:, :], st,
                      waits=[t_ld] + ([prev_ln] if prev_ln else []))
        prev_ln = t_n
        x_use[b] = t_n
        cp_toks = []
        for c in range(16):
            bk = tp_ps[c % 2]
            w = [t_n]
            if len(cp_toks) >= 2:
                w.append(cp_toks[-2])
            t_tr = p.op("t", "transpose", bk[:, :],
                        bjunk[:, c * 128:(c + 1) * 128], C.ident[:, :],
                        waits=w)
            cp_toks.append(p.op("v", "tensor_copy",
                                xn_fm[:, c * SQ + ti * 128:
                                      c * SQ + (ti + 1) * 128],
                                bk[:, :], waits=[t_tr]))
    ln_done = [p.tok("_v"), p.tok("_t")]

    wuse = [None, None]
    ust = [None, None]
    with nc.sbuf_tensor("wc0", [128, 16 * 128], BF16) as wc0, \
            nc.sbuf_tensor("wc1", [128, 16 * 128], BF16) as wc1, \
            nc.sbuf_tensor("wb0", [128, 16 * 512], BF16) as wb0:
        wc = [wc0, wc1]
        wb = [wb0, wb0]
        for kind in ("q", "k"):
            base = 0 if kind == "q" else D
            dst = q_fm if kind == "q" else k_fm
            for o in range(16):
                b = o % 2
                t_w = p.dma("g", chunk3(wc[b].ap(), 128),
                            wt_d.ap()[:, base + o * 128:base + (o + 1) * 128]
                            .rearrange("(c p) o -> p c o", p=128),
                            f"lw{b}", waits=[wuse[b]] if wuse[b] else [])
                for tb in range(nTB):
                    bk = mm_ps[tb % 2]
                    w0 = [t_w] + ln_done + \
                        ([mm_free.get(tb % 2)] if mm_free.get(tb % 2)
                         else [])
                    for c in range(16):
                        p.op("t", "matmul", bk[:, :],
                             wc[b][:, c * 128:(c + 1) * 128],
                             xn_fm[:, c * SQ + tb * 512:
                                   c * SQ + (tb + 1) * 512],
                             start=(c == 0), stop=(c == 15),
                             waits=w0 if c == 0 else ())
                    t_e = p.op("s", "activation",
                               dst[:, o * SQ + tb * 512:
                                   o * SQ + (tb + 1) * 512],
                               bk[:, :], AF.Silu, waits=[p.tok("_t")])
                    mm_free[tb % 2] = t_e
                wuse[b] = p.tok("_t")
        for kind in ("v", "u"):
            base = 2 * D if kind == "v" else 3 * D
            for ob in range(4):
                b = 0
                t_w = p.dma("g", chunk3(wb[b].ap(), 512),
                            wt_d.ap()[:, base + ob * 512:base + (ob + 1) * 512]
                            .rearrange("(c p) o -> p c o", p=128),
                            "lwb", waits=[p.tok("_t")])
                for tc in range(nQ):
                    bk = mm_ps[2 + tc % 2]
                    w0 = [t_w] + ln_done + \
                        ([mm_free.get(2 + tc % 2)]
                         if mm_free.get(2 + tc % 2) else [])
                    for c in range(16):
                        p.op("t", "matmul", bk[:, :],
                             xn_fm[:, c * SQ + tc * 128:
                                   c * SQ + (tc + 1) * 128],
                             wb[b][:, c * 512:(c + 1) * 512],
                             start=(c == 0), stop=(c == 15),
                             waits=w0 if c == 0 else ())
                    if kind == "v":
                        t_e = p.op("v", "tensor_copy",
                                   v_tm[:, tc * D + ob * 512:
                                        tc * D + (ob + 1) * 512],
                                   bk[:, :], waits=[p.tok("_t")])
                    else:
                        ou = tc % 2
                        t_e = p.op("v", "tensor_copy", uout[ou][:, :],
                                   bk[:, :],
                                   waits=[p.tok("_t")] +
                                   ([ust[ou]] if ust[ou] else []))
                        ust[ou] = p.dma("y",
                                        u_d[tc * 128:(tc + 1) * 128,
                                            ob * 512:(ob + 1) * 512],
                                        uout[ou][:, :], f"stu{ou}",
                                        waits=[t_e])
                    mm_free[2 + tc % 2] = t_e
                wuse[b] = p.tok("_t")
    proj_done = [p.tok("_t"), p.tok("_v"), p.tok("_s")]

    st_sc = None
    st_qd = None
    st_ob = None
    for h in range(NH):
        t_q = p.op("s", "activation", qrow[0:1, :], arr_f[0:1, :], AF.Exp,
                   scale=nslr[0:1, h:h + 1],
                   waits=dec_done + proj_done +
                   ([mm_free.get("qrow")] if mm_free.get("qrow") else []))
        t_k = p.op("s", "activation", krow[0:1, :], arr_f[0:1, :], AF.Exp,
                   scale=slr[0:1, h:h + 1], bias=bneg[0:1, h:h + 1],
                   waits=[t_q] + ([mm_free.get("krow")]
                                  if mm_free.get("krow") else []))
        t_qb = bcast_row(qdec_bc[:, :], qrow[0:1, :], BLOCK, [t_q])
        mm_free["qrow"] = t_qb
        t_kb = bcast_row(kdec_bc[:, :], krow[0:1, :], BLOCK, [t_k])
        mm_free["krow"] = t_kb
        dd_toks = []
        for kt in range(2):
            t_x = p.op("s", "activation", ddx[:, kt * BLOCK:(kt + 1) * BLOCK],
                       iof2[:, kt * BLOCK:(kt + 1) * BLOCK], AF.Exp,
                       scale=nsl_bc[:, h:h + 1],
                       bias=(s128_bc[:, h:h + 1] if kt == 1 else 0.0),
                       waits=dec_done + [t_k] +
                       ([mm_free.get(("dd", kt))]
                        if mm_free.get(("dd", kt)) else []))
            t_d = p.op("v", "tensor_tensor",
                       dd_t[:, kt * BLOCK:(kt + 1) * BLOCK],
                       ddx[:, kt * BLOCK:(kt + 1) * BLOCK],
                       m01[:, kt * BLOCK:(kt + 1) * BLOCK], ALU.mult,
                       waits=[t_x])
            mm_free[("dd", kt)] = t_d
            dd_toks.append(t_d)
        t_z = p.op("v", "memset", kv32[:, :], 0.0, waits=[st_sc] if st_sc
                   else [])
        t_zb = p.op("v", "memset", kv_bf[:, :], 0.0, waits=[t_z])
        kv_tok = t_zb
        for i in range(nB):
            tbase = i * BLOCK
            t_1 = p.op("v", "tensor_tensor", qdq[:, :],
                       q_fm[:, h * SQ + tbase:h * SQ + tbase + BLOCK],
                       qdec_bc[:, :], ALU.mult,
                       waits=[t_qb, kv_tok] + proj_done)
            t_2 = p.op("v", "tensor_scalar", qds[:, :], qdq[:, :],
                       bdp_bc[:, (i * NH + h):(i * NH + h) + 1], None,
                       ALU.mult, waits=[t_1] + ([st_qd] if st_qd else []))
            st_qd = p.dma("y", qd_d[h * 128:(h + 1) * 128,
                                    tbase:tbase + BLOCK],
                          qds[:, :], "stqd", waits=[t_2])
            for kt in range(2):
                bk = mm_ps[kt]
                t_s1 = p.op("t", "matmul", bk[:, 0:BLOCK],
                            k_fm[:, h * SQ + tbase + kt * 128:
                                 h * SQ + tbase + (kt + 1) * 128],
                            q_fm[:, h * SQ + tbase:h * SQ + tbase + BLOCK],
                            start=True, stop=True,
                            waits=[kv_tok] +
                            ([mm_free.get(("sc", kt))]
                             if mm_free.get(("sc", kt)) else []))
                t_s2 = p.op("v", "tensor_tensor",
                            qkd[:, kt * BLOCK:(kt + 1) * BLOCK],
                            bk[:, 0:BLOCK],
                            dd_t[:, kt * BLOCK:(kt + 1) * BLOCK], ALU.mult,
                            waits=[t_s1] + dd_toks)
                mm_free[("sc", kt)] = t_s2
            for qs in range(2):
                bk = mm_ps[2 + qs]
                w0 = [mm_free[("sc", 0)], mm_free[("sc", 1)]] + \
                    ([mm_free.get(("o", qs))] if mm_free.get(("o", qs))
                     else [])
                for kt in range(2):
                    vsl = v_tm[:, (2 * i + kt) * D + h * 128:
                               (2 * i + kt) * D + (h + 1) * 128]
                    p.op("t", "matmul", bk[:, 0:128],
                         qkd[:, kt * BLOCK + qs * 128:
                             kt * BLOCK + qs * 128 + 128],
                         vsl, start=(kt == 0), stop=False,
                         waits=w0 if kt == 0 else ())
                p.op("t", "matmul", bk[:, 0:128],
                     qdq[:, qs * 128:qs * 128 + 128], kv_bf[:, :],
                     start=False, stop=True, waits=[kv_tok])
                t_o = p.op("s", "activation", ob_sb[:, qs * 128:
                                                    (qs + 1) * 128],
                           bk[:, 0:128], AF.Copy,
                           waits=[p.tok("_t")] +
                           ([st_ob] if st_ob else []))
                mm_free[("o", qs)] = t_o
            p.dma("y", o_d[tbase:tbase + 128, h * 128:(h + 1) * 128],
                  ob_sb[:, 0:128], "stob", waits=[mm_free[("o", 0)]])
            st_ob = p.dma("y", o_d[tbase + 128:tbase + 256,
                                   h * 128:(h + 1) * 128],
                          ob_sb[:, 128:256], "stob",
                          waits=[mm_free[("o", 1)]])
            t_3 = p.op("v", "tensor_tensor", kkd[:, :],
                       k_fm[:, h * SQ + tbase:h * SQ + tbase + BLOCK],
                       kdec_bc[:, :], ALU.mult,
                       waits=[t_kb] + ([mm_free.get("kkd")]
                                       if mm_free.get("kkd") else []))
            tr_toks = []
            for kt in range(2):
                bk2 = tp_ps[kt]
                t_t = p.op("t", "transpose", bk2[:, :],
                           kkd[:, kt * 128:(kt + 1) * 128], C.ident[:, :],
                           waits=[t_3] + ([mm_free.get(("tp", kt))]
                                          if mm_free.get(("tp", kt))
                                          else []))
                t_c = p.op("v", "tensor_copy",
                           kkd_tm[:, kt * 128:(kt + 1) * 128], bk2[:, :],
                           waits=[t_t])
                mm_free[("tp", kt)] = t_c
                tr_toks.append(t_c)
            mm_free["kkd"] = tr_toks[-1]
            for kt in range(2):
                vsl = v_tm[:, (2 * i + kt) * D + h * 128:
                           (2 * i + kt) * D + (h + 1) * 128]
                p.op("t", "matmul", c_ps[:, :],
                     kkd_tm[:, kt * 128:(kt + 1) * 128], vsl,
                     start=(kt == 0), stop=(kt == 1),
                     waits=(tr_toks + [kv_tok]) if kt == 0 else ())
            t_u = p.op("v", "scalar_tensor_tensor", kv32[:, :], kv32[:, :],
                       bd_bc[:, h:h + 1], c_ps[:, :], ALU.mult, ALU.add,
                       waits=[p.tok("_t"), t_zb])
            kv_tok = p.op("v", "tensor_copy", kv_bf[:, :], kv32[:, :],
                          waits=[t_u])
        st_sc = p.dma("y", sc_d[h * 128:(h + 1) * 128, :], kv32[:, :],
                      "stsc", waits=[kv_tok])
    p.op("y", "nop", waits=[p.tok("stsc"), p.tok("stob"), p.tok("stqd"),
                            p.tok("stu0"), p.tok("stu1")])
    p.finalize()
    return nc






def build_l1a_fix(SQ=1024):
    nQ = SQ // 128
    nc = bass.Bass("TRN2", target_bir_lowering=False)
    p = Prog(nc)

    x2_d = nc.dram_tensor("x2", [SQ, D], F32, kind="ExternalInput")
    o_d = nc.dram_tensor("o_part", [SQ, D], F32, kind="ExternalInput")
    qd_d = nc.dram_tensor("qd", [D, SQ], BF16, kind="ExternalInput")
    u_d = nc.dram_tensor("u_o", [SQ, D], BF16, kind="ExternalInput")
    sp_d = [nc.dram_tensor(f"scp{j}", [NH * 128, 128], F32,
                           kind="ExternalInput") for j in (1, 2, 3)]
    sl_d = nc.dram_tensor("slope", [1, NH], F32, kind="ExternalInput")
    wo_d = nc.dram_tensor("WoutT", [D, D], F32, kind="ExternalInput")
    x3_d = nc.dram_tensor("x3", [SQ, D], F32, kind="ExternalOutput")

    ev = nc.alloc_sbuf_tensor("ev", [128, D], F32)
    ln_o = nc.alloc_sbuf_tensor("ln_o", [128, D], F32)
    x_t = [nc.alloc_sbuf_tensor(f"x_t{i}", [128, D], F32) for i in range(2)]
    bjunk = nc.alloc_sbuf_tensor("bjunk", [128, D], BF16)
    st = nc.alloc_sbuf_tensor("st", [128, 16], F32)
    qd_fm = nc.alloc_sbuf_tensor("qd_fm", [128, 16 * SQ], BF16)
    on_fm = nc.alloc_sbuf_tensor("on_fm", [128, 16 * SQ], BF16)
    o_sb = nc.alloc_sbuf_tensor("o_sb", [128, D], F32)
    u_sb = nc.alloc_sbuf_tensor("u_sb", [128, D], BF16)
    slr = nc.alloc_sbuf_tensor("slr", [1, NH], F32)
    bdr = nc.alloc_sbuf_tensor("bdr", [1, 3 * NH], F32)
    bd48 = nc.alloc_sbuf_tensor("bd48", [128, 2 * NH], F32)
    kvin = nc.alloc_sbuf_tensor("kvin", [128, 128 * 3], F32)
    kvin_b = nc.alloc_sbuf_tensor("kvin_b", [128, 16 * 128], BF16)

    tp_ps = [nc.alloc_psum_tensor(f"tp{i}", [128, 128], BF16) for i in range(2)]
    mm_ps = [nc.alloc_psum_tensor(f"mm{i}", [128, 512], F32) for i in range(4)]
    db_ps = nc.alloc_psum_tensor("db_ps", [128, 512], F32)

    C = Consts(p, fvals=(LN_EPS, 1e-6, -256.0))
    mm_free = {}

    t_sl = p.dma("y", slr[0:1, :], sl_d[0:1, :], "ldsl")
    t_bd = p.op("s", "activation", bdr[0:1, 0:NH], slr[0:1, :], AF.Exp,
                scale=-256.0, waits=[t_sl])
    t_b2 = p.op("v", "tensor_tensor", bdr[0:1, NH:2 * NH], bdr[0:1, 0:NH],
                bdr[0:1, 0:NH], ALU.mult, waits=[t_bd])
    t_b4 = p.op("v", "tensor_tensor", bdr[0:1, NH:2 * NH],
                bdr[0:1, NH:2 * NH], bdr[0:1, NH:2 * NH], ALU.mult,
                waits=[t_b2])
    t_b8 = p.op("v", "tensor_tensor", bdr[0:1, 2 * NH:3 * NH],
                bdr[0:1, NH:2 * NH], bdr[0:1, NH:2 * NH], ALU.mult,
                waits=[t_b4])
    t_bb = p.op("t", "matmul", db_ps[:, 0:2 * NH], C.ones_rf[0:1, 0:128],
                bdr[0:1, NH:3 * NH], start=True, stop=True, waits=[t_b8])
    t_bc = p.op("v", "tensor_copy", bd48[:, :], db_ps[:, 0:2 * NH],
                waits=[t_bb])

    with nc.sbuf_tensor("sp_t", [128, 3 * 128], F32) as sp_t:
        kv_toks = []
        for h in range(NH):
            t_l = [p.dma("y", sp_t[:, j * 128:(j + 1) * 128],
                         sp_d[j][h * 128:(h + 1) * 128, :], f"lsp{j}",
                         waits=kv_toks[-1:] if kv_toks else [])
                   for j in range(3)]
            t_1 = p.op("v", "scalar_tensor_tensor", kvin[:, 0:128],
                       sp_t[:, 128:256], bd48[:, h:h + 1],
                       sp_t[:, 0:128], ALU.mult, ALU.add,
                       waits=t_l + [t_bc])
            t_2 = p.op("v", "scalar_tensor_tensor", kvin[:, 128:256],
                       sp_t[:, 256:384], bd48[:, NH + h:NH + h + 1],
                       kvin[:, 0:128], ALU.mult, ALU.add, waits=[t_1])
            kv_toks.append(p.op("v", "tensor_copy",
                                kvin_b[:, h * 128:(h + 1) * 128],
                                kvin[:, 128:256], waits=[t_2]))
    kv_done = [p.tok("_v")]

    t_qd = p.dma("g", chunk3(qd_fm.ap(), SQ),
                 qd_d.ap().rearrange("(c p) t -> p c t", p=128), "ldqd")

    o_use = None
    u_use = None
    prev_ln = None
    for tc in range(nQ):
        t_lo = p.dma("y", o_sb[:, :], o_d[tc * 128:(tc + 1) * 128, :],
                     "ldo", waits=[o_use] if o_use else [])
        t_lu = p.dma("s", u_sb[:, :], u_d[tc * 128:(tc + 1) * 128, :],
                     "ldu", waits=[u_use] if u_use else [])
        add_toks = []
        for h in range(NH):
            bk = mm_ps[h % 2]
            t_m = p.op("t", "matmul", bk[:, 0:128],
                       qd_fm[:, h * SQ + tc * 128:h * SQ + (tc + 1) * 128],
                       kvin_b[:, h * 128:(h + 1) * 128],
                       start=True, stop=True,
                       waits=[t_qd] + kv_done +
                       ([mm_free.get(h % 2)] if mm_free.get(h % 2) else []))
            t_a = p.op("v", "tensor_tensor",
                       o_sb[:, h * 128:(h + 1) * 128],
                       o_sb[:, h * 128:(h + 1) * 128], bk[:, 0:128],
                       ALU.add, waits=[t_m, t_lo] + add_toks[-1:])
            mm_free[h % 2] = t_a
            add_toks.append(t_a)
        t_sq = p.op("s", "activation", bjunk[:, :], o_sb[:, :], AF.Square,
                    accum_out=st[:, 0:1],
                    waits=add_toks + ([prev_ln] if prev_ln else []))
        t_ms = p.op("v", "tensor_scalar", st[:, 1:2], st[:, 0:1], 1.0 / D,
                    None, ALU.mult, waits=[t_sq])
        t_sd = p.op("s", "activation", st[:, 2:3], st[:, 1:2], AF.Sqrt,
                    bias=1e-6, waits=[t_ms])
        t_r = p.op("v", "reciprocal", st[:, 3:4], st[:, 2:3], waits=[t_sd])
        t_n = p.op("v", "tensor_scalar", ln_o[:, :], o_sb[:, :],
                   st[:, 3:4], None, ALU.mult, waits=[t_r])
        t_g = p.op("v", "tensor_tensor", bjunk[:, :], ln_o[:, :],
                   u_sb[:, :], ALU.mult, waits=[t_n, t_lu, t_sq])
        prev_ln = t_g
        u_use = t_g
        o_use = t_g
        cp_toks = []
        for c in range(16):
            bk2 = tp_ps[c % 2]
            w = [t_g]
            if len(cp_toks) >= 2:
                w.append(cp_toks[-2])
            t_tr = p.op("t", "transpose", bk2[:, :],
                        bjunk[:, c * 128:(c + 1) * 128], C.ident[:, :],
                        waits=w)
            cp_toks.append(p.op("v", "tensor_copy",
                                on_fm[:, c * SQ + tc * 128:
                                      c * SQ + (tc + 1) * 128],
                                bk2[:, :], waits=[t_tr]))
        prev_ln = cp_toks[-1]
    gate_done = [p.tok("_v"), p.tok("_t"), p.tok("_s")]

    with nc.sbuf_tensor("wo_sb", [128, 16 * D], BF16) as wo_sb:
        t_wo = p.dma("g", chunk3(wo_sb.ap(), D), wcol_ap(wo_d, 0, D),
                     "ldwo", waits=gate_done)
        x_use = [None, None]
        st_tok = [None, None]
        prev_ln = None
        for tc in range(nQ):
            xb = tc % 2
            ev_toks = []
            for o4 in range(4):
                bidx = o4 % 2
                bank = mm_ps[bidx]
                w0 = [t_wo] + gate_done + \
                    ([mm_free.get(("c", bidx))] if mm_free.get(("c", bidx))
                     else [])
                for c in range(16):
                    p.op("t", "matmul", bank[:, :],
                         on_fm[:, c * SQ + tc * 128:c * SQ + (tc + 1) * 128],
                         wo_sb[:, c * D + o4 * 512:c * D + (o4 + 1) * 512],
                         start=(c == 0), stop=(c == 15),
                         waits=w0 if c == 0 else ())
                w_e = [p.tok("_t")]
                if o4 == 0 and prev_ln:
                    w_e.append(prev_ln)
                t_e = p.op("s", "activation", ev[:, o4 * 512:(o4 + 1) * 512],
                           bank[:, :], AF.Copy, waits=w_e)
                mm_free[("c", bidx)] = t_e
                ev_toks.append(t_e)
            t_ld = p.dma("y", x_t[xb][:, :], x2_d[tc * 128:(tc + 1) * 128, :],
                         f"lre{xb}",
                         waits=([x_use[xb]] if x_use[xb] else []) +
                         ([st_tok[xb]] if st_tok[xb] else []) +
                         ([p.tok("_v")] if tc < 2 else []))
            t_n = emit_ln(p, ev[:, :], ln_o[:, :], bjunk[:, :], st,
                          waits=ev_toks + ([prev_ln] if prev_ln else []))
            prev_ln = t_n
            t_add = p.op("v", "tensor_tensor", x_t[xb][:, :], x_t[xb][:, :],
                         ln_o[:, :], ALU.add, waits=[t_n, t_ld])
            x_use[xb] = t_add
            st_tok[xb] = p.dma("y", x3_d[tc * 128:(tc + 1) * 128, :],
                               x_t[xb][:, :], f"stx{xb}", waits=[t_add])
    p.op("y", "nop", waits=[p.tok("stx0"), p.tok("stx1")])
    p.finalize()
    return nc






def build_ffn1_dequant():
    W1, W2 = 1408, 256
    nc = bass.Bass("TRN2", target_bir_lowering=False)
    p = Prog(nc)
    gup_d = nc.dram_tensor("gup_T", [256, W1], I32, kind="ExternalInput")
    gus_d = nc.dram_tensor("gus", [256, W1], mybir.dt.float16,
                           kind="ExternalInput")
    dwn_d = nc.dram_tensor("dwn_T", [704, W2], I32, kind="ExternalInput")
    dws_d = nc.dram_tensor("dws", [704, W2], mybir.dt.float16,
                           kind="ExternalInput")
    wgu_d = nc.dram_tensor("wgu_sl", [2048, W1], BF16, kind="ExternalOutput")
    wd_d = nc.dram_tensor("wd_sl", [5632, W2], BF16, kind="ExternalOutput")

    W = max(W1, W2)
    pk = nc.alloc_sbuf_tensor("pk", [128, W], I32)
    scf = nc.alloc_sbuf_tensor("scf", [128, W], F32)
    sch = nc.alloc_sbuf_tensor("sch", [128, W], mybir.dt.float16)
    nib = nc.alloc_sbuf_tensor("nib", [128, W], I32)
    ti_ = nc.alloc_sbuf_tensor("ti_", [128, W], I32)
    si_ = nc.alloc_sbuf_tensor("si_", [128, W], I32)
    tf = nc.alloc_sbuf_tensor("tf", [128, W], F32)
    sf = nc.alloc_sbuf_tensor("sf", [128, W], F32)
    g5 = nc.alloc_sbuf_tensor("g5", [128, W], F32)
    g7 = nc.alloc_sbuf_tensor("g7", [128, W], F32)
    acc = nc.alloc_sbuf_tensor("acc", [128, W], F32)
    pl = [nc.alloc_sbuf_tensor(f"pl{i}", [128, W], BF16) for i in range(2)]
    p_ = p

    def dequant(src_d, ssrc_d, dst_d, rows, width, init_waits=()):
        nt = (rows + 127) // 128
        st_tok = [None, None]
        prev = list(init_waits)
        for t in range(nt):
            r0 = t * 128
            pr = min(128, rows - r0)
            t_l = p_.dma("y", pk[0:pr, 0:width], src_d[r0:r0 + pr, :],
                         "ldp", waits=prev)
            t_s = p_.dma("y", sch[0:pr, 0:width], ssrc_d[r0:r0 + pr, :],
                         "lds", waits=prev)
            t_sc = p_.op("v", "tensor_copy", scf[0:pr, 0:width],
                         sch[0:pr, 0:width], waits=[t_s])
            prev = []
            for k in range(8):
                t1 = p_.op("v", "tensor_scalar", nib[0:pr, 0:width],
                           pk[0:pr, 0:width], 4 * k, None,
                           ALU.logical_shift_right, waits=[t_l, t_sc])
                t2 = p_.op("v", "tensor_scalar", ti_[0:pr, 0:width],
                           nib[0:pr, 0:width], 7, None, ALU.bitwise_and,
                           waits=[t1])
                t3 = p_.op("v", "tensor_scalar", si_[0:pr, 0:width],
                           nib[0:pr, 0:width], 3, None,
                           ALU.logical_shift_right, waits=[t2])
                t3b = p_.op("v", "tensor_scalar", si_[0:pr, 0:width],
                            si_[0:pr, 0:width], 1, None, ALU.bitwise_and,
                            waits=[t3])
                t4 = p_.op("v", "tensor_copy", tf[0:pr, 0:width],
                           ti_[0:pr, 0:width], waits=[t2])
                t5 = p_.op("v", "tensor_copy", sf[0:pr, 0:width],
                           si_[0:pr, 0:width], waits=[t3b])
                t6 = p_.op("v", "tensor_scalar", g5[0:pr, 0:width],
                           tf[0:pr, 0:width], 4.5, None, ALU.is_gt,
                           waits=[t4])
                t7 = p_.op("v", "tensor_tensor", acc[0:pr, 0:width],
                           tf[0:pr, 0:width], g5[0:pr, 0:width], ALU.add,
                           waits=[t6])
                t8 = p_.op("v", "tensor_scalar", g5[0:pr, 0:width],
                           tf[0:pr, 0:width], 5.5, None, ALU.is_gt,
                           waits=[t7])
                t9 = p_.op("v", "tensor_tensor", acc[0:pr, 0:width],
                           acc[0:pr, 0:width], g5[0:pr, 0:width], ALU.add,
                           waits=[t8])
                t10 = p_.op("v", "tensor_scalar", g7[0:pr, 0:width],
                            tf[0:pr, 0:width], 6.5, None, ALU.is_gt,
                            waits=[t9])
                t11 = p_.op("v", "scalar_tensor_tensor", acc[0:pr, 0:width],
                            g7[0:pr, 0:width], 3.0, acc[0:pr, 0:width],
                            ALU.mult, ALU.add, waits=[t10])
                t12 = p_.op("v", "tensor_scalar", sf[0:pr, 0:width],
                            sf[0:pr, 0:width], -2.0, 1.0, ALU.mult,
                            ALU.add, waits=[t11, t5])
                t13 = p_.op("v", "tensor_tensor", acc[0:pr, 0:width],
                            acc[0:pr, 0:width], sf[0:pr, 0:width],
                            ALU.mult, waits=[t12])
                t14 = p_.op("v", "tensor_scalar", acc[0:pr, 0:width],
                            acc[0:pr, 0:width], 0.5, None, ALU.mult,
                            waits=[t13])
                ob = k % 2
                t15 = p_.op("v", "tensor_tensor", pl[ob][0:pr, 0:width],
                            acc[0:pr, 0:width], scf[0:pr, 0:width],
                            ALU.mult,
                            waits=[t14] + ([st_tok[ob]] if st_tok[ob]
                                           else []))
                dst = dst_d.ap().rearrange("(ip k) o -> k ip o", k=8)
                st_tok[ob] = p_.dma("s", dst[k, r0:r0 + pr, :],
                                    pl[ob][0:pr, 0:width], f"stw{ob}",
                                    waits=[t15])
                prev = [t15]
        return [p_.tok("stw0"), p_.tok("stw1")]

    dequant(gup_d, gus_d, wgu_d, 256, W1)
    toks = dequant(dwn_d, dws_d, wd_d, 704, W2,
                   init_waits=[p.tok("_v"), p.tok("stw0"), p.tok("stw1")])
    p.op("y", "nop", waits=toks)
    p.finalize()
    return nc




def build_ffn1(SQ=1024):
    I = 5632
    nQ = SQ // 128
    nTB = SQ // 512
    nEC = I // 128
    nc = bass.Bass("TRN2", target_bir_lowering=False)
    p = Prog(nc)

    x3_d = nc.dram_tensor("x3", [SQ, D], F32, kind="ExternalInput")
    wgu_d = nc.dram_tensor("wgu", [D, 2 * I], BF16, kind="ExternalInput")
    wd_d = nc.dram_tensor("wd", [I, D], BF16, kind="ExternalInput")
    x4_d = nc.dram_tensor("x4", [SQ, D], F32, kind="ExternalOutput")
    hid_d = nc.dram_tensor("hid_scr", [I, SQ], BF16)
    y_d = nc.dram_tensor("y_scr", [SQ, D], F32)

    ev = nc.alloc_sbuf_tensor("ev", [128, D], F32)
    ln_o = nc.alloc_sbuf_tensor("ln_o", [128, D], F32)
    x_t = [nc.alloc_sbuf_tensor(f"x_t{i}", [128, D], F32) for i in range(2)]
    bjunk = nc.alloc_sbuf_tensor("bjunk", [128, D], BF16)
    st = nc.alloc_sbuf_tensor("st", [128, 16], F32)
    xn_fm = nc.alloc_sbuf_tensor("xn_fm", [128, 16 * SQ], BF16)
    hb = [nc.alloc_sbuf_tensor(f"hb{i}", [128, 512], BF16) for i in range(2)]
    hout = [nc.alloc_sbuf_tensor(f"hout{i}", [128, 512], BF16)
            for i in range(2)]
    y_t = [nc.alloc_sbuf_tensor(f"y_t{i}", [128, D], F32) for i in range(2)]

    tp_ps = [nc.alloc_psum_tensor(f"tp{i}", [128, 128], BF16) for i in range(2)]
    mm_ps = [nc.alloc_psum_tensor(f"mm{i}", [128, 512], F32) for i in range(4)]

    C = Consts(p, fvals=(LN_EPS, 1e-6))
    mm_free = {}

    x_use = [None, None]
    prev_ln = None
    for ti in range(nQ):
        b = ti % 2
        t_ld = p.dma("y", x_t[b][:, :], x3_d[ti * 128:(ti + 1) * 128, :],
                     f"ldx{b}", waits=[x_use[b]] if x_use[b] else [])
        t_n = emit_ln(p, x_t[b][:, :], bjunk[:, :], ev[:, :], st,
                      waits=[t_ld] + ([prev_ln] if prev_ln else []))
        prev_ln = t_n
        x_use[b] = t_n
        cp_toks = []
        for c in range(16):
            bk = tp_ps[c % 2]
            w = [t_n]
            if len(cp_toks) >= 2:
                w.append(cp_toks[-2])
            t_tr = p.op("t", "transpose", bk[:, :],
                        bjunk[:, c * 128:(c + 1) * 128], C.ident[:, :],
                        waits=w)
            cp_toks.append(p.op("v", "tensor_copy",
                                xn_fm[:, c * SQ + ti * 128:
                                      c * SQ + (ti + 1) * 128],
                                bk[:, :], waits=[t_tr]))
    ln_done = [p.tok("_v"), p.tok("_t")]

    with nc.sbuf_tensor("wg0", [128, 16 * 128], BF16) as wg0, \
            nc.sbuf_tensor("wg1", [128, 16 * 128], BF16) as wg1, \
            nc.sbuf_tensor("wu0", [128, 16 * 128], BF16) as wu0, \
            nc.sbuf_tensor("wu1", [128, 16 * 128], BF16) as wu1:
        wg = [wg0, wg1]
        wu = [wu0, wu1]
        wguse = [None, None]
        hid_st = [None, None]
        for ec in range(nEC):
            b = ec % 2
            t_wg = p.dma("y", chunk3(wg[b].ap(), 128),
                         wgu_d.ap()[:, ec * 128:(ec + 1) * 128]
                         .rearrange("(c p) o -> p c o", p=128),
                         f"lwg{b}", waits=[wguse[b]] if wguse[b] else [])
            t_wu = p.dma("s", chunk3(wu[b].ap(), 128),
                         wgu_d.ap()[:, I + ec * 128:I + (ec + 1) * 128]
                         .rearrange("(c p) o -> p c o", p=128),
                         f"lwu{b}", waits=[wguse[b]] if wguse[b] else [])
            for tb in range(nTB):
                bg = mm_ps[tb % 2]
                bu = mm_ps[2 + tb % 2]
                w0 = [t_wg] + ln_done + \
                    ([mm_free.get(tb % 2)] if mm_free.get(tb % 2) else [])
                w1 = [t_wu] + \
                    ([mm_free.get(2 + tb % 2)] if mm_free.get(2 + tb % 2)
                     else [])
                for c in range(16):
                    p.op("t", "matmul", bg[:, :],
                         wg[b][:, c * 128:(c + 1) * 128],
                         xn_fm[:, c * SQ + tb * 512:c * SQ + (tb + 1) * 512],
                         start=(c == 0), stop=(c == 15),
                         waits=w0 if c == 0 else ())
                for c in range(16):
                    p.op("t", "matmul", bu[:, :],
                         wu[b][:, c * 128:(c + 1) * 128],
                         xn_fm[:, c * SQ + tb * 512:c * SQ + (tb + 1) * 512],
                         start=(c == 0), stop=(c == 15),
                         waits=w1 if c == 0 else ())
                t_pe = p.tok("_t")
                t_si = p.op("s", "activation", hb[tb % 2][:, :], bg[:, :],
                            AF.Silu,
                            waits=[t_pe] + ([mm_free.get(("hb", tb % 2))]
                                            if mm_free.get(("hb", tb % 2))
                                            else []))
                mm_free[tb % 2] = t_si
                ob = (ec * nTB + tb) % 2
                t_hm = p.op("v", "tensor_tensor", hout[ob][:, :],
                            hb[tb % 2][:, :], bu[:, :], ALU.mult,
                            waits=[t_si] + ([hid_st[ob]] if hid_st[ob]
                                            else []))
                mm_free[("hb", tb % 2)] = t_hm
                mm_free[2 + tb % 2] = t_hm
                hid_st[ob] = p.dma("y",
                                   hid_d[ec * 128:(ec + 1) * 128,
                                         tb * 512:(tb + 1) * 512],
                                   hout[ob][:, :], f"sth{ob}", waits=[t_hm])
            wguse[b] = p.tok("_t")
    up_done = [p.tok("_t"), p.tok("_v"), p.tok("_s"), p.tok("_g"),
               p.tok("sth0"), p.tok("sth1"), p.tok("stwd0"), p.tok("stwd1")]

    with nc.sbuf_tensor("wdb0", [128, nEC * 256], BF16) as wdb0, \
            nc.sbuf_tensor("wdb1", [128, nEC * 256], BF16) as wdb1, \
            nc.sbuf_tensor("hblk0", [128, nEC * 128], BF16) as hblk0, \
            nc.sbuf_tensor("hblk1", [128, nEC * 128], BF16) as hblk1:
        wdb = [wdb0, wdb1]
        hblk = [hblk0, hblk1]
        wd_tok = {}
        wd_use = {}
        yst = [None, None, None, None]
        huse = {}

        def load_wd(hd):
            b = hd % 2
            wd_tok[hd] = p.dma(
                "s", chunk3(wdb[b].ap(), 256, nch=nEC),
                wd_d.ap()[:, hd * 256:(hd + 1) * 256]
                .rearrange("(c p) o -> p c o", p=128), f"lwq{b}",
                waits=up_done + ([wd_use.get(b)] if wd_use.get(b) else []))

        load_wd(0)
        load_wd(1)
        for hp in range(4):
            hd0, hd1 = 2 * hp, 2 * hp + 1
            for tc in range(nQ):
                hbuf = tc % 2
                t_h = p.dma("g", chunk3(hblk[hbuf].ap(), 128, nch=nEC),
                            hid_d.ap()[:, tc * 128:(tc + 1) * 128]
                            .rearrange("(c p) t -> p c t", p=128),
                            f"lh{hbuf}", waits=up_done +
                            ([huse.get(hbuf)] if huse.get(hbuf) else []))
                for hj, hd in ((0, hd0), (1, hd1)):
                    bk = mm_ps[2 * hj + tc % 2]
                    w0 = [wd_tok[hd], t_h] + \
                        ([mm_free.get(2 * hj + tc % 2)]
                         if mm_free.get(2 * hj + tc % 2) else [])
                    for c in range(nEC):
                        p.op("t", "matmul", bk[:, 0:256],
                             hblk[hbuf][:, c * 128:(c + 1) * 128],
                             wdb[hd % 2][:, c * 256:(c + 1) * 256],
                             start=(c == 0), stop=(c == nEC - 1),
                             waits=w0 if c == 0 else ())
                    t_pe = p.tok("_t")
                    ob = 2 * hj + tc % 2
                    t_y = p.op("s", "activation",
                               ev[:, ob * 256:ob * 256 + 256],
                               bk[:, 0:256], AF.Copy,
                               waits=[t_pe] + ([yst[ob]] if yst[ob]
                                               else []))
                    mm_free[2 * hj + tc % 2] = t_y
                    yst[ob] = p.dma("y", y_d[tc * 128:(tc + 1) * 128,
                                             hd * 256:(hd + 1) * 256],
                                    ev[:, ob * 256:ob * 256 + 256],
                                    f"sty{ob}", waits=[t_y])
                huse[hbuf] = p.tok("_t")
            wd_use[0] = p.tok("_t")
            wd_use[1] = p.tok("_t")
            if 2 * hp + 2 < 8:
                load_wd(2 * hp + 2)
                load_wd(2 * hp + 3)
    down_done = [p.tok("_t"), p.tok("_v"), p.tok("_s"), p.tok("sty0"),
                 p.tok("sty1"), p.tok("sty2"), p.tok("sty3")]

    x_use = [None, None]
    st_tok = [None, None]
    prev_ln = None
    for tc in range(nQ):
        xb = tc % 2
        t_ly = p.dma("s", y_t[xb][:, :], y_d[tc * 128:(tc + 1) * 128, :],
                     f"ly{xb}", waits=down_done +
                     ([x_use[xb]] if x_use[xb] else []))
        t_lx = p.dma("y", x_t[xb][:, :], x3_d[tc * 128:(tc + 1) * 128, :],
                     f"lre{xb}", waits=down_done +
                     ([x_use[xb]] if x_use[xb] else []) +
                     ([st_tok[xb]] if st_tok[xb] else []))
        t_sq = p.op("s", "activation", bjunk[:, :], y_t[xb][:, :],
                    AF.Square, accum_out=st[:, 8:9],
                    waits=[t_ly] + ([prev_ln] if prev_ln else []))
        t_ms = p.op("v", "tensor_scalar", st[:, 9:10], st[:, 8:9], 1.0 / D,
                    None, ALU.mult, waits=[t_sq])
        t_sd = p.op("s", "activation", st[:, 10:11], st[:, 9:10], AF.Sqrt,
                    bias=1e-6, waits=[t_ms])
        t_r = p.op("v", "reciprocal", st[:, 11:12], st[:, 10:11],
                   waits=[t_sd])
        t_rm = p.op("v", "tensor_scalar", y_t[xb][:, :], y_t[xb][:, :],
                    st[:, 11:12], None, ALU.mult, waits=[t_r])
        t_n = emit_ln(p, y_t[xb][:, :], ln_o[:, :], bjunk[:, :], st,
                      waits=[t_rm])
        prev_ln = t_n
        t_add = p.op("v", "tensor_tensor", x_t[xb][:, :], x_t[xb][:, :],
                     ln_o[:, :], ALU.add, waits=[t_n, t_lx])
        x_use[xb] = t_add
        st_tok[xb] = p.dma("y", x4_d[tc * 128:(tc + 1) * 128, :],
                           x_t[xb][:, :], f"stx{xb}", waits=[t_add])
    p.op("y", "nop", waits=[p.tok("stx0"), p.tok("stx1")])
    p.finalize()
    return nc





S_FULL = 4096
SQ_OWN = 1024
CORES = list(range(NCORES))


def _run(nc, in_maps):
    return run_bass_kernel_spmd(nc, in_maps, core_ids=CORES)


def kernel(L0_attn_x, L0_attn_Wq, L0_attn_bq, L0_attn_Wk, L0_attn_bk,
           L0_attn_Wv, L0_attn_bv, L0_attn_Wo, L0_attn_bo, L0_attn_Wi,
           L0_attn_bi, L0_ffn_W_up, L0_ffn_W_gate, L0_ffn_W_down,
           L1_attn_slope_rate, L1_attn_W_qkvu, L1_attn_W_out,
           L1_ffn_gate_up_packed, L1_ffn_gate_up_scales,
           L1_ffn_down_packed, L1_ffn_down_scales):
    f32 = np.float32
    x = np.ascontiguousarray(np.asarray(L0_attn_x, dtype=f32))
    B = x.shape[0]
    row = lambda a: np.ascontiguousarray(np.asarray(a, f32).reshape(1, -1))

    W_up = np.asarray(L0_ffn_W_up, f32)
    W_gate = np.asarray(L0_ffn_W_gate, f32)
    W_down = np.asarray(L0_ffn_W_down, f32)
    gup = np.asarray(L1_ffn_gate_up_packed)
    gus = np.asarray(L1_ffn_gate_up_scales)
    dwn = np.asarray(L1_ffn_down_packed)
    dws = np.asarray(L1_ffn_down_scales)
    gupT = np.ascontiguousarray(gup.T)
    gusE = np.repeat(gus, 16, axis=0)
    dwnT = np.ascontiguousarray(dwn.T)
    dwsE = np.repeat(dws, 16, axis=0)

    nc = build_l0pre(SQ_OWN)
    common = {
        "Wq": np.asarray(L0_attn_Wq, f32), "Wk": np.asarray(L0_attn_Wk, f32),
        "Wv": np.asarray(L0_attn_Wv, f32),
        "bq": row(L0_attn_bq), "bk": row(L0_attn_bk), "bv": row(L0_attn_bv),
        "Wi": np.ascontiguousarray(np.asarray(L0_attn_Wi, f32).T),
        "bi": np.asarray(L0_attn_bi, f32).reshape(1, 1),
    }
    in_maps = []
    for c in CORES:
        b, qq = c // 4, c % 4
        in_maps.append(dict(
            common,
            x_own=np.ascontiguousarray(x[b, qq * SQ_OWN:(qq + 1) * SQ_OWN]),
            up_sl=np.ascontiguousarray(W_up[c * 1024:(c + 1) * 1024]),
            gate_sl=np.ascontiguousarray(W_gate[c * 1024:(c + 1) * 1024]),
            down_sl=np.ascontiguousarray(W_down[:, c * 1024:(c + 1) * 1024].T),
            gup_T=np.ascontiguousarray(gupT[:, c * 1408:(c + 1) * 1408]),
            gus=np.ascontiguousarray(gusE[:, c * 1408:(c + 1) * 1408]),
            dwn_T=np.ascontiguousarray(dwnT[:, c * 256:(c + 1) * 256]),
            dws=np.ascontiguousarray(dwsE[:, c * 256:(c + 1) * 256])))
    res = _run(nc, in_maps)
    q_o = [np.asarray(res.results[c]["q_o"]) for c in CORES]
    k_o = [np.asarray(res.results[c]["k_o"]) for c in CORES]
    v_o = [np.asarray(res.results[c]["v_o"]) for c in CORES]
    imp_o = [np.asarray(res.results[c]["imp_o"]) for c in CORES]
    parts = np.concatenate([np.asarray(res.results[c]["parts"])
                            for c in CORES], axis=0)
    wgu = np.concatenate([np.asarray(res.results[c]["wgu_sl"])
                          for c in CORES], axis=1)
    wd = np.concatenate([np.asarray(res.results[c]["wd_sl"])
                         for c in CORES], axis=1)

    W_downT = np.ascontiguousarray(W_down.T)
    nc = build_l0main(S_FULL, SQ_OWN)
    in_maps = []
    for c in CORES:
        b, qq = c // 4, c % 4
        grp = list(range(b * 4, b * 4 + 4))
        in_maps.append({
            "x_own": np.ascontiguousarray(
                x[b, qq * SQ_OWN:(qq + 1) * SQ_OWN]),
            "q": q_o[c],
            "kf": np.ascontiguousarray(
                np.concatenate([k_o[g] for g in grp], axis=1)),
            "vf": np.ascontiguousarray(
                np.concatenate([v_o[g] for g in grp], axis=0)),
            "impf": np.ascontiguousarray(
                np.concatenate([imp_o[g] for g in grp], axis=1)),
            "down_slq": np.ascontiguousarray(
                W_downT[c * 1024:(c + 1) * 1024]),
            "parts": np.ascontiguousarray(parts),
            "Wo": np.asarray(L0_attn_Wo, f32), "bo": row(L0_attn_bo)})
    res = _run(nc, in_maps)
    x1 = [np.asarray(res.results[c]["x1"]) for c in CORES]
    wdq = np.concatenate([np.asarray(res.results[c]["wdq_sl"])
                          for c in CORES], axis=0)

    nc = build_ffn0(SQ_OWN)
    common = {
        "upT": np.ascontiguousarray(W_up.T),
        "gateT": np.ascontiguousarray(W_gate.T),
        "wdq": wdq,
        "parts": np.ascontiguousarray(parts),
    }
    res = _run(nc, [dict(common, x1=x1[c]) for c in CORES])
    x2 = [np.asarray(res.results[c]["x2"]) for c in CORES]

    slope = np.ascontiguousarray(
        np.asarray(L1_attn_slope_rate, f32).reshape(1, NH))
    nc = build_l1a_local(SQ_OWN)
    common = {
        "WqkvuT": np.ascontiguousarray(np.asarray(L1_attn_W_qkvu, f32).T),
        "slope": slope,
    }
    res = _run(nc, [dict(common, x2=x2[c]) for c in CORES])
    sc = [np.asarray(res.results[c]["sc"]) for c in CORES]
    o_part = [np.asarray(res.results[c]["o_part"]) for c in CORES]
    qd = [np.asarray(res.results[c]["qd"]) for c in CORES]
    u_o = [np.asarray(res.results[c]["u_o"]) for c in CORES]

    z = np.zeros((NH * 128, 128), f32)
    nc = build_l1a_fix(SQ_OWN)
    in_maps = []
    for c in CORES:
        b, qq = c // 4, c % 4
        pred = lambda j: sc[b * 4 + qq - j] if qq - j >= 0 else z
        in_maps.append({
            "x2": x2[c], "o_part": o_part[c], "qd": qd[c], "u_o": u_o[c],
            "scp1": pred(1), "scp2": pred(2), "scp3": pred(3),
            "slope": slope,
            "WoutT": np.ascontiguousarray(np.asarray(L1_attn_W_out, f32).T),
        })
    res = _run(nc, in_maps)
    x3 = [np.asarray(res.results[c]["x3"]) for c in CORES]

    nc = build_ffn1(SQ_OWN)
    common = {"wgu": np.ascontiguousarray(wgu),
              "wd": np.ascontiguousarray(wd)}
    res = _run(nc, [dict(common, x3=x3[c]) for c in CORES])
    x4 = [np.asarray(res.results[c]["x4"]) for c in CORES]

    out = np.empty_like(x)
    for c in CORES:
        b, qq = c // 4, c % 4
        out[b, qq * SQ_OWN:(qq + 1) * SQ_OWN] = x4[c]
    return out






def build_l0pre(SQ=1024):
    nQ = SQ // 128
    nTB = SQ // 512
    nc = bass.Bass("TRN2", target_bir_lowering=False)
    p = Prog(nc)

    x_d = nc.dram_tensor("x_own", [SQ, D], F32, kind="ExternalInput")
    wq_d = nc.dram_tensor("Wq", [D, D], F32, kind="ExternalInput")
    wk_d = nc.dram_tensor("Wk", [D, D], F32, kind="ExternalInput")
    wv_d = nc.dram_tensor("Wv", [D, D], F32, kind="ExternalInput")
    bq_d = nc.dram_tensor("bq", [1, D], F32, kind="ExternalInput")
    bk_d = nc.dram_tensor("bk", [1, D], F32, kind="ExternalInput")
    bv_d = nc.dram_tensor("bv", [1, D], F32, kind="ExternalInput")
    wi_d = nc.dram_tensor("Wi", [1, D], F32, kind="ExternalInput")
    bi_d = nc.dram_tensor("bi", [1, 1], F32, kind="ExternalInput")
    q_od = nc.dram_tensor("q_o", [D, SQ], BF16, kind="ExternalOutput")
    k_od = nc.dram_tensor("k_o", [D, SQ], BF16, kind="ExternalOutput")
    v_od = nc.dram_tensor("v_o", [SQ, D], BF16, kind="ExternalOutput")
    imp_od = nc.dram_tensor("imp_o", [128, nQ], F32, kind="ExternalOutput")
    psl = [nc.dram_tensor(n, [1024, D], F32, kind="ExternalInput")
           for n in ("up_sl", "gate_sl", "down_sl")]
    parts_od = nc.dram_tensor("parts", [1, 4], F32, kind="ExternalOutput")
    W1, W2 = 1408, 256
    gup_d = nc.dram_tensor("gup_T", [256, W1], I32, kind="ExternalInput")
    gus_d = nc.dram_tensor("gus", [256, W1], mybir.dt.float16,
                           kind="ExternalInput")
    dwn_d = nc.dram_tensor("dwn_T", [704, W2], I32, kind="ExternalInput")
    dws_d = nc.dram_tensor("dws", [704, W2], mybir.dt.float16,
                           kind="ExternalInput")
    wgu_od = nc.dram_tensor("wgu_sl", [2048, W1], BF16, kind="ExternalOutput")
    wd_od = nc.dram_tensor("wd_sl", [5632, W2], BF16, kind="ExternalOutput")

    x_t = [nc.alloc_sbuf_tensor(f"x_t{i}", [128, D], F32) for i in range(2)]
    st = nc.alloc_sbuf_tensor("st", [128, 24], F32)
    xn_fm = nc.alloc_sbuf_tensor("xn_fm", [128, 16 * SQ], BF16)
    imp_sb = nc.alloc_sbuf_tensor("imp_sb", [128, nQ], F32)
    bias_sb = nc.alloc_sbuf_tensor("bias_sb", [1, D], BF16)
    wirow = nc.alloc_sbuf_tensor("wirow", [1, D], F32)
    bif = nc.alloc_sbuf_tensor("bif", [1, 1], F32)
    swi_c = nc.alloc_sbuf_tensor("swi_c", [128, 1], F32)
    bi_c = nc.alloc_sbuf_tensor("bi_c", [128, 1], F32)
    kv_out = [nc.alloc_sbuf_tensor(f"kv_out{i}", [128, 512], BF16)
              for i in range(2)]
    o_sb = nc.alloc_sbuf_tensor("po_sb", [1, 4], F32)
    red = nc.alloc_sbuf_tensor("pred", [128, 4], F32)
    pw0 = nc.alloc_sbuf_tensor("pw0", [128, D], F32)
    pw1 = nc.alloc_sbuf_tensor("pw1", [128, D], F32)
    pacc = nc.alloc_sbuf_tensor("pacc", [128, 26], F32)
    W1d, W2d = 1408, 256
    W = max(W1d, W2d)
    pk = nc.alloc_sbuf_tensor("dq_pk", [128, W], I32)
    scf = nc.alloc_sbuf_tensor("dq_scf", [128, W], F32)
    sch = nc.alloc_sbuf_tensor("dq_sch", [128, W], mybir.dt.float16)
    nib = nc.alloc_sbuf_tensor("dq_nib", [128, W], I32)
    ti_ = nc.alloc_sbuf_tensor("dq_ti", [128, W], I32)
    si_ = nc.alloc_sbuf_tensor("dq_si", [128, W], I32)
    tf = nc.alloc_sbuf_tensor("dq_tf", [128, W], F32)
    sf = nc.alloc_sbuf_tensor("dq_sf", [128, W], F32)
    g5 = nc.alloc_sbuf_tensor("dq_g5", [128, W], F32)
    g7 = nc.alloc_sbuf_tensor("dq_g7", [128, W], F32)
    dacc = nc.alloc_sbuf_tensor("dq_acc", [128, W], F32)
    pl = [nc.alloc_sbuf_tensor(f"dq_pl{i}", [128, W], BF16) for i in range(2)]

    tp_ps = [nc.alloc_psum_tensor(f"tp{i}", [128, 128], BF16) for i in range(2)]
    mm_ps = [nc.alloc_psum_tensor(f"mm{i}", [128, 512], F32) for i in range(4)]
    db_ps = nc.alloc_psum_tensor("db_ps", [128, 512], F32)

    C = Consts(p, fvals=(LN_EPS,))
    mm_free = {}

    t_wi = p.dma("y", wirow[0:1, :], wi_d[0:1, :], "ldwi")
    t_bi = p.dma("y", bif[0:1, :], bi_d[0:1, :], "ldbi")

    with nc.sbuf_tensor("bj0", [128, D], BF16) as bj0, \
            nc.sbuf_tensor("bj1", [128, D], BF16) as bj1, \
            nc.sbuf_tensor("jk0", [128, D], BF16) as jk0, \
            nc.sbuf_tensor("jk1", [128, D], BF16) as jk1, \
            nc.sbuf_tensor("jk2", [128, D], BF16) as jk2, \
            nc.sbuf_tensor("jk3", [128, D], BF16) as jk3, \
            nc.sbuf_tensor("ij0", [128, D], F32) as ij0, \
            nc.sbuf_tensor("ij1", [128, D], F32) as ij1, \
            nc.sbuf_tensor("wi_bc", [128, D], F32) as wi_bc:
        bj = [bj0, bj1]
        jka = [jk0, jk1]
        jkb = [jk2, jk3]
        ij = [ij0, ij1]
        wb_toks = []
        for ob in range(4):
            t_wb = p.op("t", "matmul", db_ps[:, :], C.ones_rf[0:1, 0:128],
                        wirow[0:1, ob * 512:(ob + 1) * 512], start=True,
                        stop=True, waits=[t_wi] + (wb_toks[-1:] or []))
            wb_toks.append(p.op("v", "tensor_copy",
                                wi_bc[:, ob * 512:(ob + 1) * 512],
                                db_ps[:, :], waits=[t_wb]))
        t_bb = p.op("t", "matmul", db_ps[:, 0:1], C.ones_rf[0:1, 0:128],
                    bif[0:1, 0:1], start=True, stop=True,
                    waits=[t_bi] + wb_toks[-1:])
        t_bic = p.op("v", "tensor_copy", bi_c[:, :], db_ps[:, 0:1],
                     waits=[t_bb])
        t_swi = p.op("v", "tensor_reduce", swi_c[:, :], wi_bc[:, :],
                     mybir.AxisListType.X, ALU.add, waits=wb_toks)
        imp_setup = [t_bic, t_swi]

        x_use = [None, None]
        bj_use = [None, None]
        ij_use = [None, None]
        ln_tok = [None, None]
        for ti in range(nQ):
            q = ti % 2
            t_ld = p.dma("y", x_t[q][:, :], x_d[ti * 128:(ti + 1) * 128, :],
                         f"ldx{q}", waits=list(x_use[q] or ()))
            t_n = emit_ln(p, x_t[q][:, :], bj[q][:, :], jka[q][:, :], st,
                          waits=[t_ld] + ([bj_use[q]] if bj_use[q] else []) +
                          ([ln_tok[q]] if ln_tok[q] else []),
                          sc=8 * q, junk2_ap=jkb[q][:, :])
            ln_tok[q] = t_n
            t_mu = p.op("g", "tensor_tensor", ij[q][:, :], x_t[q][:, :],
                        wi_bc[:, :], ALU.mult,
                        waits=[t_ld] + imp_setup +
                        ([ij_use[q]] if ij_use[q] else []))
            t_us = p.op("v", "tensor_reduce", st[:, 16 + 4 * q:17 + 4 * q],
                        ij[q][:, :], mybir.AxisListType.X, ALU.add,
                        waits=[t_mu])
            ij_use[q] = t_us
            x_use[q] = (t_n, t_mu, p.tok("_s"))
            t_i1 = p.op("v", "tensor_tensor", st[:, 17 + 4 * q:18 + 4 * q],
                        st[:, 8 * q + 2:8 * q + 3], swi_c[:, :], ALU.mult,
                        waits=[t_us, t_n])
            t_i2 = p.op("v", "scalar_tensor_tensor",
                        st[:, 18 + 4 * q:19 + 4 * q],
                        st[:, 17 + 4 * q:18 + 4 * q], -1.0,
                        st[:, 16 + 4 * q:17 + 4 * q], ALU.mult, ALU.add,
                        waits=[t_i1])
            t_i3 = p.op("v", "tensor_tensor", st[:, 19 + 4 * q:20 + 4 * q],
                        st[:, 18 + 4 * q:19 + 4 * q],
                        st[:, 8 * q + 6:8 * q + 7], ALU.mult, waits=[t_i2])
            t_i4 = p.op("v", "tensor_tensor", imp_sb[:, ti:ti + 1],
                        st[:, 19 + 4 * q:20 + 4 * q], bi_c[:, :],
                        ALU.add, waits=[t_i3])
            cp_toks = []
            for c in range(16):
                bk = tp_ps[c % 2]
                w = [t_n]
                if len(cp_toks) >= 2:
                    w.append(cp_toks[-2])
                t_tr = p.op("t", "transpose", bk[:, :],
                            bj[q][:, c * 128:(c + 1) * 128], C.ident[:, :],
                            waits=w)
                cp_toks.append(p.op("v", "tensor_copy",
                                    xn_fm[:, c * SQ + ti * 128:
                                          c * SQ + (ti + 1) * 128],
                                    bk[:, :], waits=[t_tr]))
            bj_use[q] = p.tok("_t")
        t_imp_st = p.dma("y", imp_od[:, :], imp_sb[:, :], "stimp",
                         waits=[t_i4])
    ln_done = [p.tok("_v"), p.tok("_t")]

    kv_tok = [None, None]
    with nc.sbuf_tensor("wc0", [128, 16 * 128], BF16) as wc0, \
            nc.sbuf_tensor("wc1", [128, 16 * 128], BF16) as wc1, \
            nc.sbuf_tensor("wb0", [128, 16 * 512], BF16) as wb0:
        wc = [wc0, wc1]
        wuse = [None, None]
        for kind, w_dram, b_dram, dst in (("q", wq_d, bq_d, q_od),
                                          ("k", wk_d, bk_d, k_od)):
            t_b = p.dma("g", bias_sb[0:1, :], b_dram[0:1, :], "ldbias",
                        waits=[p.tok("_t")])
            for o in range(16):
                b = o % 2
                t_w = p.dma("g", chunk3(wc[b].ap(), 128),
                            wcol_ap(w_dram, o * 128, 128), f"lw{b}",
                            waits=[wuse[b]] if wuse[b] else [])
                for tb in range(nTB):
                    bk = mm_ps[tb % 2]
                    w0 = [t_w, t_b] + ln_done + \
                        ([mm_free.get(tb % 2)] if mm_free.get(tb % 2)
                         else [])
                    for c in range(16):
                        p.op("t", "matmul", bk[:, :],
                             wc[b][:, c * 128:(c + 1) * 128],
                             xn_fm[:, c * SQ + tb * 512:
                                   c * SQ + (tb + 1) * 512],
                             start=(c == 0), stop=False,
                             waits=w0 if c == 0 else ())
                    t_mm = p.op("t", "matmul", bk[:, :],
                                bias_sb[0:1, o * 128:(o + 1) * 128],
                                C.ones_r[0:1, 0:512], start=False,
                                stop=True)
                    ob = (o * nTB + tb) % 2
                    t_e = p.op("v", "tensor_copy", kv_out[ob][:, :],
                               bk[:, :],
                               waits=[t_mm] + ([kv_tok[ob]] if kv_tok[ob]
                                               else []))
                    mm_free[tb % 2] = t_e
                    kv_tok[ob] = p.dma("y",
                                       dst[o * 128:(o + 1) * 128,
                                           tb * 512:(tb + 1) * 512],
                                       kv_out[ob][:, :], f"stkv{ob}",
                                       waits=[t_e])
                wuse[b] = p.tok("_t")
        t_b = p.dma("g", bias_sb[0:1, :], bv_d[0:1, :], "ldbias",
                    waits=[p.tok("_t")])
        for ob4 in range(4):
            t_w = p.dma("g", chunk3(wb0.ap(), 512),
                        wcol_ap(wv_d, ob4 * 512, 512), "lwb",
                        waits=[p.tok("_t")])
            for tc in range(nQ):
                bk = mm_ps[2 + tc % 2]
                w0 = [t_w, t_b] + ln_done + \
                    ([mm_free.get(2 + tc % 2)] if mm_free.get(2 + tc % 2)
                     else [])
                for c in range(16):
                    p.op("t", "matmul", bk[:, :],
                         xn_fm[:, c * SQ + tc * 128:c * SQ + (tc + 1) * 128],
                         wb0[:, c * 512:(c + 1) * 512],
                         start=(c == 0), stop=False,
                         waits=w0 if c == 0 else ())
                t_mm = p.op("t", "matmul", bk[:, :], C.ones_r[0:1, 0:128],
                            bias_sb[0:1, ob4 * 512:(ob4 + 1) * 512],
                            start=False, stop=True)
                ob = (ob4 * nQ + tc) % 2
                t_e = p.op("v", "tensor_copy", kv_out[ob][:, :], bk[:, :],
                           waits=[t_mm] + ([kv_tok[ob]] if kv_tok[ob]
                                           else []))
                mm_free[2 + tc % 2] = t_e
                kv_tok[ob] = p.dma("y",
                                   v_od[tc * 128:(tc + 1) * 128,
                                        ob4 * 512:(ob4 + 1) * 512],
                                   kv_out[ob][:, :], f"stkv{ob}",
                                   waits=[t_e])

    if True:
        acc = pacc
        w_t = [pw0, pw1]
        use_tok = [None, None]
        col = 0
        toks = []
        for mi, w_d in enumerate(psl):
            for ti in range(8):
                b = ti % 2
                t_ld = p.dma("s", w_t[b][:, :],
                             w_d[ti * 128:(ti + 1) * 128, :], f"pld{b}",
                             waits=[use_tok[b]] if use_tok[b] else [])
                t_a = p.op("v", "tensor_reduce", acc[:, col:col + 1],
                           w_t[b][:, :], mybir.AxisListType.X, ALU.add,
                           apply_absolute_value=True,
                           waits=[t_ld] + (toks[-1:] or []))
                use_tok[b] = t_a
                toks.append(t_a)
                col += 1
        rt = []
        for m in range(3):
            rt.append(p.op("v", "tensor_reduce", red[:, m:m + 1],
                           acc[:, m * 8:(m + 1) * 8], mybir.AxisListType.X,
                           ALU.add, waits=toks))
        t_z = p.op("v", "memset", red[:, 3:4], 0.0)
        t_s = p.op("t", "matmul", db_ps[0:1, 0:4], C.ones_cf[:, :],
                   red[:, :], start=True, stop=True,
                   waits=rt + [t_z, p.tok("_v")])
        t_c = p.op("v", "tensor_copy", o_sb[0:1, :], db_ps[0:1, 0:4],
                   waits=[t_s])
        p.dma("y", parts_od[0:1, :], o_sb[0:1, :], "stpo", waits=[t_c])


    def dequant(src_d, ssrc_d, dst_d, rows, width, init_waits=()):
        nt = (rows + 127) // 128
        st_tok = [None, None]
        prev = list(init_waits)
        for t in range(nt):
            r0 = t * 128
            pr = min(128, rows - r0)
            t_l = p.dma("s", pk[0:pr, 0:width], src_d[r0:r0 + pr, :],
                        "dqldp", waits=prev)
            t_s = p.dma("s", sch[0:pr, 0:width], ssrc_d[r0:r0 + pr, :],
                        "dqlds", waits=prev)
            t_sc = p.op("v", "tensor_copy", scf[0:pr, 0:width],
                        sch[0:pr, 0:width], waits=[t_s])
            prev = []
            for k in range(8):
                t1 = p.op("v", "tensor_scalar", nib[0:pr, 0:width],
                          pk[0:pr, 0:width], 4 * k, None,
                          ALU.logical_shift_right, waits=[t_l, t_sc])
                t2 = p.op("v", "tensor_scalar", ti_[0:pr, 0:width],
                          nib[0:pr, 0:width], 7, None, ALU.bitwise_and,
                          waits=[t1])
                t3 = p.op("v", "tensor_scalar", si_[0:pr, 0:width],
                          nib[0:pr, 0:width], 3, None,
                          ALU.logical_shift_right, waits=[t2])
                t3b = p.op("v", "tensor_scalar", si_[0:pr, 0:width],
                           si_[0:pr, 0:width], 1, None, ALU.bitwise_and,
                           waits=[t3])
                t4 = p.op("v", "tensor_copy", tf[0:pr, 0:width],
                          ti_[0:pr, 0:width], waits=[t2])
                t5 = p.op("v", "tensor_copy", sf[0:pr, 0:width],
                          si_[0:pr, 0:width], waits=[t3b])
                t6 = p.op("v", "tensor_scalar", g5[0:pr, 0:width],
                          tf[0:pr, 0:width], 4.5, None, ALU.is_gt,
                          waits=[t4])
                t7 = p.op("v", "tensor_tensor", dacc[0:pr, 0:width],
                          tf[0:pr, 0:width], g5[0:pr, 0:width], ALU.add,
                          waits=[t6])
                t8 = p.op("v", "tensor_scalar", g5[0:pr, 0:width],
                          tf[0:pr, 0:width], 5.5, None, ALU.is_gt,
                          waits=[t7])
                t9 = p.op("v", "tensor_tensor", dacc[0:pr, 0:width],
                          dacc[0:pr, 0:width], g5[0:pr, 0:width], ALU.add,
                          waits=[t8])
                t10 = p.op("v", "tensor_scalar", g7[0:pr, 0:width],
                           tf[0:pr, 0:width], 6.5, None, ALU.is_gt,
                           waits=[t9])
                t11 = p.op("v", "scalar_tensor_tensor", dacc[0:pr, 0:width],
                           g7[0:pr, 0:width], 3.0, dacc[0:pr, 0:width],
                           ALU.mult, ALU.add, waits=[t10])
                t12 = p.op("v", "tensor_scalar", sf[0:pr, 0:width],
                           sf[0:pr, 0:width], -2.0, 1.0, ALU.mult,
                           ALU.add, waits=[t11, t5])
                t13 = p.op("v", "tensor_tensor", dacc[0:pr, 0:width],
                           dacc[0:pr, 0:width], sf[0:pr, 0:width],
                           ALU.mult, waits=[t12])
                t14 = p.op("v", "tensor_scalar", dacc[0:pr, 0:width],
                           dacc[0:pr, 0:width], 0.5, None, ALU.mult,
                           waits=[t13])
                ob = k % 2
                t15 = p.op("v", "tensor_tensor", pl[ob][0:pr, 0:width],
                           dacc[0:pr, 0:width], scf[0:pr, 0:width],
                           ALU.mult,
                           waits=[t14] + ([st_tok[ob]] if st_tok[ob]
                                          else []))
                dst = dst_d.ap().rearrange("(ip k) o -> k ip o", k=8)
                st_tok[ob] = p.dma("s", dst[k, r0:r0 + pr, :],
                                   pl[ob][0:pr, 0:width], f"dqst{ob}",
                                   waits=[t15])
                prev = [t15]
        return [p.tok("dqst0"), p.tok("dqst1")]

    dequant(gup_d, gus_d, wgu_od, 256, W1)
    dq_toks = dequant(dwn_d, dws_d, wd_od, 704, W2,
                      init_waits=[p.tok("_v"), p.tok("dqst0"),
                                  p.tok("dqst1")])
    p.op("y", "nop", waits=[p.tok("stkv0"), p.tok("stkv1"), p.tok("stimp"),
                            p.tok("stpo")] + dq_toks)
    p.finalize()
    return nc






def build_l0main(S=4096, SQ=1024):
    nT = S // 128
    nTB = S // 512
    nQ = SQ // 128
    TIDX = min(max(1, int(S * 0.2)), S - 1)
    nqh = max(1, SQ // 512)
    QW = min(512, SQ)

    nc = bass.Bass("TRN2", target_bir_lowering=False)
    p = Prog(nc)
    es = contextlib.ExitStack()

    x_d = nc.dram_tensor("x_own", [SQ, D], F32, kind="ExternalInput")
    q_d = nc.dram_tensor("q", [D, SQ], BF16, kind="ExternalInput")
    kf_d = nc.dram_tensor("kf", [D, S], BF16, kind="ExternalInput")
    vf_d = nc.dram_tensor("vf", [S, D], BF16, kind="ExternalInput")
    impf_d = nc.dram_tensor("impf", [128, nT], F32, kind="ExternalInput")
    wo_d = nc.dram_tensor("Wo", [D, D], F32, kind="ExternalInput")
    bo_d = nc.dram_tensor("bo", [1, D], F32, kind="ExternalInput")
    dsl_d = nc.dram_tensor("down_slq", [1024, D], F32, kind="ExternalInput")
    parts_d = nc.dram_tensor("parts", [8, 4], F32, kind="ExternalInput")
    x1_d = nc.dram_tensor("x1", [SQ, D], F32, kind="ExternalOutput")
    wdq_od = nc.dram_tensor("wdq_sl", [1024, D], BF16, kind="ExternalOutput")
    mr_d = nc.dram_tensor("mr_scr", [nT, 128], BF16)

    at_fm = nc.alloc_sbuf_tensor("at_fm", [128, 16 * SQ], BF16)
    ev = nc.alloc_sbuf_tensor("ev", [128, D], F32)
    ln_o = nc.alloc_sbuf_tensor("ln_o", [128, D], F32)
    x_t = [nc.alloc_sbuf_tensor(f"x_t{i}", [128, D], F32) for i in range(2)]
    bjunk = nc.alloc_sbuf_tensor("bjunk", [128, D], BF16)
    st = nc.alloc_sbuf_tensor("st", [128, 24], F32)
    imp_sb = nc.alloc_sbuf_tensor("imp_sb", [128, nT], F32)
    bis = nc.alloc_sbuf_tensor("bis", [128, 8], F32)
    bmask = nc.alloc_sbuf_tensor("bmask", [128, 2], I32)
    cnt_c = nc.alloc_sbuf_tensor("cnt_c", [128, 1], F32)
    c11 = nc.alloc_sbuf_tensor("c11", [1, 1], F32)
    mrow = nc.alloc_sbuf_tensor("mrow", [1, S], BF16)
    mbc = nc.alloc_sbuf_tensor("mbc", [128, S], BF16)
    mcol = nc.alloc_sbuf_tensor("mcol", [128, nT], F32)
    mcolb = nc.alloc_sbuf_tensor("mcolb", [128, nT], BF16)
    mstage = nc.alloc_sbuf_tensor("mstage", [128, 128], BF16)
    probs = [nc.alloc_sbuf_tensor(f"probs{i}", [128, 512], BF16)
             for i in range(2)]
    vmask = nc.alloc_sbuf_tensor("vmask", [128, S], BF16)
    onesb = nc.alloc_sbuf_tensor("onesb", [128, 128], BF16)
    dnr = nc.alloc_sbuf_tensor("dnr", [1, 512], F32)
    dnb = nc.alloc_sbuf_tensor("dnb", [128, 512], F32)
    bias_sb = nc.alloc_sbuf_tensor("bias_sb", [1, D], BF16)
    parts_sb = nc.alloc_sbuf_tensor("parts_sb", [8, 4], F32)
    srowq = nc.alloc_sbuf_tensor("srowq", [1, 3], F32)
    sdc = nc.alloc_sbuf_tensor("sdc", [128, 2], F32)
    wfd = [nc.alloc_sbuf_tensor(f"wfd{i}", [128, D], F32) for i in range(2)]
    q2d = [nc.alloc_sbuf_tensor(f"q2d{i}", [128, D], BF16) for i in range(2)]
    sad = nc.alloc_sbuf_tensor("sad", [128, D], BF16)

    tp_ps = [nc.alloc_psum_tensor(f"tp{i}", [128, 128], BF16) for i in range(2)]
    mm_ps = [nc.alloc_psum_tensor(f"mm{i}", [128, 512], F32) for i in range(4)]
    dn_ps = nc.alloc_psum_tensor("dn_ps", [1, 512], F32)
    db_ps = nc.alloc_psum_tensor("db_ps", [128, 512], F32)

    C = Consts(p, fvals=(LN_EPS, ATT_SCALE))
    mm_free = {}

    q_fm = es.enter_context(nc.sbuf_tensor("q_fm", [128, 16 * SQ], BF16))
    t_q = p.dma("y", chunk3(q_fm.ap(), SQ),
                q_d.ap().rearrange("(c p) t -> p c t", p=128), "ldq")
    t_if = p.dma("y", imp_sb[:, :], impf_d[:, :], "ldimp")

    DE = 8192
    t_pl = p.dma("s", parts_sb[:, :], parts_d[:, :], "ldp")
    t_tot = p.op("t", "matmul", dn_ps[0:1, 0:4], C.ones_cf[0:8, :],
                 parts_sb[0:8, :], start=True, stop=True, waits=[t_pl])
    t_sd = p.op("v", "tensor_scalar", srowq[0:1, 0:1], dn_ps[0:1, 2:3],
                1.0 / (DE * D), 1e-8, ALU.mult, ALU.add, waits=[t_tot])
    t_h = p.op("v", "tensor_scalar", srowq[0:1, 1:2], srowq[0:1, 0:1], 0.5,
               None, ALU.mult, waits=[t_sd])
    t_nh = p.op("v", "tensor_scalar", srowq[0:1, 2:3], srowq[0:1, 0:1],
                -0.5, None, ALU.mult, waits=[t_h])
    t_bb = p.op("t", "matmul", db_ps[:, 0:2], C.ones_rf[0:1, 0:128],
                srowq[0:1, 1:3], start=True, stop=True, waits=[t_nh])
    t_sdc = p.op("v", "tensor_copy", sdc[:, 0:2], db_ps[:, 0:2],
                 waits=[t_bb])
    mm_free["dn"] = t_sd
    mm_free["db"] = t_sdc

    t0 = p.op("v", "memset", bis[:, 0:1], -1000.0, waits=[t_if])
    t1 = p.op("v", "memset", bis[:, 1:2], 1000.0)
    prev = [t0, t1]
    for _ in range(50):
        t_a = p.op("v", "tensor_scalar", bis[:, 3:4], bis[:, 1:2], 0.5, None,
                   ALU.mult, waits=prev)
        t_mid = p.op("v", "scalar_tensor_tensor", bis[:, 2:3], bis[:, 0:1],
                     0.5, bis[:, 3:4], ALU.mult, ALU.add, waits=[t_a])
        t_c = p.op("v", "tensor_scalar", mcol[:, 0:nT], imp_sb[:, 0:nT],
                   bis[:, 2:3], None, ALU.is_gt, ALU.add,
                   accum_out=cnt_c[:, 0:1], waits=[t_mid])
        t_s = p.op("t", "matmul", dn_ps[0:1, 0:1], C.ones_cf[:, :],
                   cnt_c[:, 0:1], start=True, stop=True,
                   waits=[t_c] + ([mm_free["dn"]] if "dn" in mm_free
                                  else []))
        t_cp = p.op("v", "tensor_copy", c11[0:1, 0:1], dn_ps[0:1, 0:1],
                    waits=[t_s])
        mm_free["dn"] = t_cp
        t_bc = p.op("t", "matmul", db_ps[:, 0:1], C.ones_rf[0:1, 0:128],
                    c11[0:1, 0:1], start=True, stop=True,
                    waits=[t_cp] + ([mm_free["db"]] if "db" in mm_free
                                    else []))
        t_cd = p.op("v", "tensor_scalar", bmask[:, 0:1], db_ps[:, 0:1],
                    TIDX + 0.5, None, ALU.is_lt, waits=[t_bc])
        t_ic = p.op("v", "tensor_scalar", bmask[:, 1:2], db_ps[:, 0:1],
                    TIDX + 0.5, None, ALU.is_ge, waits=[t_cd])
        mm_free["db"] = t_ic
        t_h = p.op("v", "copy_predicated", bis[:, 1:2], bmask[:, 0:1],
                   bis[:, 2:3], waits=[t_ic])
        t_l = p.op("v", "copy_predicated", bis[:, 0:1], bmask[:, 1:2],
                   bis[:, 2:3], waits=[t_h])
        prev = [t_l]
    t_mc = p.op("v", "tensor_scalar", mcol[:, 0:nT], imp_sb[:, 0:nT],
                bis[:, 1:2], None, ALU.is_gt, waits=prev)
    t_mcb = p.op("v", "tensor_copy", mcolb[:, 0:nT], mcol[:, 0:nT],
                 waits=[t_mc])
    t_tr = p.op("t", "transpose", tp_ps[0][0:nT, :], mcolb[:, 0:nT],
                C.ident[:, :], waits=[t_mcb])
    t_ms = p.op("v", "tensor_copy", mstage[0:nT, :], tp_ps[0][0:nT, :],
                waits=[t_tr])
    t_st = p.dma("s", mr_d[:, :], mstage[0:nT, :], "mrb", waits=[t_ms])
    t_lr = p.dma("s", mrow[0:1, 0:S],
                 mr_d.ap().rearrange("(o a) b -> o (a b)", o=1), "mrb2",
                 waits=[t_st])
    for tb in range(nTB):
        t_b = p.op("t", "matmul", db_ps[:, :], C.ones_r[0:1, 0:128],
                   mrow[0:1, tb * 512:(tb + 1) * 512], start=True, stop=True,
                   waits=[t_lr, mm_free["db"]])
        mm_free["db"] = p.op("v", "tensor_copy",
                             mbc[:, tb * 512:(tb + 1) * 512], db_ps[:, :],
                             waits=[t_b])
    t_ob = p.op("v", "memset", onesb[:, :], 1.0)
    vm_toks = []
    for kc in range(nT):
        vm_toks.append(p.op("v", "tensor_scalar",
                            vmask[:, kc * 128:(kc + 1) * 128], onesb[:, :],
                            mcol[:, kc:kc + 1], None, ALU.mult,
                            waits=[t_ob, t_mc]))
    mask_done = [p.tok("_v")]


    with nc.sbuf_tensor("kh0", [128, S], BF16) as kh0, \
            nc.sbuf_tensor("kh1", [128, S], BF16) as kh1, \
            nc.sbuf_tensor("vh0", [128, S], BF16) as vh0, \
            nc.sbuf_tensor("vh1", [128, S], BF16) as vh1:
        kh = [kh0, kh1]
        vh = [vh0, vh1]
        kh_use = [None, None]
        kh_tok = [None, None]
        vh_tok = [None, None]
        for h in range(NH):
            b = h % 2
            t_kl = p.dma("y", kh[b][:, :], kf_d[h * 128:(h + 1) * 128, :],
                         f"lkh{b}", waits=([kh_use[b]] if kh_use[b] else []))
            t_vl = p.dma("g", chunk3(vh[b].ap(), 128, nch=nT),
                         vf_d.ap()[:, h * 128:(h + 1) * 128]
                         .rearrange("(kc p) o -> p kc o", p=128),
                         f"lvh{b}", waits=([kh_use[b]] if kh_use[b] else []))
            kh_tok[b] = p.op("v", "tensor_tensor", kh[b][:, :], kh[b][:, :],
                             mbc[:, :], ALU.mult,
                             waits=[t_kl] + mask_done)
            vh_tok[b] = p.op("v", "tensor_tensor", vh[b][:, :],
                             vh[b][:, :], vmask[:, :], ALU.mult,
                             waits=[t_vl] + mask_done)
            for qh in range(nqh):
                oidx = 2 + qh % 2
                o_bank = mm_ps[oidx]
                pr_use = [None, None]
                ex_tok = [None, None]
                t_dn = None
                t_av = None

                def emit_sc(kc):
                    pb = kc % 2
                    sc_bank = mm_ps[pb]
                    w_sc = [kh_tok[b], vh_tok[b], t_q]
                    if ex_tok[pb]:
                        w_sc.append(ex_tok[pb])
                    elif mm_free.get(pb):
                        w_sc.append(mm_free.get(pb))
                    return p.op("t", "matmul", sc_bank[:, 0:QW],
                                kh[b][:, kc * 128:(kc + 1) * 128],
                                q_fm[:, h * SQ + qh * 512:
                                     h * SQ + qh * 512 + QW],
                                start=True, stop=True, skip_group_check=True,
                                waits=w_sc)

                sc_tok = {0: emit_sc(0)}
                for kc in range(nT):
                    pb = kc % 2
                    sc_bank = mm_ps[pb]
                    ex_tok[pb] = p.op("s", "activation", probs[pb][:, 0:QW],
                                      sc_bank[:, 0:QW], AF.Exp,
                                      scale=float(ATT_SCALE),
                                      waits=[sc_tok.pop(kc)] +
                                      ([pr_use[pb]] if pr_use[pb] else []))
                    if kc + 1 < nT:
                        sc_tok[kc + 1] = emit_sc(kc + 1)
                    w_dn = [ex_tok[pb]]
                    if kc == 0:
                        if mm_free.get("dn"):
                            w_dn.append(mm_free["dn"])
                        if mm_free.get(oidx):
                            w_dn.append(mm_free[oidx])
                    t_dn = p.op("t", "matmul", dn_ps[0:1, 0:QW],
                                C.ones_c[:, :], probs[pb][:, 0:QW],
                                start=(kc == 0), stop=(kc == nT - 1),
                                skip_group_check=True, waits=w_dn)
                    t_av = p.op("t", "matmul", o_bank[:, 0:QW],
                                vh[b][:, kc * 128:(kc + 1) * 128],
                                probs[pb][:, 0:QW],
                                start=(kc == 0), stop=(kc == nT - 1),
                                skip_group_check=True)
                    pr_use[pb] = t_av
                mm_free[0] = ex_tok[0]
                mm_free[1] = ex_tok[1]
                t_dr = p.op("v", "reciprocal", dnr[0:1, 0:QW],
                            dn_ps[0:1, 0:QW], waits=[t_dn, t_av])
                mm_free["dn"] = t_dr
                t_db = p.op("t", "matmul", db_ps[:, 0:QW],
                            C.ones_rf[0:1, 0:128], dnr[0:1, 0:QW],
                            start=True, stop=True,
                            waits=[t_dr] + ([mm_free["db"]]
                                            if "db" in mm_free else []))
                t_e1 = p.op("s", "activation", ev[:, 0:QW], o_bank[:, 0:QW],
                            AF.Copy,
                            waits=[t_av] + ([mm_free["ev"]]
                                            if "ev" in mm_free else []))
                mm_free[oidx] = t_e1
                t_e2 = p.op("v", "tensor_copy", dnb[:, 0:QW], db_ps[:, 0:QW],
                            waits=[t_db])
                mm_free["db"] = t_e2
                mm_free["ev"] = p.op(
                    "v", "tensor_tensor",
                    at_fm[:, h * SQ + qh * 512:h * SQ + qh * 512 + QW],
                    ev[:, 0:QW], dnb[:, 0:QW], ALU.mult,
                    waits=[t_e1, t_e2])
            kh_use[b] = p.tok("_t")
    attn_done = [p.tok("_v"), p.tok("_s"), p.tok("_t")]
    es.close()

    f_use = {}
    dq_st = {}
    for ec in range(8):
        qb = ec % 2
        t_wdl = p.dma("s", chunk3(wfd[qb].ap(), 512, nch=4),
                      dsl_d[ec * 128:(ec + 1) * 128, :]
                      .rearrange("(c p) o -> p c o", p=128), f"lwd{qb}",
                      waits=[f_use.get(("d", qb))] if f_use.get(("d", qb))
                      else [t_pl])
        td1 = p.op("s", "activation", sad[:, :], wfd[qb][:, :], AF.Sign,
                   bias=sdc[:, 1:2],
                   waits=[t_wdl, t_sdc] +
                   ([f_use.get("sad")] if f_use.get("sad") else []))
        td2 = p.op("s", "activation", q2d[qb][:, :], wfd[qb][:, :],
                   AF.Sign, bias=sdc[:, 0:1],
                   waits=[dq_st.get(qb)] if dq_st.get(qb) else [])
        f_use[("d", qb)] = td2
        td3 = p.op("v", "tensor_tensor", q2d[qb][:, :], q2d[qb][:, :],
                   sad[:, :], ALU.add, waits=[td1, td2])
        f_use["sad"] = td3
        dq_st[qb] = p.dma("s", wdq_od[ec * 128:(ec + 1) * 128, :],
                          q2d[qb][:, :], f"stwd{qb}", waits=[td3])

    with nc.sbuf_tensor("wo_sb", [128, 16 * D], BF16) as wo_sb:
        t_wo = p.dma("g", chunk3(wo_sb.ap(), D), wcol_ap(wo_d, 0, D), "ldwo",
                     waits=attn_done)
        t_bo = p.dma("g", bias_sb[0:1, :], bo_d[0:1, :], "ldbias")
        x_use = [None, None]
        st_tok = [None, None]
        prev_ln = None
        for tc in range(nQ):
            xb = tc % 2
            ev_toks = []
            for o4 in range(4):
                bidx = o4 % 2
                bank = mm_ps[bidx]
                w0 = [t_wo, t_bo] + attn_done + \
                    ([mm_free.get(bidx)] if mm_free.get(bidx) else [])
                for c in range(16):
                    p.op("t", "matmul", bank[:, :],
                         at_fm[:, c * SQ + tc * 128:c * SQ + (tc + 1) * 128],
                         wo_sb[:, c * D + o4 * 512:c * D + (o4 + 1) * 512],
                         start=(c == 0), stop=False,
                         waits=w0 if c == 0 else ())
                t_mm = p.op("t", "matmul", bank[:, :], C.ones_r[0:1, 0:128],
                            bias_sb[0:1, o4 * 512:(o4 + 1) * 512],
                            start=False, stop=True)
                w_e = [t_mm]
                if o4 == 0 and prev_ln:
                    w_e.append(prev_ln)
                if o4 == 0 and "ev" in mm_free:
                    w_e.append(mm_free.pop("ev"))
                t_e = p.op("s", "activation", ev[:, o4 * 512:(o4 + 1) * 512],
                           bank[:, :], AF.Copy, waits=w_e)
                mm_free[bidx] = t_e
                ev_toks.append(t_e)
            t_ld = p.dma("y", x_t[xb][:, :], x_d[tc * 128:(tc + 1) * 128, :],
                         f"lre{xb}",
                         waits=([x_use[xb]] if x_use[xb] else []) +
                         ([st_tok[xb]] if st_tok[xb] else []) +
                         ([p.tok("_v")] if tc < 2 else []))
            t_n = emit_ln(p, ev[:, :], ln_o[:, :], bjunk[:, :], st,
                          waits=ev_toks + ([prev_ln] if prev_ln else []))
            prev_ln = t_n
            t_add = p.op("v", "tensor_tensor", x_t[xb][:, :], x_t[xb][:, :],
                         ln_o[:, :], ALU.add, waits=[t_n, t_ld])
            x_use[xb] = t_add
            st_tok[xb] = p.dma("y", x1_d[tc * 128:(tc + 1) * 128, :],
                               x_t[xb][:, :], f"stx{xb}", waits=[t_add])
    p.op("y", "nop", waits=[p.tok("stx0"), p.tok("stx1"), p.tok("stwd0"),
                            p.tok("stwd1")])
    p.finalize()
    return nc


if __name__ == "__main__":
    nc = build_l0_attn(2048, 512)
    print("build ok, instructions:", nc.n_instructions())
